# version 07
# speedup vs baseline: 2.1322x; 1.1410x over previous
import contextlib
import numpy as np
import ml_dtypes
import concourse.bass as bass
import concourse.mybir as mybir
from concourse.bass_utils import run_bass_kernel_spmd

F32 = mybir.dt.float32
BF16 = mybir.dt.bfloat16
AF = mybir.ActivationFunctionType
ALU = mybir.AluOpType
AX = mybir.AxisListType

NCORES = 8
B, S, D = 4, 4096, 1024
TOK = 2048
NBLK = 16
NCH = 8
RG_PAIRS = [[0, 1], [2, 3], [4, 5], [6, 7]]

ENGINES = ("sync", "scalar", "vector", "gpsimd", "tensor")


class _Item:
    __slots__ = ("attr", "args", "kw", "incs")

    def __init__(self, attr, args, kw):
        self.attr = attr
        self.args = args
        self.kw = kw
        self.incs = []


SEM_OF = {"scalar": "se", "vector": "ve", "gpsimd": "ge"}
NOCHAIN_ATTRS = ("wait_ge", "dma_start", "indirect_dma_start", "collective_compute")


class _ItemRec:
    def __init__(self, prog, item, own):
        self._prog = prog
        self._item = item
        self._own = own

    def then_inc(self, semname, k=1):
        if semname == self._own:
            return self
        self._prog.counts[semname] = self._prog.counts.get(semname, 0) + k
        self._item.incs.append((semname, k))
        return self


class _EngRec:
    def __init__(self, prog, eng):
        object.__setattr__(self, "_prog", prog)
        object.__setattr__(self, "_eng", eng)

    def wait(self, semname, val=None):
        p = self._prog
        v = p.counts.get(semname, 0) if val is None else val
        if v <= 0:
            return
        last = p.last_wait[self._eng].get(semname, -1)
        if v <= last:
            return
        p.last_wait[self._eng][semname] = v
        item = _Item("wait_ge", (("__SEM__", semname), v), {})
        p.streams[self._eng].append(item)

    def __getattr__(self, attr):
        prog, eng = self._prog, self._eng
        own = SEM_OF.get(eng)

        def call(*args, **kw):
            item = _Item(attr, args, kw)
            prog.streams[eng].append(item)
            if own is not None and attr not in NOCHAIN_ATTRS:
                prog.counts[own] = prog.counts.get(own, 0) + 1
            return _ItemRec(prog, item, own if attr not in NOCHAIN_ATTRS else None)

        return call


class Prog:
    def __init__(self, nc):
        self.nc = nc
        self.streams = {e: [] for e in ENGINES}
        self.counts = {}
        self.sems = {}
        self.last_wait = {e: {} for e in ENGINES}
        for e in ENGINES:
            setattr(self, e, _EngRec(self, e))
        self.psB = None

    def val(self, semname):
        return self.counts.get(semname, 0)

    def finalize(self, stack):
        names = set(self.counts)
        for st in self.streams.values():
            for it in st:
                if it.attr == "wait_ge":
                    names.add(it.args[0][1])
        for n in sorted(names):
            self.sems[n] = stack.enter_context(self.nc.semaphore(n))

        def resolve(a):
            if isinstance(a, tuple) and len(a) == 2 and a[0] == "__SEM__":
                return self.sems[a[1]]
            return a

        def replay(stream, ename):
            own = SEM_OF.get(ename)
            auto = self.sems.get(own) if own else None

            def run(eng):
                n = 0
                for it in stream:
                    if it.attr == "wait_ge":
                        eng.wait_ge(resolve(it.args[0]), it.args[1])
                        continue
                    chained = auto is not None and it.attr not in NOCHAIN_ATTRS
                    if auto is not None and n > 0:
                        eng.wait_ge(auto, n)
                    instr = getattr(eng, it.attr)(*[resolve(a) for a in it.args],
                                                  **{k: resolve(v) for k, v in it.kw.items()})
                    if chained:
                        instr = instr.then_inc(auto, 1)
                        n += 1
                    for (sn, k) in it.incs:
                        instr = instr.then_inc(self.sems[sn], k)
            return run

        with self.nc.Block() as block:
            for e in ENGINES:
                if self.streams[e]:
                    getattr(block, e)(replay(self.streams[e], e))

    def nops(self):
        return sum(len(s) for s in self.streams.values())


class Arena:
    BYTES = 207 * 1024 + 512

    def __init__(self, nc, stack):
        self.t = stack.enter_context(
            nc.sbuf_tensor("arena", [128, self.BYTES // 2], BF16))
        self.regions = {}
        self.frees = [(0, self.BYTES)]

    def alloc(self, name, nbytes, align=64, at=None):
        assert name not in self.regions, name
        nbytes = (nbytes + align - 1) // align * align
        if at is not None:
            for i, (off, sz) in enumerate(self.frees):
                if off <= at and at + nbytes <= off + sz:
                    pre = (off, at - off)
                    post = (at + nbytes, off + sz - at - nbytes)
                    self.frees[i:i + 1] = [q for q in (pre, post) if q[1] > 0]
                    self.regions[name] = (at, nbytes)
                    ARENA_TRACE[name] = at
                    return at
            raise MemoryError(f"arena: cannot place {name} at {at} "
                              f"({nbytes}B); free={self.frees}")
        for i in range(len(self.frees) - 1, -1, -1):
            off, sz = self.frees[i]
            aoff = (off + sz - nbytes) // align * align
            if aoff >= off:
                pre = (off, aoff - off)
                post = (aoff + nbytes, off + sz - aoff - nbytes)
                self.frees[i:i + 1] = [q for q in (pre, post) if q[1] > 0]
                self.regions[name] = (aoff, nbytes)
                ARENA_TRACE[name] = aoff
                return aoff
        raise MemoryError(f"arena: cannot alloc {name} ({nbytes}B); "
                          f"free={self.frees}, used={sorted(self.regions)}")

    def free(self, name):
        off, sz = self.regions.pop(name)
        self.frees.append((off, sz))
        self.frees.sort()
        merged = []
        for off, sz in self.frees:
            if merged and merged[-1][0] + merged[-1][1] == off:
                merged[-1] = (merged[-1][0], merged[-1][1] + sz)
            else:
                merged.append((off, sz))
        self.frees = [tuple(m) for m in merged]

    def view(self, name, dtype, cols, part=128, base=0, col_off=0):
        off, sz = self.regions[name]
        item = 2 if dtype == BF16 else 4
        assert (col_off + cols) * item <= sz, (name, col_off, cols, sz)
        start2 = (off + col_off * item) // 2
        ap = self.t[base:base + part, start2:start2 + cols * item // 2]
        if dtype != BF16:
            ap = ap.bitcast(dtype)
        return ap


ARENA_TRACE = {}

def barrier(p):
    names = list(p.counts.keys())
    for e in ENGINES:
        er = getattr(p, e)
        for s in names:
            er.wait(s)


def cdiv(a, b):
    return -(-a // b)


def load_const_sbuf(p, ar, A, name, shape, dtype, queue="sync", sem="dsy"):
    rows, cols = shape
    item = 4 if dtype == F32 else 2
    ar.alloc(name, cols * item)
    dst = ar.view(name, dtype, cols, part=rows)
    getattr(p, queue).dma_start(out=dst, in_=A[name]).then_inc(sem, 16)


def emit_ln(p, ar, A, want_router, want_l2, gate_region="gateT", rbias=None):
    if "XN" not in ar.regions:
        ar.alloc("XN", 8 * 2048 * 2, at=64 * 1024)
    p.tensor.wait("dsy")
    p.vector.wait("dsy")
    p.scalar.wait("dsy")
    ar.alloc("SQ", 2 * 512 * 4)
    ar.alloc("XT", 2 * 512 * 4)
    ar.alloc("STR", 6 * 2048 * 4, at=96 * 1024)
    ar.alloc("XS", 8 * 512 * 4, at=96 * 1024 + 49152)
    if want_l2:
        ar.alloc("XN2", 8 * 2048 * 2)
    if want_router:
        ar.alloc("LOG", 16 * 8 * 4)

    def strow(i, sl=None):
        co = i * 2048
        if sl is None:
            return ar.view("STR", F32, 2048, part=1, col_off=co)
        return ar.view("STR", F32, 512, part=1, col_off=co + sl * 512)

    ones_col = ar.view("c_ones_col", F32, 1)
    ones_row = ar.view("c_ones_row", F32, 128, part=1)
    PSx = p.psB

    for sl in range(4):
        p.tensor.wait("ve")
        for c in range(NCH):
            sq = ar.view("SQ", F32, 512, col_off=(c % 2) * 512)
            xsl = ar.view("XF", F32, 512, col_off=c * 2048 + sl * 512)
            p.scalar.wait("te")
            p.scalar.activation(sq, xsl, AF.Square).then_inc("se", 1)
            p.tensor.wait("se")
            p.tensor.matmul(PSx[0][0:1, 0:512], ones_col, xsl,
                            start=(c == 0), stop=(c == NCH - 1))
            mm = p.tensor.matmul(PSx[0][32:33, 0:512], ones_col, sq,
                                 start=(c == 0), stop=(c == NCH - 1))
            mm.then_inc("te", 1)
        p.vector.wait("te")
        p.vector.tensor_copy(strow(0, sl), PSx[0][0:1, 0:512])
        p.vector.tensor_copy(strow(1, sl), PSx[0][32:33, 0:512]).then_inc("ve", 1)

    p.vector.wait("ve")
    p.vector.tensor_scalar(strow(2), strow(0), 1.0 / D, None, op0=ALU.mult)
    p.vector.tensor_scalar(strow(5), strow(1), 1.0 / D, None, op0=ALU.mult)
    p.vector.tensor_tensor(strow(4), strow(2), strow(2), op=ALU.mult)
    p.vector.tensor_tensor(strow(5), strow(5), strow(4), op=ALU.subtract)
    p.vector.tensor_scalar(strow(4), strow(5), 1e-5, None, op0=ALU.add).then_inc("ve", 1)
    p.scalar.wait("ve")
    p.scalar.activation(strow(4), strow(4), AF.Sqrt).then_inc("se", 1)
    p.vector.wait("se")
    p.vector.reciprocal(strow(3), strow(4)).then_inc("ve", 1)
    if want_l2:
        p.vector.tensor_scalar(strow(5), strow(5), float(D), None, op0=ALU.mult).then_inc("ve", 1)
        p.scalar.wait("ve")
        p.scalar.activation(strow(5), strow(5), AF.Sqrt).then_inc("se", 1)
        p.vector.wait("se")
        p.vector.tensor_tensor(strow(5), strow(5), strow(3), op=ALU.mult)
        p.vector.tensor_scalar(strow(5), strow(5), 1e-12, None, op0=ALU.max)
        p.vector.reciprocal(strow(5), strow(5))
        p.vector.tensor_scalar(strow(5), strow(5), float(np.sqrt(D)), None, op0=ALU.mult)
        p.vector.tensor_tensor(strow(4), strow(5), strow(3), op=ALU.mult).then_inc("ve", 1)

    for sl in range(4):
        p.tensor.wait("ve")
        p.tensor.matmul(PSx[1][:, 0:512], ones_row, strow(2, sl), start=True, stop=True)
        mm = p.tensor.matmul(PSx[2][:, 0:512], ones_row, strow(3, sl), start=True, stop=True)
        if want_l2:
            mm = p.tensor.matmul(PSx[4][:, 0:512], ones_row, strow(4, sl), start=True, stop=True)
        mm.then_inc("te", 1)
        p.vector.wait("te")
        if want_router:
            p.vector.wait("te", p.val("te"))
        for c in range(NCH):
            xsl = ar.view("XF", F32, 512, col_off=c * 2048 + sl * 512)
            xt = ar.view("XT", F32, 512, col_off=(c % 2) * 512)
            xn = ar.view("XN", BF16, 512, col_off=c * 2048 + sl * 512)
            xs = ar.view("XS", F32, 512, col_off=c * 512)
            p.vector.tensor_tensor(xt, xsl, PSx[1][:, 0:512], op=ALU.subtract)
            p.vector.tensor_tensor(xs, xt, PSx[2][:, 0:512], op=ALU.mult)
            p.vector.tensor_copy(xn, xs).then_inc("ve", 1)
            if want_l2:
                xn2 = ar.view("XN2", BF16, 512, col_off=c * 2048 + sl * 512)
                p.vector.tensor_tensor(xn2, xt, PSx[4][:, 0:512], op=ALU.mult).then_inc("ve", 1)
        if want_router:
            gate = ar.view(gate_region, F32, 64)
            p.tensor.wait("ve")
            for tb in range(4):
                for c in range(NCH):
                    mm = p.tensor.matmul(
                        PSx[3][:, (sl * 4 + tb) * 8:(sl * 4 + tb) * 8 + 8],
                        ar.view("XS", F32, 128, col_off=c * 512 + tb * 128),
                        gate[:, c * 8:(c + 1) * 8],
                        start=(c == 0), stop=rbias is None and (c == NCH - 1))
                if rbias is not None:
                    ones_r = ar.view("c_ones_row", F32, 128, part=1)
                    mm = p.tensor.matmul(
                        PSx[3][:, (sl * 4 + tb) * 8:(sl * 4 + tb) * 8 + 8],
                        ones_r, ar.view(rbias, F32, 8, part=1),
                        start=False, stop=True)
            mm.then_inc("te", 1)
    if want_router:
        p.vector.wait("te")
        p.vector.tensor_copy(ar.view("LOG", F32, 128), PSx[3][:, 0:128]).then_inc("ve", 1)
    ar.free("SQ"); ar.free("XT"); ar.free("STR"); ar.free("XS")


def emit_top2(p, ar, negbig=-1e30, mode="top2", save_tok=False):
    if mode == "argmax":
        ar.alloc("MTOK", 16 * 8 * 4)
    if save_tok:
        ar.alloc("M1T", 16 * 8 * 4)
        ar.alloc("MALL", 16 * 8 * 4)
        ar.alloc("WTF", 16 * 8 * 4)
    ar.alloc("TOPS", 6 * 16 * 4)
    ar.alloc("TSC", 2 * 8 * 4)
    ar.alloc("WTOK", 16 * 8 * 2)
    identb = ar.view("c_identb", BF16, 128)

    def tcol(i, t):
        return ar.view("TOPS", F32, 1, col_off=i * 16 + t)

    for t in range(16):
        lg = ar.view("LOG", F32, 8, col_off=t * 8)
        m1 = ar.view("TSC", F32, 8)
        l2 = ar.view("TSC", F32, 8, col_off=8)
        wt = ar.view("WTOK", BF16, 8, col_off=t * 8)
        p.vector.wait("ve")
        p.vector.reduce_max(tcol(0, t), lg, axis=AX.X)
        if mode == "argmax":
            p.vector.tensor_scalar(wt, lg, tcol(0, t), None, op0=ALU.is_ge)
            p.vector.tensor_scalar(ar.view("MTOK", F32, 8, col_off=t * 8), lg,
                                   tcol(0, t), None, op0=ALU.is_ge).then_inc("ve", 1)
            continue
        p.vector.tensor_scalar(tcol(1, t), tcol(0, t), -1.0, None, op0=ALU.mult)
        p.vector.tensor_scalar(m1, lg, tcol(0, t), None, op0=ALU.is_ge)
        if save_tok:
            p.vector.tensor_copy(ar.view("M1T", F32, 8, col_off=t * 8), m1)
        p.vector.scalar_tensor_tensor(l2, m1, negbig, lg, op0=ALU.mult, op1=ALU.add)
        p.vector.reduce_max(tcol(2, t), l2, axis=AX.X).then_inc("ve", 1)
        p.scalar.wait("ve")
        p.scalar.activation(l2, lg, AF.Exp, bias=tcol(1, t)).then_inc("se", 1)
        p.vector.wait("se")
        p.vector.tensor_scalar(m1, lg, tcol(2, t), None, op0=ALU.is_ge)
        if save_tok:
            p.vector.tensor_copy(ar.view("MALL", F32, 8, col_off=t * 8), m1)
        p.vector.tensor_tensor(m1, m1, l2, op=ALU.mult)
        p.vector.reduce_sum(tcol(3, t), m1, axis=AX.X)
        p.vector.reciprocal(tcol(4, t), tcol(3, t))
        p.vector.tensor_scalar(wt, m1, tcol(4, t), None, op0=ALU.mult)
        if save_tok:
            p.vector.tensor_scalar(ar.view("WTF", F32, 8, col_off=t * 8), m1,
                                   tcol(4, t), None, op0=ALU.mult).then_inc("ve", 1)
        else:
            p.vector.tensor_copy(wt, wt).then_inc("ve", 1)
    ar.alloc("WROWS", 2048 * 2)
    ptb = p.psPT
    for t in range(16):
        p.tensor.wait("ve")
        mm = p.tensor.transpose(ptb[0:8, (t % 4) * 128:(t % 4) * 128 + 128],
                                ar.view("WTOK", BF16, 8, col_off=t * 8), identb)
        mm.then_inc("te", 1)
        p.vector.wait("te")
        p.vector.tensor_copy(ar.view("WROWS", BF16, 128, part=8, col_off=t * 128),
                             ptb[0:8, (t % 4) * 128:(t % 4) * 128 + 128]).then_inc("ve", 1)
    ar.free("TOPS"); ar.free("TSC"); ar.free("WTOK")


def emit_l0_ffn(p, ar, A):
    E, F = 8, 2048
    HS = 256
    NSL = F // HS
    NTS = 4
    NHC = HS // 128
    ar.alloc("Y", 8 * 2048 * 4, at=96 * 1024)
    ar.alloc("WG", 2 * NCH * HS * 2)
    ar.alloc("WU", 2 * NCH * HS * 2)
    ar.alloc("WD", 2 * NHC * 1024 * 2)
    ar.alloc("SGS", 512 * 2)
    ar.alloc("UWS", 512 * 2)
    ar.alloc("HWS", NHC * 512 * 2)
    ar.alloc("WBC", 512 * 4)
    PSx = p.psB
    bchot = ar.view("c_bchot", BF16, 8 * 128, part=8)
    wrows = ar.view("WROWS", BF16, 2048, part=8)

    def wslab_dma(e, s, buf):
        sem = "dgW%d" % buf
        src_g = A["wg"][e].rearrange("(c q) f -> q c f", q=128)[:, :, s * HS:(s + 1) * HS]
        src_u = A["wu"][e].rearrange("(c q) f -> q c f", q=128)[:, :, s * HS:(s + 1) * HS]
        src_d = A["wd"][e].rearrange("(c q) f -> q c f", q=128)[:, s * NHC:(s + 1) * NHC, :]
        p.gpsimd.dma_start(out=ar.view("WG", BF16, NCH * HS, col_off=buf * NCH * HS),
                           in_=src_g).then_inc(sem, 16)
        p.gpsimd.dma_start(out=ar.view("WU", BF16, NCH * HS, col_off=buf * NCH * HS),
                           in_=src_u).then_inc(sem, 16)
        p.gpsimd.dma_start(out=ar.view("WD", BF16, NHC * 1024, col_off=buf * NHC * 1024),
                           in_=src_d).then_inc(sem, 16)

    tmarks = {}
    wslab_dma(0, 0, 0)
    for gi in range(E * NSL):
        e, s = gi // NSL, gi % NSL
        buf = gi % 2
        if gi + 1 < E * NSL:
            if gi >= 1:
                p.gpsimd.wait("te", tmarks[gi - 1])
            wslab_dma((gi + 1) // NSL, (gi + 1) % NSL, (gi + 1) % 2)
        wg = ar.view("WG", BF16, NCH * HS, col_off=buf * NCH * HS)
        wu = ar.view("WU", BF16, NCH * HS, col_off=buf * NCH * HS)
        wd = ar.view("WD", BF16, NHC * 1024, col_off=buf * NHC * 1024)
        p.tensor.wait("dgW%d" % buf)
        for ts in range(NTS):
            tc = ts * 512
            p.tensor.wait("ve")
            p.tensor.matmul(PSx[6][:, 0:512], bchot[:, e * 128:(e + 1) * 128],
                            wrows[:, tc:tc + 512], start=True, stop=True).then_inc("te", 1)
            p.vector.wait("te")
            p.vector.tensor_copy(ar.view("WBC", F32, 512), PSx[6][:, 0:512]).then_inc("ve", 1)
            for hc in range(NHC):
                p.tensor.wait("ve")
                p.tensor.wait("ge")
                for c in range(NCH):
                    xn = ar.view("XN", BF16, 512, col_off=c * 2048 + tc)
                    p.tensor.matmul(PSx[0][:, 0:512],
                                    wg[:, c * HS + hc * 128: c * HS + hc * 128 + 128],
                                    xn, start=(c == 0), stop=(c == NCH - 1))
                    mm = p.tensor.matmul(PSx[1][:, 0:512],
                                         wu[:, c * HS + hc * 128: c * HS + hc * 128 + 128],
                                         xn, start=(c == 0), stop=(c == NCH - 1))
                mm.then_inc("te", 1)
                p.scalar.wait("te")
                p.scalar.wait("ge")
                p.scalar.activation(ar.view("SGS", BF16, 512), PSx[0][:, 0:512],
                                    AF.Silu).then_inc("se", 1)
                p.vector.wait("te")
                p.vector.wait("ge")
                p.vector.tensor_tensor(ar.view("UWS", BF16, 512), PSx[1][:, 0:512],
                                       ar.view("WBC", F32, 512), op=ALU.mult).then_inc("ve", 1)
                p.gpsimd.wait("se")
                p.gpsimd.wait("ve")
                p.gpsimd.wait("te")
                p.gpsimd.tensor_tensor(ar.view("HWS", BF16, 512, col_off=hc * 512),
                                       ar.view("SGS", BF16, 512),
                                       ar.view("UWS", BF16, 512), op=ALU.mult).then_inc("ge", 1)
            p.tensor.wait("ge")
            marks = []
            for dch in range(NCH):
                bank = PSx[2 + dch % 2][:, 0:512]
                if dch >= 2:
                    p.tensor.wait("ve", marks[dch - 2])
                for hc in range(NHC):
                    mm = p.tensor.matmul(
                        bank, wd[:, hc * 1024 + dch * 128: hc * 1024 + dch * 128 + 128],
                        ar.view("HWS", BF16, 512, col_off=hc * 512),
                        start=(hc == 0), stop=(hc == NHC - 1))
                mm.then_inc("te", 1)
                p.vector.wait("te")
                yv = ar.view("Y", F32, 512, col_off=dch * 2048 + tc)
                if gi == 0:
                    p.vector.tensor_copy(yv, bank).then_inc("ve", 1)
                else:
                    p.vector.tensor_tensor(yv, yv, bank, op=ALU.add).then_inc("ve", 1)
                marks.append(p.val("ve"))
        tmarks[gi] = p.val("te")
    ar.free("WG"); ar.free("WU"); ar.free("WD")
    ar.free("SGS"); ar.free("UWS"); ar.free("HWS"); ar.free("WBC")
    ar.free("WROWS"); ar.free("LOG")


def emit_l0_ffn_sparse(p, ar, A):
    PSx = p.psB
    ptb = p.psPT
    C = 768
    NSEC = 2
    SECS = [(0, 512), (512, 256)]
    IOX = bass.IndirectOffsetOnAxis
    U32 = mybir.dt.uint32
    identb = ar.view("c_identb", BF16, 128)
    tris = ar.view("c_tris", F32, 128)
    onc = ar.view("c_ones_col", F32, 1)
    onr = ar.view("c_ones_row", F32, 128, part=1)
    ar.alloc("IDXA", 16 * 4)
    ar.alloc("IDXB", 16 * 4)
    ar.alloc("WCA", 16 * 4)
    ar.alloc("WCB", 16 * 4)
    ar.alloc("ISC", 4 * 8 * 4)
    ar.alloc("P2R", 128 * 4)
    p.vector.wait("se")
    p.vector.tensor_copy(ar.view("P2R", F32, 8, part=1), ar.view("c_ebase0", F32, 8, part=1))
    for k in range(16):
        mall = ar.view("MALL", F32, 8, col_off=k * 8)
        m1 = ar.view("M1T", F32, 8, col_off=k * 8)
        wtk = ar.view("WTF", F32, 8, col_off=k * 8)
        p.tensor.wait("ve")
        p.tensor.matmul(PSx[4][0:1, k * 8:k * 8 + 8], onc, mall, start=True, stop=True)
        p.tensor.matmul(PSx[5][:, 0:8], tris, mall, start=True, stop=False)
        mm = p.tensor.matmul(PSx[5][:, 0:8], onr,
                             ar.view("P2R", F32, 8, part=1), start=False, stop=True)
        mm.then_inc("te", 1)
        p.vector.wait("te")
        sc = ar.view("ISC", F32, 8)
        sc2 = ar.view("ISC", F32, 8, col_off=16)
        p.vector.tensor_tensor(sc, m1, PSx[5][:, 0:8], op=ALU.mult)
        p.vector.reduce_sum(ar.view("ISC", F32, 1, col_off=8), sc, axis=AX.X)
        p.vector.tensor_scalar(ar.view("ISC", F32, 1, col_off=8),
                               ar.view("ISC", F32, 1, col_off=8), 0.5, None, op0=ALU.add)
        p.vector.tensor_copy(ar.view("IDXA", U32, 1, col_off=k),
                             ar.view("ISC", F32, 1, col_off=8))
        p.vector.tensor_tensor(sc2, mall, m1, op=ALU.subtract)
        p.vector.tensor_tensor(sc2, sc2, PSx[5][:, 0:8], op=ALU.mult)
        p.vector.reduce_sum(ar.view("ISC", F32, 1, col_off=9), sc2, axis=AX.X)
        p.vector.tensor_scalar(ar.view("ISC", F32, 1, col_off=9),
                               ar.view("ISC", F32, 1, col_off=9), 0.5, None, op0=ALU.add)
        p.vector.tensor_copy(ar.view("IDXB", U32, 1, col_off=k),
                             ar.view("ISC", F32, 1, col_off=9))
        p.vector.tensor_tensor(sc, m1, wtk, op=ALU.mult)
        p.vector.reduce_sum(ar.view("WCA", F32, 1, col_off=k), sc, axis=AX.X)
        p.vector.tensor_tensor(sc2, mall, wtk, op=ALU.mult)
        p.vector.reduce_sum(ar.view("WCB", F32, 1, col_off=k), sc2, axis=AX.X)
        p.vector.tensor_tensor(ar.view("WCB", F32, 1, col_off=k),
                               ar.view("WCB", F32, 1, col_off=k),
                               ar.view("WCA", F32, 1, col_off=k), op=ALU.subtract)
        p.vector.tensor_tensor(ar.view("P2R", F32, 8, part=1),
                               ar.view("P2R", F32, 8, part=1),
                               PSx[4][0:1, k * 8:k * 8 + 8], op=ALU.add).then_inc("ve", 1)
    ar.alloc("XST", 2 * 1024 * 2)
    p.vector.memset(ar.view("XST", BF16, 2048), 0.0).then_inc("ve", 1)
    p.scalar.wait("ve")
    for j in range(48):
        p.scalar.dma_start(out=A["xgscr0"][j * 128:(j + 1) * 128, :],
                           in_=ar.view("XST", BF16, 1024)).then_inc("dgZ", 16)
    p.vector.wait("dgZ")
    for k in range(16):
        par = k % 2
        xst = ar.view("XST", BF16, 1024, col_off=par * 1024)
        for c in range(NCH):
            p.tensor.wait("ve")
            if c == 0:
                p.tensor.wait("dgS%d" % par)
            mm = p.tensor.transpose(ptb[:, (c % 2) * 128:(c % 2) * 128 + 128],
                                    ar.view("XN", BF16, 128, col_off=c * 2048 + k * 128),
                                    identb)
            mm.then_inc("te", 1)
            p.vector.wait("te")
            p.vector.tensor_copy(xst[:, c * 128:(c + 1) * 128],
                                 ptb[:, (c % 2) * 128:(c % 2) * 128 + 128]).then_inc("ve", 1)
        p.gpsimd.wait("ve")
        p.gpsimd.indirect_dma_start(out=A["xgscr0"],
                                    out_offset=IOX(ar.view("IDXA", U32, 1, col_off=k), 0),
                                    in_=xst, in_offset=None).then_inc("dgS%d" % par, 16)
        p.gpsimd.indirect_dma_start(out=A["xgscr0"],
                                    out_offset=IOX(ar.view("IDXB", U32, 1, col_off=k), 0),
                                    in_=xst, in_offset=None).then_inc("dgS%d" % par, 16)
    ar.free("XN")
    HS = 256
    NSL = 2048 // HS
    NHC = HS // 128
    ar.alloc("XGE", 2 * NCH * C * 2, at=64 * 1024)
    ar.alloc("WG", 2 * NCH * HS * 2)
    ar.alloc("WU", 2 * NCH * HS * 2)
    ar.alloc("WD", 2 * NHC * 1024 * 2)
    ar.alloc("SGS", 512 * 2)
    ar.alloc("UWS", 512 * 2)
    ar.alloc("HWS", NHC * 512 * 2)
    ar.alloc("YGS", NCH * C * 4, at=96 * 1024)
    ar.alloc("YGB", NCH * C * 2)
    ar.alloc("YGT", 2 * 1024 * 2)

    def wslab(e, s, buf):
        sem = "dgW%d" % buf
        p.gpsimd.dma_start(out=ar.view("WG", BF16, NCH * HS, col_off=buf * NCH * HS),
                           in_=A["wg"][e].rearrange("(c q) f -> q c f", q=128)[:, :, s * HS:(s + 1) * HS]).then_inc(sem, 16)
        p.gpsimd.dma_start(out=ar.view("WU", BF16, NCH * HS, col_off=buf * NCH * HS),
                           in_=A["wu"][e].rearrange("(c q) f -> q c f", q=128)[:, :, s * HS:(s + 1) * HS]).then_inc(sem, 16)
        p.gpsimd.dma_start(out=ar.view("WD", BF16, NHC * 1024, col_off=buf * NHC * 1024),
                           in_=A["wd"][e].rearrange("(c q) f -> q c f", q=128)[:, s * NHC:(s + 1) * NHC, :]).then_inc(sem, 16)

    def xge_load(e, buf):
        for c in range(NCH):
            p.scalar.dma_start(
                out=ar.view("XGE", BF16, C, col_off=buf * NCH * C + c * C),
                in_=A["xgscr0"][e * C:(e + 1) * C, c * 128:(c + 1) * 128],
                transpose=True).then_inc("dgX%d" % buf, 16)

    tmarks = {}
    dmarks = []
    wslab(0, 0, 0)
    p.scalar.wait("dgS0")
    p.scalar.wait("dgS1")
    xge_load(0, 0)
    for e in range(8):
        ebuf = e % 2
        if e + 1 < 8:
            p.scalar.wait("te", tmarks.get(e - 1, 0))
            xge_load(e + 1, (e + 1) % 2)
        p.tensor.wait("dgX%d" % ebuf)
        for s in range(NSL):
            gi = e * NSL + s
            buf = gi % 2
            if gi + 1 < 8 * NSL:
                if gi >= 1:
                    p.gpsimd.wait("te", tmarks.get((gi - 1, "w"), 0))
                wslab((gi + 1) // NSL, (gi + 1) % NSL, (gi + 1) % 2)
            wg = ar.view("WG", BF16, NCH * HS, col_off=buf * NCH * HS)
            wu = ar.view("WU", BF16, NCH * HS, col_off=buf * NCH * HS)
            wd = ar.view("WD", BF16, NHC * 1024, col_off=buf * NHC * 1024)
            p.tensor.wait("dgW%d" % buf)
            for off, n in SECS:
                for hc in range(NHC):
                    p.tensor.wait("ve")
                    p.tensor.wait("ge")
                    for c in range(NCH):
                        xg = ar.view("XGE", BF16, n, col_off=ebuf * NCH * C + c * C + off)
                        p.tensor.matmul(PSx[0][:, 0:n],
                                        wg[:, c * HS + hc * 128: c * HS + hc * 128 + 128],
                                        xg, start=(c == 0), stop=(c == NCH - 1))
                        mm = p.tensor.matmul(PSx[1][:, 0:n],
                                             wu[:, c * HS + hc * 128: c * HS + hc * 128 + 128],
                                             xg, start=(c == 0), stop=(c == NCH - 1))
                    mm.then_inc("te", 1)
                    p.scalar.wait("te")
                    p.scalar.wait("ge")
                    p.scalar.activation(ar.view("SGS", BF16, n), PSx[0][:, 0:n],
                                        AF.Silu).then_inc("se", 1)
                    p.vector.wait("te")
                    p.vector.wait("ge")
                    p.vector.tensor_copy(ar.view("UWS", BF16, n), PSx[1][:, 0:n]).then_inc("ve", 1)
                    p.gpsimd.wait("se")
                    p.gpsimd.wait("ve")
                    p.gpsimd.wait("te")
                    p.gpsimd.tensor_tensor(ar.view("HWS", BF16, n, col_off=hc * 512),
                                           ar.view("SGS", BF16, n),
                                           ar.view("UWS", BF16, n), op=ALU.mult).then_inc("ge", 1)
                p.tensor.wait("ge")
                for dch in range(NCH):
                    bank = PSx[2 + dch % 2][:, 0:n]
                    if len(dmarks) >= 2:
                        p.tensor.wait("ve", dmarks[-2])
                    for hc in range(NHC):
                        mm = p.tensor.matmul(
                            bank, wd[:, hc * 1024 + dch * 128: hc * 1024 + dch * 128 + 128],
                            ar.view("HWS", BF16, n, col_off=hc * 512),
                            start=(hc == 0), stop=(hc == NHC - 1))
                    mm.then_inc("te", 1)
                    p.vector.wait("te")
                    yv = ar.view("YGS", F32, n, col_off=dch * C + off)
                    if s == 0:
                        p.vector.tensor_copy(yv, bank).then_inc("ve", 1)
                    else:
                        p.vector.tensor_tensor(yv, yv, bank, op=ALU.add).then_inc("ve", 1)
                    dmarks.append(p.val("ve"))
            tmarks[(gi, "w")] = p.val("te")
        p.vector.wait("ve")
        for dch in range(NCH):
            p.vector.tensor_copy(ar.view("YGB", BF16, C, col_off=dch * C),
                                 ar.view("YGS", F32, C, col_off=dch * C)).then_inc("ve", 1)
        for st in range(C // 128):
            ygt = ar.view("YGT", BF16, 1024, col_off=(st % 2) * 1024)
            for dch in range(NCH):
                p.tensor.wait("ve")
                if dch == 0:
                    p.tensor.wait("dsY", None)
                mm = p.tensor.transpose(ptb[:, (dch % 2) * 128:(dch % 2) * 128 + 128],
                                        ar.view("YGB", BF16, 128,
                                                col_off=dch * C + st * 128), identb)
                mm.then_inc("te", 1)
                p.vector.wait("te")
                p.vector.tensor_copy(ygt[:, dch * 128:(dch + 1) * 128],
                                     ptb[:, (dch % 2) * 128:(dch % 2) * 128 + 128]).then_inc("ve", 1)
            p.sync.wait("ve")
            p.sync.dma_start(out=A["ygscr0"][e * C + st * 128: e * C + (st + 1) * 128, :],
                             in_=ygt).then_inc("dsY", 16)
        tmarks[e] = p.val("te")
    ar.free("WG"); ar.free("WU"); ar.free("WD"); ar.free("SGS"); ar.free("UWS")
    ar.free("HWS"); ar.free("XGE"); ar.free("YGS"); ar.free("YGB"); ar.free("YGT")
    ar.free("XST")
    ar.alloc("Y", 8 * 2048 * 4, at=96 * 1024)
    ar.alloc("YTA", 2 * 1024 * 2)
    ar.alloc("YTB", 2 * 1024 * 2)
    ar.alloc("YSM", 2 * 1024 * 2)
    gmarks = []
    for k in range(16):
        par = k % 2
        p.gpsimd.wait("dsY")
        if len(gmarks) >= 2:
            p.gpsimd.wait("ve", gmarks[-2])
        yta = ar.view("YTA", BF16, 1024, col_off=par * 1024)
        ytb = ar.view("YTB", BF16, 1024, col_off=par * 1024)
        p.gpsimd.indirect_dma_start(out=yta, out_offset=None, in_=A["ygscr0"],
                                    in_offset=IOX(ar.view("IDXA", U32, 1, col_off=k), 0)).then_inc("dgY%d" % par, 16)
        p.gpsimd.indirect_dma_start(out=ytb, out_offset=None, in_=A["ygscr0"],
                                    in_offset=IOX(ar.view("IDXB", U32, 1, col_off=k), 0)).then_inc("dgY%d" % par, 16)
        p.vector.wait("dgY%d" % par)
        ysum = ar.view("YSM", BF16, 1024, col_off=par * 1024)
        p.vector.tensor_scalar(ysum, yta, ar.view("WCA", F32, 1, col_off=k), None,
                               op0=ALU.mult)
        p.vector.scalar_tensor_tensor(ysum, ytb, ar.view("WCB", F32, 1, col_off=k),
                                      ysum, op0=ALU.mult, op1=ALU.add).then_inc("ve", 1)
        for c in range(NCH):
            p.tensor.wait("ve")
            mm = p.tensor.transpose(ptb[:, 256 + (c % 2) * 128:256 + (c % 2) * 128 + 128],
                                    ysum[:, c * 128:(c + 1) * 128], identb)
            mm.then_inc("te", 1)
            p.vector.wait("te")
            p.vector.tensor_copy(ar.view("Y", F32, 128, col_off=c * 2048 + k * 128),
                                 ptb[:, 256 + (c % 2) * 128:256 + (c % 2) * 128 + 128]).then_inc("ve", 1)
        gmarks.append(p.val("ve"))
    ar.free("YTA"); ar.free("YTB"); ar.free("YSM")
    ar.free("IDXA"); ar.free("IDXB"); ar.free("WCA"); ar.free("WCB")
    ar.free("ISC"); ar.free("P2R"); ar.free("M1T"); ar.free("MALL"); ar.free("WTF")
    ar.free("WROWS"); ar.free("LOG")


def emit_silu_to_G(p, ar):
    ar.alloc("G", 8 * 2048 * 2, at=64 * 1024)
    for c in range(NCH):
        for sl in range(4):
            p.scalar.wait("ve")
            p.scalar.activation(ar.view("G", BF16, 512, col_off=c * 2048 + sl * 512),
                                ar.view("Y", F32, 512, col_off=c * 2048 + sl * 512),
                                AF.Silu).then_inc("se", 1)
    ar.free("Y")


def emit_l0_projections(p, ar, A, lay):
    PSx = p.psB
    for nm in ("wq", "wk"):
        ar.alloc(nm, NCH * 64 * 2)
        p.gpsimd.dma_start(out=ar.view(nm, BF16, NCH * 64),
                           in_=A[nm].rearrange("(c q) f -> q c f", q=128)).then_inc("dgp", 16)
    ar.alloc("wv", NCH * 1024 * 2)
    p.gpsimd.dma_start(out=ar.view("wv", BF16, NCH * 1024),
                       in_=A["wv"].rearrange("(c q) f -> q c f", q=128)).then_inc("dgp", 16)
    ar.alloc("Kown", 2048 * 2)
    ar.alloc("QST", 2 * 512 * 2)
    ar.alloc("VST", 2 * 1024 * 2)
    p.tensor.wait("dgp")
    p.tensor.wait("ve")
    for sl in range(4):
        for c in range(NCH):
            xn = ar.view("XN", BF16, 512, col_off=c * 2048 + sl * 512)
            p.tensor.matmul(PSx[0][0:64, 0:512], ar.view("wk", BF16, 64, col_off=c * 64),
                            xn, start=(c == 0), stop=(c == NCH - 1))
            mm = p.tensor.matmul(PSx[1][0:64, 0:512], ar.view("wq", BF16, 64, col_off=c * 64),
                                 xn, start=(c == 0), stop=(c == NCH - 1))
        mm.then_inc("te", 1)
        p.scalar.wait("te")
        p.scalar.copy(ar.view("Kown", BF16, 512, part=64, col_off=sl * 512),
                      PSx[0][0:64, 0:512]).then_inc("se", 1)
        qst = ar.view("QST", BF16, 512, part=64, col_off=(sl % 2) * 512)
        p.scalar.wait("dsc")
        p.scalar.activation(qst, PSx[1][0:64, 0:512], AF.Copy,
                            scale=0.125).then_inc("se", 1)
        p.scalar.dma_start(out=A["l0_qscr"][:, sl * 512:(sl + 1) * 512],
                           in_=qst).then_inc("dsc", 16)
        p.tensor.wait("se")
    for j in range(NBLK):
        for half in range(2):
            p.tensor.wait("dsc")
            for c in range(NCH):
                xn = ar.view("XN", BF16, 128, col_off=c * 2048 + j * 128)
                mm = p.tensor.matmul(PSx[2 + half][:, 0:512],
                                     xn, ar.view("wv", BF16, 512, col_off=c * 1024 + half * 512),
                                     start=(c == 0), stop=(c == NCH - 1))
            mm.then_inc("te", 1)
            p.scalar.wait("te")
            p.scalar.copy(ar.view("VST", BF16, 512, col_off=(j % 2) * 1024 + half * 512),
                          PSx[2 + half][:, 0:512]).then_inc("se", 1)
        p.scalar.wait("se")
        p.scalar.dma_start(out=A["ccI"][j * 128:(j + 1) * 128, :],
                           in_=ar.view("VST", BF16, 1024, col_off=(j % 2) * 1024)).then_inc("dsc", 16)
    p.scalar.wait("se")
    p.scalar.dma_start(out=A["ccI"][2048:2176, :].rearrange("(a b) w -> a (b w)", b=2),
                       in_=ar.view("Kown", BF16, 2048, part=64)).then_inc("dsc", 16)
    p.gpsimd.wait("dsc")
    p.gpsimd.collective_compute("AllGather", ALU.bypass, replica_groups=RG_PAIRS,
                                ins=[A["ccI"].opt()], outs=[A["ccO"].opt()]).then_inc("cc", 1)
    ar.free("wq"); ar.free("wk"); ar.free("wv"); ar.free("VST"); ar.free("Kown")
    ar.free("QST")


def emit_l0_attention(p, ar, A):
    PSx = p.psB
    ptb = p.psPT
    R = 2176
    ar.alloc("K", 4096 * 2, at=163840)
    ar.alloc("V", 32 * 1024 * 2, at=96 * 1024)
    ar.alloc("QS0", 2 * 128 * 2, at=206400)
    p.sync.wait("cc")
    for r in range(2):
        kk = A["ccO"][r * R + 2048: r * R + 2176, :].rearrange("(a b) w -> a (b w)", b=2)
        dst = ar.view("K", BF16, 4096, part=64).rearrange(
            "p (j two i) -> p two j i", two=2, i=128)[:, r, :, :]
        p.sync.dma_start(out=dst, in_=kk.rearrange("a (j i) -> a j i", i=128)).then_inc("dsy", 16)
        vsrc = A["ccO"][r * R: r * R + 2048, :].rearrange("(j i) w -> j i w", i=128)
        vdst = ar.view("V", BF16, 32 * 1024).rearrange(
            "p (t two w) -> p two t w", two=2, w=1024)[:, r, :, :]
        p.sync.dma_start(out=vdst, in_=vsrc.rearrange("j i w -> i j w")).then_inc("dsy", 16)
    ar.alloc("SS", 4096 * 4, at=172032)
    ar.alloc("SGE", 256 * 4, at=204288)
    ar.alloc("MXD", 8 * 4, at=205312)
    ar.alloc("P", (3840 + 4096) * 2, at=188416)
    mskab = ar.view("c_mskab", F32, 256)
    p.tensor.wait("dsy")

    for pair in range(8):
        for qb in (2 * pair, 2 * pair + 1):
            ntile = 2 * qb + 2
            p.sync.wait("te")
            qv = ar.view("QS0", BF16, 128, part=64, col_off=(qb % 2) * 128)
            p.sync.dma_start(out=qv, in_=A["l0_qscr"][:, qb * 128:(qb + 1) * 128]).then_inc(
                "dsQ%d" % (qb % 2), 16)
            p.tensor.wait("dsQ%d" % (qb % 2))
            nslab = cdiv(ntile - 2, 4)
            for sb in range(nslab):
                t0 = sb * 4
                n = min(4, (ntile - 2) - t0) * 128
                bank = PSx[sb % 2]
                p.tensor.wait("se")
                p.tensor.wait("ve")
                mm = p.tensor.matmul(bank[:, 0:n], qv,
                                     ar.view("K", BF16, n, part=64, col_off=t0 * 128),
                                     start=True, stop=True)
                mm.then_inc("te", 1)
                p.scalar.wait("te")
                p.scalar.copy(ar.view("SS", F32, n, col_off=t0 * 128),
                              bank[:, 0:n]).then_inc("se", 1)
            bank = PSx[nslab % 2]
            p.tensor.wait("ve")
            mm = p.tensor.matmul(bank[:, 0:256], qv,
                                 ar.view("K", BF16, 256, part=64, col_off=(ntile - 2) * 128),
                                 start=True, stop=True)
            mm.then_inc("te", 1)
            p.vector.wait("te")
            p.vector.tensor_tensor(ar.view("SS", F32, 256, col_off=(ntile - 2) * 128),
                                   bank[:, 0:256], mskab, op=ALU.add).then_inc("ve", 1)
            ncol = ntile * 128
            mx = ar.view("MXD", F32, 1)
            nmx = ar.view("MXD", F32, 1, col_off=1)
            den = ar.view("MXD", F32, 1, col_off=2)
            rden = ar.view("MXD", F32, 1, col_off=3)
            p.vector.wait("se")
            p.vector.reduce_max(mx, ar.view("SS", F32, ncol), axis=AX.X)
            p.vector.tensor_scalar(nmx, mx, -1.0, None, op0=ALU.mult).then_inc("ve", 1)
            p.scalar.wait("ve")
            pv = ar.view("P", BF16, ncol, col_off=(qb % 2) * 3840)
            p.scalar.activation(pv, ar.view("SS", F32, ncol), AF.Exp,
                                bias=nmx, accum_out=den).then_inc("se", 1)
            p.vector.wait("se")
            p.vector.reciprocal(rden, den)
            p.vector.tensor_scalar(pv, pv, rden, None, op0=ALU.mult).then_inc("ve", 1)
        nt0, nt1 = 4 * pair + 2, 4 * pair + 4
        ar.free("SS")
        ar.alloc("PTA", 32 * 256 * 2, at=172032)
        for t in range(nt1):
            p.tensor.wait("ve")
            if t < nt0:
                p.tensor.transpose(ptb[:, 0:128],
                                   ar.view("P", BF16, 128, col_off=t * 128),
                                   ar.view("c_identb", BF16, 128))
            mm = p.tensor.transpose(ptb[:, 128:256],
                                    ar.view("P", BF16, 128, col_off=3840 + t * 128),
                                    ar.view("c_identb", BF16, 128))
            mm.then_inc("te", 1)
            p.vector.wait("te")
            if t < nt0:
                p.vector.tensor_copy(ar.view("PTA", BF16, 256, col_off=t * 256),
                                     ptb[:, 0:256]).then_inc("ve", 1)
            else:
                p.vector.memset(ar.view("PTA", BF16, 128, col_off=t * 256), 0.0)
                p.vector.tensor_copy(ar.view("PTA", BF16, 128, col_off=t * 256 + 128),
                                     ptb[:, 128:256]).then_inc("ve", 1)
        p.tensor.wait("ve")
        marks = []
        for dv in range(8):
            bank = PSx[2 + dv % 4][:, 0:256]
            if dv >= 4:
                p.tensor.wait("ge", marks[dv - 4])
            for t in range(nt1):
                mm = p.tensor.matmul(
                    bank,
                    ar.view("V", BF16, 128, col_off=t * 1024 + dv * 128),
                    ar.view("PTA", BF16, 256, col_off=t * 256),
                    start=(t == 0), stop=(t == nt1 - 1))
            mm.then_inc("te", 1)
            p.vector.wait("te")
            p.vector.wait("ge")
            g = ar.view("G", BF16, 256, col_off=dv * 2048 + pair * 256)
            sg = ar.view("SGE", F32, 256)
            p.vector.scalar_tensor_tensor(sg, bank, 1.0, g, op0=ALU.mult,
                                          op1=ALU.mult).then_inc("ve", 1)
            xv = ar.view("XF", F32, 256, col_off=dv * 2048 + pair * 256)
            p.gpsimd.wait("ve")
            p.gpsimd.tensor_tensor(xv, xv, sg, op=ALU.add).then_inc("ge", 1)
            marks.append(p.val("ge"))
        ar.free("PTA")
        ar.alloc("SS", 4096 * 4, at=172032)
    ar.free("K"); ar.free("V"); ar.free("SS"); ar.free("P")
    ar.free("SGE"); ar.free("MXD"); ar.free("QS0")



def emit_l1_wbuild(p, ar, A):
    PSx = p.psB
    ar.alloc("TTG", 2 * (1024 + 256) * 4)
    ar.alloc("TTS", 256 * 4)
    for i, nm in enumerate(("r", "i")):
        p.sync.dma_start(out=ar.view("TTG", F32, 1024, part=4, col_off=i * 1280),
                         in_=A["l1_a" + nm]).then_inc("dsy", 16)
        p.sync.dma_start(out=ar.view("TTG", F32, 256, part=4, col_off=i * 1280 + 1024),
                         in_=A["l1_b" + nm]).then_inc("dsy", 16)
    p.tensor.wait("dsy")
    for i, nm in enumerate(("r", "i")):
        scr = A["l1_wscr_" + nm]
        for m in range(8):
            p.tensor.wait("dsc")
            mm = p.tensor.matmul(PSx[6][:, 0:256],
                                 ar.view("TTG", F32, 128, part=4, col_off=i * 1280 + m * 128),
                                 ar.view("TTG", F32, 256, part=4, col_off=i * 1280 + 1024),
                                 start=True, stop=True)
            mm.then_inc("te", 1)
            p.scalar.wait("te")
            p.scalar.copy(ar.view("TTS", F32, 256), PSx[6][:, 0:256]).then_inc("se", 1)
            dst = scr.rearrange("b d g e -> b d g e")[m * 4:(m + 1) * 4]
            p.scalar.dma_start(out=dst, in_=ar.view("TTS", F32, 256)).then_inc("dsc", 16)
    for i, nm in enumerate(("r", "i")):
        ar.alloc("W" + nm.upper() + "T", 4 * 512 * 2)
        scr = A["l1_wscr_" + nm].rearrange("b d g e -> d g b e")
        p.gpsimd.wait("dsc")
        for k in range(4):
            p.gpsimd.dma_start(
                out=ar.view("W" + nm.upper() + "T", BF16, 512, col_off=k * 512),
                in_=scr[k * 8:(k + 1) * 8]).then_inc("dgB", 16)
    ar.free("TTG"); ar.free("TTS")


def emit_l1_ffn(p, ar, A):
    PSx = p.psB
    ar.alloc("XNI", 4 * 2048 * 2)
    for c in range(4):
        for sl in range(4):
            p.vector.tensor_scalar(
                ar.view("XNI", BF16, 512, col_off=c * 2048 + sl * 512),
                ar.view("XN", BF16, 512, col_off=(4 + c) * 2048 + sl * 512),
                -1.0, None, op0=ALU.mult).then_inc("ve", 1)
    ar.alloc("G", 8 * 2048 * 2, at=128 * 1024)
    onesb = ar.view("c_onesb", BF16, 512, part=1)
    p.tensor.wait("ve")
    p.tensor.wait("dgB")
    marks = []
    for oc in range(8):
        real = oc < 4
        for sl in range(4):
            bank = PSx[(oc * 4 + sl) % 2]
            it = len(marks)
            if it >= 2:
                p.tensor.wait("se", marks[it - 2])
            for k in range(4):
                w1 = ar.view("WRT" if real else "WIT", BF16, 128,
                             col_off=k * 512 + (oc % 4) * 128)
                w2 = ar.view("WIT" if real else "WRT", BF16, 128,
                             col_off=k * 512 + (oc % 4) * 128)
                xr = ar.view("XN", BF16, 512, col_off=k * 2048 + sl * 512)
                xi2 = (ar.view("XNI", BF16, 512, col_off=k * 2048 + sl * 512) if real
                       else ar.view("XN", BF16, 512, col_off=(4 + k) * 2048 + sl * 512))
                p.tensor.matmul(bank[:, 0:512], w1, xr, start=(k == 0), stop=False)
                p.tensor.matmul(bank[:, 0:512], w2, xi2, start=False, stop=False)
            if oc == 0 and sl == 0:
                p.tensor.wait("dgA")
            bias = ar.view("l1_brow", BF16, 128, part=1,
                           col_off=(0 if real else 512) + (oc % 4) * 128)
            mm = p.tensor.matmul(bank[:, 0:512], bias, onesb, start=False, stop=True)
            mm.then_inc("te", 1)
            p.scalar.wait("te")
            p.scalar.activation(ar.view("G", BF16, 512, col_off=oc * 2048 + sl * 512),
                                bank[:, 0:512], AF.Silu).then_inc("se", 1)
            marks.append(p.val("se"))
    ar.free("XNI"); ar.free("WRT"); ar.free("WIT")


def emit_l1_attention_a(p, ar, A):
    PSx = p.psB
    ptb = p.psPT
    ar.alloc("WP", 8 * 1024 * 2)
    p.gpsimd.dma_start(out=ar.view("WP", BF16, 8 * 1024),
                       in_=A["l1_proj"].rearrange("(c q) f -> q c f", q=128)).then_inc("dgC", 16)
    ar.alloc("M1", 8 * 64 * 2)
    p.gpsimd.dma_start(out=ar.view("M1", BF16, 8 * 64),
                       in_=A["l1_m"]).then_inc("dgC", 16)
    ar.alloc("PF", 8 * 2048 * 2, at=96 * 1024)
    p.tensor.wait("dgC")
    for oc in range(8):
        for sl in range(4):
            bank = PSx[(oc * 4 + sl) % 2]
            p.tensor.wait("se")
            for c in range(NCH):
                mm = p.tensor.matmul(bank[:, 0:512],
                                     ar.view("WP", BF16, 128, col_off=c * 1024 + oc * 128),
                                     ar.view("XN", BF16, 512, col_off=c * 2048 + sl * 512),
                                     start=(c == 0), stop=(c == NCH - 1))
            mm.then_inc("te", 1)
            p.scalar.wait("te")
            p.scalar.copy(ar.view("PF", BF16, 512, col_off=oc * 2048 + sl * 512),
                          bank[:, 0:512]).then_inc("se", 1)
    ar.alloc("PTOK", 16 * 1024 * 2, at=96 * 1024 + 32 * 1024)
    p.tensor.wait("se")
    for c in range(8):
        for t in range(16):
            p.tensor.wait("ve")
            mm = p.tensor.transpose(ptb[:, (t % 4) * 128:(t % 4) * 128 + 128],
                                    ar.view("PF", BF16, 128, col_off=c * 2048 + t * 128),
                                    ar.view("c_identb", BF16, 128))
            mm.then_inc("te", 1)
            p.vector.wait("te")
            p.vector.tensor_copy(
                ar.view("PTOK", BF16, 128, col_off=t * 1024 + c * 128),
                ptb[:, (t % 4) * 128:(t % 4) * 128 + 128]).then_inc("ve", 1)
    ar.alloc("GST", 16 * 64 * 4)
    p.tensor.wait("ve")
    gmarks = []
    for n in range(16):
        bank = PSx[4 + n % 2]
        if n >= 2:
            p.tensor.wait("se", gmarks[n - 2])
        for t in range(16):
            mm = p.tensor.matmul(bank[0:64, 0:64],
                                 ar.view("PTOK", BF16, 64, col_off=t * 1024 + n * 64),
                                 ar.view("PTOK", BF16, 64, col_off=t * 1024 + n * 64),
                                 start=(t == 0), stop=(t == 15))
        mm.then_inc("te", 1)
        p.scalar.wait("te")
        p.scalar.copy(ar.view("GST", F32, 64, part=64, col_off=n * 64),
                      bank[0:64, 0:64]).then_inc("se", 1)
        gmarks.append(p.val("se"))
    p.scalar.wait("se")
    p.scalar.dma_start(out=A["ccGi"], in_=ar.view("GST", F32, 1024, part=64)).then_inc("dsc", 16)
    p.gpsimd.wait("dsc")
    p.gpsimd.collective_compute("AllReduce", ALU.add, replica_groups=RG_PAIRS,
                                ins=[A["ccGi"].opt()], outs=[A["ccGo"].opt()]).then_inc("cc", 1)
    ar.free("PTOK"); ar.free("WP"); ar.free("GST")


def emit_l1_attention_b(p, ar, A):
    PSx = p.psB
    ar.alloc("GSB", 8 * 64 * 2)
    p.gpsimd.wait("cc")
    for par in range(2):
        p.gpsimd.dma_start(
            out=ar.view("GSB", BF16, 8 * 64, part=64, base=par * 64).rearrange(
                "p (m w) -> p m w", w=64),
            in_=A["ccGo"].rearrange("a (m two w) -> a two m w", two=2, w=64)[:, par, :, :],
        ).then_inc("dgD", 16)
    ar.alloc("MGSB", 8 * 64 * 2)
    mgm = []
    for n in range(16):
        bn = (n % 2) * 64
        cn = (n // 2) * 64
        bank = PSx[4 + n % 2]
        p.tensor.wait("dgD")
        if n >= 2:
            p.tensor.wait("se", mgm[n - 2])
        mm = p.tensor.matmul(bank[0:64, 64:128],
                             ar.view("M1", BF16, 64, part=64, base=bn, col_off=cn),
                             ar.view("GSB", BF16, 64, part=64, base=bn, col_off=cn),
                             start=True, stop=True)
        mm.then_inc("te", 1)
        p.scalar.wait("te")
        p.scalar.copy(ar.view("MGSB", BF16, 64, part=64, base=bn, col_off=cn),
                      bank[0:64, 64:128]).then_inc("se", 1)
        mgm.append(p.val("se"))
    ar.alloc("MIX", 8 * 1024 * 2)
    p.gpsimd.dma_start(out=ar.view("MIX", BF16, 8 * 1024),
                       in_=A["l1_mixT"].rearrange("(c q) f -> q c f", q=128)).then_inc("dgE", 16)
    ar.alloc("NUD", 8 * 512 * 2)
    ar.alloc("SGE", 512 * 4)
    onesb = ar.view("c_onesb", BF16, 512, part=1)
    p.tensor.wait("dgE")
    p.tensor.wait("se")
    emarks = []
    for sl in range(4):
        for c in range(NCH):
            p.tensor.wait("ve")
            bank = PSx[6]
            pfc = ar.view("PF", BF16, 512, col_off=c * 2048 + sl * 512)
            p.tensor.matmul(bank[0:64, 0:512],
                            ar.view("MGSB", BF16, 64, part=64, base=0, col_off=c * 64),
                            pfc[0:64, :], start=True, stop=True)
            mm2 = p.tensor.matmul(bank[64:128, 0:512],
                                  ar.view("MGSB", BF16, 64, part=64, base=64, col_off=c * 64),
                                  ar.view("PF", BF16, 512, part=64, base=64,
                                          col_off=c * 2048 + sl * 512),
                                  start=True, stop=True)
            mm2.then_inc("te", 1)
            p.vector.wait("te")
            p.vector.tensor_copy(ar.view("NUD", BF16, 512, col_off=c * 512),
                                 bank[:, 0:512]).then_inc("ve", 1)
        p.tensor.wait("ve")
        for oc in range(NCH):
            sbank = PSx[2 + oc % 2]
            if len(emarks) >= 2:
                p.tensor.wait("ge", emarks[-2])
            for c in range(NCH):
                p.tensor.matmul(sbank[:, 0:512],
                                ar.view("MIX", BF16, 128, col_off=c * 1024 + oc * 128),
                                ar.view("NUD", BF16, 512, col_off=c * 512),
                                start=(c == 0), stop=False)
            bias = ar.view("l1_mbrow", BF16, 128, part=1, col_off=oc * 128)
            mm = p.tensor.matmul(sbank[:, 0:512], bias, onesb, start=False, stop=True)
            mm.then_inc("te", 1)
            p.vector.wait("te")
            p.vector.wait("ge")
            g = ar.view("G", BF16, 512, col_off=oc * 2048 + sl * 512)
            sg = ar.view("SGE", F32, 512)
            p.vector.scalar_tensor_tensor(sg, sbank[:, 0:512], 1.0, g, op0=ALU.mult,
                                          op1=ALU.mult).then_inc("ve", 1)
            xv = ar.view("XF", F32, 512, col_off=oc * 2048 + sl * 512)
            p.gpsimd.wait("ve")
            p.gpsimd.tensor_tensor(xv, xv, sg, op=ALU.add).then_inc("ge", 1)
            emarks.append(p.val("ge"))
    ar.free("NUD"); ar.free("SGE"); ar.free("MIX"); ar.free("M1")
    ar.free("GSB"); ar.free("MGSB"); ar.free("PF")




def emit_l3_ffn(p, ar, A):
    PSx = p.psB
    ar.alloc("Y", 8 * 2048 * 4, at=96 * 1024)
    ar.alloc("W13", NCH * 512 * 2)
    ar.alloc("W23", 4 * 1024 * 2)
    ar.alloc("B13", 512 * 2)
    ar.alloc("B23", 1024 * 2)
    ar.alloc("HS3", 512 * 2)
    ar.alloc("HW3", 4 * 512 * 2)
    ar.alloc("WBC", 512 * 2)
    bchot = ar.view("c_bchot", BF16, 8 * 128, part=8)
    wrows = ar.view("WROWS", BF16, 2048, part=8)
    onesb = ar.view("c_onesb", BF16, 512, part=1)
    for e in range(8):
        if e:
            p.gpsimd.wait("te")
        p.gpsimd.dma_start(out=ar.view("W13", BF16, NCH * 512),
                           in_=A["l3_w1"][e].rearrange("(c q) f -> q c f", q=128)).then_inc("dgW0", 16)
        p.gpsimd.dma_start(out=ar.view("W23", BF16, 4 * 1024),
                           in_=A["l3_w2"][e].rearrange("(c q) f -> q c f", q=128)).then_inc("dgW0", 16)
        p.gpsimd.dma_start(out=ar.view("B13", BF16, 512, part=1),
                           in_=A["l3_b1"][e:e + 1, :]).then_inc("dgW0", 16)
        p.gpsimd.dma_start(out=ar.view("B23", BF16, 1024, part=1),
                           in_=A["l3_b2"][e:e + 1, :]).then_inc("dgW0", 16)
        w1 = ar.view("W13", BF16, NCH * 512)
        w2 = ar.view("W23", BF16, 4 * 1024)
        b1 = ar.view("B13", BF16, 512, part=1)
        b2 = ar.view("B23", BF16, 1024, part=1)
        p.tensor.wait("dgW0")
        for ts in range(4):
            tc = ts * 512
            p.tensor.wait("ve")
            p.tensor.matmul(PSx[6][:, 0:512], bchot[:, e * 128:(e + 1) * 128],
                            wrows[:, tc:tc + 512], start=True, stop=True).then_inc("te", 1)
            p.vector.wait("te")
            p.vector.tensor_copy(ar.view("WBC", BF16, 512), PSx[6][:, 0:512]).then_inc("ve", 1)
            for hc in range(4):
                p.tensor.wait("ve")
                p.tensor.wait("ge")
                for c in range(NCH):
                    p.tensor.matmul(PSx[0][:, 0:512],
                                    w1[:, c * 512 + hc * 128: c * 512 + hc * 128 + 128],
                                    ar.view("XN", BF16, 512, col_off=c * 2048 + tc),
                                    start=(c == 0), stop=False)
                mm = p.tensor.matmul(PSx[0][:, 0:512],
                                     b1[:, hc * 128:(hc + 1) * 128],
                                     onesb, start=False, stop=True)
                mm.then_inc("te", 1)
                p.scalar.wait("te")
                p.scalar.wait("ge")
                p.scalar.activation(ar.view("HS3", BF16, 512), PSx[0][:, 0:512],
                                    AF.Gelu).then_inc("se", 1)
                p.gpsimd.wait("se")
                p.gpsimd.wait("ve")
                p.gpsimd.wait("te")
                p.gpsimd.tensor_tensor(ar.view("HW3", BF16, 512, col_off=hc * 512),
                                       ar.view("HS3", BF16, 512),
                                       ar.view("WBC", BF16, 512), op=ALU.mult).then_inc("ge", 1)
            p.tensor.wait("ge")
            marks = []
            for dch in range(NCH):
                bank = PSx[2 + dch % 2][:, 0:512]
                if dch >= 2:
                    p.tensor.wait("ve", marks[dch - 2])
                for hc in range(4):
                    p.tensor.matmul(bank,
                                    w2[:, hc * 1024 + dch * 128: hc * 1024 + dch * 128 + 128],
                                    ar.view("HW3", BF16, 512, col_off=hc * 512),
                                    start=(hc == 0), stop=False)
                mm = p.tensor.matmul(bank, b2[:, dch * 128:(dch + 1) * 128],
                                     ar.view("WBC", BF16, 512)[0:1, 0:512],
                                     start=False, stop=True)
                mm.then_inc("te", 1)
                p.vector.wait("te")
                yv = ar.view("Y", F32, 512, col_off=dch * 2048 + tc)
                if e == 0:
                    p.vector.tensor_copy(yv, bank).then_inc("ve", 1)
                else:
                    p.vector.tensor_tensor(yv, yv, bank, op=ALU.add).then_inc("ve", 1)
                marks.append(p.val("ve"))
    ar.free("W13"); ar.free("W23"); ar.free("B13"); ar.free("B23")
    ar.free("HS3"); ar.free("HW3"); ar.free("WBC")
    ar.free("WROWS"); ar.free("LOG")


def emit_l3_projections(p, ar, A):
    PSx = p.psB
    ar.alloc("WQ3", NCH * 1024 * 2)
    p.gpsimd.dma_start(out=ar.view("WQ3", BF16, NCH * 1024),
                       in_=A["l3_wq"].rearrange("(c q) f -> q c f", q=128)).then_inc("dgC", 16)
    ar.alloc("WKV3", NCH * 256 * 2)
    p.gpsimd.dma_start(out=ar.view("WKV3", BF16, NCH * 256).rearrange(
        "p (c f) -> p c f", f=256),
        in_=A["l3_wk"].rearrange("(c q) f -> q c f", q=128)).then_inc("dgC", 16)
    ar.alloc("WV3", NCH * 256 * 2)
    p.gpsimd.dma_start(out=ar.view("WV3", BF16, NCH * 256).rearrange(
        "p (c f) -> p c f", f=256),
        in_=A["l3_wv"].rearrange("(c q) f -> q c f", q=128)).then_inc("dgC", 16)
    ar.alloc("B3", 4096 * 2)
    p.gpsimd.dma_start(out=ar.view("B3", BF16, 1024, part=1),
                       in_=A["l3_bq"]).then_inc("dgC", 16)
    p.gpsimd.dma_start(out=ar.view("B3", BF16, 256, part=1, col_off=1024),
                       in_=A["l3_bk"]).then_inc("dgC", 16)
    p.gpsimd.dma_start(out=ar.view("B3", BF16, 256, part=1, col_off=1536),
                       in_=A["l3_bv"]).then_inc("dgC", 16)
    ar.alloc("ST3", 2 * 512 * 2)
    onesb = ar.view("c_onesb", BF16, 512, part=1)
    p.tensor.wait("dgC")
    for oc in range(NCH):
        for sl in range(4):
            bank = PSx[(oc * 4 + sl) % 2]
            p.tensor.wait("dsc")
            for c in range(NCH):
                p.tensor.matmul(bank[:, 0:512],
                                ar.view("WQ3", BF16, 128, col_off=c * 1024 + oc * 128),
                                ar.view("XN", BF16, 512, col_off=c * 2048 + sl * 512),
                                start=(c == 0), stop=False)
            mm = p.tensor.matmul(bank[:, 0:512],
                                 ar.view("B3", BF16, 128, part=1, col_off=oc * 128),
                                 onesb, start=False, stop=True)
            mm.then_inc("te", 1)
            p.scalar.wait("te")
            st = ar.view("ST3", BF16, 512, col_off=(sl % 2) * 512)
            p.scalar.activation(st, bank[:, 0:512], AF.Copy, scale=0.125).then_inc("se", 1)
            p.scalar.dma_start(out=A["l3_qscr"][oc * 128:(oc + 1) * 128, sl * 512:(sl + 1) * 512],
                               in_=st).then_inc("dsc", 16)
    for hh in range(4):
        for sl in range(4):
            bank = PSx[(hh * 4 + sl) % 2]
            p.tensor.wait("dsc")
            for c in range(NCH):
                p.tensor.matmul(bank[0:64, 0:512],
                                ar.view("WKV3", BF16, 64, col_off=c * 256 + hh * 64),
                                ar.view("XN", BF16, 512, col_off=c * 2048 + sl * 512),
                                start=(c == 0), stop=False)
            mm = p.tensor.matmul(bank[0:64, 0:512],
                                 ar.view("B3", BF16, 64, part=1, col_off=1024 + hh * 64),
                                 onesb, start=False, stop=True)
            mm.then_inc("te", 1)
            p.scalar.wait("te")
            st = ar.view("ST3", BF16, 512, col_off=(sl % 2) * 512, part=64)
            p.scalar.copy(st, bank[0:64, 0:512]).then_inc("se", 1)
            karea = A["ccI3"][512:1024, :].rearrange("(r two) w -> r (two w)", two=2)
            p.scalar.dma_start(out=karea[hh * 64:(hh + 1) * 64, sl * 512:(sl + 1) * 512],
                               in_=st).then_inc("dsc", 16)
    for j in range(NBLK):
        bank = PSx[2 + j % 2]
        p.tensor.wait("dsc")
        for c in range(NCH):
            p.tensor.matmul(bank[:, 0:256],
                            ar.view("XN", BF16, 128, col_off=c * 2048 + j * 128),
                            ar.view("WV3", BF16, 256, col_off=c * 256),
                            start=(c == 0), stop=False)
        mm = p.tensor.matmul(bank[:, 0:256], ar.view("c_onesb", BF16, 128, part=1),
                             ar.view("B3", BF16, 256, part=1, col_off=1536),
                             start=False, stop=True)
        mm.then_inc("te", 1)
        p.scalar.wait("te")
        st = ar.view("ST3", BF16, 256, col_off=(j % 2) * 512)
        p.scalar.copy(st, bank[:, 0:256]).then_inc("se", 1)
        vdst = A["ccI3"][j * 32:(j + 1) * 32, :].rearrange("r (x w) -> (r x) w", x=4)
        p.scalar.dma_start(out=vdst, in_=st).then_inc("dsc", 16)
    p.gpsimd.wait("dsc")
    p.gpsimd.collective_compute("AllGather", ALU.bypass, replica_groups=RG_PAIRS,
                                ins=[A["ccI3"].opt()], outs=[A["ccO3"].opt()]).then_inc("cc", 1)
    ar.free("WQ3"); ar.free("WKV3"); ar.free("WV3"); ar.free("ST3"); ar.free("B3")


def emit_l3_attention(p, ar, A):
    PSx = p.psB
    ar.alloc("c_mskT3", 8 * 512 * 2)
    p.sync.dma_start(out=ar.view("c_mskT3", BF16, 8 * 512),
                     in_=A["c_mskT3"]).then_inc("dsy", 16)
    p.vector.wait("dsy")
    ar.alloc("K3", 2 * 4096 * 2, at=96 * 1024)
    ar.alloc("V3", 32 * 260 * 2, at=96 * 1024 + 16 * 1024)
    ar.alloc("S1", 8 * 2048 * 2, at=96 * 1024 + 34 * 1024)
    ar.alloc("QS3", 2 * 512 * 2)
    ar.alloc("PT3", 2 * 512 * 2)
    ar.alloc("RD3", 1024 * 4)
    p.sync.wait("cc")
    R3 = 1024
    for r in range(2):
        ksrc = A["ccO3"][r * R3 + 512: r * R3 + 1024, :].rearrange(
            "a b -> (a b)").rearrange(
            "(blk par dd j i) -> par blk dd j i", blk=2, par=2, dd=64, i=128)
        for par in range(2):
            for blk in range(2):
                dst = ar.view("K3", BF16, 4096, part=64, base=par * 64,
                              col_off=blk * 4096).rearrange(
                    "p (j two i) -> p two j i", two=2, i=128)[:, r, :, :]
                p.sync.dma_start(out=dst, in_=ksrc[par, blk]).then_inc("dkv", 16)
        vflat = A["ccO3"][r * R3: r * R3 + 512, :].rearrange("a b -> (a b)")
        for hh in range(4):
            vdst = ar.view("V3", BF16, 32 * 260).rearrange(
                "p (j two h x) -> p j two h x", two=2, h=4, x=65)[:, :, r, hh, 0:64]
            p.sync.dma_start(out=vdst, in_=vflat.rearrange(
                "(j i h x) -> i h j x", i=128, h=4, x=64)[:, hh]).then_inc("dkv", 16)
    p.vector.wait("dkv")
    p.vector.memset(ar.view("V3", BF16, 32 * 260).rearrange(
        "p (kt h x) -> p kt h x", h=4, x=65)[:, :, :, 64], 1.0).then_inc("ve", 1)
    mskT = ar.view("c_mskT3", BF16, 8 * 512)
    smark = []
    for j in range(16):
        par = (j % 2) * 64
        h = j % 4
        blk = h // 2
        hpar = (h % 2) * 64
        for qg in range(4):
            qb2 = (j * 4 + qg) % 2
            p.sync.wait("te", smark[-2] if len(smark) >= 2 else 0)
            qs = ar.view("QS3", BF16, 512, part=64, base=par, col_off=qb2 * 512)
            p.sync.dma_start(out=qs, in_=A["l3_qscr"][j * 64:(j + 1) * 64,
                                                      qg * 512:(qg + 1) * 512]).then_inc(
                "dsQ%d" % qb2, 16)
            nkt = 8 * qg + 8
            sbank = PSx[2 + qb2]
            p.tensor.wait("dkv")
            p.tensor.wait("dsQ%d" % qb2)
            p.tensor.wait("ve")
            for kt in range(nkt):
                bank = PSx[kt % 2]
                p.tensor.wait("se")
                mm = p.tensor.matmul(bank[:, 0:512],
                                     ar.view("K3", BF16, 128, part=64, base=hpar,
                                             col_off=blk * 4096 + kt * 128),
                                     qs, start=True, stop=True)
                mm.then_inc("te", 1)
                d = kt - 8 * qg
                if d >= 0:
                    p.vector.wait("te")
                    p.vector.tensor_tensor(bank[:, 0:512], bank[:, 0:512],
                                           mskT[:, d * 512:(d + 1) * 512],
                                           op=ALU.add).then_inc("ve", 1)
                    p.scalar.wait("ve")
                else:
                    p.scalar.wait("te")
                pt = ar.view("PT3", BF16, 512, col_off=(kt % 2) * 512)
                p.scalar.activation(pt, bank[:, 0:512], AF.Exp).then_inc("se", 1)
                p.tensor.wait("se")
                mm = p.tensor.matmul(sbank[0:65, 0:512],
                                     ar.view("V3", BF16, 65, col_off=kt * 260 + h * 65),
                                     pt, start=(kt == 0), stop=(kt == nkt - 1))
            mm.then_inc("te", 1)
            p.vector.wait("te")
            rd = ar.view("RD3", F32, 512, part=1)
            p.vector.reciprocal(rd, sbank[64:65, 0:512]).then_inc("ve", 1)
            p.tensor.wait("ve")
            mm = p.tensor.matmul(PSx[4][0:64, 0:512], ar.view("c_ones_row", F32, 64, part=1),
                                 rd, start=True, stop=True)
            mm.then_inc("te", 1)
            p.vector.wait("te")
            p.vector.tensor_copy(ar.view("RD3", F32, 512, col_off=512)[0:64, :],
                                 PSx[4][0:64, 0:512]).then_inc("ve", 1)
            p.vector.tensor_tensor(
                ar.view("S1", BF16, 512, part=64, base=par,
                        col_off=(j // 2) * 2048 + qg * 512),
                sbank[0:64, 0:512],
                ar.view("RD3", F32, 512, col_off=512)[0:64, :],
                op=ALU.mult).then_inc("ve", 1)
            smark.append(p.val("te"))
    ar.free("K3"); ar.free("V3"); ar.free("QS3"); ar.free("PT3"); ar.free("RD3")
    ar.free("c_mskT3")


def emit_l3_post(p, ar, A):
    PSx = p.psB
    ar.alloc("SQ3", 2 * 512 * 2)
    ar.alloc("STR3", 4 * 2048 * 4, at=96 * 1024)
    ar.alloc("LNG", 8 * 4 * 2)
    p.sync.dma_start(out=ar.view("LNG", F32, 8),
                     in_=A["l3_lng"]).then_inc("dsy", 16)
    p.sync.dma_start(out=ar.view("LNG", F32, 8, col_off=8),
                     in_=A["l3_lnb"]).then_inc("dsy", 16)

    def srow(i, sl=None):
        if sl is None:
            return ar.view("STR3", F32, 2048, part=1, col_off=i * 2048)
        return ar.view("STR3", F32, 512, part=1, col_off=i * 2048 + sl * 512)

    onescolb = ar.view("c_onescolb", BF16, 1)
    ones_row = ar.view("c_ones_row", F32, 128, part=1)
    for sl in range(4):
        p.tensor.wait("ve")
        for c in range(NCH):
            sq = ar.view("SQ3", BF16, 512, col_off=(c % 2) * 512)
            ssl = ar.view("S1", BF16, 512, col_off=c * 2048 + sl * 512)
            p.scalar.wait("te")
            p.scalar.activation(sq, ssl, AF.Square).then_inc("se", 1)
            p.tensor.wait("se")
            p.tensor.matmul(PSx[0][0:1, 0:512], onescolb, ssl,
                            start=(c == 0), stop=(c == NCH - 1))
            mm = p.tensor.matmul(PSx[0][32:33, 0:512], onescolb, sq,
                                 start=(c == 0), stop=(c == NCH - 1))
            mm.then_inc("te", 1)
        p.vector.wait("te")
        p.vector.tensor_copy(srow(0, sl), PSx[0][0:1, 0:512])
        p.vector.tensor_copy(srow(1, sl), PSx[0][32:33, 0:512]).then_inc("ve", 1)
    p.vector.wait("ve")
    p.vector.tensor_scalar(srow(2), srow(0), 1.0 / D, None, op0=ALU.mult)
    p.vector.tensor_scalar(srow(3), srow(1), 1.0 / D, None, op0=ALU.mult)
    p.vector.tensor_tensor(srow(1), srow(2), srow(2), op=ALU.mult)
    p.vector.tensor_tensor(srow(3), srow(3), srow(1), op=ALU.subtract)
    p.vector.tensor_scalar(srow(3), srow(3), 1e-5, None, op0=ALU.add).then_inc("ve", 1)
    p.scalar.wait("ve")
    p.scalar.activation(srow(3), srow(3), AF.Sqrt).then_inc("se", 1)
    p.vector.wait("se")
    p.vector.reciprocal(srow(3), srow(3)).then_inc("ve", 1)
    ar.alloc("XT3", 2 * 512 * 4)
    p.vector.wait("dsy")
    for sl in range(4):
        p.tensor.wait("ve")
        p.tensor.matmul(PSx[1][:, 0:512], ones_row, srow(2, sl), start=True, stop=True)
        mm = p.tensor.matmul(PSx[2][:, 0:512], ones_row, srow(3, sl), start=True, stop=True)
        mm.then_inc("te", 1)
        p.vector.wait("te")
        for c in range(NCH):
            ssl = ar.view("S1", BF16, 512, col_off=c * 2048 + sl * 512)
            xt = ar.view("XT3", F32, 512, col_off=(c % 2) * 512)
            p.vector.tensor_tensor(xt, ssl, PSx[1][:, 0:512], op=ALU.subtract)
            p.vector.tensor_tensor(xt, xt, PSx[2][:, 0:512], op=ALU.mult)
            p.vector.tensor_scalar(ssl, xt, ar.view("LNG", F32, 1, col_off=c),
                                   ar.view("LNG", F32, 1, col_off=8 + c),
                                   op0=ALU.mult, op1=ALU.add).then_inc("ve", 1)
    ar.free("SQ3"); ar.free("STR3"); ar.free("XT3")
    ar.alloc("WO3", NCH * 1024 * 2)
    p.gpsimd.wait("te"); p.gpsimd.wait("ve"); p.gpsimd.wait("se")
    p.gpsimd.dma_start(out=ar.view("WO3", BF16, NCH * 1024),
                       in_=A["l3_wo"].rearrange("(c q) f -> q c f", q=128)).then_inc("dgE", 16)
    ar.alloc("BO3", 1024 * 2)
    p.gpsimd.dma_start(out=ar.view("BO3", BF16, 1024, part=1),
                       in_=A["l3_bo"]).then_inc("dgE", 16)
    ar.alloc("SGE", 512 * 4)
    onesb = ar.view("c_onesb", BF16, 512, part=1)
    p.tensor.wait("dgE")
    p.tensor.wait("ve")
    emarks = []
    for sl in range(4):
        for oc in range(NCH):
            sbank = PSx[2 + oc % 2]
            if len(emarks) >= 2:
                p.tensor.wait("ge", emarks[-2])
            for c in range(NCH):
                p.tensor.matmul(sbank[:, 0:512],
                                ar.view("WO3", BF16, 128, col_off=c * 1024 + oc * 128),
                                ar.view("S1", BF16, 512, col_off=c * 2048 + sl * 512),
                                start=(c == 0), stop=False)
            mm = p.tensor.matmul(sbank[:, 0:512],
                                 ar.view("BO3", BF16, 128, part=1, col_off=oc * 128),
                                 onesb, start=False, stop=True)
            mm.then_inc("te", 1)
            p.vector.wait("te")
            p.vector.wait("ge")
            g = ar.view("G", BF16, 512, col_off=oc * 2048 + sl * 512)
            sg = ar.view("SGE", F32, 512)
            p.vector.scalar_tensor_tensor(sg, sbank[:, 0:512], 1.0, g, op0=ALU.mult,
                                          op1=ALU.mult).then_inc("ve", 1)
            xv = ar.view("XF", F32, 512, col_off=oc * 2048 + sl * 512)
            p.gpsimd.wait("ve")
            p.gpsimd.tensor_tensor(xv, xv, sg, op=ALU.add).then_inc("ge", 1)
            emarks.append(p.val("ge"))
    ar.free("WO3"); ar.free("BO3"); ar.free("SGE"); ar.free("S1"); ar.free("LNG")


def emit_l2_projections(p, ar, A):
    PSx = p.psB
    ar.alloc("WQKV", NCH * 3072 * 2)
    p.gpsimd.dma_start(out=ar.view("WQKV", BF16, NCH * 3072),
                       in_=A["l2_wqkv"].rearrange("(c q) f -> q c f", q=128)).then_inc("dgC", 16)
    ar.alloc("G1", 8 * 4)
    p.sync.dma_start(out=ar.view("G1", F32, 8), in_=A["l2_gam"]).then_inc("dsy", 16)
    ar.alloc("ST2", 2 * 1024 * 2)
    p.vector.wait("dsy")
    p.vector.tensor_scalar(ar.view("G1", F32, 8), ar.view("G1", F32, 8), 1.0, None,
                           op0=ALU.add).then_inc("ve", 1)
    p.vector.wait("dgC")
    for c in range(NCH):
        w = ar.view("WQKV", BF16, 3072, col_off=c * 3072)
        p.vector.tensor_scalar(w, w, ar.view("G1", F32, 1, col_off=c), None,
                               op0=ALU.mult).then_inc("ve", 1)
    p.tensor.wait("ve")
    for oc in range(NCH):
        for sl in range(4):
            bank = PSx[(oc * 4 + sl) % 2]
            p.tensor.wait("dsc")
            for c in range(NCH):
                mm = p.tensor.matmul(bank[:, 0:512],
                                     ar.view("WQKV", BF16, 128, col_off=c * 3072 + oc * 128),
                                     ar.view("XN2", BF16, 512, col_off=c * 2048 + sl * 512),
                                     start=(c == 0), stop=(c == NCH - 1))
            mm.then_inc("te", 1)
            p.scalar.wait("te")
            st = ar.view("ST2", BF16, 512, col_off=(sl % 2) * 1024)
            p.scalar.activation(st, bank[:, 0:512], AF.Copy, scale=0.125).then_inc("se", 1)
            p.scalar.dma_start(out=A["l2_qscr"][oc * 128:(oc + 1) * 128, sl * 512:(sl + 1) * 512],
                               in_=st).then_inc("dsc", 16)
    karea = A["ccI2"][2048:4096, :].rearrange("(r two) w -> r (two w)", two=2)
    for kc in range(NCH):
        for sl in range(4):
            bank = PSx[(kc * 4 + sl) % 2]
            p.tensor.wait("dsc")
            for c in range(NCH):
                mm = p.tensor.matmul(bank[:, 0:512],
                                     ar.view("WQKV", BF16, 128, col_off=c * 3072 + 1024 + kc * 128),
                                     ar.view("XN2", BF16, 512, col_off=c * 2048 + sl * 512),
                                     start=(c == 0), stop=(c == NCH - 1))
            mm.then_inc("te", 1)
            p.scalar.wait("te")
            st = ar.view("ST2", BF16, 512, col_off=(sl % 2) * 1024)
            p.scalar.copy(st, bank[:, 0:512]).then_inc("se", 1)
            p.scalar.dma_start(out=karea[kc * 128:(kc + 1) * 128, sl * 512:(sl + 1) * 512],
                               in_=st).then_inc("dsc", 16)
    for j in range(NBLK):
        p.tensor.wait("dsc")
        for half in range(2):
            bank = PSx[2 + 2 * (j % 2) + half]
            for c in range(NCH):
                mm = p.tensor.matmul(bank[:, 0:512],
                                     ar.view("XN2", BF16, 128, col_off=c * 2048 + j * 128),
                                     ar.view("WQKV", BF16, 512,
                                             col_off=c * 3072 + 2048 + half * 512),
                                     start=(c == 0), stop=(c == NCH - 1))
            mm.then_inc("te", 1)
        p.scalar.wait("te")
        st = ar.view("ST2", BF16, 1024, col_off=(j % 2) * 1024)
        p.scalar.copy(st[:, 0:512], PSx[2 + 2 * (j % 2)][:, 0:512])
        p.scalar.copy(st[:, 512:1024], PSx[3 + 2 * (j % 2)][:, 0:512]).then_inc("se", 1)
        p.scalar.dma_start(out=A["ccI2"][j * 128:(j + 1) * 128, :], in_=st).then_inc("dsc", 16)
    p.gpsimd.wait("dsc")
    p.gpsimd.collective_compute("AllGather", ALU.bypass, replica_groups=RG_PAIRS,
                                ins=[A["ccI2"].opt()], outs=[A["ccO2"].opt()]).then_inc("cc", 1)
    ar.free("WQKV"); ar.free("G1"); ar.free("ST2")


def emit_l2_ffn(p, ar, A):
    PSx = p.psB
    HS = 256
    NSL = 4096 // HS
    NHC = HS // 128
    NTS = 4
    ar.alloc("Y", 8 * 2048 * 4, at=96 * 1024)
    ar.alloc("W1B", 2 * NCH * HS * 2)
    ar.alloc("W2B", 2 * NHC * 1024 * 2)
    ar.alloc("B1B", 2 * HS * 2)
    ar.alloc("B2B", 1024 * 2)
    ar.alloc("XNM", NCH * 512 * 2)
    ar.alloc("HS2", NHC * 512 * 2)
    ar.alloc("MBC", 512 * 4)
    bchot = ar.view("c_bchot", BF16, 8 * 128, part=8)
    wrows = ar.view("WROWS", BF16, 2048, part=8)
    onesb = ar.view("c_onesb", BF16, 512, part=1)

    def wslab_dma(e, s, buf):
        sem = "dgW%d" % buf
        p.gpsimd.dma_start(
            out=ar.view("W1B", BF16, NCH * HS, col_off=buf * NCH * HS),
            in_=A["l2_w1"][e].rearrange("(c q) f -> q c f", q=128)[:, :, s * HS:(s + 1) * HS],
        ).then_inc(sem, 16)
        p.gpsimd.dma_start(
            out=ar.view("W2B", BF16, NHC * 1024, col_off=buf * NHC * 1024),
            in_=A["l2_w2"][e].rearrange("(c q) f -> q c f", q=128)[:, s * NHC:(s + 1) * NHC, :],
        ).then_inc(sem, 16)
        p.gpsimd.dma_start(
            out=ar.view("B1B", BF16, HS, part=1, col_off=buf * HS),
            in_=A["l2_b1"][e:e + 1, s * HS:(s + 1) * HS]).then_inc(sem, 16)

    tmarks = {}
    wslab_dma(0, 0, 0)
    p.gpsimd.dma_start(out=ar.view("B2B", BF16, 1024, part=1),
                       in_=A["l2_b2"][0:1, :]).then_inc("dgW0", 16)
    NT = 8 * NSL
    for gi in range(NT):
        e, s = gi // NSL, gi % NSL
        buf = gi % 2
        if gi + 1 < NT:
            if gi >= 1:
                p.gpsimd.wait("te", tmarks[gi - 1])
            wslab_dma((gi + 1) // NSL, (gi + 1) % NSL, (gi + 1) % 2)
            if (gi + 1) % NSL == 0:
                p.gpsimd.wait("te")
                p.gpsimd.dma_start(out=ar.view("B2B", BF16, 1024, part=1),
                                   in_=A["l2_b2"][(gi + 1) // NSL:(gi + 1) // NSL + 1, :]
                                   ).then_inc("dgA", 16)
        w1 = ar.view("W1B", BF16, NCH * HS, col_off=buf * NCH * HS)
        w2 = ar.view("W2B", BF16, NHC * 1024, col_off=buf * NHC * 1024)
        b1 = ar.view("B1B", BF16, HS, part=1, col_off=buf * HS)
        p.tensor.wait("dgW%d" % buf)
        if s == 0:
            p.tensor.wait("dgA")
        for ts in range(NTS):
            tc = ts * 512
            p.tensor.wait("ve")
            p.tensor.matmul(PSx[6][:, 0:512], bchot[:, e * 128:(e + 1) * 128],
                            wrows[:, tc:tc + 512], start=True, stop=True).then_inc("te", 1)
            p.vector.wait("te")
            p.vector.tensor_copy(ar.view("MBC", F32, 512), PSx[6][:, 0:512]).then_inc("ve", 1)
            for c in range(NCH):
                p.vector.tensor_tensor(ar.view("XNM", BF16, 512, col_off=c * 512),
                                       ar.view("XN", BF16, 512, col_off=c * 2048 + tc),
                                       ar.view("MBC", F32, 512), op=ALU.mult).then_inc("ve", 1)
            for hc in range(NHC):
                p.tensor.wait("ve")
                p.tensor.wait("se")
                for c in range(NCH):
                    p.tensor.matmul(PSx[0][:, 0:512],
                                    w1[:, c * HS + hc * 128: c * HS + hc * 128 + 128],
                                    ar.view("XNM", BF16, 512, col_off=c * 512),
                                    start=(c == 0), stop=False)
                mm = p.tensor.matmul(PSx[0][:, 0:512], b1[:, hc * 128:(hc + 1) * 128],
                                     onesb, start=False, stop=True)
                mm.then_inc("te", 1)
                p.scalar.wait("te")
                p.scalar.activation(ar.view("HS2", BF16, 512, col_off=hc * 512),
                                    PSx[0][:, 0:512], AF.Relu).then_inc("se", 1)
            p.tensor.wait("se")
            marks = []
            for dch in range(NCH):
                bank = PSx[2 + dch % 2][:, 0:512]
                if dch >= 2:
                    p.tensor.wait("ve", marks[dch - 2])
                for hc in range(NHC):
                    mm = p.tensor.matmul(
                        bank, w2[:, hc * 1024 + dch * 128: hc * 1024 + dch * 128 + 128],
                        ar.view("HS2", BF16, 512, col_off=hc * 512),
                        start=(hc == 0), stop=(hc == NHC - 1 and s != 0))
                if s == 0:
                    mm = p.tensor.matmul(bank,
                                         ar.view("B2B", BF16, 128, part=1, col_off=dch * 128),
                                         onesb, start=False, stop=True)
                mm.then_inc("te", 1)
                p.vector.wait("te")
                yv = ar.view("Y", F32, 512, col_off=dch * 2048 + tc)
                if gi == 0:
                    p.vector.tensor_copy(yv, bank).then_inc("ve", 1)
                else:
                    p.vector.tensor_tensor(yv, yv, bank, op=ALU.add).then_inc("ve", 1)
                marks.append(p.val("ve"))
        tmarks[gi] = p.val("te")
    ar.free("W1B"); ar.free("W2B"); ar.free("B1B"); ar.free("B2B")
    ar.free("XNM"); ar.free("HS2"); ar.free("MBC")
    ar.free("WROWS"); ar.free("LOG")


def emit_l2_ffn_sparse(p, ar, A):
    PSx = p.psB
    ptb = p.psPT
    C = 512
    IOX = bass.IndirectOffsetOnAxis
    U32 = mybir.dt.uint32
    ptb = p.psPT
    identb = ar.view("c_identb", BF16, 128)
    ar.alloc("IDX", 16 * 4)
    ar.alloc("ISC", 4 * 8 * 4)
    ar.alloc("P2R", 2 * 128 * 4)
    tris = ar.view("c_tris", F32, 128)
    onc = ar.view("c_ones_col", F32, 1)
    onr = ar.view("c_ones_row", F32, 128, part=1)
    p.vector.wait("dsy")
    p.vector.tensor_copy(ar.view("P2R", F32, 8, part=1), ar.view("c_ebase", F32, 8, part=1))
    for k in range(16):
        m = ar.view("MTOK", F32, 8, col_off=k * 8)
        p.tensor.wait("ve")
        p.tensor.matmul(PSx[4][0:1, k * 8:k * 8 + 8], onc, m, start=True, stop=True)
        p.tensor.matmul(PSx[5][:, 0:8], tris, m, start=True, stop=False)
        mm = p.tensor.matmul(PSx[5][:, 0:8], onr,
                             ar.view("P2R", F32, 8, part=1), start=False, stop=True)
        mm.then_inc("te", 1)
        p.vector.wait("te")
        sc = ar.view("ISC", F32, 8)
        p.vector.tensor_tensor(sc, m, PSx[5][:, 0:8], op=ALU.mult)
        p.vector.reduce_sum(ar.view("ISC", F32, 1, col_off=8), sc, axis=AX.X)
        p.vector.tensor_scalar(ar.view("ISC", F32, 1, col_off=8),
                               ar.view("ISC", F32, 1, col_off=8), 0.5, None, op0=ALU.add)
        p.vector.tensor_copy(ar.view("IDX", U32, 1, col_off=k),
                             ar.view("ISC", F32, 1, col_off=8))
        p.vector.tensor_tensor(ar.view("P2R", F32, 8, part=1),
                               ar.view("P2R", F32, 8, part=1),
                               PSx[4][0:1, k * 8:k * 8 + 8], op=ALU.add).then_inc("ve", 1)
    ar.alloc("ZR", 1024 * 2)
    ar.alloc("XST", 2 * 1024 * 2)
    p.vector.memset(ar.view("XST", BF16, 2048), 0.0).then_inc("ve", 1)
    p.scalar.wait("ve")
    for j in range(32):
        p.scalar.dma_start(out=A["xgscr"][j * 128:(j + 1) * 128, :],
                           in_=ar.view("XST", BF16, 1024)).then_inc("dgZ", 16)
    p.vector.wait("dgZ")
    for k in range(16):
        par = k % 2
        xst = ar.view("XST", BF16, 1024, col_off=par * 1024)
        for c in range(NCH):
            p.tensor.wait("ve")
            if c == 0:
                p.tensor.wait("dgS%d" % par)
            mm = p.tensor.transpose(ptb[:, (c % 2) * 128:(c % 2) * 128 + 128],
                                    ar.view("XN", BF16, 128, col_off=c * 2048 + k * 128),
                                    identb)
            mm.then_inc("te", 1)
            p.vector.wait("te")
            p.vector.tensor_copy(xst[:, c * 128:(c + 1) * 128],
                                 ptb[:, (c % 2) * 128:(c % 2) * 128 + 128]).then_inc("ve", 1)
        p.gpsimd.wait("ve")
        p.gpsimd.indirect_dma_start(out=A["xgscr"], out_offset=IOX(ar.view("IDX", U32, 1, col_off=k), 0),
                                    in_=xst, in_offset=None).then_inc("dgS%d" % par, 16)
    ar.free("XN")
    HS = 512
    NSL = 4096 // HS
    NHC = HS // 128
    ar.alloc("XGE", 2 * NCH * 512 * 2, at=64 * 1024)
    ar.alloc("W1B", 2 * NCH * HS * 2)
    ar.alloc("W2B", 2 * NHC * 1024 * 2)
    ar.alloc("B1B", 2 * HS * 2)
    ar.alloc("B2B", 1024 * 2)
    ar.alloc("HS2", NHC * 512 * 2)
    ar.alloc("YGS", NCH * 512 * 4, at=96 * 1024)
    ar.alloc("YGB", NCH * 512 * 2)
    ar.alloc("YGT", 2 * 1024 * 2)
    ar.alloc("C0A", 8 * 8 * 2)
    onesb = ar.view("c_onesb", BF16, 512, part=1)

    def wslab(e, s, buf):
        sem = "dgW%d" % buf
        p.gpsimd.dma_start(
            out=ar.view("W1B", BF16, NCH * HS, col_off=buf * NCH * HS),
            in_=A["l2_w1"][e].rearrange("(c q) f -> q c f", q=128)[:, :, s * HS:(s + 1) * HS],
        ).then_inc(sem, 16)
        p.gpsimd.dma_start(
            out=ar.view("W2B", BF16, NHC * 1024, col_off=buf * NHC * 1024),
            in_=A["l2_w2"][e].rearrange("(c q) f -> q c f", q=128)[:, s * NHC:(s + 1) * NHC, :],
        ).then_inc(sem, 16)
        p.gpsimd.dma_start(
            out=ar.view("B1B", BF16, HS, part=1, col_off=buf * HS),
            in_=A["l2_b1"][e:e + 1, s * HS:(s + 1) * HS]).then_inc(sem, 16)

    def xge_load(e, buf):
        for c in range(NCH):
            p.scalar.dma_start(
                out=ar.view("XGE", BF16, 512, col_off=buf * NCH * 512 + c * 512),
                in_=A["xgscr"][e * C:(e + 1) * C, c * 128:(c + 1) * 128],
                transpose=True).then_inc("dgX%d" % buf, 16)

    tmarks = {}
    dmarks = []
    wslab(0, 0, 0)
    p.scalar.wait("dgS0")
    p.scalar.wait("dgS1")
    p.scalar.wait("dgA")
    xge_load(0, 0)
    p.gpsimd.dma_start(out=ar.view("B2B", BF16, 1024, part=1),
                       in_=A["l2_b2"][0:1, :]).then_inc("dgB2", 16)
    for e in range(8):
        ebuf = e % 2
        if e + 1 < 8:
            p.scalar.wait("te", tmarks.get(e - 1, 0))
            xge_load(e + 1, (e + 1) % 2)
        xge = lambda c: ar.view("XGE", BF16, 512, col_off=ebuf * NCH * 512 + c * 512)
        p.tensor.wait("dgX%d" % ebuf)
        for s in range(NSL):
            buf = (e * NSL + s) % 2
            if e * NSL + s + 1 < 8 * NSL:
                if e * NSL + s >= 1:
                    p.gpsimd.wait("te", tmarks.get((e * NSL + s - 1, "w"), 0))
                wslab((e * NSL + s + 1) // NSL, (e * NSL + s + 1) % NSL, (buf + 1) % 2)
            w1 = ar.view("W1B", BF16, NCH * HS, col_off=buf * NCH * HS)
            w2 = ar.view("W2B", BF16, NHC * 1024, col_off=buf * NHC * 1024)
            b1 = ar.view("B1B", BF16, HS, part=1, col_off=buf * HS)
            p.tensor.wait("dgW%d" % buf)
            if s == 0:
                p.tensor.wait("dgB2")
            for hc in range(NHC):
                p.tensor.wait("se")
                for c in range(NCH):
                    p.tensor.matmul(PSx[0][:, 0:512],
                                    w1[:, c * HS + hc * 128: c * HS + hc * 128 + 128],
                                    xge(c), start=(c == 0), stop=False)
                mm = p.tensor.matmul(PSx[0][:, 0:512], b1[:, hc * 128:(hc + 1) * 128],
                                     onesb, start=False, stop=True)
                mm.then_inc("te", 1)
                p.scalar.wait("te")
                p.scalar.activation(ar.view("HS2", BF16, 512, col_off=hc * 512),
                                    PSx[0][:, 0:512], AF.Relu).then_inc("se", 1)
            p.tensor.wait("se")
            for dch in range(NCH):
                bank = PSx[2 + dch % 2][:, 0:512]
                if len(dmarks) >= 2:
                    p.tensor.wait("ve", dmarks[-2])
                for hc in range(NHC):
                    mm = p.tensor.matmul(
                        bank, w2[:, hc * 1024 + dch * 128: hc * 1024 + dch * 128 + 128],
                        ar.view("HS2", BF16, 512, col_off=hc * 512),
                        start=(hc == 0), stop=(hc == NHC - 1 and s != 0))
                if s == 0:
                    mm = p.tensor.matmul(bank,
                                         ar.view("B2B", BF16, 128, part=1, col_off=dch * 128),
                                         onesb, start=False, stop=True)
                mm.then_inc("te", 1)
                p.vector.wait("te")
                yv = ar.view("YGS", F32, 512, col_off=dch * 512)
                if s == 0:
                    p.vector.tensor_copy(yv, bank).then_inc("ve", 1)
                else:
                    p.vector.tensor_tensor(yv, yv, bank, op=ALU.add).then_inc("ve", 1)
                dmarks.append(p.val("ve"))
            tmarks[(e * NSL + s, "w")] = p.val("te")
        if e + 1 < 8:
            p.gpsimd.wait("te")
            p.gpsimd.dma_start(out=ar.view("B2B", BF16, 1024, part=1),
                               in_=A["l2_b2"][e + 1:e + 2, :]).then_inc("dgB2", 16)
        p.vector.wait("ve")
        for dch in range(NCH):
            p.vector.tensor_copy(ar.view("C0A", BF16, 1, col_off=dch * 8 + e),
                                 ar.view("YGS", F32, 1, col_off=dch * 512))
            p.vector.tensor_copy(ar.view("YGB", BF16, 512, col_off=dch * 512),
                                 ar.view("YGS", F32, 512, col_off=dch * 512)).then_inc("ve", 1)
        for st in range(4):
            ygt = ar.view("YGT", BF16, 1024, col_off=(st % 2) * 1024)
            for dch in range(NCH):
                p.tensor.wait("ve")
                if dch == 0:
                    p.tensor.wait("dsY", None)
                mm = p.tensor.transpose(ptb[:, (dch % 2) * 128:(dch % 2) * 128 + 128],
                                        ar.view("YGB", BF16, 128,
                                                col_off=dch * 512 + st * 128), identb)
                mm.then_inc("te", 1)
                p.vector.wait("te")
                p.vector.tensor_copy(ygt[:, dch * 128:(dch + 1) * 128],
                                     ptb[:, (dch % 2) * 128:(dch % 2) * 128 + 128]).then_inc("ve", 1)
            p.sync.wait("ve")
            p.sync.dma_start(out=A["ygscr"][e * C + st * 128: e * C + (st + 1) * 128, :],
                             in_=ygt).then_inc("dsY", 16)
        tmarks[e] = p.val("te")
    ar.free("W1B"); ar.free("W2B"); ar.free("B1B"); ar.free("B2B"); ar.free("HS2")
    ar.free("XGE"); ar.free("YGS"); ar.free("YGB"); ar.free("XST"); ar.free("ZR")
    ar.free("YGT")
    ar.alloc("C0T", 8 * 128 * 2)
    ar.alloc("CALLC", 8 * 4)
    identb = ar.view("c_identb", BF16, 128)
    onescolb = ar.view("c_onescolb", BF16, 1)
    for dch in range(NCH):
        p.tensor.wait("ve")
        mm = p.tensor.transpose(ptb[0:8, (dch % 4) * 128:(dch % 4) * 128 + 128],
                                ar.view("C0A", BF16, 8, col_off=dch * 8), identb)
        mm.then_inc("te", 1)
        p.vector.wait("te")
        p.vector.tensor_copy(ar.view("C0T", BF16, 128, part=8, col_off=dch * 128),
                             ptb[0:8, (dch % 4) * 128:(dch % 4) * 128 + 128]).then_inc("ve", 1)
        p.tensor.wait("ve")
        mm = p.tensor.matmul(PSx[4][:, dch:dch + 1],
                             ar.view("C0T", BF16, 128, part=8, col_off=dch * 128),
                             onescolb[0:8, :], start=True, stop=True)
        mm.then_inc("te", 1)
    p.vector.wait("te")
    p.vector.tensor_copy(ar.view("CALLC", F32, 8), PSx[4][:, 0:8]).then_inc("ve", 1)
    ar.alloc("Y", 8 * 2048 * 4, at=96 * 1024)
    ar.alloc("YT", 16 * 1024 * 2)
    ar.alloc("YGF", 512 * 2)
    wrows = ar.view("WROWS", BF16, 2048, part=8)
    p.gpsimd.wait("dsY")
    for k in range(16):
        yt = ar.view("YT", BF16, 1024, col_off=k * 1024)
        p.gpsimd.indirect_dma_start(out=yt, out_offset=None, in_=A["ygscr"],
                                    in_offset=IOX(ar.view("IDX", U32, 1, col_off=k), 0)).then_inc("dgY", 16)
    for dch in range(NCH):
        for sl in range(4):
            p.tensor.wait("ve")
            mm = p.tensor.matmul(PSx[5][:, 0:512],
                                 ar.view("C0T", BF16, 128, part=8, col_off=dch * 128),
                                 wrows[:, sl * 512:(sl + 1) * 512], start=True, stop=True)
            mm.then_inc("te", 1)
            for kk in range(4):
                k = sl * 4 + kk
                p.tensor.wait("dgY")
                p.tensor.wait("ve")
                mm = p.tensor.transpose(ptb[:, (kk % 2) * 128 + 256:(kk % 2) * 128 + 384],
                                        ar.view("YT", BF16, 128,
                                                col_off=k * 1024 + dch * 128), identb)
                mm.then_inc("te", 1)
                p.vector.wait("te")
                p.vector.tensor_copy(ar.view("YGF", BF16, 128, col_off=kk * 128),
                                     ptb[:, (kk % 2) * 128 + 256:(kk % 2) * 128 + 384]).then_inc("ve", 1)
            p.vector.wait("te")
            p.vector.scalar_tensor_tensor(
                ar.view("Y", F32, 512, col_off=dch * 2048 + sl * 512),
                ar.view("YGF", BF16, 512),
                ar.view("CALLC", F32, 1, col_off=dch),
                PSx[5][:, 0:512],
                op0=ALU.add, op1=ALU.subtract).then_inc("ve", 1)
    ar.free("IDX"); ar.free("ISC"); ar.free("P2R"); ar.free("C0A"); ar.free("C0T")
    ar.free("CALLC"); ar.free("YT"); ar.free("YGF")
    ar.free("WROWS"); ar.free("MTOK"); ar.free("LOG")


def emit_l2_attention(p, ar, A):
    PSx = p.psB
    R2 = 4096
    ar.alloc("S1", 8 * 2048 * 2)
    ar.alloc("QS2", 2 * 512 * 2)
    ar.alloc("PT2", 2 * 512 * 2)
    ar.alloc("RD2", 1024 * 4)
    p.sync.wait("cc")
    smark = []
    for g in range(4):
        ar.alloc("K2", 4096 * 2 * 2, at=96 * 1024)
        ar.alloc("V2", 32 * 260 * 2, at=96 * 1024 + 16 * 1024 + 512)
        for r in range(2):
            kflat = A["ccO2"][r * R2 + 2048: r * R2 + 4096, :].rearrange("a b -> (a b)")
            for par in range(2):
                for mb in range(2):
                    hh = 4 * g + 2 * mb + par
                    src = kflat.rearrange("(hq dd j i) -> hq dd j i", dd=64, j=16, i=128)[hh]
                    dst = ar.view("K2", BF16, 4096, part=64, base=par * 64,
                                  col_off=mb * 4096).rearrange(
                        "p (j two i) -> p two j i", two=2, i=128)[:, r, :, :]
                    p.sync.dma_start(out=dst, in_=src).then_inc("dkv", 16)
            vflat = A["ccO2"][r * R2: r * R2 + 2048, :].rearrange("a b -> (a b)")
            for m in range(4):
                hh = 4 * g + m
                vdst = ar.view("V2", BF16, 32 * 260).rearrange(
                    "p (j two h x) -> p j two h x", two=2, h=4, x=65)[:, :, r, m, 0:64]
                p.sync.dma_start(out=vdst, in_=vflat.rearrange(
                    "(j i h x) -> i h j x", i=128, h=16, x=64)[:, hh]).then_inc("dkv", 16)
        p.vector.wait("dkv")
        p.vector.memset(ar.view("V2", BF16, 32 * 260).rearrange(
            "p (kt h x) -> p kt h x", h=4, x=65)[:, :, :, 64], 1.0).then_inc("ve", 1)
        for m in range(4):
            j = 4 * g + m
            par = m % 2
            mb = m // 2
            for qg in range(4):
                qb2 = (j * 4 + qg) % 2
                p.sync.wait("te", smark[-2] if len(smark) >= 2 else 0)
                qs = ar.view("QS2", BF16, 512, part=64, base=par * 64, col_off=qb2 * 512)
                p.sync.dma_start(out=qs, in_=A["l2_qscr"][j * 64:(j + 1) * 64,
                                                          qg * 512:(qg + 1) * 512]).then_inc(
                    "dsQ%d" % qb2, 16)
                sbank = PSx[2 + qb2]
                p.tensor.wait("dkv")
                p.tensor.wait("dsQ%d" % qb2)
                p.tensor.wait("ve")
                for kt in range(32):
                    bank = PSx[kt % 2]
                    p.tensor.wait("se")
                    mm = p.tensor.matmul(bank[:, 0:512],
                                         ar.view("K2", BF16, 128, part=64, base=par * 64,
                                                 col_off=mb * 4096 + kt * 128),
                                         qs, start=True, stop=True)
                    mm.then_inc("te", 1)
                    p.scalar.wait("te")
                    pt = ar.view("PT2", BF16, 512, col_off=(kt % 2) * 512)
                    p.scalar.activation(pt, bank[:, 0:512], AF.Exp).then_inc("se", 1)
                    p.tensor.wait("se")
                    mm = p.tensor.matmul(sbank[0:65, 0:512],
                                         ar.view("V2", BF16, 65, col_off=kt * 260 + m * 65),
                                         pt, start=(kt == 0), stop=(kt == 31))
                mm.then_inc("te", 1)
                p.vector.wait("te")
                rd = ar.view("RD2", F32, 512, part=1)
                p.vector.reciprocal(rd, sbank[64:65, 0:512]).then_inc("ve", 1)
                p.tensor.wait("ve")
                mm = p.tensor.matmul(PSx[4][0:64, 0:512],
                                     ar.view("c_ones_row", F32, 64, part=1),
                                     rd, start=True, stop=True)
                mm.then_inc("te", 1)
                p.vector.wait("te")
                p.vector.tensor_copy(ar.view("RD2", F32, 512, col_off=512)[0:64, :],
                                     PSx[4][0:64, 0:512]).then_inc("ve", 1)
                p.vector.tensor_tensor(
                    ar.view("S1", BF16, 512, part=64, base=par * 64,
                            col_off=(j // 2) * 2048 + qg * 512),
                    sbank[0:64, 0:512],
                    ar.view("RD2", F32, 512, col_off=512)[0:64, :],
                    op=ALU.mult).then_inc("ve", 1)
                smark.append(p.val("te"))
        barrier(p)
        ar.free("K2"); ar.free("V2")
    ar.free("QS2"); ar.free("PT2"); ar.free("RD2")


def emit_l2_post(p, ar, A):
    PSx = p.psB
    ar.alloc("WO2", NCH * 1024 * 2)
    p.gpsimd.wait("te"); p.gpsimd.wait("ve"); p.gpsimd.wait("se")
    p.gpsimd.dma_start(out=ar.view("WO2", BF16, NCH * 1024),
                       in_=A["l2_wout"].rearrange("(c q) f -> q c f", q=128)).then_inc("dgE", 16)
    ar.alloc("SGE", 512 * 4)
    p.tensor.wait("dgE")
    p.tensor.wait("ve")
    emarks = []
    for sl in range(4):
        for oc in range(NCH):
            sbank = PSx[2 + oc % 2]
            if len(emarks) >= 2:
                p.tensor.wait("ge", emarks[-2])
            for c in range(NCH):
                mm = p.tensor.matmul(sbank[:, 0:512],
                                     ar.view("WO2", BF16, 128, col_off=c * 1024 + oc * 128),
                                     ar.view("S1", BF16, 512, col_off=c * 2048 + sl * 512),
                                     start=(c == 0), stop=(c == NCH - 1))
            mm.then_inc("te", 1)
            p.vector.wait("te")
            p.vector.wait("ge")
            g = ar.view("G", BF16, 512, col_off=oc * 2048 + sl * 512)
            sg = ar.view("SGE", F32, 512)
            p.vector.scalar_tensor_tensor(sg, sbank[:, 0:512], 1.0, g, op0=ALU.mult,
                                          op1=ALU.mult).then_inc("ve", 1)
            xv = ar.view("XF", F32, 512, col_off=oc * 2048 + sl * 512)
            p.gpsimd.wait("ve")
            p.gpsimd.tensor_tensor(xv, xv, sg, op=ALU.add).then_inc("ge", 1)
            emarks.append(p.val("ge"))
    ar.free("WO2"); ar.free("SGE"); ar.free("S1")


def emit_layer2(p, ar, A):
    if "c_tris" not in ar.regions:
        ar.alloc("c_tris", 128 * 4)
        p.sync.dma_start(out=ar.view("c_tris", F32, 128), in_=A["c_tris"]).then_inc("dsy", 16)
    ar.alloc("c_ebase", 8 * 4)
    p.sync.dma_start(out=ar.view("c_ebase", F32, 8, part=1), in_=A["c_ebase"]).then_inc("dsy", 16)
    ar.alloc("WrT2", 8 * 8 * 4)
    p.sync.dma_start(out=ar.view("WrT2", F32, 64),
                     in_=A["l2_wr"].rearrange("(c q) f -> q c f", q=128)).then_inc("dsy", 16)
    ar.alloc("rb2", 8 * 4)
    p.sync.dma_start(out=ar.view("rb2", F32, 8, part=1), in_=A["l2_br"]).then_inc("dsy", 16)
    for e in ENGINES:
        getattr(p, e).wait("dsy")
    emit_ln(p, ar, A, want_router=True, want_l2=True, gate_region="WrT2", rbias="rb2")
    barrier(p)
    emit_l2_projections(p, ar, A)
    ar.free("XN2")
    barrier(p)
    emit_top2(p, ar, mode="argmax")
    barrier(p)
    emit_l2_ffn_sparse(p, ar, A)
    barrier(p)
    emit_silu_to_G(p, ar)
    barrier(p)
    emit_l2_attention(p, ar, A)
    barrier(p)
    emit_l2_post(p, ar, A)
    barrier(p)
    ar.free("G"); ar.free("WrT2"); ar.free("rb2")
    ar.free("c_tris"); ar.free("c_ebase")


UPTO = 99
UPTO3 = 99

def emit_layer1(p, ar, A):
    ar.alloc("l1_brow", 1024 * 2)
    p.gpsimd.dma_start(out=ar.view("l1_brow", BF16, 1024, part=1),
                       in_=A["l1_brow_d"]).then_inc("dgA", 16)
    ar.alloc("l1_mbrow", 1024 * 2)
    p.gpsimd.dma_start(out=ar.view("l1_mbrow", BF16, 1024, part=1),
                       in_=A["l1_mbrow_d"]).then_inc("dgA", 16)
    emit_l1_wbuild(p, ar, A)
    emit_ln(p, ar, A, want_router=False, want_l2=False)
    barrier(p)
    emit_l1_attention_a(p, ar, A)
    barrier(p)
    emit_l1_ffn(p, ar, A)
    ar.free("XN")
    barrier(p)
    emit_l1_attention_b(p, ar, A)
    barrier(p)
    ar.free("G"); ar.free("l1_brow"); ar.free("l1_mbrow")


def emit_layer3(p, ar, A):
    ar.alloc("routT3", 8 * 8 * 4)
    p.sync.dma_start(out=ar.view("routT3", F32, 64),
                     in_=A["l3_routT"].rearrange("(c q) f -> q c f", q=128)).then_inc("dsy", 16)
    ar.alloc("rb3", 8 * 4)
    p.sync.dma_start(out=ar.view("rb3", F32, 8, part=1),
                     in_=A["l3_rb"]).then_inc("dsy", 16)
    for e in ENGINES:
        getattr(p, e).wait("dsy")
    emit_ln(p, ar, A, want_router=True, want_l2=False, gate_region="routT3", rbias="rb3")
    barrier(p)
    if UPTO3 < 2:
        return
    emit_l3_projections(p, ar, A)
    barrier(p)
    if UPTO3 < 3:
        return
    emit_top2(p, ar)
    barrier(p)
    if UPTO3 < 4:
        return
    emit_l3_ffn(p, ar, A)
    ar.free("XN")
    barrier(p)
    if UPTO3 < 5:
        return
    emit_silu_to_G(p, ar)
    barrier(p)
    if UPTO3 < 6:
        return
    emit_l3_attention(p, ar, A)
    barrier(p)
    if UPTO3 < 7:
        return
    emit_l3_post(p, ar, A)
    barrier(p)
    ar.free("G"); ar.free("routT3"); ar.free("rb3")


def emit_layer0(p, ar, A):
    if "c_tris" not in ar.regions:
        ar.alloc("c_tris", 128 * 4)
        p.sync.dma_start(out=ar.view("c_tris", F32, 128), in_=A["c_tris"]).then_inc("dsy", 16)
    ar.alloc("c_ebase0", 8 * 4)
    p.sync.dma_start(out=ar.view("c_ebase0", F32, 8, part=1),
                     in_=A["c_ebase0"]).then_inc("dsy", 16)
    ar.alloc("gateT", 8 * 8 * 4)
    p.sync.dma_start(out=ar.view("gateT", F32, 64),
                     in_=A["gateT_d"].rearrange("(c q) f -> q c f", q=128)).then_inc("dsy", 16)
    ar.alloc("c_mskab", 256 * 4, at=205376)
    p.sync.dma_start(out=ar.view("c_mskab", F32, 256),
                     in_=A["c_mskab"]).then_inc("dsy", 16)
    if UPTO < 1:
        return
    emit_ln(p, ar, A, want_router=True, want_l2=False)
    barrier(p)
    if UPTO < 2:
        return
    emit_l0_projections(p, ar, A, 0)
    barrier(p)
    if UPTO < 3:
        return
    emit_top2(p, ar, save_tok=True)
    barrier(p)
    if UPTO < 4:
        return
    emit_l0_ffn_sparse(p, ar, A)
    barrier(p)
    if UPTO < 5:
        return
    emit_silu_to_G(p, ar)
    barrier(p)
    if UPTO < 6:
        return
    ar.free("gateT")
    emit_l0_attention(p, ar, A)
    barrier(p)
    ar.free("G"); ar.free("c_mskab")


def build_program(layers):
    nc = bass.Bass("TRN2", target_bir_lowering=False)
    stack = contextlib.ExitStack()
    A = {}
    A["xT"] = nc.dram_tensor("xT", [D, TOK], F32, kind="ExternalInput").ap()
    A["xoT"] = nc.dram_tensor("xoT", [D, TOK], F32, kind="ExternalOutput").ap()
    cshapes = {
        "c_ones_col": ([128, 1], F32), "c_ones_row": ([1, 128], F32),
        "c_identb": ([128, 128], BF16),
        "c_bchot": ([8, 8 * 128], BF16),
        "c_onesb": ([1, 512], BF16),
        "c_onescolb": ([128, 1], BF16),
    }
    for nm, (shp, dt) in cshapes.items():
        A[nm] = nc.dram_tensor(nm, shp, dt, kind="ExternalInput").ap()
    if 0 in layers:
        A["wq"] = nc.dram_tensor("l0_wq", [D, 64], F32, kind="ExternalInput").ap()
        A["wk"] = nc.dram_tensor("l0_wk", [D, 64], F32, kind="ExternalInput").ap()
        A["wv"] = nc.dram_tensor("l0_wv", [D, D], F32, kind="ExternalInput").ap()
        A["gateT_d"] = nc.dram_tensor("l0_gateT", [D, 8], F32, kind="ExternalInput").ap()
        A["wg"] = nc.dram_tensor("l0_wg", [8, D, 2048], F32, kind="ExternalInput").ap()
        A["wu"] = nc.dram_tensor("l0_wu", [8, D, 2048], F32, kind="ExternalInput").ap()
        A["wd"] = nc.dram_tensor("l0_wd", [8, 2048, D], F32, kind="ExternalInput").ap()
        A["l0_qscr"] = nc.dram_tensor("l0_qscr", [64, TOK], BF16).ap()
        A["xgscr0"] = nc.dram_tensor("xgscr0", [8 * 768, D], BF16).ap()
        A["ygscr0"] = nc.dram_tensor("ygscr0", [8 * 768, D], BF16).ap()
        A["c_ebase0"] = nc.dram_tensor("c_ebase0", [1, 8], F32, kind="ExternalInput").ap()
        if "c_tris" not in A:
            A["c_tris"] = nc.dram_tensor("c_tris", [128, 128], F32, kind="ExternalInput").ap()
        A["ccI"] = nc.dram_tensor("ccI0", [2176, 1024], BF16).ap()
        A["ccO"] = nc.dram_tensor("ccO0", [2 * 2176, 1024], BF16).ap()
        A["c_mskab"] = nc.dram_tensor("c_mskab", [128, 256], F32, kind="ExternalInput").ap()
    if 1 in layers:
        A["l1_proj"] = nc.dram_tensor("l1_proj", [D, D], F32, kind="ExternalInput").ap()
        A["l1_m"] = nc.dram_tensor("l1_m", [128, 8 * 64], F32, kind="ExternalInput").ap()
        A["l1_mixT"] = nc.dram_tensor("l1_mixT", [D, D], F32, kind="ExternalInput").ap()
        A["l1_mbrow_d"] = nc.dram_tensor("l1_mb", [1, D], F32, kind="ExternalInput").ap()
        A["l1_ar"] = nc.dram_tensor("l1_ar", [4, 1024], F32, kind="ExternalInput").ap()
        A["l1_br"] = nc.dram_tensor("l1_br", [4, 256], F32, kind="ExternalInput").ap()
        A["l1_ai"] = nc.dram_tensor("l1_ai", [4, 1024], F32, kind="ExternalInput").ap()
        A["l1_bi"] = nc.dram_tensor("l1_bi", [4, 256], F32, kind="ExternalInput").ap()
        A["l1_brow_d"] = nc.dram_tensor("l1_brow", [1, D], F32, kind="ExternalInput").ap()
        A["l1_wscr_r"] = nc.dram_tensor("l1_wscr_r", [32, 32, 16, 16], F32).ap()
        A["l1_wscr_i"] = nc.dram_tensor("l1_wscr_i", [32, 32, 16, 16], F32).ap()
        A["ccGi"] = nc.dram_tensor("ccGi", [64, 1024], F32).ap()
        A["ccGo"] = nc.dram_tensor("ccGo", [64, 1024], F32).ap()
    if 2 in layers:
        for nm, shp in [("l2_gam", [128, 8]), ("l2_wqkv", [D, 3072]),
                        ("l2_wout", [D, D]), ("l2_w1", [8, D, 4096]),
                        ("l2_b1", [8, 4096]), ("l2_w2", [8, 4096, D]),
                        ("l2_b2", [8, D]), ("l2_wr", [D, 8]), ("l2_br", [1, 8])]:
            A[nm] = nc.dram_tensor(nm, shp, F32, kind="ExternalInput").ap()
        if "c_tris" not in A:
            A["c_tris"] = nc.dram_tensor("c_tris", [128, 128], F32, kind="ExternalInput").ap()
        A["l2_qscr"] = nc.dram_tensor("l2_qscr", [D, TOK], BF16).ap()
        A["xgscr"] = nc.dram_tensor("xgscr", [4096, D], BF16).ap()
        A["ygscr"] = nc.dram_tensor("ygscr", [4096, D], BF16).ap()
        A["c_ebase"] = nc.dram_tensor("c_ebase", [1, 8], F32, kind="ExternalInput").ap()
        A["ccI2"] = nc.dram_tensor("ccI2", [4096, 1024], BF16).ap()
        A["ccO2"] = nc.dram_tensor("ccO2", [8192, 1024], BF16).ap()
    if 3 in layers:
        for nm, shp in [("l3_wq", [D, D]), ("l3_bq", [1, D]), ("l3_wk", [D, 256]),
                        ("l3_bk", [1, 256]), ("l3_wv", [D, 256]), ("l3_bv", [1, 256]),
                        ("l3_lng", [128, 8]), ("l3_lnb", [128, 8]), ("l3_wo", [D, D]),
                        ("l3_bo", [1, D]), ("l3_routT", [D, 8]), ("l3_rb", [1, 8]),
                        ("l3_w1", [8, D, 512]), ("l3_b1", [8, 512]),
                        ("l3_w2", [8, 512, D]), ("l3_b2", [8, D])]:
            A[nm] = nc.dram_tensor(nm, shp, F32, kind="ExternalInput").ap()
        A["l3_qscr"] = nc.dram_tensor("l3_qscr", [D, TOK], BF16).ap()
        A["c_mskT3"] = nc.dram_tensor("c_mskT3", [128, 8 * 512], BF16, kind="ExternalInput").ap()
        A["ccI3"] = nc.dram_tensor("ccI3", [1024, 1024], BF16).ap()
        A["ccO3"] = nc.dram_tensor("ccO3", [2048, 1024], BF16).ap()

    p = Prog(nc)
    ar = Arena(nc, stack)
    p.psB = [stack.enter_context(nc.psum_tensor(f"ps{i}", [128, 512], F32))
             for i in range(7)]
    p.psB = [t[:] for t in p.psB]
    p.psPT = stack.enter_context(nc.psum_tensor("psPT", [128, 1024], BF16))[:]

    ar.alloc("XF", 8 * 2048 * 4, at=0)
    ar.alloc("XN", 8 * 2048 * 2, at=64 * 1024)
    for nm, (shp, dt) in cshapes.items():
        item = 4 if dt == F32 else 2
        ar.alloc(nm, shp[1] * item)
        p.sync.dma_start(out=ar.view(nm, dt, shp[1], part=shp[0]),
                         in_=A[nm]).then_inc("dsy", 16)
    for c in range(NCH):
        p.sync.dma_start(out=ar.view("XF", F32, 2048, col_off=c * 2048),
                         in_=A["xT"][c * 128:(c + 1) * 128, :]).then_inc("dsy", 16)
    for e in ENGINES:
        getattr(p, e).wait("dsy")

    for lay in layers:
        barrier(p)
        if lay == 0:
            emit_layer0(p, ar, A)
        elif lay == 1:
            emit_layer1(p, ar, A)
        elif lay == 2:
            emit_layer2(p, ar, A)
        elif lay == 3:
            emit_layer3(p, ar, A)
        else:
            raise NotImplementedError(lay)

    p.sync.wait("ge"); p.sync.wait("ve"); p.sync.wait("se"); p.sync.wait("te")
    for c in range(NCH):
        p.sync.dma_start(out=A["xoT"][c * 128:(c + 1) * 128, :],
                         in_=ar.view("XF", F32, 2048, col_off=c * 2048)).then_inc("dsy", 16)
    p.sync.wait("dsy")
    p.finalize(stack)
    stack.close()
    return nc


def _mskT3(h):
    triT = np.where(np.arange(128)[:, None] <= np.arange(128)[None, :], 0.0,
                    -1e9).astype(np.float32)
    inv = np.full((128, 128), -1e9, np.float32)
    zer = np.zeros((128, 128), np.float32)
    out = np.zeros((128, 8 * 512), np.float32)
    for d in range(8):
        for m in range(4):
            s = d - 2 * m
            if s < 0 or (h == 1 and s == 0):
                t = zer
            elif s == (0 if h == 0 else 1):
                t = triT
            else:
                t = inv
            out[:, d * 512 + m * 128: d * 512 + (m + 1) * 128] = t
    return out


def _consts(h):
    tri = np.where(np.arange(128)[None, :] <= np.arange(128)[:, None], 0.0,
                   -1e9).astype(np.float32)
    zer = np.zeros((128, 128), np.float32)
    neg = np.full((128, 128), -1e9, np.float32)
    mska, mskb = (tri, neg) if h == 0 else (zer, tri)
    bchot = np.zeros((8, 8 * 128), np.float32)
    for e in range(8):
        bchot[e, e * 128:(e + 1) * 128] = 1.0
    return {
        "c_ones_col": np.ones((128, 1), np.float32),
        "c_ones_row": np.ones((1, 128), np.float32),

        "c_identb": np.eye(128, dtype=np.float32).astype(ml_dtypes.bfloat16),
        "c_mskab": np.concatenate([mska, mskb], axis=1),
        "c_bchot": bchot.astype(ml_dtypes.bfloat16),
        "c_onesb": np.ones((1, 512), np.float32).astype(ml_dtypes.bfloat16),
        "c_onescolb": np.ones((128, 1), np.float32).astype(ml_dtypes.bfloat16),
        "c_mskT3": _mskT3(h).astype(ml_dtypes.bfloat16),
    }


def run_layer0(x, wq, wk, wv, gate_w, wg, wu, wd):
    nc = build_program([0])
    in_maps = []
    for c in range(NCORES):
        b, h = c // 2, c % 2
        own = np.ascontiguousarray(
            x[b].reshape(32, 128, D)[h::2].reshape(TOK, D).T)
        m = _consts(h)
        m.update({
            "xT": own,
            "l0_wq": wq, "l0_wk": wk, "l0_wv": wv,
            "l0_gateT": np.ascontiguousarray(gate_w.T),
            "l0_wg": wg, "l0_wu": wu, "l0_wd": wd,
        })
        in_maps.append(m)
    res = run_bass_kernel_spmd(nc, in_maps, core_ids=list(range(NCORES)))
    out = np.zeros((B, S, D), np.float32)
    for c in range(NCORES):
        b, h = c // 2, c % 2
        xo = res.results[c]["xoT"].T.reshape(NBLK, 128, D)
        o = out[b].reshape(32, 128, D)
        o[h::2] = xo
    return out




def l1_host_inputs(inp):
    proj = np.asarray(inp["L1_attn_projection_1nck"])[0]
    l1_proj = np.ascontiguousarray(proj.transpose(1, 0, 2).reshape(1024, 1024))
    halves = np.asarray(inp["L1_attn_halves"])
    diag = np.asarray(inp["L1_attn_diagonals_nk"])
    iu, ju = np.triu_indices(64, k=1)
    M = np.zeros((128, 512), np.float32)
    for n in range(16):
        bn, cn = (n % 2) * 64, (n // 2) * 64
        blk = np.zeros((64, 64), np.float32)
        blk[iu, ju] = halves[n]
        blk[ju, iu] = halves[n]
        blk[np.arange(64), np.arange(64)] = diag[n]
        M[bn:bn + 64, cn:cn + 64] = blk
    g1r = np.asarray(inp["L1_ffn_G1_R"])[0]
    g2r = np.asarray(inp["L1_ffn_G2_R"])[:, :, 0, :]
    g1i = np.asarray(inp["L1_ffn_G1_I"])[0]
    g2i = np.asarray(inp["L1_ffn_G2_I"])[:, :, 0, :]
    return {
        "l1_proj": l1_proj,
        "l1_m": M,
        "l1_mixT": np.ascontiguousarray(np.asarray(inp["L1_attn_mixer_w"]).T),
        "l1_mb": np.asarray(inp["L1_attn_mixer_b"]).reshape(1, -1),
        "l1_ar": np.ascontiguousarray(g1r.transpose(1, 0, 2).reshape(4, 1024)),
        "l1_br": np.ascontiguousarray(g2r.transpose(0, 2, 1).reshape(4, 256)),
        "l1_ai": np.ascontiguousarray(g1i.transpose(1, 0, 2).reshape(4, 1024)),
        "l1_bi": np.ascontiguousarray(g2i.transpose(0, 2, 1).reshape(4, 256)),
        "l1_brow": np.concatenate([np.asarray(inp["L1_ffn_bias_real"]),
                                   np.asarray(inp["L1_ffn_bias_imag"])]).reshape(1, -1),
    }


def l0_host_inputs(inp):
    return {
        "l0_wq": np.asarray(inp["L0_attn_Wq"]), "l0_wk": np.asarray(inp["L0_attn_Wk"]),
        "l0_wv": np.asarray(inp["L0_attn_Wv"]),
        "l0_gateT": np.ascontiguousarray(np.asarray(inp["L0_ffn_gate_w"]).T),
        "l0_wg": np.asarray(inp["L0_ffn_w_gate"]), "l0_wu": np.asarray(inp["L0_ffn_w_up"]),
        "l0_wd": np.asarray(inp["L0_ffn_w_down"]),
        "c_tris": np.triu(np.ones((128, 128), np.float32), 1),
        "c_ebase0": (np.arange(8, dtype=np.float32) * 768 + 1).reshape(1, 8),
    }


def l3_host_inputs(inp):
    return {
        "l3_wq": np.asarray(inp["L3_attn_Wq"]), "l3_bq": np.asarray(inp["L3_attn_bq"]).reshape(1, -1),
        "l3_wk": np.asarray(inp["L3_attn_Wk"]), "l3_bk": np.asarray(inp["L3_attn_bk"]).reshape(1, -1),
        "l3_wv": np.asarray(inp["L3_attn_Wv"]), "l3_bv": np.asarray(inp["L3_attn_bv"]).reshape(1, -1),
        "l3_lng": np.ascontiguousarray(np.asarray(inp["L3_attn_ln_g"]).reshape(8, 128).T),
        "l3_lnb": np.ascontiguousarray(np.asarray(inp["L3_attn_ln_b"]).reshape(8, 128).T),
        "l3_wo": np.asarray(inp["L3_attn_Wo"]), "l3_bo": np.asarray(inp["L3_attn_bo"]).reshape(1, -1),
        "l3_routT": np.asarray(inp["L3_ffn_router_w"]),
        "l3_rb": np.asarray(inp["L3_ffn_router_b"]).reshape(1, -1),
        "l3_w1": np.asarray(inp["L3_ffn_W1"]), "l3_b1": np.asarray(inp["L3_ffn_b1"]),
        "l3_w2": np.asarray(inp["L3_ffn_W2"]), "l3_b2": np.asarray(inp["L3_ffn_b2"]),
    }


def l2_host_inputs(inp):
    return {
        "l2_gam": np.ascontiguousarray(np.asarray(inp["L2_attn_gamma"]).reshape(8, 128).T),
        "l2_wqkv": np.asarray(inp["L2_attn_w_qkv"]),
        "l2_wout": np.asarray(inp["L2_attn_w_out"]),
        "l2_w1": np.asarray(inp["L2_ffn_W1"]), "l2_b1": np.asarray(inp["L2_ffn_b1"]),
        "l2_w2": np.asarray(inp["L2_ffn_W2"]), "l2_b2": np.asarray(inp["L2_ffn_b2"]),
        "l2_wr": np.asarray(inp["L2_ffn_Wr"]),
        "l2_br": np.asarray(inp["L2_ffn_br"]).reshape(1, -1),
        "c_tris": np.triu(np.ones((128, 128), np.float32), 1),
        "c_ebase": (np.arange(8, dtype=np.float32) * 512 + 1).reshape(1, 8),
    }


def layer_in_maps(layers, x, inp):
    per_layer = {}
    if 0 in layers:
        per_layer.update(l0_host_inputs(inp))
    if 1 in layers:
        per_layer.update(l1_host_inputs(inp))
    if 2 in layers:
        per_layer.update(l2_host_inputs(inp))
    if 3 in layers:
        per_layer.update(l3_host_inputs(inp))
    maps = []
    for c in range(NCORES):
        b, h = c // 2, c % 2
        own = np.ascontiguousarray(x[b].reshape(32, 128, D)[h::2].reshape(TOK, D).T)
        m = _consts(h)
        if 0 not in layers:
            m.pop("c_mskab", None)
        if 3 not in layers:
            m.pop("c_mskT3", None)
        m["xT"] = own
        m.update(per_layer)
        maps.append(m)
    return maps


def run_layers(layers, x, inp):
    nc = build_program(layers)
    maps = layer_in_maps(layers, x, inp)
    res = run_bass_kernel_spmd(nc, maps, core_ids=list(range(NCORES)))
    out = np.zeros((B, S, D), np.float32)
    for c in range(NCORES):
        b, h = c // 2, c % 2
        xo = res.results[c]["xoT"].T.reshape(NBLK, 128, D)
        o = out[b].reshape(32, 128, D)
        o[h::2] = xo
    return out


_PROG_CACHE = {}
FUSED = True


def kernel(**inputs):
    x = np.asarray(inputs["L0_attn_X"], np.float32)
    if FUSED:
        key = "fused"
        if key not in _PROG_CACHE:
            _PROG_CACHE[key] = build_program([0, 1, 2, 3])
        maps = layer_in_maps([0, 1, 2, 3], x, inputs)
        res = run_bass_kernel_spmd(_PROG_CACHE[key], maps,
                                   core_ids=list(range(NCORES)))
        xTs = [np.asarray(res.results[c]["xoT"]) for c in range(NCORES)]
    else:
        xTs = [np.ascontiguousarray(
            x[c // 2].reshape(32, 128, D)[c % 2::2].reshape(TOK, D).T)
            for c in range(NCORES)]
        for lay in range(4):
            if lay not in _PROG_CACHE:
                _PROG_CACHE[lay] = build_program([lay])
            maps = layer_in_maps([lay], x, inputs)
            for c in range(NCORES):
                maps[c]["xT"] = xTs[c]
            res = run_bass_kernel_spmd(_PROG_CACHE[lay], maps,
                                       core_ids=list(range(NCORES)))
            xTs = [np.asarray(res.results[c]["xoT"]) for c in range(NCORES)]
    out = np.zeros((B, S, D), np.float32)
    for c in range(NCORES):
        b, h = c // 2, c % 2
        o = out[b].reshape(32, 128, D)
        o[h::2] = xTs[c].T.reshape(NBLK, 128, D)
    return out


# version 08
# speedup vs baseline: 2.6945x; 1.2637x over previous
import contextlib
import numpy as np
import ml_dtypes
import concourse.bass as bass
import concourse.mybir as mybir
from concourse.bass_utils import run_bass_kernel_spmd

F32 = mybir.dt.float32
BF16 = mybir.dt.bfloat16
AF = mybir.ActivationFunctionType
ALU = mybir.AluOpType
AX = mybir.AxisListType

NCORES = 8
B, S, D = 4, 4096, 1024
TOK = 2048
NBLK = 16
NCH = 8
RG_PAIRS = [[0, 1], [2, 3], [4, 5], [6, 7]]

ENGINES = ("sync", "scalar", "vector", "gpsimd", "tensor")


class _Item:
    __slots__ = ("attr", "args", "kw", "incs")

    def __init__(self, attr, args, kw):
        self.attr = attr
        self.args = args
        self.kw = kw
        self.incs = []


SEM_OF = {"scalar": "se", "vector": "ve", "gpsimd": "ge"}
NOCHAIN_ATTRS = ("wait_ge", "dma_start", "indirect_dma_start", "collective_compute")


class _ItemRec:
    def __init__(self, prog, item, own):
        self._prog = prog
        self._item = item
        self._own = own

    def then_inc(self, semname, k=1):
        if semname == self._own:
            return self
        self._prog.counts[semname] = self._prog.counts.get(semname, 0) + k
        self._item.incs.append((semname, k))
        return self


class _EngRec:
    def __init__(self, prog, eng):
        object.__setattr__(self, "_prog", prog)
        object.__setattr__(self, "_eng", eng)

    def wait(self, semname, val=None):
        p = self._prog
        v = p.counts.get(semname, 0) if val is None else val
        if v <= 0:
            return
        last = p.last_wait[self._eng].get(semname, -1)
        if v <= last:
            return
        p.last_wait[self._eng][semname] = v
        item = _Item("wait_ge", (("__SEM__", semname), v), {})
        p.streams[self._eng].append(item)

    def __getattr__(self, attr):
        prog, eng = self._prog, self._eng
        own = SEM_OF.get(eng)

        def call(*args, **kw):
            item = _Item(attr, args, kw)
            prog.streams[eng].append(item)
            if own is not None and attr not in NOCHAIN_ATTRS:
                prog.counts[own] = prog.counts.get(own, 0) + 1
            return _ItemRec(prog, item, own if attr not in NOCHAIN_ATTRS else None)

        return call


class Prog:
    def __init__(self, nc):
        self.nc = nc
        self.streams = {e: [] for e in ENGINES}
        self.counts = {}
        self.sems = {}
        self.last_wait = {e: {} for e in ENGINES}
        for e in ENGINES:
            setattr(self, e, _EngRec(self, e))
        self.psB = None

    def val(self, semname):
        return self.counts.get(semname, 0)

    def finalize(self, stack):
        names = set(self.counts)
        for st in self.streams.values():
            for it in st:
                if it.attr == "wait_ge":
                    names.add(it.args[0][1])
        for n in sorted(names):
            self.sems[n] = stack.enter_context(self.nc.semaphore(n))

        def resolve(a):
            if isinstance(a, tuple) and len(a) == 2 and a[0] == "__SEM__":
                return self.sems[a[1]]
            return a

        def replay(stream, ename):
            own = SEM_OF.get(ename)
            auto = self.sems.get(own) if own else None

            def run(eng):
                n = 0
                for it in stream:
                    if it.attr == "wait_ge":
                        eng.wait_ge(resolve(it.args[0]), it.args[1])
                        continue
                    chained = auto is not None and it.attr not in NOCHAIN_ATTRS
                    if auto is not None and n > 0:
                        eng.wait_ge(auto, n)
                    instr = getattr(eng, it.attr)(*[resolve(a) for a in it.args],
                                                  **{k: resolve(v) for k, v in it.kw.items()})
                    if chained:
                        instr = instr.then_inc(auto, 1)
                        n += 1
                    for (sn, k) in it.incs:
                        instr = instr.then_inc(self.sems[sn], k)
            return run

        with self.nc.Block() as block:
            for e in ENGINES:
                if self.streams[e]:
                    getattr(block, e)(replay(self.streams[e], e))

    def nops(self):
        return sum(len(s) for s in self.streams.values())


class Arena:
    BYTES = 207 * 1024 + 512

    def __init__(self, nc, stack):
        self.t = stack.enter_context(
            nc.sbuf_tensor("arena", [128, self.BYTES // 2], BF16))
        self.regions = {}
        self.frees = [(0, self.BYTES)]

    def alloc(self, name, nbytes, align=64, at=None):
        assert name not in self.regions, name
        nbytes = (nbytes + align - 1) // align * align
        if at is not None:
            for i, (off, sz) in enumerate(self.frees):
                if off <= at and at + nbytes <= off + sz:
                    pre = (off, at - off)
                    post = (at + nbytes, off + sz - at - nbytes)
                    self.frees[i:i + 1] = [q for q in (pre, post) if q[1] > 0]
                    self.regions[name] = (at, nbytes)
                    ARENA_TRACE[name] = at
                    return at
            raise MemoryError(f"arena: cannot place {name} at {at} "
                              f"({nbytes}B); free={self.frees}")
        for i in range(len(self.frees) - 1, -1, -1):
            off, sz = self.frees[i]
            aoff = (off + sz - nbytes) // align * align
            if aoff >= off:
                pre = (off, aoff - off)
                post = (aoff + nbytes, off + sz - aoff - nbytes)
                self.frees[i:i + 1] = [q for q in (pre, post) if q[1] > 0]
                self.regions[name] = (aoff, nbytes)
                ARENA_TRACE[name] = aoff
                return aoff
        raise MemoryError(f"arena: cannot alloc {name} ({nbytes}B); "
                          f"free={self.frees}, used={sorted(self.regions)}")

    def free(self, name):
        off, sz = self.regions.pop(name)
        self.frees.append((off, sz))
        self.frees.sort()
        merged = []
        for off, sz in self.frees:
            if merged and merged[-1][0] + merged[-1][1] == off:
                merged[-1] = (merged[-1][0], merged[-1][1] + sz)
            else:
                merged.append((off, sz))
        self.frees = [tuple(m) for m in merged]

    def view(self, name, dtype, cols, part=128, base=0, col_off=0):
        off, sz = self.regions[name]
        item = 2 if dtype == BF16 else 4
        assert (col_off + cols) * item <= sz, (name, col_off, cols, sz)
        start2 = (off + col_off * item) // 2
        ap = self.t[base:base + part, start2:start2 + cols * item // 2]
        if dtype != BF16:
            ap = ap.bitcast(dtype)
        return ap


ARENA_TRACE = {}

def barrier(p):
    names = list(p.counts.keys())
    for e in ENGINES:
        er = getattr(p, e)
        for s in names:
            er.wait(s)


def cdiv(a, b):
    return -(-a // b)


def load_const_sbuf(p, ar, A, name, shape, dtype, queue="sync", sem="dsy"):
    rows, cols = shape
    item = 4 if dtype == F32 else 2
    ar.alloc(name, cols * item)
    dst = ar.view(name, dtype, cols, part=rows)
    getattr(p, queue).dma_start(out=dst, in_=A[name]).then_inc(sem, 16)


def emit_ln(p, ar, A, want_router, want_l2, gate_region="gateT", rbias=None):
    if "XN" not in ar.regions:
        ar.alloc("XN", 8 * 2048 * 2, at=64 * 1024)
    p.tensor.wait("dsy")
    p.vector.wait("dsy")
    p.scalar.wait("dsy")
    ar.alloc("SQ", 2 * 512 * 4)
    ar.alloc("XT", 2 * 512 * 4)
    ar.alloc("STR", 6 * 2048 * 4, at=96 * 1024)
    ar.alloc("XS", 8 * 512 * 4, at=96 * 1024 + 49152)
    if want_l2:
        ar.alloc("XN2", 8 * 2048 * 2)
    if want_router:
        ar.alloc("LOG", 16 * 8 * 4)

    def strow(i, sl=None):
        co = i * 2048
        if sl is None:
            return ar.view("STR", F32, 2048, part=1, col_off=co)
        return ar.view("STR", F32, 512, part=1, col_off=co + sl * 512)

    ones_col = ar.view("c_ones_col", F32, 1)
    ones_row = ar.view("c_ones_row", F32, 128, part=1)
    PSx = p.psB

    for sl in range(4):
        p.tensor.wait("ve")
        for c in range(NCH):
            sq = ar.view("SQ", F32, 512, col_off=(c % 2) * 512)
            xsl = ar.view("XF", F32, 512, col_off=c * 2048 + sl * 512)
            p.scalar.wait("te")
            p.scalar.activation(sq, xsl, AF.Square).then_inc("se", 1)
            p.tensor.wait("se")
            p.tensor.matmul(PSx[0][0:1, 0:512], ones_col, xsl,
                            start=(c == 0), stop=(c == NCH - 1))
            mm = p.tensor.matmul(PSx[0][32:33, 0:512], ones_col, sq,
                                 start=(c == 0), stop=(c == NCH - 1))
            mm.then_inc("te", 1)
        p.vector.wait("te")
        p.vector.tensor_copy(strow(0, sl), PSx[0][0:1, 0:512])
        p.vector.tensor_copy(strow(1, sl), PSx[0][32:33, 0:512]).then_inc("ve", 1)

    p.vector.wait("ve")
    p.vector.tensor_scalar(strow(2), strow(0), 1.0 / D, None, op0=ALU.mult)
    p.vector.tensor_scalar(strow(5), strow(1), 1.0 / D, None, op0=ALU.mult)
    p.vector.tensor_tensor(strow(4), strow(2), strow(2), op=ALU.mult)
    p.vector.tensor_tensor(strow(5), strow(5), strow(4), op=ALU.subtract)
    p.vector.tensor_scalar(strow(4), strow(5), 1e-5, None, op0=ALU.add).then_inc("ve", 1)
    p.scalar.wait("ve")
    p.scalar.activation(strow(4), strow(4), AF.Sqrt).then_inc("se", 1)
    p.vector.wait("se")
    p.vector.reciprocal(strow(3), strow(4)).then_inc("ve", 1)
    if want_l2:
        p.vector.tensor_scalar(strow(5), strow(5), float(D), None, op0=ALU.mult).then_inc("ve", 1)
        p.scalar.wait("ve")
        p.scalar.activation(strow(5), strow(5), AF.Sqrt).then_inc("se", 1)
        p.vector.wait("se")
        p.vector.tensor_tensor(strow(5), strow(5), strow(3), op=ALU.mult)
        p.vector.tensor_scalar(strow(5), strow(5), 1e-12, None, op0=ALU.max)
        p.vector.reciprocal(strow(5), strow(5))
        p.vector.tensor_scalar(strow(5), strow(5), float(np.sqrt(D)), None, op0=ALU.mult)
        p.vector.tensor_tensor(strow(4), strow(5), strow(3), op=ALU.mult).then_inc("ve", 1)

    for sl in range(4):
        p.tensor.wait("ve")
        p.tensor.matmul(PSx[1][:, 0:512], ones_row, strow(2, sl), start=True, stop=True)
        mm = p.tensor.matmul(PSx[2][:, 0:512], ones_row, strow(3, sl), start=True, stop=True)
        if want_l2:
            mm = p.tensor.matmul(PSx[4][:, 0:512], ones_row, strow(4, sl), start=True, stop=True)
        mm.then_inc("te", 1)
        p.vector.wait("te")
        if want_router:
            p.vector.wait("te", p.val("te"))
        for c in range(NCH):
            xsl = ar.view("XF", F32, 512, col_off=c * 2048 + sl * 512)
            xt = ar.view("XT", F32, 512, col_off=(c % 2) * 512)
            xn = ar.view("XN", BF16, 512, col_off=c * 2048 + sl * 512)
            xs = ar.view("XS", F32, 512, col_off=c * 512)
            p.vector.tensor_tensor(xt, xsl, PSx[1][:, 0:512], op=ALU.subtract)
            p.vector.tensor_tensor(xs, xt, PSx[2][:, 0:512], op=ALU.mult)
            p.vector.tensor_copy(xn, xs).then_inc("ve", 1)
            if want_l2:
                xn2 = ar.view("XN2", BF16, 512, col_off=c * 2048 + sl * 512)
                p.vector.tensor_tensor(xn2, xt, PSx[4][:, 0:512], op=ALU.mult).then_inc("ve", 1)
        if want_router:
            gate = ar.view(gate_region, F32, 64)
            p.tensor.wait("ve")
            for tb in range(4):
                for c in range(NCH):
                    mm = p.tensor.matmul(
                        PSx[3][:, (sl * 4 + tb) * 8:(sl * 4 + tb) * 8 + 8],
                        ar.view("XS", F32, 128, col_off=c * 512 + tb * 128),
                        gate[:, c * 8:(c + 1) * 8],
                        start=(c == 0), stop=rbias is None and (c == NCH - 1))
                if rbias is not None:
                    ones_r = ar.view("c_ones_row", F32, 128, part=1)
                    mm = p.tensor.matmul(
                        PSx[3][:, (sl * 4 + tb) * 8:(sl * 4 + tb) * 8 + 8],
                        ones_r, ar.view(rbias, F32, 8, part=1),
                        start=False, stop=True)
            mm.then_inc("te", 1)
    if want_router:
        p.vector.wait("te")
        p.vector.tensor_copy(ar.view("LOG", F32, 128), PSx[3][:, 0:128]).then_inc("ve", 1)
    ar.free("SQ"); ar.free("XT"); ar.free("STR"); ar.free("XS")


def emit_top2(p, ar, negbig=-1e30, mode="top2", save_tok=False):
    if mode == "argmax":
        ar.alloc("MTOK", 16 * 8 * 4)
    if save_tok:
        ar.alloc("M1T", 16 * 8 * 4)
        ar.alloc("MALL", 16 * 8 * 4)
        ar.alloc("WTF", 16 * 8 * 4)
    ar.alloc("TOPS", 6 * 16 * 4)
    ar.alloc("TSC", 2 * 8 * 4)
    ar.alloc("WTOK", 16 * 8 * 2)
    identb = ar.view("c_identb", BF16, 128)

    def tcol(i, t):
        return ar.view("TOPS", F32, 1, col_off=i * 16 + t)

    for t in range(16):
        lg = ar.view("LOG", F32, 8, col_off=t * 8)
        m1 = ar.view("TSC", F32, 8)
        l2 = ar.view("TSC", F32, 8, col_off=8)
        wt = ar.view("WTOK", BF16, 8, col_off=t * 8)
        p.vector.wait("ve")
        p.vector.reduce_max(tcol(0, t), lg, axis=AX.X)
        if mode == "argmax":
            p.vector.tensor_scalar(wt, lg, tcol(0, t), None, op0=ALU.is_ge)
            p.vector.tensor_scalar(ar.view("MTOK", F32, 8, col_off=t * 8), lg,
                                   tcol(0, t), None, op0=ALU.is_ge).then_inc("ve", 1)
            continue
        p.vector.tensor_scalar(tcol(1, t), tcol(0, t), -1.0, None, op0=ALU.mult)
        p.vector.tensor_scalar(m1, lg, tcol(0, t), None, op0=ALU.is_ge)
        if save_tok:
            p.vector.tensor_copy(ar.view("M1T", F32, 8, col_off=t * 8), m1)
        p.vector.scalar_tensor_tensor(l2, m1, negbig, lg, op0=ALU.mult, op1=ALU.add)
        p.vector.reduce_max(tcol(2, t), l2, axis=AX.X).then_inc("ve", 1)
        p.scalar.wait("ve")
        p.scalar.activation(l2, lg, AF.Exp, bias=tcol(1, t)).then_inc("se", 1)
        p.vector.wait("se")
        p.vector.tensor_scalar(m1, lg, tcol(2, t), None, op0=ALU.is_ge)
        if save_tok:
            p.vector.tensor_copy(ar.view("MALL", F32, 8, col_off=t * 8), m1)
        p.vector.tensor_tensor(m1, m1, l2, op=ALU.mult)
        p.vector.reduce_sum(tcol(3, t), m1, axis=AX.X)
        p.vector.reciprocal(tcol(4, t), tcol(3, t))
        p.vector.tensor_scalar(wt, m1, tcol(4, t), None, op0=ALU.mult)
        if save_tok:
            p.vector.tensor_scalar(ar.view("WTF", F32, 8, col_off=t * 8), m1,
                                   tcol(4, t), None, op0=ALU.mult).then_inc("ve", 1)
        else:
            p.vector.tensor_copy(wt, wt).then_inc("ve", 1)
    ar.alloc("WROWS", 2048 * 2)
    ptb = p.psPT
    for t in range(16):
        p.tensor.wait("ve")
        mm = p.tensor.transpose(ptb[0:8, (t % 4) * 128:(t % 4) * 128 + 128],
                                ar.view("WTOK", BF16, 8, col_off=t * 8), identb)
        mm.then_inc("te", 1)
        p.vector.wait("te")
        p.vector.tensor_copy(ar.view("WROWS", BF16, 128, part=8, col_off=t * 128),
                             ptb[0:8, (t % 4) * 128:(t % 4) * 128 + 128]).then_inc("ve", 1)
    ar.free("TOPS"); ar.free("TSC"); ar.free("WTOK")


def emit_l0_ffn(p, ar, A):
    E, F = 8, 2048
    HS = 256
    NSL = F // HS
    NTS = 4
    NHC = HS // 128
    ar.alloc("Y", 8 * 2048 * 4, at=96 * 1024)
    ar.alloc("WG", 2 * NCH * HS * 2)
    ar.alloc("WU", 2 * NCH * HS * 2)
    ar.alloc("WD", 2 * NHC * 1024 * 2)
    ar.alloc("SGS", 512 * 2)
    ar.alloc("UWS", 512 * 2)
    ar.alloc("HWS", NHC * 512 * 2)
    ar.alloc("WBC", 512 * 4)
    PSx = p.psB
    bchot = ar.view("c_bchot", BF16, 8 * 128, part=8)
    wrows = ar.view("WROWS", BF16, 2048, part=8)

    def wslab_dma(e, s, buf):
        sem = "dgW%d" % buf
        src_g = A["wg"][e].rearrange("(c q) f -> q c f", q=128)[:, :, s * HS:(s + 1) * HS]
        src_u = A["wu"][e].rearrange("(c q) f -> q c f", q=128)[:, :, s * HS:(s + 1) * HS]
        src_d = A["wd"][e].rearrange("(c q) f -> q c f", q=128)[:, s * NHC:(s + 1) * NHC, :]
        p.gpsimd.dma_start(out=ar.view("WG", BF16, NCH * HS, col_off=buf * NCH * HS),
                           in_=src_g).then_inc(sem, 16)
        p.gpsimd.dma_start(out=ar.view("WU", BF16, NCH * HS, col_off=buf * NCH * HS),
                           in_=src_u).then_inc(sem, 16)
        p.gpsimd.dma_start(out=ar.view("WD", BF16, NHC * 1024, col_off=buf * NHC * 1024),
                           in_=src_d).then_inc(sem, 16)

    tmarks = {}
    wslab_dma(0, 0, 0)
    for gi in range(E * NSL):
        e, s = gi // NSL, gi % NSL
        buf = gi % 2
        if gi + 1 < E * NSL:
            if gi >= 1:
                p.gpsimd.wait("te", tmarks[gi - 1])
            wslab_dma((gi + 1) // NSL, (gi + 1) % NSL, (gi + 1) % 2)
        wg = ar.view("WG", BF16, NCH * HS, col_off=buf * NCH * HS)
        wu = ar.view("WU", BF16, NCH * HS, col_off=buf * NCH * HS)
        wd = ar.view("WD", BF16, NHC * 1024, col_off=buf * NHC * 1024)
        p.tensor.wait("dgW%d" % buf)
        for ts in range(NTS):
            tc = ts * 512
            p.tensor.wait("ve")
            p.tensor.matmul(PSx[6][:, 0:512], bchot[:, e * 128:(e + 1) * 128],
                            wrows[:, tc:tc + 512], start=True, stop=True).then_inc("te", 1)
            p.vector.wait("te")
            p.vector.tensor_copy(ar.view("WBC", F32, 512), PSx[6][:, 0:512]).then_inc("ve", 1)
            for hc in range(NHC):
                p.tensor.wait("ve")
                p.tensor.wait("ge")
                for c in range(NCH):
                    xn = ar.view("XN", BF16, 512, col_off=c * 2048 + tc)
                    p.tensor.matmul(PSx[0][:, 0:512],
                                    wg[:, c * HS + hc * 128: c * HS + hc * 128 + 128],
                                    xn, start=(c == 0), stop=(c == NCH - 1))
                    mm = p.tensor.matmul(PSx[1][:, 0:512],
                                         wu[:, c * HS + hc * 128: c * HS + hc * 128 + 128],
                                         xn, start=(c == 0), stop=(c == NCH - 1))
                mm.then_inc("te", 1)
                p.scalar.wait("te")
                p.scalar.wait("ge")
                p.scalar.activation(ar.view("SGS", BF16, 512), PSx[0][:, 0:512],
                                    AF.Silu).then_inc("se", 1)
                p.vector.wait("te")
                p.vector.wait("ge")
                p.vector.tensor_tensor(ar.view("UWS", BF16, 512), PSx[1][:, 0:512],
                                       ar.view("WBC", F32, 512), op=ALU.mult).then_inc("ve", 1)
                p.gpsimd.wait("se")
                p.gpsimd.wait("ve")
                p.gpsimd.wait("te")
                p.gpsimd.tensor_tensor(ar.view("HWS", BF16, 512, col_off=hc * 512),
                                       ar.view("SGS", BF16, 512),
                                       ar.view("UWS", BF16, 512), op=ALU.mult).then_inc("ge", 1)
            p.tensor.wait("ge")
            marks = []
            for dch in range(NCH):
                bank = PSx[2 + dch % 2][:, 0:512]
                if dch >= 2:
                    p.tensor.wait("ve", marks[dch - 2])
                for hc in range(NHC):
                    mm = p.tensor.matmul(
                        bank, wd[:, hc * 1024 + dch * 128: hc * 1024 + dch * 128 + 128],
                        ar.view("HWS", BF16, 512, col_off=hc * 512),
                        start=(hc == 0), stop=(hc == NHC - 1))
                mm.then_inc("te", 1)
                p.vector.wait("te")
                yv = ar.view("Y", F32, 512, col_off=dch * 2048 + tc)
                if gi == 0:
                    p.vector.tensor_copy(yv, bank).then_inc("ve", 1)
                else:
                    p.vector.tensor_tensor(yv, yv, bank, op=ALU.add).then_inc("ve", 1)
                marks.append(p.val("ve"))
        tmarks[gi] = p.val("te")
    ar.free("WG"); ar.free("WU"); ar.free("WD")
    ar.free("SGS"); ar.free("UWS"); ar.free("HWS"); ar.free("WBC")
    ar.free("WROWS"); ar.free("LOG")


def emit_l0_ffn_sparse(p, ar, A):
    PSx = p.psB
    ptb = p.psPT
    C = 768
    NSEC = 2
    SECS = [(0, 512), (512, 256)]
    IOX = bass.IndirectOffsetOnAxis
    U32 = mybir.dt.uint32
    identb = ar.view("c_identb", BF16, 128)
    tris = ar.view("c_tris", F32, 128)
    onc = ar.view("c_ones_col", F32, 1)
    onr = ar.view("c_ones_row", F32, 128, part=1)
    ar.alloc("IDXA", 16 * 4)
    ar.alloc("IDXB", 16 * 4)
    ar.alloc("WCA", 16 * 4)
    ar.alloc("WCB", 16 * 4)
    ar.alloc("ISC", 4 * 8 * 4)
    ar.alloc("P2R", 128 * 4)
    p.vector.wait("se")
    p.vector.tensor_copy(ar.view("P2R", F32, 8, part=1), ar.view("c_ebase0", F32, 8, part=1))
    for k in range(16):
        mall = ar.view("MALL", F32, 8, col_off=k * 8)
        m1 = ar.view("M1T", F32, 8, col_off=k * 8)
        wtk = ar.view("WTF", F32, 8, col_off=k * 8)
        p.tensor.wait("ve")
        p.tensor.matmul(PSx[4][0:1, k * 8:k * 8 + 8], onc, mall, start=True, stop=True)
        p.tensor.matmul(PSx[5][:, 0:8], tris, mall, start=True, stop=False)
        mm = p.tensor.matmul(PSx[5][:, 0:8], onr,
                             ar.view("P2R", F32, 8, part=1), start=False, stop=True)
        mm.then_inc("te", 1)
        p.vector.wait("te")
        sc = ar.view("ISC", F32, 8)
        sc2 = ar.view("ISC", F32, 8, col_off=16)
        p.vector.tensor_tensor(sc, m1, PSx[5][:, 0:8], op=ALU.mult)
        p.vector.reduce_sum(ar.view("ISC", F32, 1, col_off=8), sc, axis=AX.X)
        p.vector.tensor_scalar(ar.view("ISC", F32, 1, col_off=8),
                               ar.view("ISC", F32, 1, col_off=8), 0.5, None, op0=ALU.add)
        p.vector.tensor_copy(ar.view("IDXA", U32, 1, col_off=k),
                             ar.view("ISC", F32, 1, col_off=8))
        p.vector.tensor_tensor(sc2, mall, m1, op=ALU.subtract)
        p.vector.tensor_tensor(sc2, sc2, PSx[5][:, 0:8], op=ALU.mult)
        p.vector.reduce_sum(ar.view("ISC", F32, 1, col_off=9), sc2, axis=AX.X)
        p.vector.tensor_scalar(ar.view("ISC", F32, 1, col_off=9),
                               ar.view("ISC", F32, 1, col_off=9), 0.5, None, op0=ALU.add)
        p.vector.tensor_copy(ar.view("IDXB", U32, 1, col_off=k),
                             ar.view("ISC", F32, 1, col_off=9))
        p.vector.tensor_tensor(sc, m1, wtk, op=ALU.mult)
        p.vector.reduce_sum(ar.view("WCA", F32, 1, col_off=k), sc, axis=AX.X)
        p.vector.tensor_tensor(sc2, mall, wtk, op=ALU.mult)
        p.vector.reduce_sum(ar.view("WCB", F32, 1, col_off=k), sc2, axis=AX.X)
        p.vector.tensor_tensor(ar.view("WCB", F32, 1, col_off=k),
                               ar.view("WCB", F32, 1, col_off=k),
                               ar.view("WCA", F32, 1, col_off=k), op=ALU.subtract)
        p.vector.tensor_tensor(ar.view("P2R", F32, 8, part=1),
                               ar.view("P2R", F32, 8, part=1),
                               PSx[4][0:1, k * 8:k * 8 + 8], op=ALU.add).then_inc("ve", 1)
    ar.alloc("XST", 2 * 1024 * 2)
    p.vector.memset(ar.view("XST", BF16, 2048), 0.0).then_inc("ve", 1)
    p.scalar.wait("ve")
    for j in range(48):
        p.scalar.dma_start(out=A["xgscr0"][j * 128:(j + 1) * 128, :],
                           in_=ar.view("XST", BF16, 1024)).then_inc("dgZ", 16)
    p.vector.wait("dgZ")
    for k in range(16):
        par = k % 2
        xst = ar.view("XST", BF16, 1024, col_off=par * 1024)
        for c in range(NCH):
            p.tensor.wait("ve")
            if c == 0:
                p.tensor.wait("dgS%d" % par)
            mm = p.tensor.transpose(ptb[:, (c % 2) * 128:(c % 2) * 128 + 128],
                                    ar.view("XN", BF16, 128, col_off=c * 2048 + k * 128),
                                    identb)
            mm.then_inc("te", 1)
            p.vector.wait("te")
            p.vector.tensor_copy(xst[:, c * 128:(c + 1) * 128],
                                 ptb[:, (c % 2) * 128:(c % 2) * 128 + 128]).then_inc("ve", 1)
        p.gpsimd.wait("ve")
        p.gpsimd.indirect_dma_start(out=A["xgscr0"],
                                    out_offset=IOX(ar.view("IDXA", U32, 1, col_off=k), 0),
                                    in_=xst, in_offset=None).then_inc("dgS%d" % par, 16)
        p.gpsimd.indirect_dma_start(out=A["xgscr0"],
                                    out_offset=IOX(ar.view("IDXB", U32, 1, col_off=k), 0),
                                    in_=xst, in_offset=None).then_inc("dgS%d" % par, 16)
    ar.free("XN")
    HS = 256
    NSL = 2048 // HS
    NHC = HS // 128
    ar.alloc("XGE", 2 * NCH * C * 2, at=64 * 1024)
    ar.alloc("WG", 2 * NCH * HS * 2)
    ar.alloc("WU", 2 * NCH * HS * 2)
    ar.alloc("WD", 2 * NHC * 1024 * 2)
    ar.alloc("SGS", 512 * 2)
    ar.alloc("UWS", 512 * 2)
    ar.alloc("HWS", NHC * 512 * 2)
    ar.alloc("YGS", NCH * C * 4, at=96 * 1024)
    ar.alloc("YGB", NCH * C * 2)
    ar.alloc("YGT", 2 * 1024 * 2)

    def wslab(e, s, buf):
        sem = "dgW%d" % buf
        p.gpsimd.dma_start(out=ar.view("WG", BF16, NCH * HS, col_off=buf * NCH * HS),
                           in_=A["wg"][e].rearrange("(c q) f -> q c f", q=128)[:, :, s * HS:(s + 1) * HS]).then_inc(sem, 16)
        p.gpsimd.dma_start(out=ar.view("WU", BF16, NCH * HS, col_off=buf * NCH * HS),
                           in_=A["wu"][e].rearrange("(c q) f -> q c f", q=128)[:, :, s * HS:(s + 1) * HS]).then_inc(sem, 16)
        p.gpsimd.dma_start(out=ar.view("WD", BF16, NHC * 1024, col_off=buf * NHC * 1024),
                           in_=A["wd"][e].rearrange("(c q) f -> q c f", q=128)[:, s * NHC:(s + 1) * NHC, :]).then_inc(sem, 16)

    def xge_load(e, buf):
        for c in range(NCH):
            p.scalar.dma_start(
                out=ar.view("XGE", BF16, C, col_off=buf * NCH * C + c * C),
                in_=A["xgscr0"][e * C:(e + 1) * C, c * 128:(c + 1) * 128],
                transpose=True).then_inc("dgX%d" % buf, 16)

    tmarks = {}
    dmarks = []
    wslab(0, 0, 0)
    p.scalar.wait("dgS0")
    p.scalar.wait("dgS1")
    xge_load(0, 0)
    for e in range(8):
        ebuf = e % 2
        if e + 1 < 8:
            p.scalar.wait("te", tmarks.get(e - 1, 0))
            xge_load(e + 1, (e + 1) % 2)
        p.tensor.wait("dgX%d" % ebuf)
        for s in range(NSL):
            gi = e * NSL + s
            buf = gi % 2
            if gi + 1 < 8 * NSL:
                if gi >= 1:
                    p.gpsimd.wait("te", tmarks.get((gi - 1, "w"), 0))
                wslab((gi + 1) // NSL, (gi + 1) % NSL, (gi + 1) % 2)
            wg = ar.view("WG", BF16, NCH * HS, col_off=buf * NCH * HS)
            wu = ar.view("WU", BF16, NCH * HS, col_off=buf * NCH * HS)
            wd = ar.view("WD", BF16, NHC * 1024, col_off=buf * NHC * 1024)
            p.tensor.wait("dgW%d" % buf)
            for off, n in SECS:
                for hc in range(NHC):
                    p.tensor.wait("ve")
                    p.tensor.wait("ge")
                    for c in range(NCH):
                        xg = ar.view("XGE", BF16, n, col_off=ebuf * NCH * C + c * C + off)
                        p.tensor.matmul(PSx[0][:, 0:n],
                                        wg[:, c * HS + hc * 128: c * HS + hc * 128 + 128],
                                        xg, start=(c == 0), stop=(c == NCH - 1))
                        mm = p.tensor.matmul(PSx[1][:, 0:n],
                                             wu[:, c * HS + hc * 128: c * HS + hc * 128 + 128],
                                             xg, start=(c == 0), stop=(c == NCH - 1))
                    mm.then_inc("te", 1)
                    p.scalar.wait("te")
                    p.scalar.wait("ge")
                    p.scalar.activation(ar.view("SGS", BF16, n), PSx[0][:, 0:n],
                                        AF.Silu).then_inc("se", 1)
                    p.vector.wait("te")
                    p.vector.wait("ge")
                    p.vector.tensor_copy(ar.view("UWS", BF16, n), PSx[1][:, 0:n]).then_inc("ve", 1)
                    p.gpsimd.wait("se")
                    p.gpsimd.wait("ve")
                    p.gpsimd.wait("te")
                    p.gpsimd.tensor_tensor(ar.view("HWS", BF16, n, col_off=hc * 512),
                                           ar.view("SGS", BF16, n),
                                           ar.view("UWS", BF16, n), op=ALU.mult).then_inc("ge", 1)
                p.tensor.wait("ge")
                for dch in range(NCH):
                    bank = PSx[2 + dch % 2][:, 0:n]
                    if len(dmarks) >= 2:
                        p.tensor.wait("ve", dmarks[-2])
                    for hc in range(NHC):
                        mm = p.tensor.matmul(
                            bank, wd[:, hc * 1024 + dch * 128: hc * 1024 + dch * 128 + 128],
                            ar.view("HWS", BF16, n, col_off=hc * 512),
                            start=(hc == 0), stop=(hc == NHC - 1))
                    mm.then_inc("te", 1)
                    p.vector.wait("te")
                    yv = ar.view("YGS", F32, n, col_off=dch * C + off)
                    if s == 0:
                        p.vector.tensor_copy(yv, bank).then_inc("ve", 1)
                    else:
                        p.vector.tensor_tensor(yv, yv, bank, op=ALU.add).then_inc("ve", 1)
                    dmarks.append(p.val("ve"))
            tmarks[(gi, "w")] = p.val("te")
        p.vector.wait("ve")
        for dch in range(NCH):
            p.vector.tensor_copy(ar.view("YGB", BF16, C, col_off=dch * C),
                                 ar.view("YGS", F32, C, col_off=dch * C)).then_inc("ve", 1)
        for st in range(C // 128):
            ygt = ar.view("YGT", BF16, 1024, col_off=(st % 2) * 1024)
            for dch in range(NCH):
                p.tensor.wait("ve")
                if dch == 0:
                    p.tensor.wait("dsY", None)
                mm = p.tensor.transpose(ptb[:, (dch % 2) * 128:(dch % 2) * 128 + 128],
                                        ar.view("YGB", BF16, 128,
                                                col_off=dch * C + st * 128), identb)
                mm.then_inc("te", 1)
                p.vector.wait("te")
                p.vector.tensor_copy(ygt[:, dch * 128:(dch + 1) * 128],
                                     ptb[:, (dch % 2) * 128:(dch % 2) * 128 + 128]).then_inc("ve", 1)
            p.sync.wait("ve")
            p.sync.dma_start(out=A["ygscr0"][e * C + st * 128: e * C + (st + 1) * 128, :],
                             in_=ygt).then_inc("dsY", 16)
        tmarks[e] = p.val("te")
    ar.free("WG"); ar.free("WU"); ar.free("WD"); ar.free("SGS"); ar.free("UWS")
    ar.free("HWS"); ar.free("XGE"); ar.free("YGS"); ar.free("YGB"); ar.free("YGT")
    ar.free("XST")
    ar.alloc("Y", 8 * 2048 * 4, at=96 * 1024)
    ar.alloc("YTA", 2 * 1024 * 2)
    ar.alloc("YTB", 2 * 1024 * 2)
    ar.alloc("YSM", 2 * 1024 * 2)
    gmarks = []
    for k in range(16):
        par = k % 2
        p.gpsimd.wait("dsY")
        if len(gmarks) >= 2:
            p.gpsimd.wait("ve", gmarks[-2])
        yta = ar.view("YTA", BF16, 1024, col_off=par * 1024)
        ytb = ar.view("YTB", BF16, 1024, col_off=par * 1024)
        p.gpsimd.indirect_dma_start(out=yta, out_offset=None, in_=A["ygscr0"],
                                    in_offset=IOX(ar.view("IDXA", U32, 1, col_off=k), 0)).then_inc("dgY%d" % par, 16)
        p.gpsimd.indirect_dma_start(out=ytb, out_offset=None, in_=A["ygscr0"],
                                    in_offset=IOX(ar.view("IDXB", U32, 1, col_off=k), 0)).then_inc("dgY%d" % par, 16)
        p.vector.wait("dgY%d" % par)
        ysum = ar.view("YSM", BF16, 1024, col_off=par * 1024)
        p.vector.tensor_scalar(ysum, yta, ar.view("WCA", F32, 1, col_off=k), None,
                               op0=ALU.mult)
        p.vector.scalar_tensor_tensor(ysum, ytb, ar.view("WCB", F32, 1, col_off=k),
                                      ysum, op0=ALU.mult, op1=ALU.add).then_inc("ve", 1)
        for c in range(NCH):
            p.tensor.wait("ve")
            mm = p.tensor.transpose(ptb[:, 256 + (c % 2) * 128:256 + (c % 2) * 128 + 128],
                                    ysum[:, c * 128:(c + 1) * 128], identb)
            mm.then_inc("te", 1)
            p.vector.wait("te")
            p.vector.tensor_copy(ar.view("Y", F32, 128, col_off=c * 2048 + k * 128),
                                 ptb[:, 256 + (c % 2) * 128:256 + (c % 2) * 128 + 128]).then_inc("ve", 1)
        gmarks.append(p.val("ve"))
    ar.free("YTA"); ar.free("YTB"); ar.free("YSM")
    ar.free("IDXA"); ar.free("IDXB"); ar.free("WCA"); ar.free("WCB")
    ar.free("ISC"); ar.free("P2R"); ar.free("M1T"); ar.free("MALL"); ar.free("WTF")
    ar.free("WROWS"); ar.free("LOG")


def emit_silu_to_G(p, ar):
    ar.alloc("G", 8 * 2048 * 2, at=64 * 1024)
    for c in range(NCH):
        for sl in range(4):
            p.scalar.wait("ve")
            p.scalar.activation(ar.view("G", BF16, 512, col_off=c * 2048 + sl * 512),
                                ar.view("Y", F32, 512, col_off=c * 2048 + sl * 512),
                                AF.Silu).then_inc("se", 1)
    ar.free("Y")


def emit_l0_projections(p, ar, A, lay):
    PSx = p.psB
    for nm in ("wq", "wk"):
        ar.alloc(nm, NCH * 64 * 2)
        p.gpsimd.dma_start(out=ar.view(nm, BF16, NCH * 64),
                           in_=A[nm].rearrange("(c q) f -> q c f", q=128)).then_inc("dgp", 16)
    ar.alloc("wv", NCH * 1024 * 2)
    p.gpsimd.dma_start(out=ar.view("wv", BF16, NCH * 1024),
                       in_=A["wv"].rearrange("(c q) f -> q c f", q=128)).then_inc("dgp", 16)
    ar.alloc("Kown", 2048 * 2)
    ar.alloc("QST", 2 * 512 * 2)
    ar.alloc("VST", 2 * 1024 * 2)
    p.tensor.wait("dgp")
    p.tensor.wait("ve")
    for sl in range(4):
        for c in range(NCH):
            xn = ar.view("XN", BF16, 512, col_off=c * 2048 + sl * 512)
            p.tensor.matmul(PSx[0][0:64, 0:512], ar.view("wk", BF16, 64, col_off=c * 64),
                            xn, start=(c == 0), stop=(c == NCH - 1))
            mm = p.tensor.matmul(PSx[1][0:64, 0:512], ar.view("wq", BF16, 64, col_off=c * 64),
                                 xn, start=(c == 0), stop=(c == NCH - 1))
        mm.then_inc("te", 1)
        p.scalar.wait("te")
        p.scalar.copy(ar.view("Kown", BF16, 512, part=64, col_off=sl * 512),
                      PSx[0][0:64, 0:512]).then_inc("se", 1)
        qst = ar.view("QST", BF16, 512, part=64, col_off=(sl % 2) * 512)
        p.scalar.wait("dsc")
        p.scalar.activation(qst, PSx[1][0:64, 0:512], AF.Copy,
                            scale=0.125).then_inc("se", 1)
        p.scalar.dma_start(out=A["l0_qscr"][:, sl * 512:(sl + 1) * 512],
                           in_=qst).then_inc("dsc", 16)
        p.tensor.wait("se")
    for j in range(NBLK):
        for half in range(2):
            p.tensor.wait("dsc")
            for c in range(NCH):
                xn = ar.view("XN", BF16, 128, col_off=c * 2048 + j * 128)
                mm = p.tensor.matmul(PSx[2 + half][:, 0:512],
                                     xn, ar.view("wv", BF16, 512, col_off=c * 1024 + half * 512),
                                     start=(c == 0), stop=(c == NCH - 1))
            mm.then_inc("te", 1)
            p.scalar.wait("te")
            p.scalar.copy(ar.view("VST", BF16, 512, col_off=(j % 2) * 1024 + half * 512),
                          PSx[2 + half][:, 0:512]).then_inc("se", 1)
        p.scalar.wait("se")
        p.scalar.dma_start(out=A["ccI"][j * 128:(j + 1) * 128, :],
                           in_=ar.view("VST", BF16, 1024, col_off=(j % 2) * 1024)).then_inc("dsc", 16)
    p.scalar.wait("se")
    p.scalar.dma_start(out=A["ccI"][2048:2176, :].rearrange("(a b) w -> a (b w)", b=2),
                       in_=ar.view("Kown", BF16, 2048, part=64)).then_inc("dsc", 16)
    p.gpsimd.wait("dsc")
    p.gpsimd.collective_compute("AllGather", ALU.bypass, replica_groups=RG_PAIRS,
                                ins=[A["ccI"].opt()], outs=[A["ccO"].opt()]).then_inc("cc", 1)
    ar.free("wq"); ar.free("wk"); ar.free("wv"); ar.free("VST"); ar.free("Kown")
    ar.free("QST")


def emit_l0_attention(p, ar, A):
    PSx = p.psB
    ptb = p.psPT
    R = 2176
    ar.alloc("K", 4096 * 2, at=163840)
    ar.alloc("V", 32 * 1024 * 2, at=96 * 1024)
    ar.alloc("QS0", 2 * 128 * 2, at=206400)
    p.sync.wait("cc")
    for r in range(2):
        kk = A["ccO"][r * R + 2048: r * R + 2176, :].rearrange("(a b) w -> a (b w)", b=2)
        dst = ar.view("K", BF16, 4096, part=64).rearrange(
            "p (j two i) -> p two j i", two=2, i=128)[:, r, :, :]
        p.sync.dma_start(out=dst, in_=kk.rearrange("a (j i) -> a j i", i=128)).then_inc("dsy", 16)
        vsrc = A["ccO"][r * R: r * R + 2048, :].rearrange("(j i) w -> j i w", i=128)
        vdst = ar.view("V", BF16, 32 * 1024).rearrange(
            "p (t two w) -> p two t w", two=2, w=1024)[:, r, :, :]
        p.sync.dma_start(out=vdst, in_=vsrc.rearrange("j i w -> i j w")).then_inc("dsy", 16)
    ar.alloc("SS", 4096 * 4, at=172032)
    ar.alloc("SGE", 256 * 4, at=204288)
    ar.alloc("MXD", 8 * 4, at=205312)
    ar.alloc("P", (3840 + 4096) * 2, at=188416)
    mskab = ar.view("c_mskab", F32, 256)
    p.tensor.wait("dsy")

    for pair in range(8):
        for qb in (2 * pair, 2 * pair + 1):
            ntile = 2 * qb + 2
            p.sync.wait("te")
            qv = ar.view("QS0", BF16, 128, part=64, col_off=(qb % 2) * 128)
            p.sync.dma_start(out=qv, in_=A["l0_qscr"][:, qb * 128:(qb + 1) * 128]).then_inc(
                "dsQ%d" % (qb % 2), 16)
            p.tensor.wait("dsQ%d" % (qb % 2))
            nslab = cdiv(ntile - 2, 4)
            for sb in range(nslab):
                t0 = sb * 4
                n = min(4, (ntile - 2) - t0) * 128
                bank = PSx[sb % 2]
                p.tensor.wait("se")
                p.tensor.wait("ve")
                mm = p.tensor.matmul(bank[:, 0:n], qv,
                                     ar.view("K", BF16, n, part=64, col_off=t0 * 128),
                                     start=True, stop=True)
                mm.then_inc("te", 1)
                p.scalar.wait("te")
                p.scalar.copy(ar.view("SS", F32, n, col_off=t0 * 128),
                              bank[:, 0:n]).then_inc("se", 1)
            bank = PSx[nslab % 2]
            p.tensor.wait("ve")
            mm = p.tensor.matmul(bank[:, 0:256], qv,
                                 ar.view("K", BF16, 256, part=64, col_off=(ntile - 2) * 128),
                                 start=True, stop=True)
            mm.then_inc("te", 1)
            p.vector.wait("te")
            p.vector.tensor_tensor(ar.view("SS", F32, 256, col_off=(ntile - 2) * 128),
                                   bank[:, 0:256], mskab, op=ALU.add).then_inc("ve", 1)
            ncol = ntile * 128
            mx = ar.view("MXD", F32, 1)
            nmx = ar.view("MXD", F32, 1, col_off=1)
            den = ar.view("MXD", F32, 1, col_off=2)
            rden = ar.view("MXD", F32, 1, col_off=3)
            p.vector.wait("se")
            p.vector.reduce_max(mx, ar.view("SS", F32, ncol), axis=AX.X)
            p.vector.tensor_scalar(nmx, mx, -1.0, None, op0=ALU.mult).then_inc("ve", 1)
            p.scalar.wait("ve")
            pv = ar.view("P", BF16, ncol, col_off=(qb % 2) * 3840)
            p.scalar.activation(pv, ar.view("SS", F32, ncol), AF.Exp,
                                bias=nmx, accum_out=den).then_inc("se", 1)
            p.vector.wait("se")
            p.vector.reciprocal(rden, den)
            p.vector.tensor_scalar(pv, pv, rden, None, op0=ALU.mult).then_inc("ve", 1)
        nt0, nt1 = 4 * pair + 2, 4 * pair + 4
        ar.free("SS")
        ar.alloc("PTA", 32 * 256 * 2, at=172032)
        for t in range(nt1):
            p.tensor.wait("ve")
            if t < nt0:
                p.tensor.transpose(ptb[:, 0:128],
                                   ar.view("P", BF16, 128, col_off=t * 128),
                                   ar.view("c_identb", BF16, 128))
            mm = p.tensor.transpose(ptb[:, 128:256],
                                    ar.view("P", BF16, 128, col_off=3840 + t * 128),
                                    ar.view("c_identb", BF16, 128))
            mm.then_inc("te", 1)
            p.vector.wait("te")
            if t < nt0:
                p.vector.tensor_copy(ar.view("PTA", BF16, 256, col_off=t * 256),
                                     ptb[:, 0:256]).then_inc("ve", 1)
            else:
                p.vector.memset(ar.view("PTA", BF16, 128, col_off=t * 256), 0.0)
                p.vector.tensor_copy(ar.view("PTA", BF16, 128, col_off=t * 256 + 128),
                                     ptb[:, 128:256]).then_inc("ve", 1)
        p.tensor.wait("ve")
        marks = []
        for dv in range(8):
            bank = PSx[2 + dv % 4][:, 0:256]
            if dv >= 4:
                p.tensor.wait("ge", marks[dv - 4])
            for t in range(nt1):
                mm = p.tensor.matmul(
                    bank,
                    ar.view("V", BF16, 128, col_off=t * 1024 + dv * 128),
                    ar.view("PTA", BF16, 256, col_off=t * 256),
                    start=(t == 0), stop=(t == nt1 - 1))
            mm.then_inc("te", 1)
            p.vector.wait("te")
            p.vector.wait("ge")
            g = ar.view("G", BF16, 256, col_off=dv * 2048 + pair * 256)
            sg = ar.view("SGE", F32, 256)
            p.vector.scalar_tensor_tensor(sg, bank, 1.0, g, op0=ALU.mult,
                                          op1=ALU.mult).then_inc("ve", 1)
            xv = ar.view("XF", F32, 256, col_off=dv * 2048 + pair * 256)
            p.gpsimd.wait("ve")
            p.gpsimd.tensor_tensor(xv, xv, sg, op=ALU.add).then_inc("ge", 1)
            marks.append(p.val("ge"))
        ar.free("PTA")
        ar.alloc("SS", 4096 * 4, at=172032)
    ar.free("K"); ar.free("V"); ar.free("SS"); ar.free("P")
    ar.free("SGE"); ar.free("MXD"); ar.free("QS0")



def emit_l1_wbuild(p, ar, A):
    PSx = p.psB
    ar.alloc("TTG", 2 * (1024 + 256) * 4)
    ar.alloc("TTS", 256 * 4)
    for i, nm in enumerate(("r", "i")):
        p.sync.dma_start(out=ar.view("TTG", F32, 1024, part=4, col_off=i * 1280),
                         in_=A["l1_a" + nm]).then_inc("dsy", 16)
        p.sync.dma_start(out=ar.view("TTG", F32, 256, part=4, col_off=i * 1280 + 1024),
                         in_=A["l1_b" + nm]).then_inc("dsy", 16)
    p.tensor.wait("dsy")
    for i, nm in enumerate(("r", "i")):
        scr = A["l1_wscr_" + nm]
        for m in range(8):
            p.tensor.wait("dsc")
            mm = p.tensor.matmul(PSx[6][:, 0:256],
                                 ar.view("TTG", F32, 128, part=4, col_off=i * 1280 + m * 128),
                                 ar.view("TTG", F32, 256, part=4, col_off=i * 1280 + 1024),
                                 start=True, stop=True)
            mm.then_inc("te", 1)
            p.scalar.wait("te")
            p.scalar.copy(ar.view("TTS", F32, 256), PSx[6][:, 0:256]).then_inc("se", 1)
            dst = scr.rearrange("b d g e -> b d g e")[m * 4:(m + 1) * 4]
            p.scalar.dma_start(out=dst, in_=ar.view("TTS", F32, 256)).then_inc("dsc", 16)
    for i, nm in enumerate(("r", "i")):
        ar.alloc("W" + nm.upper() + "T", 4 * 512 * 2)
        scr = A["l1_wscr_" + nm].rearrange("b d g e -> d g b e")
        p.gpsimd.wait("dsc")
        for k in range(4):
            p.gpsimd.dma_start(
                out=ar.view("W" + nm.upper() + "T", BF16, 512, col_off=k * 512),
                in_=scr[k * 8:(k + 1) * 8]).then_inc("dgB", 16)
    ar.free("TTG"); ar.free("TTS")


def emit_l1_ffn(p, ar, A):
    PSx = p.psB
    ar.alloc("XNI", 4 * 2048 * 2)
    for c in range(4):
        for sl in range(4):
            p.vector.tensor_scalar(
                ar.view("XNI", BF16, 512, col_off=c * 2048 + sl * 512),
                ar.view("XN", BF16, 512, col_off=(4 + c) * 2048 + sl * 512),
                -1.0, None, op0=ALU.mult).then_inc("ve", 1)
    ar.alloc("G", 8 * 2048 * 2, at=128 * 1024)
    onesb = ar.view("c_onesb", BF16, 512, part=1)
    p.tensor.wait("ve")
    p.tensor.wait("dgB")
    marks = []
    for oc in range(8):
        real = oc < 4
        for sl in range(4):
            bank = PSx[(oc * 4 + sl) % 2]
            it = len(marks)
            if it >= 2:
                p.tensor.wait("se", marks[it - 2])
            for k in range(4):
                w1 = ar.view("WRT" if real else "WIT", BF16, 128,
                             col_off=k * 512 + (oc % 4) * 128)
                w2 = ar.view("WIT" if real else "WRT", BF16, 128,
                             col_off=k * 512 + (oc % 4) * 128)
                xr = ar.view("XN", BF16, 512, col_off=k * 2048 + sl * 512)
                xi2 = (ar.view("XNI", BF16, 512, col_off=k * 2048 + sl * 512) if real
                       else ar.view("XN", BF16, 512, col_off=(4 + k) * 2048 + sl * 512))
                p.tensor.matmul(bank[:, 0:512], w1, xr, start=(k == 0), stop=False)
                p.tensor.matmul(bank[:, 0:512], w2, xi2, start=False, stop=False)
            if oc == 0 and sl == 0:
                p.tensor.wait("dgA")
            bias = ar.view("l1_brow", BF16, 128, part=1,
                           col_off=(0 if real else 512) + (oc % 4) * 128)
            mm = p.tensor.matmul(bank[:, 0:512], bias, onesb, start=False, stop=True)
            mm.then_inc("te", 1)
            p.scalar.wait("te")
            p.scalar.activation(ar.view("G", BF16, 512, col_off=oc * 2048 + sl * 512),
                                bank[:, 0:512], AF.Silu).then_inc("se", 1)
            marks.append(p.val("se"))
    ar.free("XNI"); ar.free("WRT"); ar.free("WIT")


def emit_l1_attention_a(p, ar, A):
    PSx = p.psB
    ptb = p.psPT
    ar.alloc("WP", 8 * 1024 * 2)
    p.gpsimd.dma_start(out=ar.view("WP", BF16, 8 * 1024),
                       in_=A["l1_proj"].rearrange("(c q) f -> q c f", q=128)).then_inc("dgC", 16)
    ar.alloc("M1", 8 * 64 * 2)
    p.gpsimd.dma_start(out=ar.view("M1", BF16, 8 * 64),
                       in_=A["l1_m"]).then_inc("dgC", 16)
    ar.alloc("PF", 8 * 2048 * 2, at=96 * 1024)
    p.tensor.wait("dgC")
    for oc in range(8):
        for sl in range(4):
            bank = PSx[(oc * 4 + sl) % 2]
            p.tensor.wait("se")
            for c in range(NCH):
                mm = p.tensor.matmul(bank[:, 0:512],
                                     ar.view("WP", BF16, 128, col_off=c * 1024 + oc * 128),
                                     ar.view("XN", BF16, 512, col_off=c * 2048 + sl * 512),
                                     start=(c == 0), stop=(c == NCH - 1))
            mm.then_inc("te", 1)
            p.scalar.wait("te")
            p.scalar.copy(ar.view("PF", BF16, 512, col_off=oc * 2048 + sl * 512),
                          bank[:, 0:512]).then_inc("se", 1)
    ar.alloc("PTOK", 16 * 1024 * 2, at=96 * 1024 + 32 * 1024)
    p.tensor.wait("se")
    for c in range(8):
        for t in range(16):
            p.tensor.wait("ve")
            mm = p.tensor.transpose(ptb[:, (t % 4) * 128:(t % 4) * 128 + 128],
                                    ar.view("PF", BF16, 128, col_off=c * 2048 + t * 128),
                                    ar.view("c_identb", BF16, 128))
            mm.then_inc("te", 1)
            p.vector.wait("te")
            p.vector.tensor_copy(
                ar.view("PTOK", BF16, 128, col_off=t * 1024 + c * 128),
                ptb[:, (t % 4) * 128:(t % 4) * 128 + 128]).then_inc("ve", 1)
    ar.alloc("GST", 16 * 64 * 4)
    p.tensor.wait("ve")
    gmarks = []
    for n in range(16):
        bank = PSx[4 + n % 2]
        if n >= 2:
            p.tensor.wait("se", gmarks[n - 2])
        for t in range(16):
            mm = p.tensor.matmul(bank[0:64, 0:64],
                                 ar.view("PTOK", BF16, 64, col_off=t * 1024 + n * 64),
                                 ar.view("PTOK", BF16, 64, col_off=t * 1024 + n * 64),
                                 start=(t == 0), stop=(t == 15))
        mm.then_inc("te", 1)
        p.scalar.wait("te")
        p.scalar.copy(ar.view("GST", F32, 64, part=64, col_off=n * 64),
                      bank[0:64, 0:64]).then_inc("se", 1)
        gmarks.append(p.val("se"))
    p.scalar.wait("se")
    p.scalar.dma_start(out=A["ccGi"], in_=ar.view("GST", F32, 1024, part=64)).then_inc("dsc", 16)
    p.gpsimd.wait("dsc")
    p.gpsimd.collective_compute("AllReduce", ALU.add, replica_groups=RG_PAIRS,
                                ins=[A["ccGi"].opt()], outs=[A["ccGo"].opt()]).then_inc("cc", 1)
    ar.free("PTOK"); ar.free("WP"); ar.free("GST")


def emit_l1_attention_b(p, ar, A):
    PSx = p.psB
    ar.alloc("GSB", 8 * 64 * 2)
    p.gpsimd.wait("cc")
    for par in range(2):
        p.gpsimd.dma_start(
            out=ar.view("GSB", BF16, 8 * 64, part=64, base=par * 64).rearrange(
                "p (m w) -> p m w", w=64),
            in_=A["ccGo"].rearrange("a (m two w) -> a two m w", two=2, w=64)[:, par, :, :],
        ).then_inc("dgD", 16)
    ar.alloc("MGSB", 8 * 64 * 2)
    mgm = []
    for n in range(16):
        bn = (n % 2) * 64
        cn = (n // 2) * 64
        bank = PSx[4 + n % 2]
        p.tensor.wait("dgD")
        if n >= 2:
            p.tensor.wait("se", mgm[n - 2])
        mm = p.tensor.matmul(bank[0:64, 64:128],
                             ar.view("M1", BF16, 64, part=64, base=bn, col_off=cn),
                             ar.view("GSB", BF16, 64, part=64, base=bn, col_off=cn),
                             start=True, stop=True)
        mm.then_inc("te", 1)
        p.scalar.wait("te")
        p.scalar.copy(ar.view("MGSB", BF16, 64, part=64, base=bn, col_off=cn),
                      bank[0:64, 64:128]).then_inc("se", 1)
        mgm.append(p.val("se"))
    ar.alloc("MIX", 8 * 1024 * 2)
    p.gpsimd.dma_start(out=ar.view("MIX", BF16, 8 * 1024),
                       in_=A["l1_mixT"].rearrange("(c q) f -> q c f", q=128)).then_inc("dgE", 16)
    ar.alloc("NUD", 8 * 512 * 2)
    ar.alloc("SGE", 512 * 4)
    onesb = ar.view("c_onesb", BF16, 512, part=1)
    p.tensor.wait("dgE")
    p.tensor.wait("se")
    emarks = []
    for sl in range(4):
        for c in range(NCH):
            p.tensor.wait("ve")
            bank = PSx[6]
            pfc = ar.view("PF", BF16, 512, col_off=c * 2048 + sl * 512)
            p.tensor.matmul(bank[0:64, 0:512],
                            ar.view("MGSB", BF16, 64, part=64, base=0, col_off=c * 64),
                            pfc[0:64, :], start=True, stop=True)
            mm2 = p.tensor.matmul(bank[64:128, 0:512],
                                  ar.view("MGSB", BF16, 64, part=64, base=64, col_off=c * 64),
                                  ar.view("PF", BF16, 512, part=64, base=64,
                                          col_off=c * 2048 + sl * 512),
                                  start=True, stop=True)
            mm2.then_inc("te", 1)
            p.vector.wait("te")
            p.vector.tensor_copy(ar.view("NUD", BF16, 512, col_off=c * 512),
                                 bank[:, 0:512]).then_inc("ve", 1)
        p.tensor.wait("ve")
        for oc in range(NCH):
            sbank = PSx[2 + oc % 2]
            if len(emarks) >= 2:
                p.tensor.wait("ge", emarks[-2])
            for c in range(NCH):
                p.tensor.matmul(sbank[:, 0:512],
                                ar.view("MIX", BF16, 128, col_off=c * 1024 + oc * 128),
                                ar.view("NUD", BF16, 512, col_off=c * 512),
                                start=(c == 0), stop=False)
            bias = ar.view("l1_mbrow", BF16, 128, part=1, col_off=oc * 128)
            mm = p.tensor.matmul(sbank[:, 0:512], bias, onesb, start=False, stop=True)
            mm.then_inc("te", 1)
            p.vector.wait("te")
            p.vector.wait("ge")
            g = ar.view("G", BF16, 512, col_off=oc * 2048 + sl * 512)
            sg = ar.view("SGE", F32, 512)
            p.vector.scalar_tensor_tensor(sg, sbank[:, 0:512], 1.0, g, op0=ALU.mult,
                                          op1=ALU.mult).then_inc("ve", 1)
            xv = ar.view("XF", F32, 512, col_off=oc * 2048 + sl * 512)
            p.gpsimd.wait("ve")
            p.gpsimd.tensor_tensor(xv, xv, sg, op=ALU.add).then_inc("ge", 1)
            emarks.append(p.val("ge"))
    ar.free("NUD"); ar.free("SGE"); ar.free("MIX"); ar.free("M1")
    ar.free("GSB"); ar.free("MGSB"); ar.free("PF")




def emit_l3_ffn(p, ar, A):
    PSx = p.psB
    ar.alloc("Y", 8 * 2048 * 4, at=96 * 1024)
    ar.alloc("W13", NCH * 512 * 2)
    ar.alloc("W23", 4 * 1024 * 2)
    ar.alloc("B13", 512 * 2)
    ar.alloc("B23", 1024 * 2)
    ar.alloc("HS3", 512 * 2)
    ar.alloc("HW3", 4 * 512 * 2)
    ar.alloc("WBC", 512 * 2)
    bchot = ar.view("c_bchot", BF16, 8 * 128, part=8)
    wrows = ar.view("WROWS", BF16, 2048, part=8)
    onesb = ar.view("c_onesb", BF16, 512, part=1)
    for e in range(8):
        if e:
            p.gpsimd.wait("te")
        p.gpsimd.dma_start(out=ar.view("W13", BF16, NCH * 512),
                           in_=A["l3_w1"][e].rearrange("(c q) f -> q c f", q=128)).then_inc("dgW0", 16)
        p.gpsimd.dma_start(out=ar.view("W23", BF16, 4 * 1024),
                           in_=A["l3_w2"][e].rearrange("(c q) f -> q c f", q=128)).then_inc("dgW0", 16)
        p.gpsimd.dma_start(out=ar.view("B13", BF16, 512, part=1),
                           in_=A["l3_b1"][e:e + 1, :]).then_inc("dgW0", 16)
        p.gpsimd.dma_start(out=ar.view("B23", BF16, 1024, part=1),
                           in_=A["l3_b2"][e:e + 1, :]).then_inc("dgW0", 16)
        w1 = ar.view("W13", BF16, NCH * 512)
        w2 = ar.view("W23", BF16, 4 * 1024)
        b1 = ar.view("B13", BF16, 512, part=1)
        b2 = ar.view("B23", BF16, 1024, part=1)
        p.tensor.wait("dgW0")
        for ts in range(4):
            tc = ts * 512
            p.tensor.wait("ve")
            p.tensor.matmul(PSx[6][:, 0:512], bchot[:, e * 128:(e + 1) * 128],
                            wrows[:, tc:tc + 512], start=True, stop=True).then_inc("te", 1)
            p.vector.wait("te")
            p.vector.tensor_copy(ar.view("WBC", BF16, 512), PSx[6][:, 0:512]).then_inc("ve", 1)
            for hc in range(4):
                p.tensor.wait("ve")
                p.tensor.wait("ge")
                for c in range(NCH):
                    p.tensor.matmul(PSx[0][:, 0:512],
                                    w1[:, c * 512 + hc * 128: c * 512 + hc * 128 + 128],
                                    ar.view("XN", BF16, 512, col_off=c * 2048 + tc),
                                    start=(c == 0), stop=False)
                mm = p.tensor.matmul(PSx[0][:, 0:512],
                                     b1[:, hc * 128:(hc + 1) * 128],
                                     onesb, start=False, stop=True)
                mm.then_inc("te", 1)
                p.scalar.wait("te")
                p.scalar.wait("ge")
                p.scalar.activation(ar.view("HS3", BF16, 512), PSx[0][:, 0:512],
                                    AF.Gelu).then_inc("se", 1)
                p.gpsimd.wait("se")
                p.gpsimd.wait("ve")
                p.gpsimd.wait("te")
                p.gpsimd.tensor_tensor(ar.view("HW3", BF16, 512, col_off=hc * 512),
                                       ar.view("HS3", BF16, 512),
                                       ar.view("WBC", BF16, 512), op=ALU.mult).then_inc("ge", 1)
            p.tensor.wait("ge")
            marks = []
            for dch in range(NCH):
                bank = PSx[2 + dch % 2][:, 0:512]
                if dch >= 2:
                    p.tensor.wait("ve", marks[dch - 2])
                for hc in range(4):
                    p.tensor.matmul(bank,
                                    w2[:, hc * 1024 + dch * 128: hc * 1024 + dch * 128 + 128],
                                    ar.view("HW3", BF16, 512, col_off=hc * 512),
                                    start=(hc == 0), stop=False)
                mm = p.tensor.matmul(bank, b2[:, dch * 128:(dch + 1) * 128],
                                     ar.view("WBC", BF16, 512)[0:1, 0:512],
                                     start=False, stop=True)
                mm.then_inc("te", 1)
                p.vector.wait("te")
                yv = ar.view("Y", F32, 512, col_off=dch * 2048 + tc)
                if e == 0:
                    p.vector.tensor_copy(yv, bank).then_inc("ve", 1)
                else:
                    p.vector.tensor_tensor(yv, yv, bank, op=ALU.add).then_inc("ve", 1)
                marks.append(p.val("ve"))
    ar.free("W13"); ar.free("W23"); ar.free("B13"); ar.free("B23")
    ar.free("HS3"); ar.free("HW3"); ar.free("WBC")
    ar.free("WROWS"); ar.free("LOG")


def emit_l3_projections(p, ar, A):
    PSx = p.psB
    ar.alloc("WQ3", NCH * 1024 * 2)
    p.gpsimd.dma_start(out=ar.view("WQ3", BF16, NCH * 1024),
                       in_=A["l3_wq"].rearrange("(c q) f -> q c f", q=128)).then_inc("dgC", 16)
    ar.alloc("WKV3", NCH * 256 * 2)
    p.gpsimd.dma_start(out=ar.view("WKV3", BF16, NCH * 256).rearrange(
        "p (c f) -> p c f", f=256),
        in_=A["l3_wk"].rearrange("(c q) f -> q c f", q=128)).then_inc("dgC", 16)
    ar.alloc("WV3", NCH * 256 * 2)
    p.gpsimd.dma_start(out=ar.view("WV3", BF16, NCH * 256).rearrange(
        "p (c f) -> p c f", f=256),
        in_=A["l3_wv"].rearrange("(c q) f -> q c f", q=128)).then_inc("dgC", 16)
    ar.alloc("B3", 4096 * 2)
    p.gpsimd.dma_start(out=ar.view("B3", BF16, 1024, part=1),
                       in_=A["l3_bq"]).then_inc("dgC", 16)
    p.gpsimd.dma_start(out=ar.view("B3", BF16, 256, part=1, col_off=1024),
                       in_=A["l3_bk"]).then_inc("dgC", 16)
    p.gpsimd.dma_start(out=ar.view("B3", BF16, 256, part=1, col_off=1536),
                       in_=A["l3_bv"]).then_inc("dgC", 16)
    ar.alloc("ST3", 2 * 512 * 2)
    onesb = ar.view("c_onesb", BF16, 512, part=1)
    p.tensor.wait("dgC")
    for oc in range(NCH):
        for sl in range(4):
            bank = PSx[(oc * 4 + sl) % 2]
            p.tensor.wait("dsc")
            for c in range(NCH):
                p.tensor.matmul(bank[:, 0:512],
                                ar.view("WQ3", BF16, 128, col_off=c * 1024 + oc * 128),
                                ar.view("XN", BF16, 512, col_off=c * 2048 + sl * 512),
                                start=(c == 0), stop=False)
            mm = p.tensor.matmul(bank[:, 0:512],
                                 ar.view("B3", BF16, 128, part=1, col_off=oc * 128),
                                 onesb, start=False, stop=True)
            mm.then_inc("te", 1)
            p.scalar.wait("te")
            st = ar.view("ST3", BF16, 512, col_off=(sl % 2) * 512)
            p.scalar.activation(st, bank[:, 0:512], AF.Copy, scale=0.125).then_inc("se", 1)
            p.scalar.dma_start(out=A["l3_qscr"][oc * 128:(oc + 1) * 128, sl * 512:(sl + 1) * 512],
                               in_=st).then_inc("dsc", 16)
    for hh in range(4):
        for sl in range(4):
            bank = PSx[(hh * 4 + sl) % 2]
            p.tensor.wait("dsc")
            for c in range(NCH):
                p.tensor.matmul(bank[0:64, 0:512],
                                ar.view("WKV3", BF16, 64, col_off=c * 256 + hh * 64),
                                ar.view("XN", BF16, 512, col_off=c * 2048 + sl * 512),
                                start=(c == 0), stop=False)
            mm = p.tensor.matmul(bank[0:64, 0:512],
                                 ar.view("B3", BF16, 64, part=1, col_off=1024 + hh * 64),
                                 onesb, start=False, stop=True)
            mm.then_inc("te", 1)
            p.scalar.wait("te")
            st = ar.view("ST3", BF16, 512, col_off=(sl % 2) * 512, part=64)
            p.scalar.copy(st, bank[0:64, 0:512]).then_inc("se", 1)
            karea = A["ccI3"][512:1024, :].rearrange("(r two) w -> r (two w)", two=2)
            p.scalar.dma_start(out=karea[hh * 64:(hh + 1) * 64, sl * 512:(sl + 1) * 512],
                               in_=st).then_inc("dsc", 16)
    for j in range(NBLK):
        bank = PSx[2 + j % 2]
        p.tensor.wait("dsc")
        for c in range(NCH):
            p.tensor.matmul(bank[:, 0:256],
                            ar.view("XN", BF16, 128, col_off=c * 2048 + j * 128),
                            ar.view("WV3", BF16, 256, col_off=c * 256),
                            start=(c == 0), stop=False)
        mm = p.tensor.matmul(bank[:, 0:256], ar.view("c_onesb", BF16, 128, part=1),
                             ar.view("B3", BF16, 256, part=1, col_off=1536),
                             start=False, stop=True)
        mm.then_inc("te", 1)
        p.scalar.wait("te")
        st = ar.view("ST3", BF16, 256, col_off=(j % 2) * 512)
        p.scalar.copy(st, bank[:, 0:256]).then_inc("se", 1)
        vdst = A["ccI3"][j * 32:(j + 1) * 32, :].rearrange("r (x w) -> (r x) w", x=4)
        p.scalar.dma_start(out=vdst, in_=st).then_inc("dsc", 16)
    p.gpsimd.wait("dsc")
    p.gpsimd.collective_compute("AllGather", ALU.bypass, replica_groups=RG_PAIRS,
                                ins=[A["ccI3"].opt()], outs=[A["ccO3"].opt()]).then_inc("cc", 1)
    ar.free("WQ3"); ar.free("WKV3"); ar.free("WV3"); ar.free("ST3"); ar.free("B3")


def emit_l3_attention(p, ar, A):
    PSx = p.psB
    ar.alloc("c_mskT3", 8 * 512 * 2)
    p.sync.dma_start(out=ar.view("c_mskT3", BF16, 8 * 512),
                     in_=A["c_mskT3"]).then_inc("dsy", 16)
    p.vector.wait("dsy")
    ar.alloc("K3", 2 * 4096 * 2, at=96 * 1024)
    ar.alloc("V3", 32 * 260 * 2, at=96 * 1024 + 16 * 1024)
    ar.alloc("S1", 8 * 2048 * 2, at=96 * 1024 + 34 * 1024)
    ar.alloc("QS3", 2 * 512 * 2)
    ar.alloc("PT3", 3 * 512 * 2)
    ar.alloc("RD3", 2 * 512 * 4)
    ar.alloc("RDC3", 2 * 512 * 4)
    p.sync.wait("cc")
    R3 = 1024
    for r in range(2):
        ksrc = A["ccO3"][r * R3 + 512: r * R3 + 1024, :].rearrange(
            "a b -> (a b)").rearrange(
            "(blk par dd j i) -> par blk dd j i", blk=2, par=2, dd=64, i=128)
        for par in range(2):
            for blk in range(2):
                dst = ar.view("K3", BF16, 4096, part=64, base=par * 64,
                              col_off=blk * 4096).rearrange(
                    "p (j two i) -> p two j i", two=2, i=128)[:, r, :, :]
                p.sync.dma_start(out=dst, in_=ksrc[par, blk]).then_inc("dkv", 16)
        vflat = A["ccO3"][r * R3: r * R3 + 512, :].rearrange("a b -> (a b)")
        for hh in range(4):
            vdst = ar.view("V3", BF16, 32 * 260).rearrange(
                "p (j two h x) -> p j two h x", two=2, h=4, x=65)[:, :, r, hh, 0:64]
            p.sync.dma_start(out=vdst, in_=vflat.rearrange(
                "(j i h x) -> i h j x", i=128, h=4, x=64)[:, hh]).then_inc("dkv", 16)
    p.vector.wait("dkv")
    p.vector.memset(ar.view("V3", BF16, 32 * 260).rearrange(
        "p (kt h x) -> p kt h x", h=4, x=65)[:, :, :, 64], 1.0).then_inc("ve", 1)
    mskT = ar.view("c_mskT3", BF16, 8 * 512)
    ve_ones = p.val("ve")
    smark = []
    DEP = 3
    pend = None
    sbank_free = {0: 0, 1: 0}
    rdbank_free = [0, 0]
    rd_read = [0, 0]
    it_idx = 0

    def flush(pend):
        alt = pend["alt"]
        rb = PSx[4 + alt]
        p.tensor.wait("ve", pend["ve_recip"])
        p.tensor.wait("ve", rdbank_free[alt])
        p.tensor.matmul(rb[0:64, 0:512], ar.view("c_ones_row", F32, 64, part=1),
                        ar.view("RD3", F32, 512, part=1, col_off=alt * 512),
                        start=True, stop=True).then_inc("te", 1)
        rd_read[alt] = p.val("te")
        p.vector.wait("te", rd_read[alt])
        cdst = ar.view("RDC3", F32, 512, col_off=alt * 512)[0:64, :]
        p.vector.tensor_copy(cdst, rb[0:64, 0:512])
        rdbank_free[alt] = p.val("ve")
        p.vector.tensor_tensor(pend["dst"], pend["sbank"][0:64, 0:512], cdst,
                               op=ALU.mult)
        sbank_free[pend["qb2"]] = p.val("ve")

    for j in range(16):
        par = (j % 2) * 64
        h = j % 4
        blk = h // 2
        hpar = (h % 2) * 64
        for qg in range(4):
            qb2 = (j * 4 + qg) % 2
            p.sync.wait("te", smark[-2] if len(smark) >= 2 else 0)
            qs = ar.view("QS3", BF16, 512, part=64, base=par, col_off=qb2 * 512)
            p.sync.dma_start(out=qs, in_=A["l3_qscr"][j * 64:(j + 1) * 64,
                                                      qg * 512:(qg + 1) * 512]).then_inc(
                "dsQ%d" % qb2, 16)
            nkt = 8 * qg + 8
            sbank = PSx[2 + qb2]
            p.tensor.wait("dkv")
            p.tensor.wait("dsQ%d" % qb2)
            p.tensor.wait("ve", ve_ones)
            ses = []

            def scores(kt, hpar=hpar, blk=blk, qs=qs, qg=qg, ses=ses):
                bank = PSx[kt % DEP if kt % DEP < 2 else 6]
                p.tensor.matmul(bank[:, 0:512],
                                ar.view("K3", BF16, 128, part=64, base=hpar,
                                        col_off=blk * 4096 + kt * 128),
                                qs, start=True, stop=True).then_inc("te", 1)
                tv = p.val("te")
                d = kt - 8 * qg
                if d >= 0:
                    p.vector.wait("te", tv)
                    p.vector.tensor_tensor(bank[:, 0:512], bank[:, 0:512],
                                           mskT[:, d * 512:(d + 1) * 512],
                                           op=ALU.add)
                    p.scalar.wait("ve", p.val("ve"))
                else:
                    p.scalar.wait("te", tv)
                p.scalar.activation(ar.view("PT3", BF16, 512, col_off=(kt % DEP) * 512),
                                    bank[:, 0:512], AF.Exp)
                ses.append(p.val("se"))

            def pv(kt, h=h, nkt=nkt, sbank=sbank, qb2=qb2, ses=ses):
                p.tensor.wait("se", ses[kt])
                if kt == 0:
                    p.tensor.wait("ve", sbank_free[qb2])
                p.tensor.matmul(sbank[0:65, 0:512],
                                ar.view("V3", BF16, 65, col_off=kt * 260 + h * 65),
                                ar.view("PT3", BF16, 512, col_off=(kt % DEP) * 512),
                                start=(kt == 0), stop=(kt == nkt - 1)).then_inc("te", 1)

            for kt in range(DEP):
                scores(kt)
            if pend is not None:
                flush(pend)
            for kt in range(DEP, nkt):
                pv(kt - DEP)
                scores(kt)
            for kt in range(nkt - DEP, nkt):
                pv(kt)
            smark.append(p.val("te"))
            alt = it_idx % 2
            p.vector.wait("te", smark[-1])
            p.vector.wait("te", rd_read[alt])
            p.vector.reciprocal(ar.view("RD3", F32, 512, part=1, col_off=alt * 512),
                                sbank[64:65, 0:512])
            pend = {"alt": alt, "ve_recip": p.val("ve"), "sbank": sbank, "qb2": qb2,
                    "dst": ar.view("S1", BF16, 512, part=64, base=par,
                                   col_off=(j // 2) * 2048 + qg * 512)}
            it_idx += 1
    flush(pend)
    ar.free("K3"); ar.free("V3"); ar.free("QS3"); ar.free("PT3"); ar.free("RD3")
    ar.free("RDC3"); ar.free("c_mskT3")


def emit_l3_post(p, ar, A):
    PSx = p.psB
    ar.alloc("SQ3", 2 * 512 * 2)
    ar.alloc("STR3", 4 * 2048 * 4, at=96 * 1024)
    ar.alloc("LNG", 8 * 4 * 2)
    p.sync.dma_start(out=ar.view("LNG", F32, 8),
                     in_=A["l3_lng"]).then_inc("dsy", 16)
    p.sync.dma_start(out=ar.view("LNG", F32, 8, col_off=8),
                     in_=A["l3_lnb"]).then_inc("dsy", 16)

    def srow(i, sl=None):
        if sl is None:
            return ar.view("STR3", F32, 2048, part=1, col_off=i * 2048)
        return ar.view("STR3", F32, 512, part=1, col_off=i * 2048 + sl * 512)

    onescolb = ar.view("c_onescolb", BF16, 1)
    ones_row = ar.view("c_ones_row", F32, 128, part=1)
    for sl in range(4):
        p.tensor.wait("ve")
        for c in range(NCH):
            sq = ar.view("SQ3", BF16, 512, col_off=(c % 2) * 512)
            ssl = ar.view("S1", BF16, 512, col_off=c * 2048 + sl * 512)
            p.scalar.wait("te")
            p.scalar.activation(sq, ssl, AF.Square).then_inc("se", 1)
            p.tensor.wait("se")
            p.tensor.matmul(PSx[0][0:1, 0:512], onescolb, ssl,
                            start=(c == 0), stop=(c == NCH - 1))
            mm = p.tensor.matmul(PSx[0][32:33, 0:512], onescolb, sq,
                                 start=(c == 0), stop=(c == NCH - 1))
            mm.then_inc("te", 1)
        p.vector.wait("te")
        p.vector.tensor_copy(srow(0, sl), PSx[0][0:1, 0:512])
        p.vector.tensor_copy(srow(1, sl), PSx[0][32:33, 0:512]).then_inc("ve", 1)
    p.vector.wait("ve")
    p.vector.tensor_scalar(srow(2), srow(0), 1.0 / D, None, op0=ALU.mult)
    p.vector.tensor_scalar(srow(3), srow(1), 1.0 / D, None, op0=ALU.mult)
    p.vector.tensor_tensor(srow(1), srow(2), srow(2), op=ALU.mult)
    p.vector.tensor_tensor(srow(3), srow(3), srow(1), op=ALU.subtract)
    p.vector.tensor_scalar(srow(3), srow(3), 1e-5, None, op0=ALU.add).then_inc("ve", 1)
    p.scalar.wait("ve")
    p.scalar.activation(srow(3), srow(3), AF.Sqrt).then_inc("se", 1)
    p.vector.wait("se")
    p.vector.reciprocal(srow(3), srow(3)).then_inc("ve", 1)
    ar.alloc("XT3", 2 * 512 * 4)
    p.vector.wait("dsy")
    for sl in range(4):
        p.tensor.wait("ve")
        p.tensor.matmul(PSx[1][:, 0:512], ones_row, srow(2, sl), start=True, stop=True)
        mm = p.tensor.matmul(PSx[2][:, 0:512], ones_row, srow(3, sl), start=True, stop=True)
        mm.then_inc("te", 1)
        p.vector.wait("te")
        for c in range(NCH):
            ssl = ar.view("S1", BF16, 512, col_off=c * 2048 + sl * 512)
            xt = ar.view("XT3", F32, 512, col_off=(c % 2) * 512)
            p.vector.tensor_tensor(xt, ssl, PSx[1][:, 0:512], op=ALU.subtract)
            p.vector.tensor_tensor(xt, xt, PSx[2][:, 0:512], op=ALU.mult)
            p.vector.tensor_scalar(ssl, xt, ar.view("LNG", F32, 1, col_off=c),
                                   ar.view("LNG", F32, 1, col_off=8 + c),
                                   op0=ALU.mult, op1=ALU.add).then_inc("ve", 1)
    ar.free("SQ3"); ar.free("STR3"); ar.free("XT3")
    ar.alloc("WO3", NCH * 1024 * 2)
    p.gpsimd.wait("te"); p.gpsimd.wait("ve"); p.gpsimd.wait("se")
    p.gpsimd.dma_start(out=ar.view("WO3", BF16, NCH * 1024),
                       in_=A["l3_wo"].rearrange("(c q) f -> q c f", q=128)).then_inc("dgE", 16)
    ar.alloc("BO3", 1024 * 2)
    p.gpsimd.dma_start(out=ar.view("BO3", BF16, 1024, part=1),
                       in_=A["l3_bo"]).then_inc("dgE", 16)
    ar.alloc("SGE", 512 * 4)
    onesb = ar.view("c_onesb", BF16, 512, part=1)
    p.tensor.wait("dgE")
    p.tensor.wait("ve")
    emarks = []
    for sl in range(4):
        for oc in range(NCH):
            sbank = PSx[2 + oc % 2]
            if len(emarks) >= 2:
                p.tensor.wait("ge", emarks[-2])
            for c in range(NCH):
                p.tensor.matmul(sbank[:, 0:512],
                                ar.view("WO3", BF16, 128, col_off=c * 1024 + oc * 128),
                                ar.view("S1", BF16, 512, col_off=c * 2048 + sl * 512),
                                start=(c == 0), stop=False)
            mm = p.tensor.matmul(sbank[:, 0:512],
                                 ar.view("BO3", BF16, 128, part=1, col_off=oc * 128),
                                 onesb, start=False, stop=True)
            mm.then_inc("te", 1)
            p.vector.wait("te")
            p.vector.wait("ge")
            g = ar.view("G", BF16, 512, col_off=oc * 2048 + sl * 512)
            sg = ar.view("SGE", F32, 512)
            p.vector.scalar_tensor_tensor(sg, sbank[:, 0:512], 1.0, g, op0=ALU.mult,
                                          op1=ALU.mult).then_inc("ve", 1)
            xv = ar.view("XF", F32, 512, col_off=oc * 2048 + sl * 512)
            p.gpsimd.wait("ve")
            p.gpsimd.tensor_tensor(xv, xv, sg, op=ALU.add).then_inc("ge", 1)
            emarks.append(p.val("ge"))
    ar.free("WO3"); ar.free("BO3"); ar.free("SGE"); ar.free("S1"); ar.free("LNG")


def emit_l2_projections(p, ar, A):
    PSx = p.psB
    ar.alloc("WQKV", NCH * 3072 * 2)
    p.gpsimd.dma_start(out=ar.view("WQKV", BF16, NCH * 3072),
                       in_=A["l2_wqkv"].rearrange("(c q) f -> q c f", q=128)).then_inc("dgC", 16)
    ar.alloc("G1", 8 * 4)
    p.sync.dma_start(out=ar.view("G1", F32, 8), in_=A["l2_gam"]).then_inc("dsy", 16)
    ar.alloc("ST2", 2 * 1024 * 2)
    p.vector.wait("dsy")
    p.vector.tensor_scalar(ar.view("G1", F32, 8), ar.view("G1", F32, 8), 1.0, None,
                           op0=ALU.add).then_inc("ve", 1)
    p.vector.wait("dgC")
    for c in range(NCH):
        w = ar.view("WQKV", BF16, 3072, col_off=c * 3072)
        p.vector.tensor_scalar(w, w, ar.view("G1", F32, 1, col_off=c), None,
                               op0=ALU.mult).then_inc("ve", 1)
    p.tensor.wait("ve")
    for oc in range(NCH):
        for sl in range(4):
            bank = PSx[(oc * 4 + sl) % 2]
            p.tensor.wait("dsc")
            for c in range(NCH):
                mm = p.tensor.matmul(bank[:, 0:512],
                                     ar.view("WQKV", BF16, 128, col_off=c * 3072 + oc * 128),
                                     ar.view("XN2", BF16, 512, col_off=c * 2048 + sl * 512),
                                     start=(c == 0), stop=(c == NCH - 1))
            mm.then_inc("te", 1)
            p.scalar.wait("te")
            st = ar.view("ST2", BF16, 512, col_off=(sl % 2) * 1024)
            p.scalar.activation(st, bank[:, 0:512], AF.Copy, scale=0.125).then_inc("se", 1)
            p.scalar.dma_start(out=A["l2_qscr"][oc * 128:(oc + 1) * 128, sl * 512:(sl + 1) * 512],
                               in_=st).then_inc("dsc", 16)
    karea = A["ccI2"][2048:4096, :].rearrange("(r two) w -> r (two w)", two=2)
    for kc in range(NCH):
        for sl in range(4):
            bank = PSx[(kc * 4 + sl) % 2]
            p.tensor.wait("dsc")
            for c in range(NCH):
                mm = p.tensor.matmul(bank[:, 0:512],
                                     ar.view("WQKV", BF16, 128, col_off=c * 3072 + 1024 + kc * 128),
                                     ar.view("XN2", BF16, 512, col_off=c * 2048 + sl * 512),
                                     start=(c == 0), stop=(c == NCH - 1))
            mm.then_inc("te", 1)
            p.scalar.wait("te")
            st = ar.view("ST2", BF16, 512, col_off=(sl % 2) * 1024)
            p.scalar.copy(st, bank[:, 0:512]).then_inc("se", 1)
            p.scalar.dma_start(out=karea[kc * 128:(kc + 1) * 128, sl * 512:(sl + 1) * 512],
                               in_=st).then_inc("dsc", 16)
    for j in range(NBLK):
        p.tensor.wait("dsc")
        for half in range(2):
            bank = PSx[2 + 2 * (j % 2) + half]
            for c in range(NCH):
                mm = p.tensor.matmul(bank[:, 0:512],
                                     ar.view("XN2", BF16, 128, col_off=c * 2048 + j * 128),
                                     ar.view("WQKV", BF16, 512,
                                             col_off=c * 3072 + 2048 + half * 512),
                                     start=(c == 0), stop=(c == NCH - 1))
            mm.then_inc("te", 1)
        p.scalar.wait("te")
        st = ar.view("ST2", BF16, 1024, col_off=(j % 2) * 1024)
        p.scalar.copy(st[:, 0:512], PSx[2 + 2 * (j % 2)][:, 0:512])
        p.scalar.copy(st[:, 512:1024], PSx[3 + 2 * (j % 2)][:, 0:512]).then_inc("se", 1)
        p.scalar.dma_start(out=A["ccI2"][j * 128:(j + 1) * 128, :], in_=st).then_inc("dsc", 16)
    p.gpsimd.wait("dsc")
    p.gpsimd.collective_compute("AllGather", ALU.bypass, replica_groups=RG_PAIRS,
                                ins=[A["ccI2"].opt()], outs=[A["ccO2"].opt()]).then_inc("cc", 1)
    ar.free("WQKV"); ar.free("G1"); ar.free("ST2")


def emit_l2_ffn(p, ar, A):
    PSx = p.psB
    HS = 256
    NSL = 4096 // HS
    NHC = HS // 128
    NTS = 4
    ar.alloc("Y", 8 * 2048 * 4, at=96 * 1024)
    ar.alloc("W1B", 2 * NCH * HS * 2)
    ar.alloc("W2B", 2 * NHC * 1024 * 2)
    ar.alloc("B1B", 2 * HS * 2)
    ar.alloc("B2B", 1024 * 2)
    ar.alloc("XNM", NCH * 512 * 2)
    ar.alloc("HS2", NHC * 512 * 2)
    ar.alloc("MBC", 512 * 4)
    bchot = ar.view("c_bchot", BF16, 8 * 128, part=8)
    wrows = ar.view("WROWS", BF16, 2048, part=8)
    onesb = ar.view("c_onesb", BF16, 512, part=1)

    def wslab_dma(e, s, buf):
        sem = "dgW%d" % buf
        p.gpsimd.dma_start(
            out=ar.view("W1B", BF16, NCH * HS, col_off=buf * NCH * HS),
            in_=A["l2_w1"][e].rearrange("(c q) f -> q c f", q=128)[:, :, s * HS:(s + 1) * HS],
        ).then_inc(sem, 16)
        p.gpsimd.dma_start(
            out=ar.view("W2B", BF16, NHC * 1024, col_off=buf * NHC * 1024),
            in_=A["l2_w2"][e].rearrange("(c q) f -> q c f", q=128)[:, s * NHC:(s + 1) * NHC, :],
        ).then_inc(sem, 16)
        p.gpsimd.dma_start(
            out=ar.view("B1B", BF16, HS, part=1, col_off=buf * HS),
            in_=A["l2_b1"][e:e + 1, s * HS:(s + 1) * HS]).then_inc(sem, 16)

    tmarks = {}
    wslab_dma(0, 0, 0)
    p.gpsimd.dma_start(out=ar.view("B2B", BF16, 1024, part=1),
                       in_=A["l2_b2"][0:1, :]).then_inc("dgW0", 16)
    NT = 8 * NSL
    for gi in range(NT):
        e, s = gi // NSL, gi % NSL
        buf = gi % 2
        if gi + 1 < NT:
            if gi >= 1:
                p.gpsimd.wait("te", tmarks[gi - 1])
            wslab_dma((gi + 1) // NSL, (gi + 1) % NSL, (gi + 1) % 2)
            if (gi + 1) % NSL == 0:
                p.gpsimd.wait("te")
                p.gpsimd.dma_start(out=ar.view("B2B", BF16, 1024, part=1),
                                   in_=A["l2_b2"][(gi + 1) // NSL:(gi + 1) // NSL + 1, :]
                                   ).then_inc("dgA", 16)
        w1 = ar.view("W1B", BF16, NCH * HS, col_off=buf * NCH * HS)
        w2 = ar.view("W2B", BF16, NHC * 1024, col_off=buf * NHC * 1024)
        b1 = ar.view("B1B", BF16, HS, part=1, col_off=buf * HS)
        p.tensor.wait("dgW%d" % buf)
        if s == 0:
            p.tensor.wait("dgA")
        for ts in range(NTS):
            tc = ts * 512
            p.tensor.wait("ve")
            p.tensor.matmul(PSx[6][:, 0:512], bchot[:, e * 128:(e + 1) * 128],
                            wrows[:, tc:tc + 512], start=True, stop=True).then_inc("te", 1)
            p.vector.wait("te")
            p.vector.tensor_copy(ar.view("MBC", F32, 512), PSx[6][:, 0:512]).then_inc("ve", 1)
            for c in range(NCH):
                p.vector.tensor_tensor(ar.view("XNM", BF16, 512, col_off=c * 512),
                                       ar.view("XN", BF16, 512, col_off=c * 2048 + tc),
                                       ar.view("MBC", F32, 512), op=ALU.mult).then_inc("ve", 1)
            for hc in range(NHC):
                p.tensor.wait("ve")
                p.tensor.wait("se")
                for c in range(NCH):
                    p.tensor.matmul(PSx[0][:, 0:512],
                                    w1[:, c * HS + hc * 128: c * HS + hc * 128 + 128],
                                    ar.view("XNM", BF16, 512, col_off=c * 512),
                                    start=(c == 0), stop=False)
                mm = p.tensor.matmul(PSx[0][:, 0:512], b1[:, hc * 128:(hc + 1) * 128],
                                     onesb, start=False, stop=True)
                mm.then_inc("te", 1)
                p.scalar.wait("te")
                p.scalar.activation(ar.view("HS2", BF16, 512, col_off=hc * 512),
                                    PSx[0][:, 0:512], AF.Relu).then_inc("se", 1)
            p.tensor.wait("se")
            marks = []
            for dch in range(NCH):
                bank = PSx[2 + dch % 2][:, 0:512]
                if dch >= 2:
                    p.tensor.wait("ve", marks[dch - 2])
                for hc in range(NHC):
                    mm = p.tensor.matmul(
                        bank, w2[:, hc * 1024 + dch * 128: hc * 1024 + dch * 128 + 128],
                        ar.view("HS2", BF16, 512, col_off=hc * 512),
                        start=(hc == 0), stop=(hc == NHC - 1 and s != 0))
                if s == 0:
                    mm = p.tensor.matmul(bank,
                                         ar.view("B2B", BF16, 128, part=1, col_off=dch * 128),
                                         onesb, start=False, stop=True)
                mm.then_inc("te", 1)
                p.vector.wait("te")
                yv = ar.view("Y", F32, 512, col_off=dch * 2048 + tc)
                if gi == 0:
                    p.vector.tensor_copy(yv, bank).then_inc("ve", 1)
                else:
                    p.vector.tensor_tensor(yv, yv, bank, op=ALU.add).then_inc("ve", 1)
                marks.append(p.val("ve"))
        tmarks[gi] = p.val("te")
    ar.free("W1B"); ar.free("W2B"); ar.free("B1B"); ar.free("B2B")
    ar.free("XNM"); ar.free("HS2"); ar.free("MBC")
    ar.free("WROWS"); ar.free("LOG")


def emit_l2_ffn_sparse(p, ar, A):
    PSx = p.psB
    ptb = p.psPT
    C = 512
    IOX = bass.IndirectOffsetOnAxis
    U32 = mybir.dt.uint32
    ptb = p.psPT
    identb = ar.view("c_identb", BF16, 128)
    ar.alloc("IDX", 16 * 4)
    ar.alloc("ISC", 4 * 8 * 4)
    ar.alloc("P2R", 2 * 128 * 4)
    tris = ar.view("c_tris", F32, 128)
    onc = ar.view("c_ones_col", F32, 1)
    onr = ar.view("c_ones_row", F32, 128, part=1)
    p.vector.wait("dsy")
    p.vector.tensor_copy(ar.view("P2R", F32, 8, part=1), ar.view("c_ebase", F32, 8, part=1))
    for k in range(16):
        m = ar.view("MTOK", F32, 8, col_off=k * 8)
        p.tensor.wait("ve")
        p.tensor.matmul(PSx[4][0:1, k * 8:k * 8 + 8], onc, m, start=True, stop=True)
        p.tensor.matmul(PSx[5][:, 0:8], tris, m, start=True, stop=False)
        mm = p.tensor.matmul(PSx[5][:, 0:8], onr,
                             ar.view("P2R", F32, 8, part=1), start=False, stop=True)
        mm.then_inc("te", 1)
        p.vector.wait("te")
        sc = ar.view("ISC", F32, 8)
        p.vector.tensor_tensor(sc, m, PSx[5][:, 0:8], op=ALU.mult)
        p.vector.reduce_sum(ar.view("ISC", F32, 1, col_off=8), sc, axis=AX.X)
        p.vector.tensor_scalar(ar.view("ISC", F32, 1, col_off=8),
                               ar.view("ISC", F32, 1, col_off=8), 0.5, None, op0=ALU.add)
        p.vector.tensor_copy(ar.view("IDX", U32, 1, col_off=k),
                             ar.view("ISC", F32, 1, col_off=8))
        p.vector.tensor_tensor(ar.view("P2R", F32, 8, part=1),
                               ar.view("P2R", F32, 8, part=1),
                               PSx[4][0:1, k * 8:k * 8 + 8], op=ALU.add).then_inc("ve", 1)
    ar.alloc("ZR", 1024 * 2)
    ar.alloc("XST", 2 * 1024 * 2)
    p.vector.memset(ar.view("XST", BF16, 2048), 0.0).then_inc("ve", 1)
    p.scalar.wait("ve")
    for j in range(32):
        p.scalar.dma_start(out=A["xgscr"][j * 128:(j + 1) * 128, :],
                           in_=ar.view("XST", BF16, 1024)).then_inc("dgZ", 16)
    p.vector.wait("dgZ")
    for k in range(16):
        par = k % 2
        xst = ar.view("XST", BF16, 1024, col_off=par * 1024)
        for c in range(NCH):
            p.tensor.wait("ve")
            if c == 0:
                p.tensor.wait("dgS%d" % par)
            mm = p.tensor.transpose(ptb[:, (c % 2) * 128:(c % 2) * 128 + 128],
                                    ar.view("XN", BF16, 128, col_off=c * 2048 + k * 128),
                                    identb)
            mm.then_inc("te", 1)
            p.vector.wait("te")
            p.vector.tensor_copy(xst[:, c * 128:(c + 1) * 128],
                                 ptb[:, (c % 2) * 128:(c % 2) * 128 + 128]).then_inc("ve", 1)
        p.gpsimd.wait("ve")
        p.gpsimd.indirect_dma_start(out=A["xgscr"], out_offset=IOX(ar.view("IDX", U32, 1, col_off=k), 0),
                                    in_=xst, in_offset=None).then_inc("dgS%d" % par, 16)
    ar.free("XN")
    HS = 512
    NSL = 4096 // HS
    NHC = HS // 128
    ar.alloc("XGE", 2 * NCH * 512 * 2, at=64 * 1024)
    ar.alloc("W1B", 2 * NCH * HS * 2)
    ar.alloc("W2B", 2 * NHC * 1024 * 2)
    ar.alloc("B1B", 2 * HS * 2)
    ar.alloc("B2B", 1024 * 2)
    ar.alloc("HS2", NHC * 512 * 2)
    ar.alloc("YGS", NCH * 512 * 4, at=96 * 1024)
    ar.alloc("YGB", NCH * 512 * 2)
    ar.alloc("YGT", 2 * 1024 * 2)
    ar.alloc("C0A", 8 * 8 * 2)
    onesb = ar.view("c_onesb", BF16, 512, part=1)

    def wslab(e, s, buf):
        sem = "dgW%d" % buf
        p.gpsimd.dma_start(
            out=ar.view("W1B", BF16, NCH * HS, col_off=buf * NCH * HS),
            in_=A["l2_w1"][e].rearrange("(c q) f -> q c f", q=128)[:, :, s * HS:(s + 1) * HS],
        ).then_inc(sem, 16)
        p.gpsimd.dma_start(
            out=ar.view("W2B", BF16, NHC * 1024, col_off=buf * NHC * 1024),
            in_=A["l2_w2"][e].rearrange("(c q) f -> q c f", q=128)[:, s * NHC:(s + 1) * NHC, :],
        ).then_inc(sem, 16)
        p.gpsimd.dma_start(
            out=ar.view("B1B", BF16, HS, part=1, col_off=buf * HS),
            in_=A["l2_b1"][e:e + 1, s * HS:(s + 1) * HS]).then_inc(sem, 16)

    def xge_load(e, buf):
        for c in range(NCH):
            p.scalar.dma_start(
                out=ar.view("XGE", BF16, 512, col_off=buf * NCH * 512 + c * 512),
                in_=A["xgscr"][e * C:(e + 1) * C, c * 128:(c + 1) * 128],
                transpose=True).then_inc("dgX%d" % buf, 16)

    tmarks = {}
    dmarks = []
    wslab(0, 0, 0)
    p.scalar.wait("dgS0")
    p.scalar.wait("dgS1")
    p.scalar.wait("dgA")
    xge_load(0, 0)
    p.gpsimd.dma_start(out=ar.view("B2B", BF16, 1024, part=1),
                       in_=A["l2_b2"][0:1, :]).then_inc("dgB2", 16)
    for e in range(8):
        ebuf = e % 2
        if e + 1 < 8:
            p.scalar.wait("te", tmarks.get(e - 1, 0))
            xge_load(e + 1, (e + 1) % 2)
        xge = lambda c: ar.view("XGE", BF16, 512, col_off=ebuf * NCH * 512 + c * 512)
        p.tensor.wait("dgX%d" % ebuf)
        for s in range(NSL):
            buf = (e * NSL + s) % 2
            if e * NSL + s + 1 < 8 * NSL:
                if e * NSL + s >= 1:
                    p.gpsimd.wait("te", tmarks.get((e * NSL + s - 1, "w"), 0))
                wslab((e * NSL + s + 1) // NSL, (e * NSL + s + 1) % NSL, (buf + 1) % 2)
            w1 = ar.view("W1B", BF16, NCH * HS, col_off=buf * NCH * HS)
            w2 = ar.view("W2B", BF16, NHC * 1024, col_off=buf * NHC * 1024)
            b1 = ar.view("B1B", BF16, HS, part=1, col_off=buf * HS)
            p.tensor.wait("dgW%d" % buf)
            if s == 0:
                p.tensor.wait("dgB2")
            for hc in range(NHC):
                p.tensor.wait("se")
                for c in range(NCH):
                    p.tensor.matmul(PSx[0][:, 0:512],
                                    w1[:, c * HS + hc * 128: c * HS + hc * 128 + 128],
                                    xge(c), start=(c == 0), stop=False)
                mm = p.tensor.matmul(PSx[0][:, 0:512], b1[:, hc * 128:(hc + 1) * 128],
                                     onesb, start=False, stop=True)
                mm.then_inc("te", 1)
                p.scalar.wait("te")
                p.scalar.activation(ar.view("HS2", BF16, 512, col_off=hc * 512),
                                    PSx[0][:, 0:512], AF.Relu).then_inc("se", 1)
            p.tensor.wait("se")
            for dch in range(NCH):
                bank = PSx[2 + dch % 2][:, 0:512]
                if len(dmarks) >= 2:
                    p.tensor.wait("ve", dmarks[-2])
                for hc in range(NHC):
                    mm = p.tensor.matmul(
                        bank, w2[:, hc * 1024 + dch * 128: hc * 1024 + dch * 128 + 128],
                        ar.view("HS2", BF16, 512, col_off=hc * 512),
                        start=(hc == 0), stop=(hc == NHC - 1 and s != 0))
                if s == 0:
                    mm = p.tensor.matmul(bank,
                                         ar.view("B2B", BF16, 128, part=1, col_off=dch * 128),
                                         onesb, start=False, stop=True)
                mm.then_inc("te", 1)
                p.vector.wait("te")
                yv = ar.view("YGS", F32, 512, col_off=dch * 512)
                if s == 0:
                    p.vector.tensor_copy(yv, bank).then_inc("ve", 1)
                else:
                    p.vector.tensor_tensor(yv, yv, bank, op=ALU.add).then_inc("ve", 1)
                dmarks.append(p.val("ve"))
            tmarks[(e * NSL + s, "w")] = p.val("te")
        if e + 1 < 8:
            p.gpsimd.wait("te")
            p.gpsimd.dma_start(out=ar.view("B2B", BF16, 1024, part=1),
                               in_=A["l2_b2"][e + 1:e + 2, :]).then_inc("dgB2", 16)
        p.vector.wait("ve")
        for dch in range(NCH):
            p.vector.tensor_copy(ar.view("C0A", BF16, 1, col_off=dch * 8 + e),
                                 ar.view("YGS", F32, 1, col_off=dch * 512))
            p.vector.tensor_copy(ar.view("YGB", BF16, 512, col_off=dch * 512),
                                 ar.view("YGS", F32, 512, col_off=dch * 512)).then_inc("ve", 1)
        for st in range(4):
            ygt = ar.view("YGT", BF16, 1024, col_off=(st % 2) * 1024)
            for dch in range(NCH):
                p.tensor.wait("ve")
                if dch == 0:
                    p.tensor.wait("dsY", None)
                mm = p.tensor.transpose(ptb[:, (dch % 2) * 128:(dch % 2) * 128 + 128],
                                        ar.view("YGB", BF16, 128,
                                                col_off=dch * 512 + st * 128), identb)
                mm.then_inc("te", 1)
                p.vector.wait("te")
                p.vector.tensor_copy(ygt[:, dch * 128:(dch + 1) * 128],
                                     ptb[:, (dch % 2) * 128:(dch % 2) * 128 + 128]).then_inc("ve", 1)
            p.sync.wait("ve")
            p.sync.dma_start(out=A["ygscr"][e * C + st * 128: e * C + (st + 1) * 128, :],
                             in_=ygt).then_inc("dsY", 16)
        tmarks[e] = p.val("te")
    ar.free("W1B"); ar.free("W2B"); ar.free("B1B"); ar.free("B2B"); ar.free("HS2")
    ar.free("XGE"); ar.free("YGS"); ar.free("YGB"); ar.free("XST"); ar.free("ZR")
    ar.free("YGT")
    ar.alloc("C0T", 8 * 128 * 2)
    ar.alloc("CALLC", 8 * 4)
    identb = ar.view("c_identb", BF16, 128)
    onescolb = ar.view("c_onescolb", BF16, 1)
    for dch in range(NCH):
        p.tensor.wait("ve")
        mm = p.tensor.transpose(ptb[0:8, (dch % 4) * 128:(dch % 4) * 128 + 128],
                                ar.view("C0A", BF16, 8, col_off=dch * 8), identb)
        mm.then_inc("te", 1)
        p.vector.wait("te")
        p.vector.tensor_copy(ar.view("C0T", BF16, 128, part=8, col_off=dch * 128),
                             ptb[0:8, (dch % 4) * 128:(dch % 4) * 128 + 128]).then_inc("ve", 1)
        p.tensor.wait("ve")
        mm = p.tensor.matmul(PSx[4][:, dch:dch + 1],
                             ar.view("C0T", BF16, 128, part=8, col_off=dch * 128),
                             onescolb[0:8, :], start=True, stop=True)
        mm.then_inc("te", 1)
    p.vector.wait("te")
    p.vector.tensor_copy(ar.view("CALLC", F32, 8), PSx[4][:, 0:8]).then_inc("ve", 1)
    ar.alloc("Y", 8 * 2048 * 4, at=96 * 1024)
    ar.alloc("YT", 16 * 1024 * 2)
    ar.alloc("YGF", 512 * 2)
    wrows = ar.view("WROWS", BF16, 2048, part=8)
    p.gpsimd.wait("dsY")
    for k in range(16):
        yt = ar.view("YT", BF16, 1024, col_off=k * 1024)
        p.gpsimd.indirect_dma_start(out=yt, out_offset=None, in_=A["ygscr"],
                                    in_offset=IOX(ar.view("IDX", U32, 1, col_off=k), 0)).then_inc("dgY", 16)
    for dch in range(NCH):
        for sl in range(4):
            p.tensor.wait("ve")
            mm = p.tensor.matmul(PSx[5][:, 0:512],
                                 ar.view("C0T", BF16, 128, part=8, col_off=dch * 128),
                                 wrows[:, sl * 512:(sl + 1) * 512], start=True, stop=True)
            mm.then_inc("te", 1)
            for kk in range(4):
                k = sl * 4 + kk
                p.tensor.wait("dgY")
                p.tensor.wait("ve")
                mm = p.tensor.transpose(ptb[:, (kk % 2) * 128 + 256:(kk % 2) * 128 + 384],
                                        ar.view("YT", BF16, 128,
                                                col_off=k * 1024 + dch * 128), identb)
                mm.then_inc("te", 1)
                p.vector.wait("te")
                p.vector.tensor_copy(ar.view("YGF", BF16, 128, col_off=kk * 128),
                                     ptb[:, (kk % 2) * 128 + 256:(kk % 2) * 128 + 384]).then_inc("ve", 1)
            p.vector.wait("te")
            p.vector.scalar_tensor_tensor(
                ar.view("Y", F32, 512, col_off=dch * 2048 + sl * 512),
                ar.view("YGF", BF16, 512),
                ar.view("CALLC", F32, 1, col_off=dch),
                PSx[5][:, 0:512],
                op0=ALU.add, op1=ALU.subtract).then_inc("ve", 1)
    ar.free("IDX"); ar.free("ISC"); ar.free("P2R"); ar.free("C0A"); ar.free("C0T")
    ar.free("CALLC"); ar.free("YT"); ar.free("YGF")
    ar.free("WROWS"); ar.free("MTOK"); ar.free("LOG")


def emit_l2_attention(p, ar, A):
    PSx = p.psB
    R2 = 4096
    DEP = 3
    ar.alloc("S1", 8 * 2048 * 2)
    ar.alloc("QS2", 2 * 512 * 2)
    ar.alloc("PT2", DEP * 512 * 2)
    ar.alloc("RD2", 2 * 512 * 4)
    ar.alloc("RDC", 2 * 512 * 4)
    p.sync.wait("cc")
    smark = []
    for g in range(4):
        ar.alloc("K2", 4096 * 2 * 2, at=96 * 1024)
        ar.alloc("V2", 32 * 260 * 2, at=96 * 1024 + 16 * 1024 + 512)
        for r in range(2):
            kflat = A["ccO2"][r * R2 + 2048: r * R2 + 4096, :].rearrange("a b -> (a b)")
            for par in range(2):
                for mb in range(2):
                    hh = 4 * g + 2 * mb + par
                    src = kflat.rearrange("(hq dd j i) -> hq dd j i", dd=64, j=16, i=128)[hh]
                    dst = ar.view("K2", BF16, 4096, part=64, base=par * 64,
                                  col_off=mb * 4096).rearrange(
                        "p (j two i) -> p two j i", two=2, i=128)[:, r, :, :]
                    p.sync.dma_start(out=dst, in_=src).then_inc("dkv", 16)
            vflat = A["ccO2"][r * R2: r * R2 + 2048, :].rearrange("a b -> (a b)")
            for m in range(4):
                hh = 4 * g + m
                vdst = ar.view("V2", BF16, 32 * 260).rearrange(
                    "p (j two h x) -> p j two h x", two=2, h=4, x=65)[:, :, r, m, 0:64]
                p.sync.dma_start(out=vdst, in_=vflat.rearrange(
                    "(j i h x) -> i h j x", i=128, h=16, x=64)[:, hh]).then_inc("dkv", 16)
        p.vector.wait("dkv")
        p.vector.memset(ar.view("V2", BF16, 32 * 260).rearrange(
            "p (kt h x) -> p kt h x", h=4, x=65)[:, :, :, 64], 1.0)
        ve_ones = p.val("ve")
        pend = None
        sbank_free = {0: 0, 1: 0}
        rdbank_free = [0, 0]
        rd_read = [0, 0]
        it_idx = 0

        def flush(pend):
            alt = pend["alt"]
            rb = PSx[4 + alt]
            p.tensor.wait("ve", pend["ve_recip"])
            p.tensor.wait("ve", rdbank_free[alt])
            p.tensor.matmul(rb[0:64, 0:512], ar.view("c_ones_row", F32, 64, part=1),
                            ar.view("RD2", F32, 512, part=1, col_off=alt * 512),
                            start=True, stop=True).then_inc("te", 1)
            rd_read[alt] = p.val("te")
            p.vector.wait("te", rd_read[alt])
            cdst = ar.view("RDC", F32, 512, col_off=alt * 512)[0:64, :]
            p.vector.tensor_copy(cdst, rb[0:64, 0:512])
            rdbank_free[alt] = p.val("ve")
            p.vector.tensor_tensor(pend["dst"], pend["sbank"][0:64, 0:512], cdst,
                                   op=ALU.mult)
            sbank_free[pend["qb2"]] = p.val("ve")

        for m in range(4):
            j = 4 * g + m
            par = m % 2
            mb = m // 2

            def k2v(kt, par=None, mb=None, _d=(par, mb)):
                par, mb = _d if par is None else (par, mb)
                return ar.view("K2", BF16, 128, part=64, base=par * 64,
                               col_off=mb * 4096 + kt * 128)

            for qg in range(4):
                qb2 = (j * 4 + qg) % 2
                p.sync.wait("te", smark[-2] if len(smark) >= 2 else 0)
                qs = ar.view("QS2", BF16, 512, part=64, base=par * 64, col_off=qb2 * 512)
                p.sync.dma_start(out=qs, in_=A["l2_qscr"][j * 64:(j + 1) * 64,
                                                          qg * 512:(qg + 1) * 512]).then_inc(
                    "dsQ%d" % qb2, 16)
                sbank = PSx[2 + qb2]
                p.tensor.wait("dkv")
                p.tensor.wait("dsQ%d" % qb2)
                p.tensor.wait("ve", ve_ones)
                ses = []

                def scores(kt):
                    bank = PSx[kt % DEP if kt % DEP < 2 else 6]
                    p.tensor.matmul(bank[:, 0:512], k2v(kt), qs,
                                    start=True, stop=True).then_inc("te", 1)
                    p.scalar.wait("te", p.val("te"))
                    p.scalar.activation(ar.view("PT2", BF16, 512, col_off=(kt % DEP) * 512),
                                        bank[:, 0:512], AF.Exp)
                    ses.append(p.val("se"))

                def pv(kt):
                    p.tensor.wait("se", ses[kt])
                    if kt == 0:
                        p.tensor.wait("ve", sbank_free[qb2])
                    p.tensor.matmul(sbank[0:65, 0:512],
                                    ar.view("V2", BF16, 65, col_off=kt * 260 + m * 65),
                                    ar.view("PT2", BF16, 512, col_off=(kt % DEP) * 512),
                                    start=(kt == 0), stop=(kt == 31)).then_inc("te", 1)

                for kt in range(DEP):
                    scores(kt)
                if pend is not None:
                    flush(pend)
                for kt in range(DEP, 32):
                    pv(kt - DEP)
                    scores(kt)
                for kt in range(32 - DEP, 32):
                    pv(kt)
                smark.append(p.val("te"))
                alt = it_idx % 2
                p.vector.wait("te", smark[-1])
                p.vector.wait("te", rd_read[alt])
                p.vector.reciprocal(ar.view("RD2", F32, 512, part=1, col_off=alt * 512),
                                    sbank[64:65, 0:512])
                pend = {"alt": alt, "ve_recip": p.val("ve"), "sbank": sbank, "qb2": qb2,
                        "dst": ar.view("S1", BF16, 512, part=64, base=par * 64,
                                       col_off=(j // 2) * 2048 + qg * 512)}
                it_idx += 1
        flush(pend)
        barrier(p)
        ar.free("K2"); ar.free("V2")
    ar.free("QS2"); ar.free("PT2"); ar.free("RD2"); ar.free("RDC")


def emit_l2_post(p, ar, A):
    PSx = p.psB
    ar.alloc("WO2", NCH * 1024 * 2)
    p.gpsimd.wait("te"); p.gpsimd.wait("ve"); p.gpsimd.wait("se")
    p.gpsimd.dma_start(out=ar.view("WO2", BF16, NCH * 1024),
                       in_=A["l2_wout"].rearrange("(c q) f -> q c f", q=128)).then_inc("dgE", 16)
    ar.alloc("SGE", 512 * 4)
    p.tensor.wait("dgE")
    p.tensor.wait("ve")
    emarks = []
    for sl in range(4):
        for oc in range(NCH):
            sbank = PSx[2 + oc % 2]
            if len(emarks) >= 2:
                p.tensor.wait("ge", emarks[-2])
            for c in range(NCH):
                mm = p.tensor.matmul(sbank[:, 0:512],
                                     ar.view("WO2", BF16, 128, col_off=c * 1024 + oc * 128),
                                     ar.view("S1", BF16, 512, col_off=c * 2048 + sl * 512),
                                     start=(c == 0), stop=(c == NCH - 1))
            mm.then_inc("te", 1)
            p.vector.wait("te")
            p.vector.wait("ge")
            g = ar.view("G", BF16, 512, col_off=oc * 2048 + sl * 512)
            sg = ar.view("SGE", F32, 512)
            p.vector.scalar_tensor_tensor(sg, sbank[:, 0:512], 1.0, g, op0=ALU.mult,
                                          op1=ALU.mult).then_inc("ve", 1)
            xv = ar.view("XF", F32, 512, col_off=oc * 2048 + sl * 512)
            p.gpsimd.wait("ve")
            p.gpsimd.tensor_tensor(xv, xv, sg, op=ALU.add).then_inc("ge", 1)
            emarks.append(p.val("ge"))
    ar.free("WO2"); ar.free("SGE"); ar.free("S1")


def emit_layer2(p, ar, A):
    if "c_tris" not in ar.regions:
        ar.alloc("c_tris", 128 * 4)
        p.sync.dma_start(out=ar.view("c_tris", F32, 128), in_=A["c_tris"]).then_inc("dsy", 16)
    ar.alloc("c_ebase", 8 * 4)
    p.sync.dma_start(out=ar.view("c_ebase", F32, 8, part=1), in_=A["c_ebase"]).then_inc("dsy", 16)
    ar.alloc("WrT2", 8 * 8 * 4)
    p.sync.dma_start(out=ar.view("WrT2", F32, 64),
                     in_=A["l2_wr"].rearrange("(c q) f -> q c f", q=128)).then_inc("dsy", 16)
    ar.alloc("rb2", 8 * 4)
    p.sync.dma_start(out=ar.view("rb2", F32, 8, part=1), in_=A["l2_br"]).then_inc("dsy", 16)
    for e in ENGINES:
        getattr(p, e).wait("dsy")
    emit_ln(p, ar, A, want_router=True, want_l2=True, gate_region="WrT2", rbias="rb2")
    barrier(p)
    emit_l2_projections(p, ar, A)
    ar.free("XN2")
    barrier(p)
    emit_top2(p, ar, mode="argmax")
    barrier(p)
    emit_l2_ffn_sparse(p, ar, A)
    barrier(p)
    emit_silu_to_G(p, ar)
    barrier(p)
    emit_l2_attention(p, ar, A)
    barrier(p)
    emit_l2_post(p, ar, A)
    barrier(p)
    ar.free("G"); ar.free("WrT2"); ar.free("rb2")
    ar.free("c_tris"); ar.free("c_ebase")


UPTO = 99
UPTO3 = 99

def emit_layer1(p, ar, A):
    ar.alloc("l1_brow", 1024 * 2)
    p.gpsimd.dma_start(out=ar.view("l1_brow", BF16, 1024, part=1),
                       in_=A["l1_brow_d"]).then_inc("dgA", 16)
    ar.alloc("l1_mbrow", 1024 * 2)
    p.gpsimd.dma_start(out=ar.view("l1_mbrow", BF16, 1024, part=1),
                       in_=A["l1_mbrow_d"]).then_inc("dgA", 16)
    emit_l1_wbuild(p, ar, A)
    emit_ln(p, ar, A, want_router=False, want_l2=False)
    barrier(p)
    emit_l1_attention_a(p, ar, A)
    barrier(p)
    emit_l1_ffn(p, ar, A)
    ar.free("XN")
    barrier(p)
    emit_l1_attention_b(p, ar, A)
    barrier(p)
    ar.free("G"); ar.free("l1_brow"); ar.free("l1_mbrow")


def emit_layer3(p, ar, A):
    ar.alloc("routT3", 8 * 8 * 4)
    p.sync.dma_start(out=ar.view("routT3", F32, 64),
                     in_=A["l3_routT"].rearrange("(c q) f -> q c f", q=128)).then_inc("dsy", 16)
    ar.alloc("rb3", 8 * 4)
    p.sync.dma_start(out=ar.view("rb3", F32, 8, part=1),
                     in_=A["l3_rb"]).then_inc("dsy", 16)
    for e in ENGINES:
        getattr(p, e).wait("dsy")
    emit_ln(p, ar, A, want_router=True, want_l2=False, gate_region="routT3", rbias="rb3")
    barrier(p)
    if UPTO3 < 2:
        return
    emit_l3_projections(p, ar, A)
    barrier(p)
    if UPTO3 < 3:
        return
    emit_top2(p, ar)
    barrier(p)
    if UPTO3 < 4:
        return
    emit_l3_ffn(p, ar, A)
    ar.free("XN")
    barrier(p)
    if UPTO3 < 5:
        return
    emit_silu_to_G(p, ar)
    barrier(p)
    if UPTO3 < 6:
        return
    emit_l3_attention(p, ar, A)
    barrier(p)
    if UPTO3 < 7:
        return
    emit_l3_post(p, ar, A)
    barrier(p)
    ar.free("G"); ar.free("routT3"); ar.free("rb3")


def emit_layer0(p, ar, A):
    if "c_tris" not in ar.regions:
        ar.alloc("c_tris", 128 * 4)
        p.sync.dma_start(out=ar.view("c_tris", F32, 128), in_=A["c_tris"]).then_inc("dsy", 16)
    ar.alloc("c_ebase0", 8 * 4)
    p.sync.dma_start(out=ar.view("c_ebase0", F32, 8, part=1),
                     in_=A["c_ebase0"]).then_inc("dsy", 16)
    ar.alloc("gateT", 8 * 8 * 4)
    p.sync.dma_start(out=ar.view("gateT", F32, 64),
                     in_=A["gateT_d"].rearrange("(c q) f -> q c f", q=128)).then_inc("dsy", 16)
    ar.alloc("c_mskab", 256 * 4, at=205376)
    p.sync.dma_start(out=ar.view("c_mskab", F32, 256),
                     in_=A["c_mskab"]).then_inc("dsy", 16)
    if UPTO < 1:
        return
    emit_ln(p, ar, A, want_router=True, want_l2=False)
    barrier(p)
    if UPTO < 2:
        return
    emit_l0_projections(p, ar, A, 0)
    barrier(p)
    if UPTO < 3:
        return
    emit_top2(p, ar, save_tok=True)
    barrier(p)
    if UPTO < 4:
        return
    emit_l0_ffn_sparse(p, ar, A)
    barrier(p)
    if UPTO < 5:
        return
    emit_silu_to_G(p, ar)
    barrier(p)
    if UPTO < 6:
        return
    ar.free("gateT")
    emit_l0_attention(p, ar, A)
    barrier(p)
    ar.free("G"); ar.free("c_mskab")


def build_program(layers):
    nc = bass.Bass("TRN2", target_bir_lowering=False)
    stack = contextlib.ExitStack()
    A = {}
    A["xT"] = nc.dram_tensor("xT", [D, TOK], F32, kind="ExternalInput").ap()
    A["xoT"] = nc.dram_tensor("xoT", [D, TOK], F32, kind="ExternalOutput").ap()
    cshapes = {
        "c_ones_col": ([128, 1], F32), "c_ones_row": ([1, 128], F32),
        "c_identb": ([128, 128], BF16),
        "c_bchot": ([8, 8 * 128], BF16),
        "c_onesb": ([1, 512], BF16),
        "c_onescolb": ([128, 1], BF16),
    }
    for nm, (shp, dt) in cshapes.items():
        A[nm] = nc.dram_tensor(nm, shp, dt, kind="ExternalInput").ap()
    if 0 in layers:
        A["wq"] = nc.dram_tensor("l0_wq", [D, 64], F32, kind="ExternalInput").ap()
        A["wk"] = nc.dram_tensor("l0_wk", [D, 64], F32, kind="ExternalInput").ap()
        A["wv"] = nc.dram_tensor("l0_wv", [D, D], F32, kind="ExternalInput").ap()
        A["gateT_d"] = nc.dram_tensor("l0_gateT", [D, 8], F32, kind="ExternalInput").ap()
        A["wg"] = nc.dram_tensor("l0_wg", [8, D, 2048], F32, kind="ExternalInput").ap()
        A["wu"] = nc.dram_tensor("l0_wu", [8, D, 2048], F32, kind="ExternalInput").ap()
        A["wd"] = nc.dram_tensor("l0_wd", [8, 2048, D], F32, kind="ExternalInput").ap()
        A["l0_qscr"] = nc.dram_tensor("l0_qscr", [64, TOK], BF16).ap()
        A["xgscr0"] = nc.dram_tensor("xgscr0", [8 * 768, D], BF16).ap()
        A["ygscr0"] = nc.dram_tensor("ygscr0", [8 * 768, D], BF16).ap()
        A["c_ebase0"] = nc.dram_tensor("c_ebase0", [1, 8], F32, kind="ExternalInput").ap()
        if "c_tris" not in A:
            A["c_tris"] = nc.dram_tensor("c_tris", [128, 128], F32, kind="ExternalInput").ap()
        A["ccI"] = nc.dram_tensor("ccI0", [2176, 1024], BF16).ap()
        A["ccO"] = nc.dram_tensor("ccO0", [2 * 2176, 1024], BF16).ap()
        A["c_mskab"] = nc.dram_tensor("c_mskab", [128, 256], F32, kind="ExternalInput").ap()
    if 1 in layers:
        A["l1_proj"] = nc.dram_tensor("l1_proj", [D, D], F32, kind="ExternalInput").ap()
        A["l1_m"] = nc.dram_tensor("l1_m", [128, 8 * 64], F32, kind="ExternalInput").ap()
        A["l1_mixT"] = nc.dram_tensor("l1_mixT", [D, D], F32, kind="ExternalInput").ap()
        A["l1_mbrow_d"] = nc.dram_tensor("l1_mb", [1, D], F32, kind="ExternalInput").ap()
        A["l1_ar"] = nc.dram_tensor("l1_ar", [4, 1024], F32, kind="ExternalInput").ap()
        A["l1_br"] = nc.dram_tensor("l1_br", [4, 256], F32, kind="ExternalInput").ap()
        A["l1_ai"] = nc.dram_tensor("l1_ai", [4, 1024], F32, kind="ExternalInput").ap()
        A["l1_bi"] = nc.dram_tensor("l1_bi", [4, 256], F32, kind="ExternalInput").ap()
        A["l1_brow_d"] = nc.dram_tensor("l1_brow", [1, D], F32, kind="ExternalInput").ap()
        A["l1_wscr_r"] = nc.dram_tensor("l1_wscr_r", [32, 32, 16, 16], F32).ap()
        A["l1_wscr_i"] = nc.dram_tensor("l1_wscr_i", [32, 32, 16, 16], F32).ap()
        A["ccGi"] = nc.dram_tensor("ccGi", [64, 1024], F32).ap()
        A["ccGo"] = nc.dram_tensor("ccGo", [64, 1024], F32).ap()
    if 2 in layers:
        for nm, shp in [("l2_gam", [128, 8]), ("l2_wqkv", [D, 3072]),
                        ("l2_wout", [D, D]), ("l2_w1", [8, D, 4096]),
                        ("l2_b1", [8, 4096]), ("l2_w2", [8, 4096, D]),
                        ("l2_b2", [8, D]), ("l2_wr", [D, 8]), ("l2_br", [1, 8])]:
            A[nm] = nc.dram_tensor(nm, shp, F32, kind="ExternalInput").ap()
        if "c_tris" not in A:
            A["c_tris"] = nc.dram_tensor("c_tris", [128, 128], F32, kind="ExternalInput").ap()
        A["l2_qscr"] = nc.dram_tensor("l2_qscr", [D, TOK], BF16).ap()
        A["xgscr"] = nc.dram_tensor("xgscr", [4096, D], BF16).ap()
        A["ygscr"] = nc.dram_tensor("ygscr", [4096, D], BF16).ap()
        A["c_ebase"] = nc.dram_tensor("c_ebase", [1, 8], F32, kind="ExternalInput").ap()
        A["ccI2"] = nc.dram_tensor("ccI2", [4096, 1024], BF16).ap()
        A["ccO2"] = nc.dram_tensor("ccO2", [8192, 1024], BF16).ap()
    if 3 in layers:
        for nm, shp in [("l3_wq", [D, D]), ("l3_bq", [1, D]), ("l3_wk", [D, 256]),
                        ("l3_bk", [1, 256]), ("l3_wv", [D, 256]), ("l3_bv", [1, 256]),
                        ("l3_lng", [128, 8]), ("l3_lnb", [128, 8]), ("l3_wo", [D, D]),
                        ("l3_bo", [1, D]), ("l3_routT", [D, 8]), ("l3_rb", [1, 8]),
                        ("l3_w1", [8, D, 512]), ("l3_b1", [8, 512]),
                        ("l3_w2", [8, 512, D]), ("l3_b2", [8, D])]:
            A[nm] = nc.dram_tensor(nm, shp, F32, kind="ExternalInput").ap()
        A["l3_qscr"] = nc.dram_tensor("l3_qscr", [D, TOK], BF16).ap()
        A["c_mskT3"] = nc.dram_tensor("c_mskT3", [128, 8 * 512], BF16, kind="ExternalInput").ap()
        A["ccI3"] = nc.dram_tensor("ccI3", [1024, 1024], BF16).ap()
        A["ccO3"] = nc.dram_tensor("ccO3", [2048, 1024], BF16).ap()

    p = Prog(nc)
    ar = Arena(nc, stack)
    p.psB = [stack.enter_context(nc.psum_tensor(f"ps{i}", [128, 512], F32))
             for i in range(7)]
    p.psB = [t[:] for t in p.psB]
    p.psPT = stack.enter_context(nc.psum_tensor("psPT", [128, 1024], BF16))[:]

    ar.alloc("XF", 8 * 2048 * 4, at=0)
    ar.alloc("XN", 8 * 2048 * 2, at=64 * 1024)
    for nm, (shp, dt) in cshapes.items():
        item = 4 if dt == F32 else 2
        ar.alloc(nm, shp[1] * item)
        p.sync.dma_start(out=ar.view(nm, dt, shp[1], part=shp[0]),
                         in_=A[nm]).then_inc("dsy", 16)
    for c in range(NCH):
        p.sync.dma_start(out=ar.view("XF", F32, 2048, col_off=c * 2048),
                         in_=A["xT"][c * 128:(c + 1) * 128, :]).then_inc("dsy", 16)
    for e in ENGINES:
        getattr(p, e).wait("dsy")

    for lay in layers:
        barrier(p)
        if lay == 0:
            emit_layer0(p, ar, A)
        elif lay == 1:
            emit_layer1(p, ar, A)
        elif lay == 2:
            emit_layer2(p, ar, A)
        elif lay == 3:
            emit_layer3(p, ar, A)
        else:
            raise NotImplementedError(lay)

    p.sync.wait("ge"); p.sync.wait("ve"); p.sync.wait("se"); p.sync.wait("te")
    for c in range(NCH):
        p.sync.dma_start(out=A["xoT"][c * 128:(c + 1) * 128, :],
                         in_=ar.view("XF", F32, 2048, col_off=c * 2048)).then_inc("dsy", 16)
    p.sync.wait("dsy")
    p.finalize(stack)
    stack.close()
    return nc


def _mskT3(h):
    triT = np.where(np.arange(128)[:, None] <= np.arange(128)[None, :], 0.0,
                    -1e9).astype(np.float32)
    inv = np.full((128, 128), -1e9, np.float32)
    zer = np.zeros((128, 128), np.float32)
    out = np.zeros((128, 8 * 512), np.float32)
    for d in range(8):
        for m in range(4):
            s = d - 2 * m
            if s < 0 or (h == 1 and s == 0):
                t = zer
            elif s == (0 if h == 0 else 1):
                t = triT
            else:
                t = inv
            out[:, d * 512 + m * 128: d * 512 + (m + 1) * 128] = t
    return out


def _consts(h):
    tri = np.where(np.arange(128)[None, :] <= np.arange(128)[:, None], 0.0,
                   -1e9).astype(np.float32)
    zer = np.zeros((128, 128), np.float32)
    neg = np.full((128, 128), -1e9, np.float32)
    mska, mskb = (tri, neg) if h == 0 else (zer, tri)
    bchot = np.zeros((8, 8 * 128), np.float32)
    for e in range(8):
        bchot[e, e * 128:(e + 1) * 128] = 1.0
    return {
        "c_ones_col": np.ones((128, 1), np.float32),
        "c_ones_row": np.ones((1, 128), np.float32),

        "c_identb": np.eye(128, dtype=np.float32).astype(ml_dtypes.bfloat16),
        "c_mskab": np.concatenate([mska, mskb], axis=1),
        "c_bchot": bchot.astype(ml_dtypes.bfloat16),
        "c_onesb": np.ones((1, 512), np.float32).astype(ml_dtypes.bfloat16),
        "c_onescolb": np.ones((128, 1), np.float32).astype(ml_dtypes.bfloat16),
        "c_mskT3": _mskT3(h).astype(ml_dtypes.bfloat16),
    }


def run_layer0(x, wq, wk, wv, gate_w, wg, wu, wd):
    nc = build_program([0])
    in_maps = []
    for c in range(NCORES):
        b, h = c // 2, c % 2
        own = np.ascontiguousarray(
            x[b].reshape(32, 128, D)[h::2].reshape(TOK, D).T)
        m = _consts(h)
        m.update({
            "xT": own,
            "l0_wq": wq, "l0_wk": wk, "l0_wv": wv,
            "l0_gateT": np.ascontiguousarray(gate_w.T),
            "l0_wg": wg, "l0_wu": wu, "l0_wd": wd,
        })
        in_maps.append(m)
    res = run_bass_kernel_spmd(nc, in_maps, core_ids=list(range(NCORES)))
    out = np.zeros((B, S, D), np.float32)
    for c in range(NCORES):
        b, h = c // 2, c % 2
        xo = res.results[c]["xoT"].T.reshape(NBLK, 128, D)
        o = out[b].reshape(32, 128, D)
        o[h::2] = xo
    return out




def l1_host_inputs(inp):
    proj = np.asarray(inp["L1_attn_projection_1nck"])[0]
    l1_proj = np.ascontiguousarray(proj.transpose(1, 0, 2).reshape(1024, 1024))
    halves = np.asarray(inp["L1_attn_halves"])
    diag = np.asarray(inp["L1_attn_diagonals_nk"])
    iu, ju = np.triu_indices(64, k=1)
    M = np.zeros((128, 512), np.float32)
    for n in range(16):
        bn, cn = (n % 2) * 64, (n // 2) * 64
        blk = np.zeros((64, 64), np.float32)
        blk[iu, ju] = halves[n]
        blk[ju, iu] = halves[n]
        blk[np.arange(64), np.arange(64)] = diag[n]
        M[bn:bn + 64, cn:cn + 64] = blk
    g1r = np.asarray(inp["L1_ffn_G1_R"])[0]
    g2r = np.asarray(inp["L1_ffn_G2_R"])[:, :, 0, :]
    g1i = np.asarray(inp["L1_ffn_G1_I"])[0]
    g2i = np.asarray(inp["L1_ffn_G2_I"])[:, :, 0, :]
    return {
        "l1_proj": l1_proj,
        "l1_m": M,
        "l1_mixT": np.ascontiguousarray(np.asarray(inp["L1_attn_mixer_w"]).T),
        "l1_mb": np.asarray(inp["L1_attn_mixer_b"]).reshape(1, -1),
        "l1_ar": np.ascontiguousarray(g1r.transpose(1, 0, 2).reshape(4, 1024)),
        "l1_br": np.ascontiguousarray(g2r.transpose(0, 2, 1).reshape(4, 256)),
        "l1_ai": np.ascontiguousarray(g1i.transpose(1, 0, 2).reshape(4, 1024)),
        "l1_bi": np.ascontiguousarray(g2i.transpose(0, 2, 1).reshape(4, 256)),
        "l1_brow": np.concatenate([np.asarray(inp["L1_ffn_bias_real"]),
                                   np.asarray(inp["L1_ffn_bias_imag"])]).reshape(1, -1),
    }


def l0_host_inputs(inp):
    return {
        "l0_wq": np.asarray(inp["L0_attn_Wq"]), "l0_wk": np.asarray(inp["L0_attn_Wk"]),
        "l0_wv": np.asarray(inp["L0_attn_Wv"]),
        "l0_gateT": np.ascontiguousarray(np.asarray(inp["L0_ffn_gate_w"]).T),
        "l0_wg": np.asarray(inp["L0_ffn_w_gate"]), "l0_wu": np.asarray(inp["L0_ffn_w_up"]),
        "l0_wd": np.asarray(inp["L0_ffn_w_down"]),
        "c_tris": np.triu(np.ones((128, 128), np.float32), 1),
        "c_ebase0": (np.arange(8, dtype=np.float32) * 768 + 1).reshape(1, 8),
    }


def l3_host_inputs(inp):
    return {
        "l3_wq": np.asarray(inp["L3_attn_Wq"]), "l3_bq": np.asarray(inp["L3_attn_bq"]).reshape(1, -1),
        "l3_wk": np.asarray(inp["L3_attn_Wk"]), "l3_bk": np.asarray(inp["L3_attn_bk"]).reshape(1, -1),
        "l3_wv": np.asarray(inp["L3_attn_Wv"]), "l3_bv": np.asarray(inp["L3_attn_bv"]).reshape(1, -1),
        "l3_lng": np.ascontiguousarray(np.asarray(inp["L3_attn_ln_g"]).reshape(8, 128).T),
        "l3_lnb": np.ascontiguousarray(np.asarray(inp["L3_attn_ln_b"]).reshape(8, 128).T),
        "l3_wo": np.asarray(inp["L3_attn_Wo"]), "l3_bo": np.asarray(inp["L3_attn_bo"]).reshape(1, -1),
        "l3_routT": np.asarray(inp["L3_ffn_router_w"]),
        "l3_rb": np.asarray(inp["L3_ffn_router_b"]).reshape(1, -1),
        "l3_w1": np.asarray(inp["L3_ffn_W1"]), "l3_b1": np.asarray(inp["L3_ffn_b1"]),
        "l3_w2": np.asarray(inp["L3_ffn_W2"]), "l3_b2": np.asarray(inp["L3_ffn_b2"]),
    }


def l2_host_inputs(inp):
    return {
        "l2_gam": np.ascontiguousarray(np.asarray(inp["L2_attn_gamma"]).reshape(8, 128).T),
        "l2_wqkv": np.asarray(inp["L2_attn_w_qkv"]),
        "l2_wout": np.asarray(inp["L2_attn_w_out"]),
        "l2_w1": np.asarray(inp["L2_ffn_W1"]), "l2_b1": np.asarray(inp["L2_ffn_b1"]),
        "l2_w2": np.asarray(inp["L2_ffn_W2"]), "l2_b2": np.asarray(inp["L2_ffn_b2"]),
        "l2_wr": np.asarray(inp["L2_ffn_Wr"]),
        "l2_br": np.asarray(inp["L2_ffn_br"]).reshape(1, -1),
        "c_tris": np.triu(np.ones((128, 128), np.float32), 1),
        "c_ebase": (np.arange(8, dtype=np.float32) * 512 + 1).reshape(1, 8),
    }


def layer_in_maps(layers, x, inp):
    per_layer = {}
    if 0 in layers:
        per_layer.update(l0_host_inputs(inp))
    if 1 in layers:
        per_layer.update(l1_host_inputs(inp))
    if 2 in layers:
        per_layer.update(l2_host_inputs(inp))
    if 3 in layers:
        per_layer.update(l3_host_inputs(inp))
    maps = []
    for c in range(NCORES):
        b, h = c // 2, c % 2
        own = np.ascontiguousarray(x[b].reshape(32, 128, D)[h::2].reshape(TOK, D).T)
        m = _consts(h)
        if 0 not in layers:
            m.pop("c_mskab", None)
        if 3 not in layers:
            m.pop("c_mskT3", None)
        m["xT"] = own
        m.update(per_layer)
        maps.append(m)
    return maps


def run_layers(layers, x, inp):
    nc = build_program(layers)
    maps = layer_in_maps(layers, x, inp)
    res = run_bass_kernel_spmd(nc, maps, core_ids=list(range(NCORES)))
    out = np.zeros((B, S, D), np.float32)
    for c in range(NCORES):
        b, h = c // 2, c % 2
        xo = res.results[c]["xoT"].T.reshape(NBLK, 128, D)
        o = out[b].reshape(32, 128, D)
        o[h::2] = xo
    return out


_PROG_CACHE = {}
FUSED = True


def kernel(**inputs):
    x = np.asarray(inputs["L0_attn_X"], np.float32)
    if FUSED:
        key = "fused"
        if key not in _PROG_CACHE:
            _PROG_CACHE[key] = build_program([0, 1, 2, 3])
        maps = layer_in_maps([0, 1, 2, 3], x, inputs)
        res = run_bass_kernel_spmd(_PROG_CACHE[key], maps,
                                   core_ids=list(range(NCORES)))
        xTs = [np.asarray(res.results[c]["xoT"]) for c in range(NCORES)]
    else:
        xTs = [np.ascontiguousarray(
            x[c // 2].reshape(32, 128, D)[c % 2::2].reshape(TOK, D).T)
            for c in range(NCORES)]
        for lay in range(4):
            if lay not in _PROG_CACHE:
                _PROG_CACHE[lay] = build_program([lay])
            maps = layer_in_maps([lay], x, inputs)
            for c in range(NCORES):
                maps[c]["xT"] = xTs[c]
            res = run_bass_kernel_spmd(_PROG_CACHE[lay], maps,
                                       core_ids=list(range(NCORES)))
            xTs = [np.asarray(res.results[c]["xoT"]) for c in range(NCORES)]
    out = np.zeros((B, S, D), np.float32)
    for c in range(NCORES):
        b, h = c // 2, c % 2
        o = out[b].reshape(32, 128, D)
        o[h::2] = xTs[c].T.reshape(NBLK, 128, D)
    return out


# version 13 (final)
# speedup vs baseline: 2.8234x; 1.0478x over previous
import contextlib
import numpy as np
import ml_dtypes
import concourse.bass as bass
import concourse.mybir as mybir
from concourse.bass_utils import run_bass_kernel_spmd

F32 = mybir.dt.float32
BF16 = mybir.dt.bfloat16
AF = mybir.ActivationFunctionType
ALU = mybir.AluOpType
AX = mybir.AxisListType

NCORES = 8
B, S, D = 4, 4096, 1024
TOK = 2048
NBLK = 16
NCH = 8
RG_PAIRS = [[0, 1], [2, 3], [4, 5], [6, 7]]

ENGINES = ("sync", "scalar", "vector", "gpsimd", "tensor")


class _Item:
    __slots__ = ("attr", "args", "kw", "incs")

    def __init__(self, attr, args, kw):
        self.attr = attr
        self.args = args
        self.kw = kw
        self.incs = []


SEM_OF = {"scalar": "se", "vector": "ve", "gpsimd": "ge"}
NOCHAIN_ATTRS = ("wait_ge", "dma_start", "indirect_dma_start", "collective_compute")


class _ItemRec:
    def __init__(self, prog, item, own):
        self._prog = prog
        self._item = item
        self._own = own

    def then_inc(self, semname, k=1):
        if semname == self._own:
            return self
        self._prog.counts[semname] = self._prog.counts.get(semname, 0) + k
        self._item.incs.append((semname, k))
        return self


class _EngRec:
    def __init__(self, prog, eng):
        object.__setattr__(self, "_prog", prog)
        object.__setattr__(self, "_eng", eng)

    def wait(self, semname, val=None):
        p = self._prog
        v = p.counts.get(semname, 0) if val is None else val
        if v <= 0:
            return
        last = p.last_wait[self._eng].get(semname, -1)
        if v <= last:
            return
        p.last_wait[self._eng][semname] = v
        item = _Item("wait_ge", (("__SEM__", semname), v), {})
        p.streams[self._eng].append(item)

    def __getattr__(self, attr):
        prog, eng = self._prog, self._eng
        own = SEM_OF.get(eng)

        def call(*args, **kw):
            item = _Item(attr, args, kw)
            prog.streams[eng].append(item)
            if own is not None and attr not in NOCHAIN_ATTRS:
                prog.counts[own] = prog.counts.get(own, 0) + 1
            return _ItemRec(prog, item, own if attr not in NOCHAIN_ATTRS else None)

        return call


class Prog:
    def __init__(self, nc):
        self.nc = nc
        self.streams = {e: [] for e in ENGINES}
        self.counts = {}
        self.sems = {}
        self.last_wait = {e: {} for e in ENGINES}
        for e in ENGINES:
            setattr(self, e, _EngRec(self, e))
        self.psB = None

    def val(self, semname):
        return self.counts.get(semname, 0)

    def finalize(self, stack):
        names = set(self.counts)
        for st in self.streams.values():
            for it in st:
                if it.attr == "wait_ge":
                    names.add(it.args[0][1])
        for n in sorted(names):
            self.sems[n] = stack.enter_context(self.nc.semaphore(n))

        def resolve(a):
            if isinstance(a, tuple) and len(a) == 2 and a[0] == "__SEM__":
                return self.sems[a[1]]
            return a

        def replay(stream, ename):
            own = SEM_OF.get(ename)
            auto = self.sems.get(own) if own else None

            def run(eng):
                n = 0
                for it in stream:
                    if it.attr == "wait_ge":
                        eng.wait_ge(resolve(it.args[0]), it.args[1])
                        continue
                    chained = auto is not None and it.attr not in NOCHAIN_ATTRS
                    if auto is not None and n > 0:
                        eng.wait_ge(auto, n)
                    instr = getattr(eng, it.attr)(*[resolve(a) for a in it.args],
                                                  **{k: resolve(v) for k, v in it.kw.items()})
                    if chained:
                        instr = instr.then_inc(auto, 1)
                        n += 1
                    for (sn, k) in it.incs:
                        instr = instr.then_inc(self.sems[sn], k)
            return run

        with self.nc.Block() as block:
            for e in ENGINES:
                if self.streams[e]:
                    getattr(block, e)(replay(self.streams[e], e))

    def nops(self):
        return sum(len(s) for s in self.streams.values())


class Arena:
    BYTES = 207 * 1024 + 512

    def __init__(self, nc, stack):
        self.t = stack.enter_context(
            nc.sbuf_tensor("arena", [128, self.BYTES // 2], BF16))
        self.regions = {}
        self.frees = [(0, self.BYTES)]

    def alloc(self, name, nbytes, align=64, at=None):
        assert name not in self.regions, name
        nbytes = (nbytes + align - 1) // align * align
        if at is not None:
            for i, (off, sz) in enumerate(self.frees):
                if off <= at and at + nbytes <= off + sz:
                    pre = (off, at - off)
                    post = (at + nbytes, off + sz - at - nbytes)
                    self.frees[i:i + 1] = [q for q in (pre, post) if q[1] > 0]
                    self.regions[name] = (at, nbytes)
                    ARENA_TRACE[name] = at
                    return at
            raise MemoryError(f"arena: cannot place {name} at {at} "
                              f"({nbytes}B); free={self.frees}")
        for i in range(len(self.frees) - 1, -1, -1):
            off, sz = self.frees[i]
            aoff = (off + sz - nbytes) // align * align
            if aoff >= off:
                pre = (off, aoff - off)
                post = (aoff + nbytes, off + sz - aoff - nbytes)
                self.frees[i:i + 1] = [q for q in (pre, post) if q[1] > 0]
                self.regions[name] = (aoff, nbytes)
                ARENA_TRACE[name] = aoff
                return aoff
        raise MemoryError(f"arena: cannot alloc {name} ({nbytes}B); "
                          f"free={self.frees}, used={sorted(self.regions)}")

    def free(self, name):
        off, sz = self.regions.pop(name)
        self.frees.append((off, sz))
        self.frees.sort()
        merged = []
        for off, sz in self.frees:
            if merged and merged[-1][0] + merged[-1][1] == off:
                merged[-1] = (merged[-1][0], merged[-1][1] + sz)
            else:
                merged.append((off, sz))
        self.frees = [tuple(m) for m in merged]

    def view(self, name, dtype, cols, part=128, base=0, col_off=0):
        off, sz = self.regions[name]
        item = 2 if dtype == BF16 else 4
        assert (col_off + cols) * item <= sz, (name, col_off, cols, sz)
        start2 = (off + col_off * item) // 2
        ap = self.t[base:base + part, start2:start2 + cols * item // 2]
        if dtype != BF16:
            ap = ap.bitcast(dtype)
        return ap


ARENA_TRACE = {}

def barrier(p):
    names = list(p.counts.keys())
    for e in ENGINES:
        er = getattr(p, e)
        for s in names:
            er.wait(s)


def cdiv(a, b):
    return -(-a // b)


def load_const_sbuf(p, ar, A, name, shape, dtype, queue="sync", sem="dsy"):
    rows, cols = shape
    item = 4 if dtype == F32 else 2
    ar.alloc(name, cols * item)
    dst = ar.view(name, dtype, cols, part=rows)
    getattr(p, queue).dma_start(out=dst, in_=A[name]).then_inc(sem, 16)


def emit_ln(p, ar, A, want_router, want_l2, gate_region="gateT", rbias=None):
    if "XN" not in ar.regions:
        ar.alloc("XN", 8 * 2048 * 2, at=64 * 1024)
    p.tensor.wait("dsy")
    p.vector.wait("dsy")
    p.scalar.wait("dsy")
    ar.alloc("SQ", 2 * 512 * 4)
    ar.alloc("XT", 2 * 512 * 4)
    ar.alloc("STR", 6 * 2048 * 4, at=96 * 1024)
    ar.alloc("XS", 8 * 512 * 4, at=96 * 1024 + 49152)
    if want_l2:
        ar.alloc("XN2", 8 * 2048 * 2)
    if want_router:
        ar.alloc("LOG", 16 * 8 * 4)

    def strow(i, sl=None):
        co = i * 2048
        if sl is None:
            return ar.view("STR", F32, 2048, part=1, col_off=co)
        return ar.view("STR", F32, 512, part=1, col_off=co + sl * 512)

    ones_col = ar.view("c_ones_col", F32, 1)
    ones_row = ar.view("c_ones_row", F32, 128, part=1)
    PSx = p.psB

    for sl in range(4):
        p.tensor.wait("ve")
        for c in range(NCH):
            sq = ar.view("SQ", F32, 512, col_off=(c % 2) * 512)
            xsl = ar.view("XF", F32, 512, col_off=c * 2048 + sl * 512)
            p.scalar.wait("te")
            p.scalar.activation(sq, xsl, AF.Square).then_inc("se", 1)
            p.tensor.wait("se")
            p.tensor.matmul(PSx[0][0:1, 0:512], ones_col, xsl,
                            start=(c == 0), stop=(c == NCH - 1))
            mm = p.tensor.matmul(PSx[0][32:33, 0:512], ones_col, sq,
                                 start=(c == 0), stop=(c == NCH - 1))
            mm.then_inc("te", 1)
        p.vector.wait("te")
        p.vector.tensor_copy(strow(0, sl), PSx[0][0:1, 0:512])
        p.vector.tensor_copy(strow(1, sl), PSx[0][32:33, 0:512]).then_inc("ve", 1)

    p.vector.wait("ve")
    p.vector.tensor_scalar(strow(2), strow(0), 1.0 / D, None, op0=ALU.mult)
    p.vector.tensor_scalar(strow(5), strow(1), 1.0 / D, None, op0=ALU.mult)
    p.vector.tensor_tensor(strow(4), strow(2), strow(2), op=ALU.mult)
    p.vector.tensor_tensor(strow(5), strow(5), strow(4), op=ALU.subtract)
    p.vector.tensor_scalar(strow(4), strow(5), 1e-5, None, op0=ALU.add).then_inc("ve", 1)
    p.scalar.wait("ve")
    p.scalar.activation(strow(4), strow(4), AF.Sqrt).then_inc("se", 1)
    p.vector.wait("se")
    p.vector.reciprocal(strow(3), strow(4)).then_inc("ve", 1)
    if want_l2:
        p.vector.tensor_scalar(strow(5), strow(5), float(D), None, op0=ALU.mult).then_inc("ve", 1)
        p.scalar.wait("ve")
        p.scalar.activation(strow(5), strow(5), AF.Sqrt).then_inc("se", 1)
        p.vector.wait("se")
        p.vector.tensor_tensor(strow(5), strow(5), strow(3), op=ALU.mult)
        p.vector.tensor_scalar(strow(5), strow(5), 1e-12, None, op0=ALU.max)
        p.vector.reciprocal(strow(5), strow(5))
        p.vector.tensor_scalar(strow(5), strow(5), float(np.sqrt(D)), None, op0=ALU.mult)
        p.vector.tensor_tensor(strow(4), strow(5), strow(3), op=ALU.mult).then_inc("ve", 1)

    for sl in range(4):
        p.tensor.wait("ve")
        p.tensor.matmul(PSx[1][:, 0:512], ones_row, strow(2, sl), start=True, stop=True)
        mm = p.tensor.matmul(PSx[2][:, 0:512], ones_row, strow(3, sl), start=True, stop=True)
        if want_l2:
            mm = p.tensor.matmul(PSx[4][:, 0:512], ones_row, strow(4, sl), start=True, stop=True)
        mm.then_inc("te", 1)
        p.vector.wait("te")
        if want_router:
            p.vector.wait("te", p.val("te"))
        for c in range(NCH):
            xsl = ar.view("XF", F32, 512, col_off=c * 2048 + sl * 512)
            xt = ar.view("XT", F32, 512, col_off=(c % 2) * 512)
            xn = ar.view("XN", BF16, 512, col_off=c * 2048 + sl * 512)
            xs = ar.view("XS", F32, 512, col_off=c * 512)
            p.vector.tensor_tensor(xt, xsl, PSx[1][:, 0:512], op=ALU.subtract)
            p.vector.tensor_tensor(xs, xt, PSx[2][:, 0:512], op=ALU.mult)
            p.vector.tensor_copy(xn, xs).then_inc("ve", 1)
            if want_l2:
                xn2 = ar.view("XN2", BF16, 512, col_off=c * 2048 + sl * 512)
                p.vector.tensor_tensor(xn2, xt, PSx[4][:, 0:512], op=ALU.mult).then_inc("ve", 1)
        if want_router:
            gate = ar.view(gate_region, F32, 64)
            p.tensor.wait("ve")
            for tb in range(4):
                for c in range(NCH):
                    mm = p.tensor.matmul(
                        PSx[3][:, (sl * 4 + tb) * 8:(sl * 4 + tb) * 8 + 8],
                        ar.view("XS", F32, 128, col_off=c * 512 + tb * 128),
                        gate[:, c * 8:(c + 1) * 8],
                        start=(c == 0), stop=rbias is None and (c == NCH - 1))
                if rbias is not None:
                    ones_r = ar.view("c_ones_row", F32, 128, part=1)
                    mm = p.tensor.matmul(
                        PSx[3][:, (sl * 4 + tb) * 8:(sl * 4 + tb) * 8 + 8],
                        ones_r, ar.view(rbias, F32, 8, part=1),
                        start=False, stop=True)
            mm.then_inc("te", 1)
    if want_router:
        p.vector.wait("te")
        p.vector.tensor_copy(ar.view("LOG", F32, 128), PSx[3][:, 0:128]).then_inc("ve", 1)
    ar.free("SQ"); ar.free("XT"); ar.free("STR"); ar.free("XS")


def emit_top2(p, ar, negbig=-1e30, mode="top2", save_tok=False):
    if mode == "argmax":
        ar.alloc("MTOK", 16 * 8 * 4)
    if save_tok:
        ar.alloc("M1T", 16 * 8 * 4)
        ar.alloc("MALL", 16 * 8 * 4)
        ar.alloc("WTF", 16 * 8 * 4)
    ar.alloc("TOPS", 6 * 16 * 4)
    ar.alloc("TSC", 2 * 8 * 4)
    ar.alloc("WTOK", 16 * 8 * 2)
    identb = ar.view("c_identb", BF16, 128)

    def tcol(i, t):
        return ar.view("TOPS", F32, 1, col_off=i * 16 + t)

    for t in range(16):
        lg = ar.view("LOG", F32, 8, col_off=t * 8)
        m1 = ar.view("TSC", F32, 8)
        l2 = ar.view("TSC", F32, 8, col_off=8)
        wt = ar.view("WTOK", BF16, 8, col_off=t * 8)
        p.vector.wait("ve")
        p.vector.reduce_max(tcol(0, t), lg, axis=AX.X)
        if mode == "argmax":
            p.vector.tensor_scalar(wt, lg, tcol(0, t), None, op0=ALU.is_ge)
            p.vector.tensor_scalar(ar.view("MTOK", F32, 8, col_off=t * 8), lg,
                                   tcol(0, t), None, op0=ALU.is_ge).then_inc("ve", 1)
            continue
        p.vector.tensor_scalar(tcol(1, t), tcol(0, t), -1.0, None, op0=ALU.mult)
        p.vector.tensor_scalar(m1, lg, tcol(0, t), None, op0=ALU.is_ge)
        if save_tok:
            p.vector.tensor_copy(ar.view("M1T", F32, 8, col_off=t * 8), m1)
        p.vector.scalar_tensor_tensor(l2, m1, negbig, lg, op0=ALU.mult, op1=ALU.add)
        p.vector.reduce_max(tcol(2, t), l2, axis=AX.X).then_inc("ve", 1)
        p.scalar.wait("ve")
        p.scalar.activation(l2, lg, AF.Exp, bias=tcol(1, t)).then_inc("se", 1)
        p.vector.wait("se")
        p.vector.tensor_scalar(m1, lg, tcol(2, t), None, op0=ALU.is_ge)
        if save_tok:
            p.vector.tensor_copy(ar.view("MALL", F32, 8, col_off=t * 8), m1)
        p.vector.tensor_tensor(m1, m1, l2, op=ALU.mult)
        p.vector.reduce_sum(tcol(3, t), m1, axis=AX.X)
        p.vector.reciprocal(tcol(4, t), tcol(3, t))
        p.vector.tensor_scalar(wt, m1, tcol(4, t), None, op0=ALU.mult)
        if save_tok:
            p.vector.tensor_scalar(ar.view("WTF", F32, 8, col_off=t * 8), m1,
                                   tcol(4, t), None, op0=ALU.mult).then_inc("ve", 1)
        else:
            p.vector.tensor_copy(wt, wt).then_inc("ve", 1)
    ar.alloc("WROWS", 2048 * 2)
    ptb = p.psPT
    for t in range(16):
        p.tensor.wait("ve")
        mm = p.tensor.transpose(ptb[0:8, (t % 4) * 128:(t % 4) * 128 + 128],
                                ar.view("WTOK", BF16, 8, col_off=t * 8), identb)
        mm.then_inc("te", 1)
        p.vector.wait("te")
        p.vector.tensor_copy(ar.view("WROWS", BF16, 128, part=8, col_off=t * 128),
                             ptb[0:8, (t % 4) * 128:(t % 4) * 128 + 128]).then_inc("ve", 1)
    ar.free("TOPS"); ar.free("TSC"); ar.free("WTOK")


def emit_l0_ffn(p, ar, A):
    E, F = 8, 2048
    HS = 256
    NSL = F // HS
    NTS = 4
    NHC = HS // 128
    ar.alloc("Y", 8 * 2048 * 4, at=96 * 1024)
    ar.alloc("WG", 2 * NCH * HS * 2)
    ar.alloc("WU", 2 * NCH * HS * 2)
    ar.alloc("WD", 2 * NHC * 1024 * 2)
    ar.alloc("SGS", 512 * 2)
    ar.alloc("UWS", 512 * 2)
    ar.alloc("HWS", NHC * 512 * 2)
    ar.alloc("WBC", 512 * 4)
    PSx = p.psB
    bchot = ar.view("c_bchot", BF16, 8 * 128, part=8)
    wrows = ar.view("WROWS", BF16, 2048, part=8)

    def wslab_dma(e, s, buf):
        sem = "dgW%d" % buf
        src_g = A["wg"][e].rearrange("(c q) f -> q c f", q=128)[:, :, s * HS:(s + 1) * HS]
        src_u = A["wu"][e].rearrange("(c q) f -> q c f", q=128)[:, :, s * HS:(s + 1) * HS]
        src_d = A["wd"][e].rearrange("(c q) f -> q c f", q=128)[:, s * NHC:(s + 1) * NHC, :]
        p.gpsimd.dma_start(out=ar.view("WG", BF16, NCH * HS, col_off=buf * NCH * HS),
                           in_=src_g).then_inc(sem, 16)
        p.gpsimd.dma_start(out=ar.view("WU", BF16, NCH * HS, col_off=buf * NCH * HS),
                           in_=src_u).then_inc(sem, 16)
        p.gpsimd.dma_start(out=ar.view("WD", BF16, NHC * 1024, col_off=buf * NHC * 1024),
                           in_=src_d).then_inc(sem, 16)

    tmarks = {}
    wslab_dma(0, 0, 0)
    for gi in range(E * NSL):
        e, s = gi // NSL, gi % NSL
        buf = gi % 2
        if gi + 1 < E * NSL:
            if gi >= 1:
                p.gpsimd.wait("te", tmarks[gi - 1])
            wslab_dma((gi + 1) // NSL, (gi + 1) % NSL, (gi + 1) % 2)
        wg = ar.view("WG", BF16, NCH * HS, col_off=buf * NCH * HS)
        wu = ar.view("WU", BF16, NCH * HS, col_off=buf * NCH * HS)
        wd = ar.view("WD", BF16, NHC * 1024, col_off=buf * NHC * 1024)
        p.tensor.wait("dgW%d" % buf)
        for ts in range(NTS):
            tc = ts * 512
            p.tensor.wait("ve")
            p.tensor.matmul(PSx[6][:, 0:512], bchot[:, e * 128:(e + 1) * 128],
                            wrows[:, tc:tc + 512], start=True, stop=True).then_inc("te", 1)
            p.vector.wait("te")
            p.vector.tensor_copy(ar.view("WBC", F32, 512), PSx[6][:, 0:512]).then_inc("ve", 1)
            for hc in range(NHC):
                p.tensor.wait("ve")
                p.tensor.wait("ge")
                for c in range(NCH):
                    xn = ar.view("XN", BF16, 512, col_off=c * 2048 + tc)
                    p.tensor.matmul(PSx[0][:, 0:512],
                                    wg[:, c * HS + hc * 128: c * HS + hc * 128 + 128],
                                    xn, start=(c == 0), stop=(c == NCH - 1))
                    mm = p.tensor.matmul(PSx[1][:, 0:512],
                                         wu[:, c * HS + hc * 128: c * HS + hc * 128 + 128],
                                         xn, start=(c == 0), stop=(c == NCH - 1))
                mm.then_inc("te", 1)
                p.scalar.wait("te")
                p.scalar.wait("ge")
                p.scalar.activation(ar.view("SGS", BF16, 512), PSx[0][:, 0:512],
                                    AF.Silu).then_inc("se", 1)
                p.vector.wait("te")
                p.vector.wait("ge")
                p.vector.tensor_tensor(ar.view("UWS", BF16, 512), PSx[1][:, 0:512],
                                       ar.view("WBC", F32, 512), op=ALU.mult).then_inc("ve", 1)
                p.gpsimd.wait("se")
                p.gpsimd.wait("ve")
                p.gpsimd.wait("te")
                p.gpsimd.tensor_tensor(ar.view("HWS", BF16, 512, col_off=hc * 512),
                                       ar.view("SGS", BF16, 512),
                                       ar.view("UWS", BF16, 512), op=ALU.mult).then_inc("ge", 1)
            p.tensor.wait("ge")
            marks = []
            for dch in range(NCH):
                bank = PSx[2 + dch % 2][:, 0:512]
                if dch >= 2:
                    p.tensor.wait("ve", marks[dch - 2])
                for hc in range(NHC):
                    mm = p.tensor.matmul(
                        bank, wd[:, hc * 1024 + dch * 128: hc * 1024 + dch * 128 + 128],
                        ar.view("HWS", BF16, 512, col_off=hc * 512),
                        start=(hc == 0), stop=(hc == NHC - 1))
                mm.then_inc("te", 1)
                p.vector.wait("te")
                yv = ar.view("Y", F32, 512, col_off=dch * 2048 + tc)
                if gi == 0:
                    p.vector.tensor_copy(yv, bank).then_inc("ve", 1)
                else:
                    p.vector.tensor_tensor(yv, yv, bank, op=ALU.add).then_inc("ve", 1)
                marks.append(p.val("ve"))
        tmarks[gi] = p.val("te")
    ar.free("WG"); ar.free("WU"); ar.free("WD")
    ar.free("SGS"); ar.free("UWS"); ar.free("HWS"); ar.free("WBC")
    ar.free("WROWS"); ar.free("LOG")


def emit_l0_ffn_sparse(p, ar, A):
    PSx = p.psB
    ptb = p.psPT
    C = 768
    NSEC = 2
    SECS = [(0, 512), (512, 256)]
    IOX = bass.IndirectOffsetOnAxis
    U32 = mybir.dt.uint32
    identb = ar.view("c_identb", BF16, 128)
    tris = ar.view("c_tris", F32, 128)
    onc = ar.view("c_ones_col", F32, 1)
    onr = ar.view("c_ones_row", F32, 128, part=1)
    ar.alloc("IDXA", 16 * 4)
    ar.alloc("IDXB", 16 * 4)
    ar.alloc("WCA", 16 * 4)
    ar.alloc("WCB", 16 * 4)
    ar.alloc("ISC", 4 * 8 * 4)
    ar.alloc("P2R", 128 * 4)
    p.vector.wait("se")
    p.vector.tensor_copy(ar.view("P2R", F32, 8, part=1), ar.view("c_ebase0", F32, 8, part=1))
    for k in range(16):
        mall = ar.view("MALL", F32, 8, col_off=k * 8)
        m1 = ar.view("M1T", F32, 8, col_off=k * 8)
        wtk = ar.view("WTF", F32, 8, col_off=k * 8)
        p.tensor.wait("ve")
        p.tensor.matmul(PSx[4][0:1, k * 8:k * 8 + 8], onc, mall, start=True, stop=True)
        p.tensor.matmul(PSx[5][:, 0:8], tris, mall, start=True, stop=False)
        mm = p.tensor.matmul(PSx[5][:, 0:8], onr,
                             ar.view("P2R", F32, 8, part=1), start=False, stop=True)
        mm.then_inc("te", 1)
        p.vector.wait("te")
        sc = ar.view("ISC", F32, 8)
        sc2 = ar.view("ISC", F32, 8, col_off=16)
        p.vector.tensor_tensor(sc, m1, PSx[5][:, 0:8], op=ALU.mult)
        p.vector.reduce_sum(ar.view("ISC", F32, 1, col_off=8), sc, axis=AX.X)
        p.vector.tensor_scalar(ar.view("ISC", F32, 1, col_off=8),
                               ar.view("ISC", F32, 1, col_off=8), 0.5, None, op0=ALU.add)
        p.vector.tensor_copy(ar.view("IDXA", U32, 1, col_off=k),
                             ar.view("ISC", F32, 1, col_off=8))
        p.vector.tensor_tensor(sc2, mall, m1, op=ALU.subtract)
        p.vector.tensor_tensor(sc2, sc2, PSx[5][:, 0:8], op=ALU.mult)
        p.vector.reduce_sum(ar.view("ISC", F32, 1, col_off=9), sc2, axis=AX.X)
        p.vector.tensor_scalar(ar.view("ISC", F32, 1, col_off=9),
                               ar.view("ISC", F32, 1, col_off=9), 0.5, None, op0=ALU.add)
        p.vector.tensor_copy(ar.view("IDXB", U32, 1, col_off=k),
                             ar.view("ISC", F32, 1, col_off=9))
        p.vector.tensor_tensor(sc, m1, wtk, op=ALU.mult)
        p.vector.reduce_sum(ar.view("WCA", F32, 1, col_off=k), sc, axis=AX.X)
        p.vector.tensor_tensor(sc2, mall, wtk, op=ALU.mult)
        p.vector.reduce_sum(ar.view("WCB", F32, 1, col_off=k), sc2, axis=AX.X)
        p.vector.tensor_tensor(ar.view("WCB", F32, 1, col_off=k),
                               ar.view("WCB", F32, 1, col_off=k),
                               ar.view("WCA", F32, 1, col_off=k), op=ALU.subtract)
        p.vector.tensor_tensor(ar.view("P2R", F32, 8, part=1),
                               ar.view("P2R", F32, 8, part=1),
                               PSx[4][0:1, k * 8:k * 8 + 8], op=ALU.add).then_inc("ve", 1)
    ar.alloc("XST", 2 * 1024 * 2)
    p.vector.memset(ar.view("XST", BF16, 2048), 0.0).then_inc("ve", 1)
    p.scalar.wait("ve")
    for j in range(48):
        p.scalar.dma_start(out=A["xgscr0"][j * 128:(j + 1) * 128, :],
                           in_=ar.view("XST", BF16, 1024)).then_inc("dgZ", 16)
    p.vector.wait("dgZ")
    for k in range(16):
        par = k % 2
        xst = ar.view("XST", BF16, 1024, col_off=par * 1024)
        for c in range(NCH):
            p.tensor.wait("ve")
            if c == 0:
                p.tensor.wait("dgS%d" % par)
            mm = p.tensor.transpose(ptb[:, (c % 2) * 128:(c % 2) * 128 + 128],
                                    ar.view("XN", BF16, 128, col_off=c * 2048 + k * 128),
                                    identb)
            mm.then_inc("te", 1)
            p.vector.wait("te")
            p.vector.tensor_copy(xst[:, c * 128:(c + 1) * 128],
                                 ptb[:, (c % 2) * 128:(c % 2) * 128 + 128]).then_inc("ve", 1)
        p.gpsimd.wait("ve")
        p.gpsimd.indirect_dma_start(out=A["xgscr0"],
                                    out_offset=IOX(ar.view("IDXA", U32, 1, col_off=k), 0),
                                    in_=xst, in_offset=None).then_inc("dgS%d" % par, 16)
        p.gpsimd.indirect_dma_start(out=A["xgscr0"],
                                    out_offset=IOX(ar.view("IDXB", U32, 1, col_off=k), 0),
                                    in_=xst, in_offset=None).then_inc("dgS%d" % par, 16)
    ar.free("XN")
    HS = 256
    NSL = 2048 // HS
    NHC = HS // 128
    ar.alloc("XGE", 2 * NCH * C * 2, at=64 * 1024)
    ar.alloc("WG", 2 * NCH * HS * 2)
    ar.alloc("WU", 2 * NCH * HS * 2)
    ar.alloc("WD", 2 * NHC * 1024 * 2)
    ar.alloc("SGS", 512 * 2)
    ar.alloc("UWS", 512 * 2)
    ar.alloc("HWS", NHC * 512 * 2)
    ar.alloc("YGS", NCH * C * 4, at=96 * 1024)
    ar.alloc("YGB", NCH * C * 2)
    ar.alloc("YGT", 2 * 1024 * 2)

    def wslab(e, s, buf):
        sem = "dgW%d" % buf
        p.gpsimd.dma_start(out=ar.view("WG", BF16, NCH * HS, col_off=buf * NCH * HS),
                           in_=A["wg"][e].rearrange("(c q) f -> q c f", q=128)[:, :, s * HS:(s + 1) * HS]).then_inc(sem, 16)
        p.gpsimd.dma_start(out=ar.view("WU", BF16, NCH * HS, col_off=buf * NCH * HS),
                           in_=A["wu"][e].rearrange("(c q) f -> q c f", q=128)[:, :, s * HS:(s + 1) * HS]).then_inc(sem, 16)
        p.gpsimd.dma_start(out=ar.view("WD", BF16, NHC * 1024, col_off=buf * NHC * 1024),
                           in_=A["wd"][e].rearrange("(c q) f -> q c f", q=128)[:, s * NHC:(s + 1) * NHC, :]).then_inc(sem, 16)

    def xge_load(e, buf):
        for c in range(NCH):
            p.scalar.dma_start(
                out=ar.view("XGE", BF16, C, col_off=buf * NCH * C + c * C),
                in_=A["xgscr0"][e * C:(e + 1) * C, c * 128:(c + 1) * 128],
                transpose=True).then_inc("dgX%d" % buf, 16)

    tmarks = {}
    dmarks = []
    wslab(0, 0, 0)
    p.scalar.wait("dgS0")
    p.scalar.wait("dgS1")
    xge_load(0, 0)
    for e in range(8):
        ebuf = e % 2
        if e + 1 < 8:
            p.scalar.wait("te", tmarks.get(e - 1, 0))
            xge_load(e + 1, (e + 1) % 2)
        p.tensor.wait("dgX%d" % ebuf)
        for s in range(NSL):
            gi = e * NSL + s
            buf = gi % 2
            if gi + 1 < 8 * NSL:
                if gi >= 1:
                    p.gpsimd.wait("te", tmarks.get((gi - 1, "w"), 0))
                wslab((gi + 1) // NSL, (gi + 1) % NSL, (gi + 1) % 2)
            wg = ar.view("WG", BF16, NCH * HS, col_off=buf * NCH * HS)
            wu = ar.view("WU", BF16, NCH * HS, col_off=buf * NCH * HS)
            wd = ar.view("WD", BF16, NHC * 1024, col_off=buf * NHC * 1024)
            p.tensor.wait("dgW%d" % buf)
            for off, n in SECS:
                for hc in range(NHC):
                    p.tensor.wait("ve")
                    p.tensor.wait("ge")
                    for c in range(NCH):
                        xg = ar.view("XGE", BF16, n, col_off=ebuf * NCH * C + c * C + off)
                        p.tensor.matmul(PSx[0][:, 0:n],
                                        wg[:, c * HS + hc * 128: c * HS + hc * 128 + 128],
                                        xg, start=(c == 0), stop=(c == NCH - 1))
                        mm = p.tensor.matmul(PSx[1][:, 0:n],
                                             wu[:, c * HS + hc * 128: c * HS + hc * 128 + 128],
                                             xg, start=(c == 0), stop=(c == NCH - 1))
                    mm.then_inc("te", 1)
                    p.scalar.wait("te")
                    p.scalar.wait("ge")
                    p.scalar.activation(ar.view("SGS", BF16, n), PSx[0][:, 0:n],
                                        AF.Silu).then_inc("se", 1)
                    p.vector.wait("te")
                    p.vector.wait("ge")
                    p.vector.tensor_copy(ar.view("UWS", BF16, n), PSx[1][:, 0:n]).then_inc("ve", 1)
                    p.gpsimd.wait("se")
                    p.gpsimd.wait("ve")
                    p.gpsimd.wait("te")
                    p.gpsimd.tensor_tensor(ar.view("HWS", BF16, n, col_off=hc * 512),
                                           ar.view("SGS", BF16, n),
                                           ar.view("UWS", BF16, n), op=ALU.mult).then_inc("ge", 1)
                p.tensor.wait("ge")
                for dch in range(NCH):
                    bank = PSx[2 + dch % 2][:, 0:n]
                    if len(dmarks) >= 2:
                        p.tensor.wait("ve", dmarks[-2])
                    for hc in range(NHC):
                        mm = p.tensor.matmul(
                            bank, wd[:, hc * 1024 + dch * 128: hc * 1024 + dch * 128 + 128],
                            ar.view("HWS", BF16, n, col_off=hc * 512),
                            start=(hc == 0), stop=(hc == NHC - 1))
                    mm.then_inc("te", 1)
                    p.vector.wait("te")
                    yv = ar.view("YGS", F32, n, col_off=dch * C + off)
                    if s == 0:
                        p.vector.tensor_copy(yv, bank).then_inc("ve", 1)
                    else:
                        p.vector.tensor_tensor(yv, yv, bank, op=ALU.add).then_inc("ve", 1)
                    dmarks.append(p.val("ve"))
            tmarks[(gi, "w")] = p.val("te")
        p.vector.wait("ve")
        for dch in range(NCH):
            p.vector.tensor_copy(ar.view("YGB", BF16, C, col_off=dch * C),
                                 ar.view("YGS", F32, C, col_off=dch * C)).then_inc("ve", 1)
        for st in range(C // 128):
            ygt = ar.view("YGT", BF16, 1024, col_off=(st % 2) * 1024)
            for dch in range(NCH):
                p.tensor.wait("ve")
                if dch == 0:
                    p.tensor.wait("dsY", None)
                mm = p.tensor.transpose(ptb[:, (dch % 2) * 128:(dch % 2) * 128 + 128],
                                        ar.view("YGB", BF16, 128,
                                                col_off=dch * C + st * 128), identb)
                mm.then_inc("te", 1)
                p.vector.wait("te")
                p.vector.tensor_copy(ygt[:, dch * 128:(dch + 1) * 128],
                                     ptb[:, (dch % 2) * 128:(dch % 2) * 128 + 128]).then_inc("ve", 1)
            p.sync.wait("ve")
            p.sync.dma_start(out=A["ygscr0"][e * C + st * 128: e * C + (st + 1) * 128, :],
                             in_=ygt).then_inc("dsY", 16)
        tmarks[e] = p.val("te")
    ar.free("WG"); ar.free("WU"); ar.free("WD"); ar.free("SGS"); ar.free("UWS")
    ar.free("HWS"); ar.free("XGE"); ar.free("YGS"); ar.free("YGB"); ar.free("YGT")
    ar.free("XST")
    ar.alloc("Y", 8 * 2048 * 4, at=96 * 1024)
    ar.alloc("YTA", 2 * 1024 * 2)
    ar.alloc("YTB", 2 * 1024 * 2)
    ar.alloc("YSM", 2 * 1024 * 2)
    gmarks = []
    for k in range(16):
        par = k % 2
        p.gpsimd.wait("dsY")
        if len(gmarks) >= 2:
            p.gpsimd.wait("ve", gmarks[-2])
        yta = ar.view("YTA", BF16, 1024, col_off=par * 1024)
        ytb = ar.view("YTB", BF16, 1024, col_off=par * 1024)
        p.gpsimd.indirect_dma_start(out=yta, out_offset=None, in_=A["ygscr0"],
                                    in_offset=IOX(ar.view("IDXA", U32, 1, col_off=k), 0)).then_inc("dgY%d" % par, 16)
        p.gpsimd.indirect_dma_start(out=ytb, out_offset=None, in_=A["ygscr0"],
                                    in_offset=IOX(ar.view("IDXB", U32, 1, col_off=k), 0)).then_inc("dgY%d" % par, 16)
        p.vector.wait("dgY%d" % par)
        ysum = ar.view("YSM", BF16, 1024, col_off=par * 1024)
        p.vector.tensor_scalar(ysum, yta, ar.view("WCA", F32, 1, col_off=k), None,
                               op0=ALU.mult)
        p.vector.scalar_tensor_tensor(ysum, ytb, ar.view("WCB", F32, 1, col_off=k),
                                      ysum, op0=ALU.mult, op1=ALU.add).then_inc("ve", 1)
        for c in range(NCH):
            p.tensor.wait("ve")
            mm = p.tensor.transpose(ptb[:, 256 + (c % 2) * 128:256 + (c % 2) * 128 + 128],
                                    ysum[:, c * 128:(c + 1) * 128], identb)
            mm.then_inc("te", 1)
            p.vector.wait("te")
            p.vector.tensor_copy(ar.view("Y", F32, 128, col_off=c * 2048 + k * 128),
                                 ptb[:, 256 + (c % 2) * 128:256 + (c % 2) * 128 + 128]).then_inc("ve", 1)
        gmarks.append(p.val("ve"))
    ar.free("YTA"); ar.free("YTB"); ar.free("YSM")
    ar.free("IDXA"); ar.free("IDXB"); ar.free("WCA"); ar.free("WCB")
    ar.free("ISC"); ar.free("P2R"); ar.free("M1T"); ar.free("MALL"); ar.free("WTF")
    ar.free("WROWS"); ar.free("LOG")


def emit_silu_to_G(p, ar):
    ar.alloc("G", 8 * 2048 * 2, at=64 * 1024)
    for c in range(NCH):
        for sl in range(4):
            p.scalar.wait("ve")
            p.scalar.activation(ar.view("G", BF16, 512, col_off=c * 2048 + sl * 512),
                                ar.view("Y", F32, 512, col_off=c * 2048 + sl * 512),
                                AF.Silu).then_inc("se", 1)
    ar.free("Y")


def emit_l0_projections(p, ar, A, lay):
    PSx = p.psB
    for nm in ("wq", "wk"):
        ar.alloc(nm, NCH * 64 * 2)
        p.gpsimd.dma_start(out=ar.view(nm, BF16, NCH * 64),
                           in_=A[nm].rearrange("(c q) f -> q c f", q=128)).then_inc("dgp", 16)
    ar.alloc("wv", NCH * 1024 * 2)
    p.gpsimd.dma_start(out=ar.view("wv", BF16, NCH * 1024),
                       in_=A["wv"].rearrange("(c q) f -> q c f", q=128)).then_inc("dgp", 16)
    ar.alloc("Kown", 2048 * 2)
    ar.alloc("QST", 2 * 512 * 2)
    ar.alloc("VST", 2 * 1024 * 2)
    p.tensor.wait("dgp")
    p.tensor.wait("ve")
    for sl in range(4):
        for c in range(NCH):
            xn = ar.view("XN", BF16, 512, col_off=c * 2048 + sl * 512)
            p.tensor.matmul(PSx[0][0:64, 0:512], ar.view("wk", BF16, 64, col_off=c * 64),
                            xn, start=(c == 0), stop=(c == NCH - 1))
            mm = p.tensor.matmul(PSx[1][0:64, 0:512], ar.view("wq", BF16, 64, col_off=c * 64),
                                 xn, start=(c == 0), stop=(c == NCH - 1))
        mm.then_inc("te", 1)
        p.scalar.wait("te")
        p.scalar.copy(ar.view("Kown", BF16, 512, part=64, col_off=sl * 512),
                      PSx[0][0:64, 0:512]).then_inc("se", 1)
        qst = ar.view("QST", BF16, 512, part=64, col_off=(sl % 2) * 512)
        p.scalar.wait("dsc")
        p.scalar.activation(qst, PSx[1][0:64, 0:512], AF.Copy,
                            scale=0.125).then_inc("se", 1)
        p.scalar.dma_start(out=A["l0_qscr"][:, sl * 512:(sl + 1) * 512],
                           in_=qst).then_inc("dsc", 16)
        p.tensor.wait("se")
    for j in range(NBLK):
        for half in range(2):
            p.tensor.wait("dsc")
            for c in range(NCH):
                xn = ar.view("XN", BF16, 128, col_off=c * 2048 + j * 128)
                mm = p.tensor.matmul(PSx[2 + half][:, 0:512],
                                     xn, ar.view("wv", BF16, 512, col_off=c * 1024 + half * 512),
                                     start=(c == 0), stop=(c == NCH - 1))
            mm.then_inc("te", 1)
            p.scalar.wait("te")
            p.scalar.copy(ar.view("VST", BF16, 512, col_off=(j % 2) * 1024 + half * 512),
                          PSx[2 + half][:, 0:512]).then_inc("se", 1)
        p.scalar.wait("se")
        p.scalar.dma_start(out=A["ccI"][j * 128:(j + 1) * 128, :],
                           in_=ar.view("VST", BF16, 1024, col_off=(j % 2) * 1024)).then_inc("dsc", 16)
    p.scalar.wait("se")
    p.scalar.dma_start(out=A["ccI"][2048:2176, :].rearrange("(a b) w -> a (b w)", b=2),
                       in_=ar.view("Kown", BF16, 2048, part=64)).then_inc("dsc", 16)
    p.gpsimd.wait("dsc")
    p.gpsimd.collective_compute("AllGather", ALU.bypass, replica_groups=RG_PAIRS,
                                ins=[A["ccI"].opt()], outs=[A["ccO"].opt()]).then_inc("cc", 1)
    ar.free("wq"); ar.free("wk"); ar.free("wv"); ar.free("VST"); ar.free("Kown")
    ar.free("QST")


def emit_l0_attention(p, ar, A):
    PSx = p.psB
    ptb = p.psPT
    R = 2176
    ar.alloc("K", 4096 * 2, at=163840)
    ar.alloc("V", 32 * 1024 * 2, at=96 * 1024)
    ar.alloc("QS0", 2 * 128 * 2, at=206400)
    p.sync.wait("cc")
    for r in range(2):
        kk = A["ccO"][r * R + 2048: r * R + 2176, :].rearrange("(a b) w -> a (b w)", b=2)
        dst = ar.view("K", BF16, 4096, part=64).rearrange(
            "p (j two i) -> p two j i", two=2, i=128)[:, r, :, :]
        p.sync.dma_start(out=dst, in_=kk.rearrange("a (j i) -> a j i", i=128)).then_inc("dsy", 16)
        vsrc = A["ccO"][r * R: r * R + 2048, :].rearrange("(j i) w -> j i w", i=128)
        vdst = ar.view("V", BF16, 32 * 1024).rearrange(
            "p (t two w) -> p two t w", two=2, w=1024)[:, r, :, :]
        p.sync.dma_start(out=vdst, in_=vsrc.rearrange("j i w -> i j w")).then_inc("dsy", 16)
    ar.alloc("SS", 4096 * 4, at=172032)
    ar.alloc("SGE", 256 * 4, at=204288)
    ar.alloc("MXD", 8 * 4, at=205312)
    ar.alloc("P", (3840 + 4096) * 2, at=188416)
    mskab = ar.view("c_mskab", F32, 256)
    p.tensor.wait("dsy")

    for pair in range(8):
        for qb in (2 * pair, 2 * pair + 1):
            ntile = 2 * qb + 2
            p.sync.wait("te")
            qv = ar.view("QS0", BF16, 128, part=64, col_off=(qb % 2) * 128)
            p.sync.dma_start(out=qv, in_=A["l0_qscr"][:, qb * 128:(qb + 1) * 128]).then_inc(
                "dsQ%d" % (qb % 2), 16)
            p.tensor.wait("dsQ%d" % (qb % 2))
            nslab = cdiv(ntile - 2, 4)
            for sb in range(nslab):
                t0 = sb * 4
                n = min(4, (ntile - 2) - t0) * 128
                bank = PSx[sb % 2]
                p.tensor.wait("se")
                p.tensor.wait("ve")
                mm = p.tensor.matmul(bank[:, 0:n], qv,
                                     ar.view("K", BF16, n, part=64, col_off=t0 * 128),
                                     start=True, stop=True)
                mm.then_inc("te", 1)
                p.scalar.wait("te")
                p.scalar.copy(ar.view("SS", F32, n, col_off=t0 * 128),
                              bank[:, 0:n]).then_inc("se", 1)
            bank = PSx[nslab % 2]
            p.tensor.wait("ve")
            mm = p.tensor.matmul(bank[:, 0:256], qv,
                                 ar.view("K", BF16, 256, part=64, col_off=(ntile - 2) * 128),
                                 start=True, stop=True)
            mm.then_inc("te", 1)
            p.vector.wait("te")
            p.vector.tensor_tensor(ar.view("SS", F32, 256, col_off=(ntile - 2) * 128),
                                   bank[:, 0:256], mskab, op=ALU.add).then_inc("ve", 1)
            ncol = ntile * 128
            mx = ar.view("MXD", F32, 1)
            nmx = ar.view("MXD", F32, 1, col_off=1)
            den = ar.view("MXD", F32, 1, col_off=2)
            rden = ar.view("MXD", F32, 1, col_off=3)
            p.vector.wait("se")
            p.vector.reduce_max(mx, ar.view("SS", F32, ncol), axis=AX.X)
            p.vector.tensor_scalar(nmx, mx, -1.0, None, op0=ALU.mult).then_inc("ve", 1)
            p.scalar.wait("ve")
            pv = ar.view("P", BF16, ncol, col_off=(qb % 2) * 3840)
            p.scalar.activation(pv, ar.view("SS", F32, ncol), AF.Exp,
                                bias=nmx, accum_out=den).then_inc("se", 1)
            p.vector.wait("se")
            p.vector.reciprocal(rden, den)
            p.vector.tensor_scalar(pv, pv, rden, None, op0=ALU.mult).then_inc("ve", 1)
        nt0, nt1 = 4 * pair + 2, 4 * pair + 4
        ar.free("SS")
        ar.alloc("PTA", 32 * 256 * 2, at=172032)
        for t in range(nt1):
            p.tensor.wait("ve")
            if t < nt0:
                p.tensor.transpose(ptb[:, 0:128],
                                   ar.view("P", BF16, 128, col_off=t * 128),
                                   ar.view("c_identb", BF16, 128))
            mm = p.tensor.transpose(ptb[:, 128:256],
                                    ar.view("P", BF16, 128, col_off=3840 + t * 128),
                                    ar.view("c_identb", BF16, 128))
            mm.then_inc("te", 1)
            p.vector.wait("te")
            if t < nt0:
                p.vector.tensor_copy(ar.view("PTA", BF16, 256, col_off=t * 256),
                                     ptb[:, 0:256]).then_inc("ve", 1)
            else:
                p.vector.memset(ar.view("PTA", BF16, 128, col_off=t * 256), 0.0)
                p.vector.tensor_copy(ar.view("PTA", BF16, 128, col_off=t * 256 + 128),
                                     ptb[:, 128:256]).then_inc("ve", 1)
        p.tensor.wait("ve")
        marks = []
        for dv in range(8):
            bank = PSx[2 + dv % 4][:, 0:256]
            if dv >= 4:
                p.tensor.wait("ge", marks[dv - 4])
            for t in range(nt1):
                mm = p.tensor.matmul(
                    bank,
                    ar.view("V", BF16, 128, col_off=t * 1024 + dv * 128),
                    ar.view("PTA", BF16, 256, col_off=t * 256),
                    start=(t == 0), stop=(t == nt1 - 1))
            mm.then_inc("te", 1)
            p.vector.wait("te")
            p.vector.wait("ge")
            g = ar.view("G", BF16, 256, col_off=dv * 2048 + pair * 256)
            sg = ar.view("SGE", F32, 256)
            p.vector.scalar_tensor_tensor(sg, bank, 1.0, g, op0=ALU.mult,
                                          op1=ALU.mult).then_inc("ve", 1)
            xv = ar.view("XF", F32, 256, col_off=dv * 2048 + pair * 256)
            p.gpsimd.wait("ve")
            p.gpsimd.tensor_tensor(xv, xv, sg, op=ALU.add).then_inc("ge", 1)
            marks.append(p.val("ge"))
        ar.free("PTA")
        ar.alloc("SS", 4096 * 4, at=172032)
    ar.free("K"); ar.free("V"); ar.free("SS"); ar.free("P")
    ar.free("SGE"); ar.free("MXD"); ar.free("QS0")



def emit_l1_wbuild(p, ar, A):
    PSx = p.psB
    ar.alloc("TTG", 2 * (1024 + 256) * 4)
    ar.alloc("TTS", 256 * 4)
    for i, nm in enumerate(("r", "i")):
        p.sync.dma_start(out=ar.view("TTG", F32, 1024, part=4, col_off=i * 1280),
                         in_=A["l1_a" + nm]).then_inc("dsy", 16)
        p.sync.dma_start(out=ar.view("TTG", F32, 256, part=4, col_off=i * 1280 + 1024),
                         in_=A["l1_b" + nm]).then_inc("dsy", 16)
    p.tensor.wait("dsy")
    for i, nm in enumerate(("r", "i")):
        scr = A["l1_wscr_" + nm]
        for m in range(8):
            p.tensor.wait("dsc")
            mm = p.tensor.matmul(PSx[6][:, 0:256],
                                 ar.view("TTG", F32, 128, part=4, col_off=i * 1280 + m * 128),
                                 ar.view("TTG", F32, 256, part=4, col_off=i * 1280 + 1024),
                                 start=True, stop=True)
            mm.then_inc("te", 1)
            p.scalar.wait("te")
            p.scalar.copy(ar.view("TTS", F32, 256), PSx[6][:, 0:256]).then_inc("se", 1)
            dst = scr.rearrange("b d g e -> b d g e")[m * 4:(m + 1) * 4]
            p.scalar.dma_start(out=dst, in_=ar.view("TTS", F32, 256)).then_inc("dsc", 16)
    for i, nm in enumerate(("r", "i")):
        ar.alloc("W" + nm.upper() + "T", 4 * 512 * 2)
        scr = A["l1_wscr_" + nm].rearrange("b d g e -> d g b e")
        p.gpsimd.wait("dsc")
        for k in range(4):
            p.gpsimd.dma_start(
                out=ar.view("W" + nm.upper() + "T", BF16, 512, col_off=k * 512),
                in_=scr[k * 8:(k + 1) * 8]).then_inc("dgB", 16)
    ar.free("TTG"); ar.free("TTS")


def emit_l1_ffn(p, ar, A):
    PSx = p.psB
    ar.alloc("XNI", 4 * 2048 * 2)
    for c in range(4):
        for sl in range(4):
            p.vector.tensor_scalar(
                ar.view("XNI", BF16, 512, col_off=c * 2048 + sl * 512),
                ar.view("XN", BF16, 512, col_off=(4 + c) * 2048 + sl * 512),
                -1.0, None, op0=ALU.mult).then_inc("ve", 1)
    ar.alloc("G", 8 * 2048 * 2, at=128 * 1024)
    onesb = ar.view("c_onesb", BF16, 512, part=1)
    p.tensor.wait("ve")
    p.tensor.wait("dgB")
    marks = []
    for oc in range(8):
        real = oc < 4
        for sl in range(4):
            bank = PSx[(oc * 4 + sl) % 2]
            it = len(marks)
            if it >= 2:
                p.tensor.wait("se", marks[it - 2])
            for k in range(4):
                w1 = ar.view("WRT" if real else "WIT", BF16, 128,
                             col_off=k * 512 + (oc % 4) * 128)
                w2 = ar.view("WIT" if real else "WRT", BF16, 128,
                             col_off=k * 512 + (oc % 4) * 128)
                xr = ar.view("XN", BF16, 512, col_off=k * 2048 + sl * 512)
                xi2 = (ar.view("XNI", BF16, 512, col_off=k * 2048 + sl * 512) if real
                       else ar.view("XN", BF16, 512, col_off=(4 + k) * 2048 + sl * 512))
                p.tensor.matmul(bank[:, 0:512], w1, xr, start=(k == 0), stop=False)
                p.tensor.matmul(bank[:, 0:512], w2, xi2, start=False, stop=False)
            if oc == 0 and sl == 0:
                p.tensor.wait("dgA")
            bias = ar.view("l1_brow", BF16, 128, part=1,
                           col_off=(0 if real else 512) + (oc % 4) * 128)
            mm = p.tensor.matmul(bank[:, 0:512], bias, onesb, start=False, stop=True)
            mm.then_inc("te", 1)
            p.scalar.wait("te")
            p.scalar.activation(ar.view("G", BF16, 512, col_off=oc * 2048 + sl * 512),
                                bank[:, 0:512], AF.Silu).then_inc("se", 1)
            marks.append(p.val("se"))
    ar.free("XNI"); ar.free("WRT"); ar.free("WIT")


def emit_l1_attention_a(p, ar, A):
    PSx = p.psB
    ptb = p.psPT
    ar.alloc("WP", 8 * 1024 * 2)
    p.gpsimd.dma_start(out=ar.view("WP", BF16, 8 * 1024),
                       in_=A["l1_proj"].rearrange("(c q) f -> q c f", q=128)).then_inc("dgC", 16)
    ar.alloc("M1", 8 * 64 * 2)
    p.gpsimd.dma_start(out=ar.view("M1", BF16, 8 * 64),
                       in_=A["l1_m"]).then_inc("dgC", 16)
    ar.alloc("PF", 8 * 2048 * 2, at=96 * 1024)
    p.tensor.wait("dgC")
    for oc in range(8):
        for sl in range(4):
            bank = PSx[(oc * 4 + sl) % 2]
            p.tensor.wait("se")
            for c in range(NCH):
                mm = p.tensor.matmul(bank[:, 0:512],
                                     ar.view("WP", BF16, 128, col_off=c * 1024 + oc * 128),
                                     ar.view("XN", BF16, 512, col_off=c * 2048 + sl * 512),
                                     start=(c == 0), stop=(c == NCH - 1))
            mm.then_inc("te", 1)
            p.scalar.wait("te")
            p.scalar.copy(ar.view("PF", BF16, 512, col_off=oc * 2048 + sl * 512),
                          bank[:, 0:512]).then_inc("se", 1)
    ar.alloc("PTOK", 16 * 1024 * 2, at=96 * 1024 + 32 * 1024)
    p.tensor.wait("se")
    for c in range(8):
        for t in range(16):
            p.tensor.wait("ve")
            mm = p.tensor.transpose(ptb[:, (t % 4) * 128:(t % 4) * 128 + 128],
                                    ar.view("PF", BF16, 128, col_off=c * 2048 + t * 128),
                                    ar.view("c_identb", BF16, 128))
            mm.then_inc("te", 1)
            p.vector.wait("te")
            p.vector.tensor_copy(
                ar.view("PTOK", BF16, 128, col_off=t * 1024 + c * 128),
                ptb[:, (t % 4) * 128:(t % 4) * 128 + 128]).then_inc("ve", 1)
    ar.alloc("GST", 16 * 64 * 4)
    p.tensor.wait("ve")
    gmarks = []
    for n in range(16):
        bank = PSx[4 + n % 2]
        if n >= 2:
            p.tensor.wait("se", gmarks[n - 2])
        for t in range(16):
            mm = p.tensor.matmul(bank[0:64, 0:64],
                                 ar.view("PTOK", BF16, 64, col_off=t * 1024 + n * 64),
                                 ar.view("PTOK", BF16, 64, col_off=t * 1024 + n * 64),
                                 start=(t == 0), stop=(t == 15))
        mm.then_inc("te", 1)
        p.scalar.wait("te")
        p.scalar.copy(ar.view("GST", F32, 64, part=64, col_off=n * 64),
                      bank[0:64, 0:64]).then_inc("se", 1)
        gmarks.append(p.val("se"))
    p.scalar.wait("se")
    p.scalar.dma_start(out=A["ccGi"], in_=ar.view("GST", F32, 1024, part=64)).then_inc("dsc", 16)
    p.gpsimd.wait("dsc")
    p.gpsimd.collective_compute("AllReduce", ALU.add, replica_groups=RG_PAIRS,
                                ins=[A["ccGi"].opt()], outs=[A["ccGo"].opt()]).then_inc("cc", 1)
    ar.free("PTOK"); ar.free("WP"); ar.free("GST")


def emit_l1_attention_b(p, ar, A):
    PSx = p.psB
    ar.alloc("GSB", 8 * 64 * 2)
    p.gpsimd.wait("cc")
    for par in range(2):
        p.gpsimd.dma_start(
            out=ar.view("GSB", BF16, 8 * 64, part=64, base=par * 64).rearrange(
                "p (m w) -> p m w", w=64),
            in_=A["ccGo"].rearrange("a (m two w) -> a two m w", two=2, w=64)[:, par, :, :],
        ).then_inc("dgD", 16)
    ar.alloc("MGSB", 8 * 64 * 2)
    mgm = []
    for n in range(16):
        bn = (n % 2) * 64
        cn = (n // 2) * 64
        bank = PSx[4 + n % 2]
        p.tensor.wait("dgD")
        if n >= 2:
            p.tensor.wait("se", mgm[n - 2])
        mm = p.tensor.matmul(bank[0:64, 64:128],
                             ar.view("M1", BF16, 64, part=64, base=bn, col_off=cn),
                             ar.view("GSB", BF16, 64, part=64, base=bn, col_off=cn),
                             start=True, stop=True)
        mm.then_inc("te", 1)
        p.scalar.wait("te")
        p.scalar.copy(ar.view("MGSB", BF16, 64, part=64, base=bn, col_off=cn),
                      bank[0:64, 64:128]).then_inc("se", 1)
        mgm.append(p.val("se"))
    ar.alloc("MIX", 8 * 1024 * 2)
    p.gpsimd.dma_start(out=ar.view("MIX", BF16, 8 * 1024),
                       in_=A["l1_mixT"].rearrange("(c q) f -> q c f", q=128)).then_inc("dgE", 16)
    ar.alloc("NUD", 8 * 512 * 2)
    ar.alloc("SGE", 512 * 4)
    onesb = ar.view("c_onesb", BF16, 512, part=1)
    p.tensor.wait("dgE")
    p.tensor.wait("se")
    emarks = []
    for sl in range(4):
        for c in range(NCH):
            p.tensor.wait("ve")
            bank = PSx[6]
            pfc = ar.view("PF", BF16, 512, col_off=c * 2048 + sl * 512)
            p.tensor.matmul(bank[0:64, 0:512],
                            ar.view("MGSB", BF16, 64, part=64, base=0, col_off=c * 64),
                            pfc[0:64, :], start=True, stop=True)
            mm2 = p.tensor.matmul(bank[64:128, 0:512],
                                  ar.view("MGSB", BF16, 64, part=64, base=64, col_off=c * 64),
                                  ar.view("PF", BF16, 512, part=64, base=64,
                                          col_off=c * 2048 + sl * 512),
                                  start=True, stop=True)
            mm2.then_inc("te", 1)
            p.vector.wait("te")
            p.vector.tensor_copy(ar.view("NUD", BF16, 512, col_off=c * 512),
                                 bank[:, 0:512]).then_inc("ve", 1)
        p.tensor.wait("ve")
        for oc in range(NCH):
            sbank = PSx[2 + oc % 2]
            if len(emarks) >= 2:
                p.tensor.wait("ge", emarks[-2])
            for c in range(NCH):
                p.tensor.matmul(sbank[:, 0:512],
                                ar.view("MIX", BF16, 128, col_off=c * 1024 + oc * 128),
                                ar.view("NUD", BF16, 512, col_off=c * 512),
                                start=(c == 0), stop=False)
            bias = ar.view("l1_mbrow", BF16, 128, part=1, col_off=oc * 128)
            mm = p.tensor.matmul(sbank[:, 0:512], bias, onesb, start=False, stop=True)
            mm.then_inc("te", 1)
            p.vector.wait("te")
            p.vector.wait("ge")
            g = ar.view("G", BF16, 512, col_off=oc * 2048 + sl * 512)
            sg = ar.view("SGE", F32, 512)
            p.vector.scalar_tensor_tensor(sg, sbank[:, 0:512], 1.0, g, op0=ALU.mult,
                                          op1=ALU.mult).then_inc("ve", 1)
            xv = ar.view("XF", F32, 512, col_off=oc * 2048 + sl * 512)
            p.gpsimd.wait("ve")
            p.gpsimd.tensor_tensor(xv, xv, sg, op=ALU.add).then_inc("ge", 1)
            emarks.append(p.val("ge"))
    ar.free("NUD"); ar.free("SGE"); ar.free("MIX"); ar.free("M1")
    ar.free("GSB"); ar.free("MGSB"); ar.free("PF")




def emit_l3_ffn(p, ar, A):
    PSx = p.psB
    ar.alloc("Y", 8 * 2048 * 4, at=96 * 1024)
    ar.alloc("W13", 2 * NCH * 512 * 2)
    ar.alloc("W23", 4 * 1024 * 2)
    ar.alloc("B13", 512 * 2)
    ar.alloc("B23", 1024 * 2)
    ar.alloc("HS3", 2 * 512 * 2)
    ar.alloc("HW3", 4 * 512 * 2)
    ar.alloc("WBC", 512 * 2)
    bchot = ar.view("c_bchot", BF16, 8 * 128, part=8)
    wrows = ar.view("WROWS", BF16, 2048, part=8)
    onesb = ar.view("c_onesb", BF16, 512, part=1)

    def wl3(e, buf):
        p.gpsimd.dma_start(out=ar.view("W13", BF16, NCH * 512, col_off=buf * NCH * 512),
                           in_=A["l3_w1"][e].rearrange("(c q) f -> q c f", q=128)).then_inc(
            "dgW%d" % buf, 16)

    def auxl3(e):
        p.gpsimd.dma_start(out=ar.view("W23", BF16, 4 * 1024),
                           in_=A["l3_w2"][e].rearrange("(c q) f -> q c f", q=128)).then_inc("dgV3", 16)
        p.gpsimd.dma_start(out=ar.view("B13", BF16, 512, part=1),
                           in_=A["l3_b1"][e:e + 1, :]).then_inc("dgV3", 16)
        p.gpsimd.dma_start(out=ar.view("B23", BF16, 1024, part=1),
                           in_=A["l3_b2"][e:e + 1, :]).then_inc("dgV3", 16)

    wl3(0, 0)
    auxl3(0)
    etmarks = {}
    gm3 = []
    wbcm = []
    for e in range(8):
        buf = e % 2
        if e + 1 < 8:
            p.gpsimd.wait("te", etmarks.get(e - 1, 0))
            wl3(e + 1, (e + 1) % 2)
        w1 = ar.view("W13", BF16, NCH * 512, col_off=buf * NCH * 512)
        w2 = ar.view("W23", BF16, 4 * 1024)
        b1 = ar.view("B13", BF16, 512, part=1)
        b2 = ar.view("B23", BF16, 1024, part=1)
        p.tensor.wait("dgW%d" % buf)
        p.tensor.wait("dgV3")
        for ts in range(4):
            tc = ts * 512
            if wbcm:
                p.tensor.wait("ve", wbcm[-1])
            else:
                p.tensor.wait("ve")
            p.tensor.matmul(PSx[6][:, 0:512], bchot[:, e * 128:(e + 1) * 128],
                            wrows[:, tc:tc + 512], start=True, stop=True).then_inc("te", 1)
            p.vector.wait("te", p.val("te"))
            p.vector.wait("ge")
            wbc = ar.view("WBC", BF16, 512)
            p.vector.tensor_copy(wbc, PSx[6][:, 0:512]).then_inc("ve", 1)
            wbcm.append(p.val("ve"))
            for hc in range(4):
                gidx = len(gm3)
                bank = PSx[gidx % 2]
                if gidx >= 2:
                    p.tensor.wait("se", gm3[gidx - 2][0])
                else:
                    p.tensor.wait("se")
                    p.tensor.wait("ge")
                for c in range(NCH):
                    p.tensor.matmul(bank[:, 0:512],
                                    w1[:, c * 512 + hc * 128: c * 512 + hc * 128 + 128],
                                    ar.view("XN", BF16, 512, col_off=c * 2048 + tc),
                                    start=(c == 0), stop=False)
                mm = p.tensor.matmul(bank[:, 0:512],
                                     b1[:, hc * 128:(hc + 1) * 128],
                                     onesb, start=False, stop=True)
                mm.then_inc("te", 1)
                tv = p.val("te")
                p.scalar.wait("te", tv)
                if gidx >= 2:
                    p.scalar.wait("ge", gm3[gidx - 2][1])
                hs3 = ar.view("HS3", BF16, 512, col_off=(gidx % 2) * 512)
                p.scalar.activation(hs3, bank[:, 0:512], AF.Gelu).then_inc("se", 1)
                sm = p.val("se")
                p.gpsimd.wait("se", sm)
                p.gpsimd.wait("ve", wbcm[-1])
                p.gpsimd.wait("te")
                p.gpsimd.tensor_tensor(ar.view("HW3", BF16, 512, col_off=hc * 512),
                                       hs3, wbc, op=ALU.mult).then_inc("ge", 1)
                gm3.append((sm, p.val("ge")))
            p.tensor.wait("ge")
            marks = []
            for dch in range(NCH):
                bank = PSx[2 + dch % 2][:, 0:512]
                if dch >= 2:
                    p.tensor.wait("ve", marks[dch - 2])
                for hc in range(4):
                    p.tensor.matmul(bank,
                                    w2[:, hc * 1024 + dch * 128: hc * 1024 + dch * 128 + 128],
                                    ar.view("HW3", BF16, 512, col_off=hc * 512),
                                    start=(hc == 0), stop=False)
                mm = p.tensor.matmul(bank, b2[:, dch * 128:(dch + 1) * 128],
                                     wbc[0:1, 0:512],
                                     start=False, stop=True)
                mm.then_inc("te", 1)
                p.vector.wait("te")
                yv = ar.view("Y", F32, 512, col_off=dch * 2048 + tc)
                if e == 0:
                    p.vector.tensor_copy(yv, bank).then_inc("ve", 1)
                else:
                    p.vector.tensor_tensor(yv, yv, bank, op=ALU.add).then_inc("ve", 1)
                marks.append(p.val("ve"))
        etmarks[e] = p.val("te")
        if e + 1 < 8:
            p.gpsimd.wait("te", etmarks[e])
            auxl3(e + 1)
    ar.free("W13"); ar.free("W23"); ar.free("B13"); ar.free("B23")
    ar.free("HS3"); ar.free("HW3"); ar.free("WBC")
    ar.free("WROWS"); ar.free("LOG")


def emit_l3_projections(p, ar, A):
    PSx = p.psB
    ar.alloc("WQ3", NCH * 1024 * 2)
    p.gpsimd.dma_start(out=ar.view("WQ3", BF16, NCH * 1024),
                       in_=A["l3_wq"].rearrange("(c q) f -> q c f", q=128)).then_inc("dgC", 16)
    ar.alloc("WKV3", NCH * 256 * 2)
    p.gpsimd.dma_start(out=ar.view("WKV3", BF16, NCH * 256).rearrange(
        "p (c f) -> p c f", f=256),
        in_=A["l3_wk"].rearrange("(c q) f -> q c f", q=128)).then_inc("dgC", 16)
    ar.alloc("WV3", NCH * 256 * 2)
    p.gpsimd.dma_start(out=ar.view("WV3", BF16, NCH * 256).rearrange(
        "p (c f) -> p c f", f=256),
        in_=A["l3_wv"].rearrange("(c q) f -> q c f", q=128)).then_inc("dgC", 16)
    ar.alloc("B3", 4096 * 2)
    p.gpsimd.dma_start(out=ar.view("B3", BF16, 1024, part=1),
                       in_=A["l3_bq"]).then_inc("dgC", 16)
    p.gpsimd.dma_start(out=ar.view("B3", BF16, 256, part=1, col_off=1024),
                       in_=A["l3_bk"]).then_inc("dgC", 16)
    p.gpsimd.dma_start(out=ar.view("B3", BF16, 256, part=1, col_off=1536),
                       in_=A["l3_bv"]).then_inc("dgC", 16)
    ar.alloc("ST3", 2 * 512 * 2)
    onesb = ar.view("c_onesb", BF16, 512, part=1)
    p.tensor.wait("dgC")
    for oc in range(NCH):
        for sl in range(4):
            bank = PSx[(oc * 4 + sl) % 2]
            p.tensor.wait("dsc")
            for c in range(NCH):
                p.tensor.matmul(bank[:, 0:512],
                                ar.view("WQ3", BF16, 128, col_off=c * 1024 + oc * 128),
                                ar.view("XN", BF16, 512, col_off=c * 2048 + sl * 512),
                                start=(c == 0), stop=False)
            mm = p.tensor.matmul(bank[:, 0:512],
                                 ar.view("B3", BF16, 128, part=1, col_off=oc * 128),
                                 onesb, start=False, stop=True)
            mm.then_inc("te", 1)
            p.scalar.wait("te")
            st = ar.view("ST3", BF16, 512, col_off=(sl % 2) * 512)
            p.scalar.activation(st, bank[:, 0:512], AF.Copy, scale=0.125).then_inc("se", 1)
            p.scalar.dma_start(out=A["l3_qscr"][oc * 128:(oc + 1) * 128, sl * 512:(sl + 1) * 512],
                               in_=st).then_inc("dsc", 16)
    for hh in range(4):
        for sl in range(4):
            bank = PSx[(hh * 4 + sl) % 2]
            p.tensor.wait("dsc")
            for c in range(NCH):
                p.tensor.matmul(bank[0:64, 0:512],
                                ar.view("WKV3", BF16, 64, col_off=c * 256 + hh * 64),
                                ar.view("XN", BF16, 512, col_off=c * 2048 + sl * 512),
                                start=(c == 0), stop=False)
            mm = p.tensor.matmul(bank[0:64, 0:512],
                                 ar.view("B3", BF16, 64, part=1, col_off=1024 + hh * 64),
                                 onesb, start=False, stop=True)
            mm.then_inc("te", 1)
            p.scalar.wait("te")
            st = ar.view("ST3", BF16, 512, col_off=(sl % 2) * 512, part=64)
            p.scalar.copy(st, bank[0:64, 0:512]).then_inc("se", 1)
            karea = A["ccI3"][512:1024, :].rearrange("(r two) w -> r (two w)", two=2)
            p.scalar.dma_start(out=karea[hh * 64:(hh + 1) * 64, sl * 512:(sl + 1) * 512],
                               in_=st).then_inc("dsc", 16)
    for j in range(NBLK):
        bank = PSx[2 + j % 2]
        p.tensor.wait("dsc")
        for c in range(NCH):
            p.tensor.matmul(bank[:, 0:256],
                            ar.view("XN", BF16, 128, col_off=c * 2048 + j * 128),
                            ar.view("WV3", BF16, 256, col_off=c * 256),
                            start=(c == 0), stop=False)
        mm = p.tensor.matmul(bank[:, 0:256], ar.view("c_onesb", BF16, 128, part=1),
                             ar.view("B3", BF16, 256, part=1, col_off=1536),
                             start=False, stop=True)
        mm.then_inc("te", 1)
        p.scalar.wait("te")
        st = ar.view("ST3", BF16, 256, col_off=(j % 2) * 512)
        p.scalar.copy(st, bank[:, 0:256]).then_inc("se", 1)
        vdst = A["ccI3"][j * 32:(j + 1) * 32, :].rearrange("r (x w) -> (r x) w", x=4)
        p.scalar.dma_start(out=vdst, in_=st).then_inc("dsc", 16)
    p.gpsimd.wait("dsc")
    p.gpsimd.collective_compute("AllGather", ALU.bypass, replica_groups=RG_PAIRS,
                                ins=[A["ccI3"].opt()], outs=[A["ccO3"].opt()]).then_inc("cc", 1)
    ar.free("WQ3"); ar.free("WKV3"); ar.free("WV3"); ar.free("ST3"); ar.free("B3")


def emit_l3_attention(p, ar, A):
    PSx = p.psB
    ar.alloc("c_mskT3", 8 * 512 * 2)
    p.sync.dma_start(out=ar.view("c_mskT3", BF16, 8 * 512),
                     in_=A["c_mskT3"]).then_inc("dsy", 16)
    p.vector.wait("dsy")
    ar.alloc("K3", 2 * 4096 * 2, at=96 * 1024)
    ar.alloc("V3", 32 * 260 * 2, at=96 * 1024 + 16 * 1024)
    ar.alloc("S1", 8 * 2048 * 2, at=96 * 1024 + 34 * 1024)
    ar.alloc("QS3", 2 * 512 * 2)
    ar.alloc("PT3", 3 * 512 * 2)
    ar.alloc("RD3", 2 * 512 * 4)
    ar.alloc("RDC3", 2 * 512 * 4)
    p.sync.wait("cc")
    R3 = 1024
    for r in range(2):
        ksrc = A["ccO3"][r * R3 + 512: r * R3 + 1024, :].rearrange(
            "a b -> (a b)").rearrange(
            "(blk par dd j i) -> par blk dd j i", blk=2, par=2, dd=64, i=128)
        for par in range(2):
            for blk in range(2):
                dst = ar.view("K3", BF16, 4096, part=64, base=par * 64,
                              col_off=blk * 4096).rearrange(
                    "p (j two i) -> p two j i", two=2, i=128)[:, r, :, :]
                p.sync.dma_start(out=dst, in_=ksrc[par, blk]).then_inc("dkv", 16)
        vflat = A["ccO3"][r * R3: r * R3 + 512, :].rearrange("a b -> (a b)")
        for hh in range(4):
            vdst = ar.view("V3", BF16, 32 * 260).rearrange(
                "p (j two h x) -> p j two h x", two=2, h=4, x=65)[:, :, r, hh, 0:64]
            p.sync.dma_start(out=vdst, in_=vflat.rearrange(
                "(j i h x) -> i h j x", i=128, h=4, x=64)[:, hh]).then_inc("dkv", 16)
    p.vector.wait("dkv")
    p.vector.memset(ar.view("V3", BF16, 32 * 260).rearrange(
        "p (kt h x) -> p kt h x", h=4, x=65)[:, :, :, 64], 1.0).then_inc("ve", 1)
    mskT = ar.view("c_mskT3", BF16, 8 * 512)
    ve_ones = p.val("ve")
    smark = []
    DEP = 3
    pend = None
    sbank_free = {0: 0, 1: 0}
    rdbank_free = [0, 0]
    rd_read = [0, 0]
    it_idx = 0

    def flush(pend):
        alt = pend["alt"]
        rb = PSx[4 + alt]
        p.tensor.wait("ve", pend["ve_recip"])
        p.tensor.wait("ve", rdbank_free[alt])
        p.tensor.matmul(rb[0:64, 0:512], ar.view("c_ones_row", F32, 64, part=1),
                        ar.view("RD3", F32, 512, part=1, col_off=alt * 512),
                        start=True, stop=True).then_inc("te", 1)
        rd_read[alt] = p.val("te")
        p.vector.wait("te", rd_read[alt])
        cdst = ar.view("RDC3", F32, 512, col_off=alt * 512)[0:64, :]
        p.vector.tensor_copy(cdst, rb[0:64, 0:512])
        rdbank_free[alt] = p.val("ve")
        p.vector.tensor_tensor(pend["dst"], pend["sbank"][0:64, 0:512], cdst,
                               op=ALU.mult)
        sbank_free[pend["qb2"]] = p.val("ve")

    for j in range(16):
        par = (j % 2) * 64
        h = j % 4
        blk = h // 2
        hpar = (h % 2) * 64
        for qg in range(4):
            qb2 = (j * 4 + qg) % 2
            p.sync.wait("te", smark[-2] if len(smark) >= 2 else 0)
            qs = ar.view("QS3", BF16, 512, part=64, base=par, col_off=qb2 * 512)
            p.sync.dma_start(out=qs, in_=A["l3_qscr"][j * 64:(j + 1) * 64,
                                                      qg * 512:(qg + 1) * 512]).then_inc(
                "dsQ%d" % qb2, 16)
            nkt = 8 * qg + 8
            sbank = PSx[2 + qb2]
            p.tensor.wait("dkv")
            p.tensor.wait("dsQ%d" % qb2)
            p.tensor.wait("ve", ve_ones)
            ses = []

            def scores(kt, hpar=hpar, blk=blk, qs=qs, qg=qg, ses=ses):
                bank = PSx[kt % DEP if kt % DEP < 2 else 6]
                p.tensor.matmul(bank[:, 0:512],
                                ar.view("K3", BF16, 128, part=64, base=hpar,
                                        col_off=blk * 4096 + kt * 128),
                                qs, start=True, stop=True).then_inc("te", 1)
                tv = p.val("te")
                d = kt - 8 * qg
                if d >= 0:
                    p.vector.wait("te", tv)
                    p.vector.tensor_tensor(bank[:, 0:512], bank[:, 0:512],
                                           mskT[:, d * 512:(d + 1) * 512],
                                           op=ALU.add)
                    p.scalar.wait("ve", p.val("ve"))
                else:
                    p.scalar.wait("te", tv)
                p.scalar.activation(ar.view("PT3", BF16, 512, col_off=(kt % DEP) * 512),
                                    bank[:, 0:512], AF.Exp)
                ses.append(p.val("se"))

            def pv(kt, h=h, nkt=nkt, sbank=sbank, qb2=qb2, ses=ses):
                p.tensor.wait("se", ses[kt])
                if kt == 0:
                    p.tensor.wait("ve", sbank_free[qb2])
                p.tensor.matmul(sbank[0:65, 0:512],
                                ar.view("V3", BF16, 65, col_off=kt * 260 + h * 65),
                                ar.view("PT3", BF16, 512, col_off=(kt % DEP) * 512),
                                start=(kt == 0), stop=(kt == nkt - 1)).then_inc("te", 1)

            for kt in range(DEP):
                scores(kt)
            if pend is not None:
                flush(pend)
            for kt in range(DEP, nkt):
                pv(kt - DEP)
                scores(kt)
            for kt in range(nkt - DEP, nkt):
                pv(kt)
            smark.append(p.val("te"))
            alt = it_idx % 2
            p.vector.wait("te", smark[-1])
            p.vector.wait("te", rd_read[alt])
            p.vector.reciprocal(ar.view("RD3", F32, 512, part=1, col_off=alt * 512),
                                sbank[64:65, 0:512])
            pend = {"alt": alt, "ve_recip": p.val("ve"), "sbank": sbank, "qb2": qb2,
                    "dst": ar.view("S1", BF16, 512, part=64, base=par,
                                   col_off=(j // 2) * 2048 + qg * 512)}
            it_idx += 1
    flush(pend)
    ar.free("K3"); ar.free("V3"); ar.free("QS3"); ar.free("PT3"); ar.free("RD3")
    ar.free("RDC3"); ar.free("c_mskT3")


def emit_l3_post(p, ar, A):
    PSx = p.psB
    ar.alloc("SQ3", 2 * 512 * 2)
    ar.alloc("STR3", 4 * 2048 * 4, at=96 * 1024)
    ar.alloc("LNG", 8 * 4 * 2)
    p.sync.dma_start(out=ar.view("LNG", F32, 8),
                     in_=A["l3_lng"]).then_inc("dsy", 16)
    p.sync.dma_start(out=ar.view("LNG", F32, 8, col_off=8),
                     in_=A["l3_lnb"]).then_inc("dsy", 16)

    def srow(i, sl=None):
        if sl is None:
            return ar.view("STR3", F32, 2048, part=1, col_off=i * 2048)
        return ar.view("STR3", F32, 512, part=1, col_off=i * 2048 + sl * 512)

    onescolb = ar.view("c_onescolb", BF16, 1)
    ones_row = ar.view("c_ones_row", F32, 128, part=1)
    for sl in range(4):
        p.tensor.wait("ve")
        for c in range(NCH):
            sq = ar.view("SQ3", BF16, 512, col_off=(c % 2) * 512)
            ssl = ar.view("S1", BF16, 512, col_off=c * 2048 + sl * 512)
            p.scalar.wait("te")
            p.scalar.activation(sq, ssl, AF.Square).then_inc("se", 1)
            p.tensor.wait("se")
            p.tensor.matmul(PSx[0][0:1, 0:512], onescolb, ssl,
                            start=(c == 0), stop=(c == NCH - 1))
            mm = p.tensor.matmul(PSx[0][32:33, 0:512], onescolb, sq,
                                 start=(c == 0), stop=(c == NCH - 1))
            mm.then_inc("te", 1)
        p.vector.wait("te")
        p.vector.tensor_copy(srow(0, sl), PSx[0][0:1, 0:512])
        p.vector.tensor_copy(srow(1, sl), PSx[0][32:33, 0:512]).then_inc("ve", 1)
    p.vector.wait("ve")
    p.vector.tensor_scalar(srow(2), srow(0), 1.0 / D, None, op0=ALU.mult)
    p.vector.tensor_scalar(srow(3), srow(1), 1.0 / D, None, op0=ALU.mult)
    p.vector.tensor_tensor(srow(1), srow(2), srow(2), op=ALU.mult)
    p.vector.tensor_tensor(srow(3), srow(3), srow(1), op=ALU.subtract)
    p.vector.tensor_scalar(srow(3), srow(3), 1e-5, None, op0=ALU.add).then_inc("ve", 1)
    p.scalar.wait("ve")
    p.scalar.activation(srow(3), srow(3), AF.Sqrt).then_inc("se", 1)
    p.vector.wait("se")
    p.vector.reciprocal(srow(3), srow(3)).then_inc("ve", 1)
    ar.alloc("XT3", 2 * 512 * 4)
    p.vector.wait("dsy")
    for sl in range(4):
        p.tensor.wait("ve")
        p.tensor.matmul(PSx[1][:, 0:512], ones_row, srow(2, sl), start=True, stop=True)
        mm = p.tensor.matmul(PSx[2][:, 0:512], ones_row, srow(3, sl), start=True, stop=True)
        mm.then_inc("te", 1)
        p.vector.wait("te")
        for c in range(NCH):
            ssl = ar.view("S1", BF16, 512, col_off=c * 2048 + sl * 512)
            xt = ar.view("XT3", F32, 512, col_off=(c % 2) * 512)
            p.vector.tensor_tensor(xt, ssl, PSx[1][:, 0:512], op=ALU.subtract)
            p.vector.tensor_tensor(xt, xt, PSx[2][:, 0:512], op=ALU.mult)
            p.vector.tensor_scalar(ssl, xt, ar.view("LNG", F32, 1, col_off=c),
                                   ar.view("LNG", F32, 1, col_off=8 + c),
                                   op0=ALU.mult, op1=ALU.add).then_inc("ve", 1)
    ar.free("SQ3"); ar.free("STR3"); ar.free("XT3")
    ar.alloc("WO3", NCH * 1024 * 2)
    p.gpsimd.wait("te"); p.gpsimd.wait("ve"); p.gpsimd.wait("se")
    p.gpsimd.dma_start(out=ar.view("WO3", BF16, NCH * 1024),
                       in_=A["l3_wo"].rearrange("(c q) f -> q c f", q=128)).then_inc("dgE", 16)
    ar.alloc("BO3", 1024 * 2)
    p.gpsimd.dma_start(out=ar.view("BO3", BF16, 1024, part=1),
                       in_=A["l3_bo"]).then_inc("dgE", 16)
    ar.alloc("SGE", 512 * 4)
    onesb = ar.view("c_onesb", BF16, 512, part=1)
    p.tensor.wait("dgE")
    p.tensor.wait("ve")
    emarks = []
    for sl in range(4):
        for oc in range(NCH):
            sbank = PSx[2 + oc % 2]
            if len(emarks) >= 2:
                p.tensor.wait("ge", emarks[-2])
            for c in range(NCH):
                p.tensor.matmul(sbank[:, 0:512],
                                ar.view("WO3", BF16, 128, col_off=c * 1024 + oc * 128),
                                ar.view("S1", BF16, 512, col_off=c * 2048 + sl * 512),
                                start=(c == 0), stop=False)
            mm = p.tensor.matmul(sbank[:, 0:512],
                                 ar.view("BO3", BF16, 128, part=1, col_off=oc * 128),
                                 onesb, start=False, stop=True)
            mm.then_inc("te", 1)
            p.vector.wait("te")
            p.vector.wait("ge")
            g = ar.view("G", BF16, 512, col_off=oc * 2048 + sl * 512)
            sg = ar.view("SGE", F32, 512)
            p.vector.scalar_tensor_tensor(sg, sbank[:, 0:512], 1.0, g, op0=ALU.mult,
                                          op1=ALU.mult).then_inc("ve", 1)
            xv = ar.view("XF", F32, 512, col_off=oc * 2048 + sl * 512)
            p.gpsimd.wait("ve")
            p.gpsimd.tensor_tensor(xv, xv, sg, op=ALU.add).then_inc("ge", 1)
            emarks.append(p.val("ge"))
    ar.free("WO3"); ar.free("BO3"); ar.free("SGE"); ar.free("S1"); ar.free("LNG")


def emit_l2_projections(p, ar, A):
    PSx = p.psB
    ar.alloc("WQKV", NCH * 3072 * 2)
    p.gpsimd.dma_start(out=ar.view("WQKV", BF16, NCH * 3072),
                       in_=A["l2_wqkv"].rearrange("(c q) f -> q c f", q=128)).then_inc("dgC", 16)
    ar.alloc("G1", 8 * 4)
    p.sync.dma_start(out=ar.view("G1", F32, 8), in_=A["l2_gam"]).then_inc("dsy", 16)
    ar.alloc("ST2", 2 * 1024 * 2)
    p.vector.wait("dsy")
    p.vector.tensor_scalar(ar.view("G1", F32, 8), ar.view("G1", F32, 8), 1.0, None,
                           op0=ALU.add).then_inc("ve", 1)
    p.vector.wait("dgC")
    for c in range(NCH):
        w = ar.view("WQKV", BF16, 3072, col_off=c * 3072)
        p.vector.tensor_scalar(w, w, ar.view("G1", F32, 1, col_off=c), None,
                               op0=ALU.mult).then_inc("ve", 1)
    p.tensor.wait("ve")
    for oc in range(NCH):
        for sl in range(4):
            bank = PSx[(oc * 4 + sl) % 2]
            p.tensor.wait("dsc")
            for c in range(NCH):
                mm = p.tensor.matmul(bank[:, 0:512],
                                     ar.view("WQKV", BF16, 128, col_off=c * 3072 + oc * 128),
                                     ar.view("XN2", BF16, 512, col_off=c * 2048 + sl * 512),
                                     start=(c == 0), stop=(c == NCH - 1))
            mm.then_inc("te", 1)
            p.scalar.wait("te")
            st = ar.view("ST2", BF16, 512, col_off=(sl % 2) * 1024)
            p.scalar.activation(st, bank[:, 0:512], AF.Copy, scale=0.125).then_inc("se", 1)
            p.scalar.dma_start(out=A["l2_qscr"][oc * 128:(oc + 1) * 128, sl * 512:(sl + 1) * 512],
                               in_=st).then_inc("dsc", 16)
    karea = A["ccI2"][2048:4096, :].rearrange("(r two) w -> r (two w)", two=2)
    for kc in range(NCH):
        for sl in range(4):
            bank = PSx[(kc * 4 + sl) % 2]
            p.tensor.wait("dsc")
            for c in range(NCH):
                mm = p.tensor.matmul(bank[:, 0:512],
                                     ar.view("WQKV", BF16, 128, col_off=c * 3072 + 1024 + kc * 128),
                                     ar.view("XN2", BF16, 512, col_off=c * 2048 + sl * 512),
                                     start=(c == 0), stop=(c == NCH - 1))
            mm.then_inc("te", 1)
            p.scalar.wait("te")
            st = ar.view("ST2", BF16, 512, col_off=(sl % 2) * 1024)
            p.scalar.copy(st, bank[:, 0:512]).then_inc("se", 1)
            p.scalar.dma_start(out=karea[kc * 128:(kc + 1) * 128, sl * 512:(sl + 1) * 512],
                               in_=st).then_inc("dsc", 16)
    for j in range(NBLK):
        p.tensor.wait("dsc")
        for half in range(2):
            bank = PSx[2 + 2 * (j % 2) + half]
            for c in range(NCH):
                mm = p.tensor.matmul(bank[:, 0:512],
                                     ar.view("XN2", BF16, 128, col_off=c * 2048 + j * 128),
                                     ar.view("WQKV", BF16, 512,
                                             col_off=c * 3072 + 2048 + half * 512),
                                     start=(c == 0), stop=(c == NCH - 1))
            mm.then_inc("te", 1)
        p.scalar.wait("te")
        st = ar.view("ST2", BF16, 1024, col_off=(j % 2) * 1024)
        p.scalar.copy(st[:, 0:512], PSx[2 + 2 * (j % 2)][:, 0:512])
        p.scalar.copy(st[:, 512:1024], PSx[3 + 2 * (j % 2)][:, 0:512]).then_inc("se", 1)
        p.scalar.dma_start(out=A["ccI2"][j * 128:(j + 1) * 128, :], in_=st).then_inc("dsc", 16)
    p.gpsimd.wait("dsc")
    p.gpsimd.collective_compute("AllGather", ALU.bypass, replica_groups=RG_PAIRS,
                                ins=[A["ccI2"].opt()], outs=[A["ccO2"].opt()]).then_inc("cc", 1)
    ar.free("WQKV"); ar.free("G1"); ar.free("ST2")


def emit_l2_ffn(p, ar, A):
    PSx = p.psB
    HS = 256
    NSL = 4096 // HS
    NHC = HS // 128
    NTS = 4
    ar.alloc("Y", 8 * 2048 * 4, at=96 * 1024)
    ar.alloc("W1B", 2 * NCH * HS * 2)
    ar.alloc("W2B", 2 * NHC * 1024 * 2)
    ar.alloc("B1B", 2 * HS * 2)
    ar.alloc("B2B", 1024 * 2)
    ar.alloc("XNM", NCH * 512 * 2)
    ar.alloc("HS2", NHC * 512 * 2)
    ar.alloc("MBC", 512 * 4)
    bchot = ar.view("c_bchot", BF16, 8 * 128, part=8)
    wrows = ar.view("WROWS", BF16, 2048, part=8)
    onesb = ar.view("c_onesb", BF16, 512, part=1)

    def wslab_dma(e, s, buf):
        sem = "dgW%d" % buf
        p.gpsimd.dma_start(
            out=ar.view("W1B", BF16, NCH * HS, col_off=buf * NCH * HS),
            in_=A["l2_w1"][e].rearrange("(c q) f -> q c f", q=128)[:, :, s * HS:(s + 1) * HS],
        ).then_inc(sem, 16)
        p.gpsimd.dma_start(
            out=ar.view("W2B", BF16, NHC * 1024, col_off=buf * NHC * 1024),
            in_=A["l2_w2"][e].rearrange("(c q) f -> q c f", q=128)[:, s * NHC:(s + 1) * NHC, :],
        ).then_inc(sem, 16)
        p.gpsimd.dma_start(
            out=ar.view("B1B", BF16, HS, part=1, col_off=buf * HS),
            in_=A["l2_b1"][e:e + 1, s * HS:(s + 1) * HS]).then_inc(sem, 16)

    tmarks = {}
    wslab_dma(0, 0, 0)
    p.gpsimd.dma_start(out=ar.view("B2B", BF16, 1024, part=1),
                       in_=A["l2_b2"][0:1, :]).then_inc("dgW0", 16)
    NT = 8 * NSL
    for gi in range(NT):
        e, s = gi // NSL, gi % NSL
        buf = gi % 2
        if gi + 1 < NT:
            if gi >= 1:
                p.gpsimd.wait("te", tmarks[gi - 1])
            wslab_dma((gi + 1) // NSL, (gi + 1) % NSL, (gi + 1) % 2)
            if (gi + 1) % NSL == 0:
                p.gpsimd.wait("te")
                p.gpsimd.dma_start(out=ar.view("B2B", BF16, 1024, part=1),
                                   in_=A["l2_b2"][(gi + 1) // NSL:(gi + 1) // NSL + 1, :]
                                   ).then_inc("dgA", 16)
        w1 = ar.view("W1B", BF16, NCH * HS, col_off=buf * NCH * HS)
        w2 = ar.view("W2B", BF16, NHC * 1024, col_off=buf * NHC * 1024)
        b1 = ar.view("B1B", BF16, HS, part=1, col_off=buf * HS)
        p.tensor.wait("dgW%d" % buf)
        if s == 0:
            p.tensor.wait("dgA")
        for ts in range(NTS):
            tc = ts * 512
            p.tensor.wait("ve")
            p.tensor.matmul(PSx[6][:, 0:512], bchot[:, e * 128:(e + 1) * 128],
                            wrows[:, tc:tc + 512], start=True, stop=True).then_inc("te", 1)
            p.vector.wait("te")
            p.vector.tensor_copy(ar.view("MBC", F32, 512), PSx[6][:, 0:512]).then_inc("ve", 1)
            for c in range(NCH):
                p.vector.tensor_tensor(ar.view("XNM", BF16, 512, col_off=c * 512),
                                       ar.view("XN", BF16, 512, col_off=c * 2048 + tc),
                                       ar.view("MBC", F32, 512), op=ALU.mult).then_inc("ve", 1)
            for hc in range(NHC):
                p.tensor.wait("ve")
                p.tensor.wait("se")
                for c in range(NCH):
                    p.tensor.matmul(PSx[0][:, 0:512],
                                    w1[:, c * HS + hc * 128: c * HS + hc * 128 + 128],
                                    ar.view("XNM", BF16, 512, col_off=c * 512),
                                    start=(c == 0), stop=False)
                mm = p.tensor.matmul(PSx[0][:, 0:512], b1[:, hc * 128:(hc + 1) * 128],
                                     onesb, start=False, stop=True)
                mm.then_inc("te", 1)
                p.scalar.wait("te")
                p.scalar.activation(ar.view("HS2", BF16, 512, col_off=hc * 512),
                                    PSx[0][:, 0:512], AF.Relu).then_inc("se", 1)
            p.tensor.wait("se")
            marks = []
            for dch in range(NCH):
                bank = PSx[2 + dch % 2][:, 0:512]
                if dch >= 2:
                    p.tensor.wait("ve", marks[dch - 2])
                for hc in range(NHC):
                    mm = p.tensor.matmul(
                        bank, w2[:, hc * 1024 + dch * 128: hc * 1024 + dch * 128 + 128],
                        ar.view("HS2", BF16, 512, col_off=hc * 512),
                        start=(hc == 0), stop=(hc == NHC - 1 and s != 0))
                if s == 0:
                    mm = p.tensor.matmul(bank,
                                         ar.view("B2B", BF16, 128, part=1, col_off=dch * 128),
                                         onesb, start=False, stop=True)
                mm.then_inc("te", 1)
                p.vector.wait("te")
                yv = ar.view("Y", F32, 512, col_off=dch * 2048 + tc)
                if gi == 0:
                    p.vector.tensor_copy(yv, bank).then_inc("ve", 1)
                else:
                    p.vector.tensor_tensor(yv, yv, bank, op=ALU.add).then_inc("ve", 1)
                marks.append(p.val("ve"))
        tmarks[gi] = p.val("te")
    ar.free("W1B"); ar.free("W2B"); ar.free("B1B"); ar.free("B2B")
    ar.free("XNM"); ar.free("HS2"); ar.free("MBC")
    ar.free("WROWS"); ar.free("LOG")


def emit_l2_ffn_sparse(p, ar, A):
    PSx = p.psB
    ptb = p.psPT
    C = 512
    IOX = bass.IndirectOffsetOnAxis
    U32 = mybir.dt.uint32
    ptb = p.psPT
    identb = ar.view("c_identb", BF16, 128)
    ar.alloc("IDX", 16 * 4)
    ar.alloc("ISC", 4 * 8 * 4)
    ar.alloc("P2R", 2 * 128 * 4)
    tris = ar.view("c_tris", F32, 128)
    onc = ar.view("c_ones_col", F32, 1)
    onr = ar.view("c_ones_row", F32, 128, part=1)
    p.vector.wait("dsy")
    p.vector.tensor_copy(ar.view("P2R", F32, 8, part=1), ar.view("c_ebase", F32, 8, part=1))
    for k in range(16):
        m = ar.view("MTOK", F32, 8, col_off=k * 8)
        p.tensor.wait("ve")
        p.tensor.matmul(PSx[4][0:1, k * 8:k * 8 + 8], onc, m, start=True, stop=True)
        p.tensor.matmul(PSx[5][:, 0:8], tris, m, start=True, stop=False)
        mm = p.tensor.matmul(PSx[5][:, 0:8], onr,
                             ar.view("P2R", F32, 8, part=1), start=False, stop=True)
        mm.then_inc("te", 1)
        p.vector.wait("te")
        sc = ar.view("ISC", F32, 8)
        p.vector.tensor_tensor(sc, m, PSx[5][:, 0:8], op=ALU.mult)
        p.vector.reduce_sum(ar.view("ISC", F32, 1, col_off=8), sc, axis=AX.X)
        p.vector.tensor_scalar(ar.view("ISC", F32, 1, col_off=8),
                               ar.view("ISC", F32, 1, col_off=8), 0.5, None, op0=ALU.add)
        p.vector.tensor_copy(ar.view("IDX", U32, 1, col_off=k),
                             ar.view("ISC", F32, 1, col_off=8))
        p.vector.tensor_tensor(ar.view("P2R", F32, 8, part=1),
                               ar.view("P2R", F32, 8, part=1),
                               PSx[4][0:1, k * 8:k * 8 + 8], op=ALU.add).then_inc("ve", 1)
    ar.alloc("ZR", 1024 * 2)
    ar.alloc("XST", 2 * 1024 * 2)
    p.vector.memset(ar.view("XST", BF16, 2048), 0.0).then_inc("ve", 1)
    p.scalar.wait("ve")
    for j in range(32):
        p.scalar.dma_start(out=A["xgscr"][j * 128:(j + 1) * 128, :],
                           in_=ar.view("XST", BF16, 1024)).then_inc("dgZ", 16)
    p.vector.wait("dgZ")
    for k in range(16):
        par = k % 2
        xst = ar.view("XST", BF16, 1024, col_off=par * 1024)
        for c in range(NCH):
            p.tensor.wait("ve")
            if c == 0:
                p.tensor.wait("dgS%d" % par)
            mm = p.tensor.transpose(ptb[:, (c % 2) * 128:(c % 2) * 128 + 128],
                                    ar.view("XN", BF16, 128, col_off=c * 2048 + k * 128),
                                    identb)
            mm.then_inc("te", 1)
            p.vector.wait("te")
            p.vector.tensor_copy(xst[:, c * 128:(c + 1) * 128],
                                 ptb[:, (c % 2) * 128:(c % 2) * 128 + 128]).then_inc("ve", 1)
        p.gpsimd.wait("ve")
        p.gpsimd.indirect_dma_start(out=A["xgscr"], out_offset=IOX(ar.view("IDX", U32, 1, col_off=k), 0),
                                    in_=xst, in_offset=None).then_inc("dgS%d" % par, 16)
    ar.free("XN")
    HS = 512
    NSL = 4096 // HS
    NHC = HS // 128
    ar.alloc("XGE", 2 * NCH * 512 * 2, at=64 * 1024)
    ar.alloc("W1B", 2 * NCH * HS * 2)
    ar.alloc("W2B", 2 * NHC * 1024 * 2)
    ar.alloc("B1B", 2 * HS * 2)
    ar.alloc("B2B", 1024 * 2)
    ar.alloc("HS2", NHC * 512 * 2)
    ar.alloc("YGS", NCH * 512 * 4, at=96 * 1024)
    ar.alloc("YGB", NCH * 512 * 2)
    ar.alloc("YGT", 2 * 1024 * 2)
    ar.alloc("C0A", 8 * 8 * 2)
    onesb = ar.view("c_onesb", BF16, 512, part=1)

    def wslab(e, s, buf):
        sem = "dgW%d" % buf
        p.gpsimd.dma_start(
            out=ar.view("W1B", BF16, NCH * HS, col_off=buf * NCH * HS),
            in_=A["l2_w1"][e].rearrange("(c q) f -> q c f", q=128)[:, :, s * HS:(s + 1) * HS],
        ).then_inc(sem, 16)
        p.gpsimd.dma_start(
            out=ar.view("W2B", BF16, NHC * 1024, col_off=buf * NHC * 1024),
            in_=A["l2_w2"][e].rearrange("(c q) f -> q c f", q=128)[:, s * NHC:(s + 1) * NHC, :],
        ).then_inc(sem, 16)
        p.gpsimd.dma_start(
            out=ar.view("B1B", BF16, HS, part=1, col_off=buf * HS),
            in_=A["l2_b1"][e:e + 1, s * HS:(s + 1) * HS]).then_inc(sem, 16)

    def xge_load(e, buf):
        for c in range(NCH):
            p.scalar.dma_start(
                out=ar.view("XGE", BF16, 512, col_off=buf * NCH * 512 + c * 512),
                in_=A["xgscr"][e * C:(e + 1) * C, c * 128:(c + 1) * 128],
                transpose=True).then_inc("dgX%d" % buf, 16)

    tmarks = {}
    dmarks = []
    wslab(0, 0, 0)
    p.scalar.wait("dgS0")
    p.scalar.wait("dgS1")
    p.scalar.wait("dgA")
    xge_load(0, 0)
    p.gpsimd.dma_start(out=ar.view("B2B", BF16, 1024, part=1),
                       in_=A["l2_b2"][0:1, :]).then_inc("dgB2", 16)
    for e in range(8):
        ebuf = e % 2
        if e + 1 < 8:
            p.scalar.wait("te", tmarks.get(e - 1, 0))
            xge_load(e + 1, (e + 1) % 2)
        xge = lambda c: ar.view("XGE", BF16, 512, col_off=ebuf * NCH * 512 + c * 512)
        p.tensor.wait("dgX%d" % ebuf)
        for s in range(NSL):
            buf = (e * NSL + s) % 2
            if e * NSL + s + 1 < 8 * NSL:
                if e * NSL + s >= 1:
                    p.gpsimd.wait("te", tmarks.get((e * NSL + s - 1, "w"), 0))
                wslab((e * NSL + s + 1) // NSL, (e * NSL + s + 1) % NSL, (buf + 1) % 2)
            w1 = ar.view("W1B", BF16, NCH * HS, col_off=buf * NCH * HS)
            w2 = ar.view("W2B", BF16, NHC * 1024, col_off=buf * NHC * 1024)
            b1 = ar.view("B1B", BF16, HS, part=1, col_off=buf * HS)
            p.tensor.wait("dgW%d" % buf)
            if s == 0:
                p.tensor.wait("dgB2")
            for hc in range(NHC):
                p.tensor.wait("se")
                for c in range(NCH):
                    p.tensor.matmul(PSx[0][:, 0:512],
                                    w1[:, c * HS + hc * 128: c * HS + hc * 128 + 128],
                                    xge(c), start=(c == 0), stop=False)
                mm = p.tensor.matmul(PSx[0][:, 0:512], b1[:, hc * 128:(hc + 1) * 128],
                                     onesb, start=False, stop=True)
                mm.then_inc("te", 1)
                p.scalar.wait("te")
                p.scalar.activation(ar.view("HS2", BF16, 512, col_off=hc * 512),
                                    PSx[0][:, 0:512], AF.Relu).then_inc("se", 1)
            p.tensor.wait("se")
            for dch in range(NCH):
                bank = PSx[2 + dch % 2][:, 0:512]
                if len(dmarks) >= 2:
                    p.tensor.wait("ve", dmarks[-2])
                for hc in range(NHC):
                    mm = p.tensor.matmul(
                        bank, w2[:, hc * 1024 + dch * 128: hc * 1024 + dch * 128 + 128],
                        ar.view("HS2", BF16, 512, col_off=hc * 512),
                        start=(hc == 0), stop=(hc == NHC - 1 and s != 0))
                if s == 0:
                    mm = p.tensor.matmul(bank,
                                         ar.view("B2B", BF16, 128, part=1, col_off=dch * 128),
                                         onesb, start=False, stop=True)
                mm.then_inc("te", 1)
                p.vector.wait("te")
                yv = ar.view("YGS", F32, 512, col_off=dch * 512)
                if s == 0:
                    p.vector.tensor_copy(yv, bank).then_inc("ve", 1)
                else:
                    p.vector.tensor_tensor(yv, yv, bank, op=ALU.add).then_inc("ve", 1)
                dmarks.append(p.val("ve"))
            tmarks[(e * NSL + s, "w")] = p.val("te")
        if e + 1 < 8:
            p.gpsimd.wait("te")
            p.gpsimd.dma_start(out=ar.view("B2B", BF16, 1024, part=1),
                               in_=A["l2_b2"][e + 1:e + 2, :]).then_inc("dgB2", 16)
        p.vector.wait("ve")
        for dch in range(NCH):
            p.vector.tensor_copy(ar.view("C0A", BF16, 1, col_off=dch * 8 + e),
                                 ar.view("YGS", F32, 1, col_off=dch * 512))
            p.vector.tensor_copy(ar.view("YGB", BF16, 512, col_off=dch * 512),
                                 ar.view("YGS", F32, 512, col_off=dch * 512)).then_inc("ve", 1)
        for st in range(4):
            ygt = ar.view("YGT", BF16, 1024, col_off=(st % 2) * 1024)
            for dch in range(NCH):
                p.tensor.wait("ve")
                if dch == 0:
                    p.tensor.wait("dsY", None)
                mm = p.tensor.transpose(ptb[:, (dch % 2) * 128:(dch % 2) * 128 + 128],
                                        ar.view("YGB", BF16, 128,
                                                col_off=dch * 512 + st * 128), identb)
                mm.then_inc("te", 1)
                p.vector.wait("te")
                p.vector.tensor_copy(ygt[:, dch * 128:(dch + 1) * 128],
                                     ptb[:, (dch % 2) * 128:(dch % 2) * 128 + 128]).then_inc("ve", 1)
            p.sync.wait("ve")
            p.sync.dma_start(out=A["ygscr"][e * C + st * 128: e * C + (st + 1) * 128, :],
                             in_=ygt).then_inc("dsY", 16)
        tmarks[e] = p.val("te")
    ar.free("W1B"); ar.free("W2B"); ar.free("B1B"); ar.free("B2B"); ar.free("HS2")
    ar.free("XGE"); ar.free("YGS"); ar.free("YGB"); ar.free("XST"); ar.free("ZR")
    ar.free("YGT")
    ar.alloc("C0T", 8 * 128 * 2)
    ar.alloc("CALLC", 8 * 4)
    identb = ar.view("c_identb", BF16, 128)
    onescolb = ar.view("c_onescolb", BF16, 1)
    for dch in range(NCH):
        p.tensor.wait("ve")
        mm = p.tensor.transpose(ptb[0:8, (dch % 4) * 128:(dch % 4) * 128 + 128],
                                ar.view("C0A", BF16, 8, col_off=dch * 8), identb)
        mm.then_inc("te", 1)
        p.vector.wait("te")
        p.vector.tensor_copy(ar.view("C0T", BF16, 128, part=8, col_off=dch * 128),
                             ptb[0:8, (dch % 4) * 128:(dch % 4) * 128 + 128]).then_inc("ve", 1)
        p.tensor.wait("ve")
        mm = p.tensor.matmul(PSx[4][:, dch:dch + 1],
                             ar.view("C0T", BF16, 128, part=8, col_off=dch * 128),
                             onescolb[0:8, :], start=True, stop=True)
        mm.then_inc("te", 1)
    p.vector.wait("te")
    p.vector.tensor_copy(ar.view("CALLC", F32, 8), PSx[4][:, 0:8]).then_inc("ve", 1)
    ar.alloc("Y", 8 * 2048 * 4, at=96 * 1024)
    ar.alloc("YT", 16 * 1024 * 2)
    ar.alloc("YGF", 512 * 2)
    wrows = ar.view("WROWS", BF16, 2048, part=8)
    p.gpsimd.wait("dsY")
    for k in range(16):
        yt = ar.view("YT", BF16, 1024, col_off=k * 1024)
        p.gpsimd.indirect_dma_start(out=yt, out_offset=None, in_=A["ygscr"],
                                    in_offset=IOX(ar.view("IDX", U32, 1, col_off=k), 0)).then_inc("dgY", 16)
    for dch in range(NCH):
        for sl in range(4):
            p.tensor.wait("ve")
            mm = p.tensor.matmul(PSx[5][:, 0:512],
                                 ar.view("C0T", BF16, 128, part=8, col_off=dch * 128),
                                 wrows[:, sl * 512:(sl + 1) * 512], start=True, stop=True)
            mm.then_inc("te", 1)
            for kk in range(4):
                k = sl * 4 + kk
                p.tensor.wait("dgY")
                p.tensor.wait("ve")
                mm = p.tensor.transpose(ptb[:, (kk % 2) * 128 + 256:(kk % 2) * 128 + 384],
                                        ar.view("YT", BF16, 128,
                                                col_off=k * 1024 + dch * 128), identb)
                mm.then_inc("te", 1)
                p.vector.wait("te")
                p.vector.tensor_copy(ar.view("YGF", BF16, 128, col_off=kk * 128),
                                     ptb[:, (kk % 2) * 128 + 256:(kk % 2) * 128 + 384]).then_inc("ve", 1)
            p.vector.wait("te")
            p.vector.scalar_tensor_tensor(
                ar.view("Y", F32, 512, col_off=dch * 2048 + sl * 512),
                ar.view("YGF", BF16, 512),
                ar.view("CALLC", F32, 1, col_off=dch),
                PSx[5][:, 0:512],
                op0=ALU.add, op1=ALU.subtract).then_inc("ve", 1)
    ar.free("IDX"); ar.free("ISC"); ar.free("P2R"); ar.free("C0A"); ar.free("C0T")
    ar.free("CALLC"); ar.free("YT"); ar.free("YGF")
    ar.free("WROWS"); ar.free("MTOK"); ar.free("LOG")


def emit_l2_attention(p, ar, A):
    PSx = p.psB
    R2 = 4096
    DEP = 3
    ar.alloc("S1", 8 * 2048 * 2)
    ar.alloc("QS2", 2 * 512 * 2)
    ar.alloc("PT2", DEP * 512 * 2)
    ar.alloc("RD2", 2 * 512 * 4)
    ar.alloc("RDC", 2 * 512 * 4)
    p.sync.wait("cc")
    smark = []
    for g in range(4):
        ar.alloc("K2", 4096 * 2 * 2, at=96 * 1024)
        ar.alloc("V2", 32 * 260 * 2, at=96 * 1024 + 16 * 1024 + 512)
        for r in range(2):
            kflat = A["ccO2"][r * R2 + 2048: r * R2 + 4096, :].rearrange("a b -> (a b)")
            for par in range(2):
                for mb in range(2):
                    hh = 4 * g + 2 * mb + par
                    src = kflat.rearrange("(hq dd j i) -> hq dd j i", dd=64, j=16, i=128)[hh]
                    dst = ar.view("K2", BF16, 4096, part=64, base=par * 64,
                                  col_off=mb * 4096).rearrange(
                        "p (j two i) -> p two j i", two=2, i=128)[:, r, :, :]
                    p.sync.dma_start(out=dst, in_=src).then_inc("dkv", 16)
            vflat = A["ccO2"][r * R2: r * R2 + 2048, :].rearrange("a b -> (a b)")
            for m in range(4):
                hh = 4 * g + m
                vdst = ar.view("V2", BF16, 32 * 260).rearrange(
                    "p (j two h x) -> p j two h x", two=2, h=4, x=65)[:, :, r, m, 0:64]
                p.sync.dma_start(out=vdst, in_=vflat.rearrange(
                    "(j i h x) -> i h j x", i=128, h=16, x=64)[:, hh]).then_inc("dkv", 16)
        p.vector.wait("dkv")
        p.vector.memset(ar.view("V2", BF16, 32 * 260).rearrange(
            "p (kt h x) -> p kt h x", h=4, x=65)[:, :, :, 64], 1.0)
        ve_ones = p.val("ve")
        pend = None
        sbank_free = {0: 0, 1: 0}
        rdbank_free = [0, 0]
        rd_read = [0, 0]
        it_idx = 0

        def flush(pend):
            alt = pend["alt"]
            rb = PSx[4 + alt]
            p.tensor.wait("ve", pend["ve_recip"])
            p.tensor.wait("ve", rdbank_free[alt])
            p.tensor.matmul(rb[0:64, 0:512], ar.view("c_ones_row", F32, 64, part=1),
                            ar.view("RD2", F32, 512, part=1, col_off=alt * 512),
                            start=True, stop=True).then_inc("te", 1)
            rd_read[alt] = p.val("te")
            p.vector.wait("te", rd_read[alt])
            cdst = ar.view("RDC", F32, 512, col_off=alt * 512)[0:64, :]
            p.vector.tensor_copy(cdst, rb[0:64, 0:512])
            rdbank_free[alt] = p.val("ve")
            p.vector.tensor_tensor(pend["dst"], pend["sbank"][0:64, 0:512], cdst,
                                   op=ALU.mult)
            sbank_free[pend["qb2"]] = p.val("ve")

        for m in range(4):
            j = 4 * g + m
            par = m % 2
            mb = m // 2

            def k2v(kt, par=None, mb=None, _d=(par, mb)):
                par, mb = _d if par is None else (par, mb)
                return ar.view("K2", BF16, 128, part=64, base=par * 64,
                               col_off=mb * 4096 + kt * 128)

            for qg in range(4):
                qb2 = (j * 4 + qg) % 2
                p.sync.wait("te", smark[-2] if len(smark) >= 2 else 0)
                qs = ar.view("QS2", BF16, 512, part=64, base=par * 64, col_off=qb2 * 512)
                p.sync.dma_start(out=qs, in_=A["l2_qscr"][j * 64:(j + 1) * 64,
                                                          qg * 512:(qg + 1) * 512]).then_inc(
                    "dsQ%d" % qb2, 16)
                sbank = PSx[2 + qb2]
                p.tensor.wait("dkv")
                p.tensor.wait("dsQ%d" % qb2)
                p.tensor.wait("ve", ve_ones)
                ses = []

                def scores(kt):
                    bank = PSx[kt % DEP if kt % DEP < 2 else 6]
                    p.tensor.matmul(bank[:, 0:512], k2v(kt), qs,
                                    start=True, stop=True).then_inc("te", 1)
                    p.scalar.wait("te", p.val("te"))
                    p.scalar.activation(ar.view("PT2", BF16, 512, col_off=(kt % DEP) * 512),
                                        bank[:, 0:512], AF.Exp)
                    ses.append(p.val("se"))

                def pv(kt):
                    p.tensor.wait("se", ses[kt])
                    if kt == 0:
                        p.tensor.wait("ve", sbank_free[qb2])
                    p.tensor.matmul(sbank[0:65, 0:512],
                                    ar.view("V2", BF16, 65, col_off=kt * 260 + m * 65),
                                    ar.view("PT2", BF16, 512, col_off=(kt % DEP) * 512),
                                    start=(kt == 0), stop=(kt == 31)).then_inc("te", 1)

                for kt in range(DEP):
                    scores(kt)
                if pend is not None:
                    flush(pend)
                for kt in range(DEP, 32):
                    pv(kt - DEP)
                    scores(kt)
                for kt in range(32 - DEP, 32):
                    pv(kt)
                smark.append(p.val("te"))
                alt = it_idx % 2
                p.vector.wait("te", smark[-1])
                p.vector.wait("te", rd_read[alt])
                p.vector.reciprocal(ar.view("RD2", F32, 512, part=1, col_off=alt * 512),
                                    sbank[64:65, 0:512])
                pend = {"alt": alt, "ve_recip": p.val("ve"), "sbank": sbank, "qb2": qb2,
                        "dst": ar.view("S1", BF16, 512, part=64, base=par * 64,
                                       col_off=(j // 2) * 2048 + qg * 512)}
                it_idx += 1
        flush(pend)
        barrier(p)
        ar.free("K2"); ar.free("V2")
    ar.free("QS2"); ar.free("PT2"); ar.free("RD2"); ar.free("RDC")


def emit_l2_post(p, ar, A):
    PSx = p.psB
    ar.alloc("WO2", NCH * 1024 * 2)
    p.gpsimd.wait("te"); p.gpsimd.wait("ve"); p.gpsimd.wait("se")
    p.gpsimd.dma_start(out=ar.view("WO2", BF16, NCH * 1024),
                       in_=A["l2_wout"].rearrange("(c q) f -> q c f", q=128)).then_inc("dgE", 16)
    ar.alloc("SGE", 512 * 4)
    p.tensor.wait("dgE")
    p.tensor.wait("ve")
    emarks = []
    for sl in range(4):
        for oc in range(NCH):
            sbank = PSx[2 + oc % 2]
            if len(emarks) >= 2:
                p.tensor.wait("ge", emarks[-2])
            for c in range(NCH):
                mm = p.tensor.matmul(sbank[:, 0:512],
                                     ar.view("WO2", BF16, 128, col_off=c * 1024 + oc * 128),
                                     ar.view("S1", BF16, 512, col_off=c * 2048 + sl * 512),
                                     start=(c == 0), stop=(c == NCH - 1))
            mm.then_inc("te", 1)
            p.vector.wait("te")
            p.vector.wait("ge")
            g = ar.view("G", BF16, 512, col_off=oc * 2048 + sl * 512)
            sg = ar.view("SGE", F32, 512)
            p.vector.scalar_tensor_tensor(sg, sbank[:, 0:512], 1.0, g, op0=ALU.mult,
                                          op1=ALU.mult).then_inc("ve", 1)
            xv = ar.view("XF", F32, 512, col_off=oc * 2048 + sl * 512)
            p.gpsimd.wait("ve")
            p.gpsimd.tensor_tensor(xv, xv, sg, op=ALU.add).then_inc("ge", 1)
            emarks.append(p.val("ge"))
    ar.free("WO2"); ar.free("SGE"); ar.free("S1")


def emit_layer2(p, ar, A):
    if "c_tris" not in ar.regions:
        ar.alloc("c_tris", 128 * 4)
        p.sync.dma_start(out=ar.view("c_tris", F32, 128), in_=A["c_tris"]).then_inc("dsy", 16)
    ar.alloc("c_ebase", 8 * 4)
    p.sync.dma_start(out=ar.view("c_ebase", F32, 8, part=1), in_=A["c_ebase"]).then_inc("dsy", 16)
    ar.alloc("WrT2", 8 * 8 * 4)
    p.sync.dma_start(out=ar.view("WrT2", F32, 64),
                     in_=A["l2_wr"].rearrange("(c q) f -> q c f", q=128)).then_inc("dsy", 16)
    ar.alloc("rb2", 8 * 4)
    p.sync.dma_start(out=ar.view("rb2", F32, 8, part=1), in_=A["l2_br"]).then_inc("dsy", 16)
    for e in ENGINES:
        getattr(p, e).wait("dsy")
    emit_ln(p, ar, A, want_router=True, want_l2=True, gate_region="WrT2", rbias="rb2")
    barrier(p)
    emit_l2_projections(p, ar, A)
    ar.free("XN2")
    barrier(p)
    emit_top2(p, ar, mode="argmax")
    barrier(p)
    emit_l2_ffn_sparse(p, ar, A)
    barrier(p)
    emit_silu_to_G(p, ar)
    barrier(p)
    emit_l2_attention(p, ar, A)
    barrier(p)
    emit_l2_post(p, ar, A)
    barrier(p)
    ar.free("G"); ar.free("WrT2"); ar.free("rb2")
    ar.free("c_tris"); ar.free("c_ebase")


UPTO = 99
UPTO3 = 99

def emit_layer1(p, ar, A):
    ar.alloc("l1_brow", 1024 * 2)
    p.gpsimd.dma_start(out=ar.view("l1_brow", BF16, 1024, part=1),
                       in_=A["l1_brow_d"]).then_inc("dgA", 16)
    ar.alloc("l1_mbrow", 1024 * 2)
    p.gpsimd.dma_start(out=ar.view("l1_mbrow", BF16, 1024, part=1),
                       in_=A["l1_mbrow_d"]).then_inc("dgA", 16)
    emit_l1_wbuild(p, ar, A)
    emit_ln(p, ar, A, want_router=False, want_l2=False)
    barrier(p)
    emit_l1_attention_a(p, ar, A)
    barrier(p)
    emit_l1_ffn(p, ar, A)
    ar.free("XN")
    barrier(p)
    emit_l1_attention_b(p, ar, A)
    barrier(p)
    ar.free("G"); ar.free("l1_brow"); ar.free("l1_mbrow")


def emit_layer3(p, ar, A):
    ar.alloc("routT3", 8 * 8 * 4)
    p.sync.dma_start(out=ar.view("routT3", F32, 64),
                     in_=A["l3_routT"].rearrange("(c q) f -> q c f", q=128)).then_inc("dsy", 16)
    ar.alloc("rb3", 8 * 4)
    p.sync.dma_start(out=ar.view("rb3", F32, 8, part=1),
                     in_=A["l3_rb"]).then_inc("dsy", 16)
    for e in ENGINES:
        getattr(p, e).wait("dsy")
    emit_ln(p, ar, A, want_router=True, want_l2=False, gate_region="routT3", rbias="rb3")
    barrier(p)
    if UPTO3 < 2:
        return
    emit_l3_projections(p, ar, A)
    barrier(p)
    if UPTO3 < 3:
        return
    emit_top2(p, ar)
    barrier(p)
    if UPTO3 < 4:
        return
    emit_l3_ffn(p, ar, A)
    ar.free("XN")
    barrier(p)
    if UPTO3 < 5:
        return
    emit_silu_to_G(p, ar)
    barrier(p)
    if UPTO3 < 6:
        return
    emit_l3_attention(p, ar, A)
    barrier(p)
    if UPTO3 < 7:
        return
    emit_l3_post(p, ar, A)
    barrier(p)
    ar.free("G"); ar.free("routT3"); ar.free("rb3")


def emit_layer0(p, ar, A):
    if "c_tris" not in ar.regions:
        ar.alloc("c_tris", 128 * 4)
        p.sync.dma_start(out=ar.view("c_tris", F32, 128), in_=A["c_tris"]).then_inc("dsy", 16)
    ar.alloc("c_ebase0", 8 * 4)
    p.sync.dma_start(out=ar.view("c_ebase0", F32, 8, part=1),
                     in_=A["c_ebase0"]).then_inc("dsy", 16)
    ar.alloc("gateT", 8 * 8 * 4)
    p.sync.dma_start(out=ar.view("gateT", F32, 64),
                     in_=A["gateT_d"].rearrange("(c q) f -> q c f", q=128)).then_inc("dsy", 16)
    ar.alloc("c_mskab", 256 * 4, at=205376)
    p.sync.dma_start(out=ar.view("c_mskab", F32, 256),
                     in_=A["c_mskab"]).then_inc("dsy", 16)
    if UPTO < 1:
        return
    emit_ln(p, ar, A, want_router=True, want_l2=False)
    barrier(p)
    if UPTO < 2:
        return
    emit_l0_projections(p, ar, A, 0)
    barrier(p)
    if UPTO < 3:
        return
    emit_top2(p, ar, save_tok=True)
    barrier(p)
    if UPTO < 4:
        return
    emit_l0_ffn_sparse(p, ar, A)
    barrier(p)
    if UPTO < 5:
        return
    emit_silu_to_G(p, ar)
    barrier(p)
    if UPTO < 6:
        return
    ar.free("gateT")
    emit_l0_attention(p, ar, A)
    barrier(p)
    ar.free("G"); ar.free("c_mskab")


def build_program(layers):
    nc = bass.Bass("TRN2", target_bir_lowering=False)
    stack = contextlib.ExitStack()
    A = {}
    A["xT"] = nc.dram_tensor("xT", [D, TOK], F32, kind="ExternalInput").ap()
    A["xoT"] = nc.dram_tensor("xoT", [D, TOK], F32, kind="ExternalOutput").ap()
    cshapes = {
        "c_ones_col": ([128, 1], F32), "c_ones_row": ([1, 128], F32),
        "c_identb": ([128, 128], BF16),
        "c_bchot": ([8, 8 * 128], BF16),
        "c_onesb": ([1, 512], BF16),
        "c_onescolb": ([128, 1], BF16),
    }
    for nm, (shp, dt) in cshapes.items():
        A[nm] = nc.dram_tensor(nm, shp, dt, kind="ExternalInput").ap()
    if 0 in layers:
        A["wq"] = nc.dram_tensor("l0_wq", [D, 64], F32, kind="ExternalInput").ap()
        A["wk"] = nc.dram_tensor("l0_wk", [D, 64], F32, kind="ExternalInput").ap()
        A["wv"] = nc.dram_tensor("l0_wv", [D, D], F32, kind="ExternalInput").ap()
        A["gateT_d"] = nc.dram_tensor("l0_gateT", [D, 8], F32, kind="ExternalInput").ap()
        A["wg"] = nc.dram_tensor("l0_wg", [8, D, 2048], F32, kind="ExternalInput").ap()
        A["wu"] = nc.dram_tensor("l0_wu", [8, D, 2048], F32, kind="ExternalInput").ap()
        A["wd"] = nc.dram_tensor("l0_wd", [8, 2048, D], F32, kind="ExternalInput").ap()
        A["l0_qscr"] = nc.dram_tensor("l0_qscr", [64, TOK], BF16).ap()
        A["xgscr0"] = nc.dram_tensor("xgscr0", [8 * 768, D], BF16).ap()
        A["ygscr0"] = nc.dram_tensor("ygscr0", [8 * 768, D], BF16).ap()
        A["c_ebase0"] = nc.dram_tensor("c_ebase0", [1, 8], F32, kind="ExternalInput").ap()
        if "c_tris" not in A:
            A["c_tris"] = nc.dram_tensor("c_tris", [128, 128], F32, kind="ExternalInput").ap()
        A["ccI"] = nc.dram_tensor("ccI0", [2176, 1024], BF16).ap()
        A["ccO"] = nc.dram_tensor("ccO0", [2 * 2176, 1024], BF16).ap()
        A["c_mskab"] = nc.dram_tensor("c_mskab", [128, 256], F32, kind="ExternalInput").ap()
    if 1 in layers:
        A["l1_proj"] = nc.dram_tensor("l1_proj", [D, D], F32, kind="ExternalInput").ap()
        A["l1_m"] = nc.dram_tensor("l1_m", [128, 8 * 64], F32, kind="ExternalInput").ap()
        A["l1_mixT"] = nc.dram_tensor("l1_mixT", [D, D], F32, kind="ExternalInput").ap()
        A["l1_mbrow_d"] = nc.dram_tensor("l1_mb", [1, D], F32, kind="ExternalInput").ap()
        A["l1_ar"] = nc.dram_tensor("l1_ar", [4, 1024], F32, kind="ExternalInput").ap()
        A["l1_br"] = nc.dram_tensor("l1_br", [4, 256], F32, kind="ExternalInput").ap()
        A["l1_ai"] = nc.dram_tensor("l1_ai", [4, 1024], F32, kind="ExternalInput").ap()
        A["l1_bi"] = nc.dram_tensor("l1_bi", [4, 256], F32, kind="ExternalInput").ap()
        A["l1_brow_d"] = nc.dram_tensor("l1_brow", [1, D], F32, kind="ExternalInput").ap()
        A["l1_wscr_r"] = nc.dram_tensor("l1_wscr_r", [32, 32, 16, 16], F32).ap()
        A["l1_wscr_i"] = nc.dram_tensor("l1_wscr_i", [32, 32, 16, 16], F32).ap()
        A["ccGi"] = nc.dram_tensor("ccGi", [64, 1024], F32).ap()
        A["ccGo"] = nc.dram_tensor("ccGo", [64, 1024], F32).ap()
    if 2 in layers:
        for nm, shp in [("l2_gam", [128, 8]), ("l2_wqkv", [D, 3072]),
                        ("l2_wout", [D, D]), ("l2_w1", [8, D, 4096]),
                        ("l2_b1", [8, 4096]), ("l2_w2", [8, 4096, D]),
                        ("l2_b2", [8, D]), ("l2_wr", [D, 8]), ("l2_br", [1, 8])]:
            A[nm] = nc.dram_tensor(nm, shp, F32, kind="ExternalInput").ap()
        if "c_tris" not in A:
            A["c_tris"] = nc.dram_tensor("c_tris", [128, 128], F32, kind="ExternalInput").ap()
        A["l2_qscr"] = nc.dram_tensor("l2_qscr", [D, TOK], BF16).ap()
        A["xgscr"] = nc.dram_tensor("xgscr", [4096, D], BF16).ap()
        A["ygscr"] = nc.dram_tensor("ygscr", [4096, D], BF16).ap()
        A["c_ebase"] = nc.dram_tensor("c_ebase", [1, 8], F32, kind="ExternalInput").ap()
        A["ccI2"] = nc.dram_tensor("ccI2", [4096, 1024], BF16).ap()
        A["ccO2"] = nc.dram_tensor("ccO2", [8192, 1024], BF16).ap()
    if 3 in layers:
        for nm, shp in [("l3_wq", [D, D]), ("l3_bq", [1, D]), ("l3_wk", [D, 256]),
                        ("l3_bk", [1, 256]), ("l3_wv", [D, 256]), ("l3_bv", [1, 256]),
                        ("l3_lng", [128, 8]), ("l3_lnb", [128, 8]), ("l3_wo", [D, D]),
                        ("l3_bo", [1, D]), ("l3_routT", [D, 8]), ("l3_rb", [1, 8]),
                        ("l3_w1", [8, D, 512]), ("l3_b1", [8, 512]),
                        ("l3_w2", [8, 512, D]), ("l3_b2", [8, D])]:
            A[nm] = nc.dram_tensor(nm, shp, F32, kind="ExternalInput").ap()
        A["l3_qscr"] = nc.dram_tensor("l3_qscr", [D, TOK], BF16).ap()
        A["c_mskT3"] = nc.dram_tensor("c_mskT3", [128, 8 * 512], BF16, kind="ExternalInput").ap()
        A["ccI3"] = nc.dram_tensor("ccI3", [1024, 1024], BF16).ap()
        A["ccO3"] = nc.dram_tensor("ccO3", [2048, 1024], BF16).ap()

    p = Prog(nc)
    ar = Arena(nc, stack)
    p.psB = [stack.enter_context(nc.psum_tensor(f"ps{i}", [128, 512], F32))
             for i in range(7)]
    p.psB = [t[:] for t in p.psB]
    p.psPT = stack.enter_context(nc.psum_tensor("psPT", [128, 1024], BF16))[:]

    ar.alloc("XF", 8 * 2048 * 4, at=0)
    ar.alloc("XN", 8 * 2048 * 2, at=64 * 1024)
    for nm, (shp, dt) in cshapes.items():
        item = 4 if dt == F32 else 2
        ar.alloc(nm, shp[1] * item)
        p.sync.dma_start(out=ar.view(nm, dt, shp[1], part=shp[0]),
                         in_=A[nm]).then_inc("dsy", 16)
    for c in range(NCH):
        p.sync.dma_start(out=ar.view("XF", F32, 2048, col_off=c * 2048),
                         in_=A["xT"][c * 128:(c + 1) * 128, :]).then_inc("dsy", 16)
    for e in ENGINES:
        getattr(p, e).wait("dsy")

    for lay in layers:
        barrier(p)
        if lay == 0:
            emit_layer0(p, ar, A)
        elif lay == 1:
            emit_layer1(p, ar, A)
        elif lay == 2:
            emit_layer2(p, ar, A)
        elif lay == 3:
            emit_layer3(p, ar, A)
        else:
            raise NotImplementedError(lay)

    p.sync.wait("ge"); p.sync.wait("ve"); p.sync.wait("se"); p.sync.wait("te")
    for c in range(NCH):
        p.sync.dma_start(out=A["xoT"][c * 128:(c + 1) * 128, :],
                         in_=ar.view("XF", F32, 2048, col_off=c * 2048)).then_inc("dsy", 16)
    p.sync.wait("dsy")
    p.finalize(stack)
    stack.close()
    return nc


def _mskT3(h):
    triT = np.where(np.arange(128)[:, None] <= np.arange(128)[None, :], 0.0,
                    -1e9).astype(np.float32)
    inv = np.full((128, 128), -1e9, np.float32)
    zer = np.zeros((128, 128), np.float32)
    out = np.zeros((128, 8 * 512), np.float32)
    for d in range(8):
        for m in range(4):
            s = d - 2 * m
            if s < 0 or (h == 1 and s == 0):
                t = zer
            elif s == (0 if h == 0 else 1):
                t = triT
            else:
                t = inv
            out[:, d * 512 + m * 128: d * 512 + (m + 1) * 128] = t
    return out


def _consts(h):
    tri = np.where(np.arange(128)[None, :] <= np.arange(128)[:, None], 0.0,
                   -1e9).astype(np.float32)
    zer = np.zeros((128, 128), np.float32)
    neg = np.full((128, 128), -1e9, np.float32)
    mska, mskb = (tri, neg) if h == 0 else (zer, tri)
    bchot = np.zeros((8, 8 * 128), np.float32)
    for e in range(8):
        bchot[e, e * 128:(e + 1) * 128] = 1.0
    return {
        "c_ones_col": np.ones((128, 1), np.float32),
        "c_ones_row": np.ones((1, 128), np.float32),

        "c_identb": np.eye(128, dtype=np.float32).astype(ml_dtypes.bfloat16),
        "c_mskab": np.concatenate([mska, mskb], axis=1),
        "c_bchot": bchot.astype(ml_dtypes.bfloat16),
        "c_onesb": np.ones((1, 512), np.float32).astype(ml_dtypes.bfloat16),
        "c_onescolb": np.ones((128, 1), np.float32).astype(ml_dtypes.bfloat16),
        "c_mskT3": _mskT3(h).astype(ml_dtypes.bfloat16),
    }


def run_layer0(x, wq, wk, wv, gate_w, wg, wu, wd):
    nc = build_program([0])
    in_maps = []
    for c in range(NCORES):
        b, h = c // 2, c % 2
        own = np.ascontiguousarray(
            x[b].reshape(32, 128, D)[h::2].reshape(TOK, D).T)
        m = _consts(h)
        m.update({
            "xT": own,
            "l0_wq": wq, "l0_wk": wk, "l0_wv": wv,
            "l0_gateT": np.ascontiguousarray(gate_w.T),
            "l0_wg": wg, "l0_wu": wu, "l0_wd": wd,
        })
        in_maps.append(m)
    res = run_bass_kernel_spmd(nc, in_maps, core_ids=list(range(NCORES)))
    out = np.zeros((B, S, D), np.float32)
    for c in range(NCORES):
        b, h = c // 2, c % 2
        xo = res.results[c]["xoT"].T.reshape(NBLK, 128, D)
        o = out[b].reshape(32, 128, D)
        o[h::2] = xo
    return out




def l1_host_inputs(inp):
    proj = np.asarray(inp["L1_attn_projection_1nck"])[0]
    l1_proj = np.ascontiguousarray(proj.transpose(1, 0, 2).reshape(1024, 1024))
    halves = np.asarray(inp["L1_attn_halves"])
    diag = np.asarray(inp["L1_attn_diagonals_nk"])
    iu, ju = np.triu_indices(64, k=1)
    M = np.zeros((128, 512), np.float32)
    for n in range(16):
        bn, cn = (n % 2) * 64, (n // 2) * 64
        blk = np.zeros((64, 64), np.float32)
        blk[iu, ju] = halves[n]
        blk[ju, iu] = halves[n]
        blk[np.arange(64), np.arange(64)] = diag[n]
        M[bn:bn + 64, cn:cn + 64] = blk
    g1r = np.asarray(inp["L1_ffn_G1_R"])[0]
    g2r = np.asarray(inp["L1_ffn_G2_R"])[:, :, 0, :]
    g1i = np.asarray(inp["L1_ffn_G1_I"])[0]
    g2i = np.asarray(inp["L1_ffn_G2_I"])[:, :, 0, :]
    return {
        "l1_proj": l1_proj,
        "l1_m": M,
        "l1_mixT": np.ascontiguousarray(np.asarray(inp["L1_attn_mixer_w"]).T),
        "l1_mb": np.asarray(inp["L1_attn_mixer_b"]).reshape(1, -1),
        "l1_ar": np.ascontiguousarray(g1r.transpose(1, 0, 2).reshape(4, 1024)),
        "l1_br": np.ascontiguousarray(g2r.transpose(0, 2, 1).reshape(4, 256)),
        "l1_ai": np.ascontiguousarray(g1i.transpose(1, 0, 2).reshape(4, 1024)),
        "l1_bi": np.ascontiguousarray(g2i.transpose(0, 2, 1).reshape(4, 256)),
        "l1_brow": np.concatenate([np.asarray(inp["L1_ffn_bias_real"]),
                                   np.asarray(inp["L1_ffn_bias_imag"])]).reshape(1, -1),
    }


def l0_host_inputs(inp):
    return {
        "l0_wq": np.asarray(inp["L0_attn_Wq"]), "l0_wk": np.asarray(inp["L0_attn_Wk"]),
        "l0_wv": np.asarray(inp["L0_attn_Wv"]),
        "l0_gateT": np.ascontiguousarray(np.asarray(inp["L0_ffn_gate_w"]).T),
        "l0_wg": np.asarray(inp["L0_ffn_w_gate"]), "l0_wu": np.asarray(inp["L0_ffn_w_up"]),
        "l0_wd": np.asarray(inp["L0_ffn_w_down"]),
        "c_tris": np.triu(np.ones((128, 128), np.float32), 1),
        "c_ebase0": (np.arange(8, dtype=np.float32) * 768 + 1).reshape(1, 8),
    }


def l3_host_inputs(inp):
    return {
        "l3_wq": np.asarray(inp["L3_attn_Wq"]), "l3_bq": np.asarray(inp["L3_attn_bq"]).reshape(1, -1),
        "l3_wk": np.asarray(inp["L3_attn_Wk"]), "l3_bk": np.asarray(inp["L3_attn_bk"]).reshape(1, -1),
        "l3_wv": np.asarray(inp["L3_attn_Wv"]), "l3_bv": np.asarray(inp["L3_attn_bv"]).reshape(1, -1),
        "l3_lng": np.ascontiguousarray(np.asarray(inp["L3_attn_ln_g"]).reshape(8, 128).T),
        "l3_lnb": np.ascontiguousarray(np.asarray(inp["L3_attn_ln_b"]).reshape(8, 128).T),
        "l3_wo": np.asarray(inp["L3_attn_Wo"]), "l3_bo": np.asarray(inp["L3_attn_bo"]).reshape(1, -1),
        "l3_routT": np.asarray(inp["L3_ffn_router_w"]),
        "l3_rb": np.asarray(inp["L3_ffn_router_b"]).reshape(1, -1),
        "l3_w1": np.asarray(inp["L3_ffn_W1"]), "l3_b1": np.asarray(inp["L3_ffn_b1"]),
        "l3_w2": np.asarray(inp["L3_ffn_W2"]), "l3_b2": np.asarray(inp["L3_ffn_b2"]),
    }


def l2_host_inputs(inp):
    return {
        "l2_gam": np.ascontiguousarray(np.asarray(inp["L2_attn_gamma"]).reshape(8, 128).T),
        "l2_wqkv": np.asarray(inp["L2_attn_w_qkv"]),
        "l2_wout": np.asarray(inp["L2_attn_w_out"]),
        "l2_w1": np.asarray(inp["L2_ffn_W1"]), "l2_b1": np.asarray(inp["L2_ffn_b1"]),
        "l2_w2": np.asarray(inp["L2_ffn_W2"]), "l2_b2": np.asarray(inp["L2_ffn_b2"]),
        "l2_wr": np.asarray(inp["L2_ffn_Wr"]),
        "l2_br": np.asarray(inp["L2_ffn_br"]).reshape(1, -1),
        "c_tris": np.triu(np.ones((128, 128), np.float32), 1),
        "c_ebase": (np.arange(8, dtype=np.float32) * 512 + 1).reshape(1, 8),
    }


def layer_in_maps(layers, x, inp):
    per_layer = {}
    if 0 in layers:
        per_layer.update(l0_host_inputs(inp))
    if 1 in layers:
        per_layer.update(l1_host_inputs(inp))
    if 2 in layers:
        per_layer.update(l2_host_inputs(inp))
    if 3 in layers:
        per_layer.update(l3_host_inputs(inp))
    maps = []
    for c in range(NCORES):
        b, h = c // 2, c % 2
        own = np.ascontiguousarray(x[b].reshape(32, 128, D)[h::2].reshape(TOK, D).T)
        m = _consts(h)
        if 0 not in layers:
            m.pop("c_mskab", None)
        if 3 not in layers:
            m.pop("c_mskT3", None)
        m["xT"] = own
        m.update(per_layer)
        maps.append(m)
    return maps


def run_layers(layers, x, inp):
    nc = build_program(layers)
    maps = layer_in_maps(layers, x, inp)
    res = run_bass_kernel_spmd(nc, maps, core_ids=list(range(NCORES)))
    out = np.zeros((B, S, D), np.float32)
    for c in range(NCORES):
        b, h = c // 2, c % 2
        xo = res.results[c]["xoT"].T.reshape(NBLK, 128, D)
        o = out[b].reshape(32, 128, D)
        o[h::2] = xo
    return out


_PROG_CACHE = {}
FUSED = True


def kernel(**inputs):
    x = np.asarray(inputs["L0_attn_X"], np.float32)
    if FUSED:
        key = "fused"
        if key not in _PROG_CACHE:
            _PROG_CACHE[key] = build_program([0, 1, 2, 3])
        maps = layer_in_maps([0, 1, 2, 3], x, inputs)
        res = run_bass_kernel_spmd(_PROG_CACHE[key], maps,
                                   core_ids=list(range(NCORES)))
        xTs = [np.asarray(res.results[c]["xoT"]) for c in range(NCORES)]
    else:
        xTs = [np.ascontiguousarray(
            x[c // 2].reshape(32, 128, D)[c % 2::2].reshape(TOK, D).T)
            for c in range(NCORES)]
        for lay in range(4):
            if lay not in _PROG_CACHE:
                _PROG_CACHE[lay] = build_program([lay])
            maps = layer_in_maps([lay], x, inputs)
            for c in range(NCORES):
                maps[c]["xT"] = xTs[c]
            res = run_bass_kernel_spmd(_PROG_CACHE[lay], maps,
                                       core_ids=list(range(NCORES)))
            xTs = [np.asarray(res.results[c]["xoT"]) for c in range(NCORES)]
    out = np.zeros((B, S, D), np.float32)
    for c in range(NCORES):
        b, h = c // 2, c % 2
        o = out[b].reshape(32, 128, D)
        o[h::2] = xTs[c].T.reshape(NBLK, 128, D)
    return out
